# version 04
# baseline (speedup 1.0000x reference)
import math
from contextlib import ExitStack

import numpy as np
import ml_dtypes
import concourse.bass as bass
import concourse.mybir as mybir
from concourse.bass_utils import run_bass_kernel_spmd

F32 = mybir.dt.float32
BF16 = mybir.dt.bfloat16
FP8 = mybir.dt.float8e4
U8 = mybir.dt.uint8
U32 = mybir.dt.uint32
AX = mybir.AluOpType
AF = mybir.ActivationFunctionType
AXL = mybir.AxisListType

BT = 2
S = 8192
D = 1024
H = 16
HD = 64
NCORE = 8
NT = BT * S

bf = ml_dtypes.bfloat16


class Sched:
    ENGINES = ("sync", "vector", "scalar", "tensor", "gpsimd")

    def __init__(self):
        self.q = {e: [] for e in self.ENGINES}
        self.cnt = {}
        self.trace = {e: [] for e in self.ENGINES}

    def emit(self, eng, fn, inc=None):
        if inc is None:
            self.q[eng].append(fn)
            self.trace[eng].append(("op", None, 0))
            return None
        sem, amt = inc
        self.cnt[sem.name] = self.cnt.get(sem.name, 0) + amt
        self.trace[eng].append(("inc", sem.name, amt))

        def wrapped(e, fn=fn, sem=sem, amt=amt):
            fn(e).then_inc(sem, amt)
        self.q[eng].append(wrapped)
        return self.cnt[sem.name]

    def wait(self, eng, sem, val):
        if val is None or val <= 0:
            return
        self.trace[eng].append(("wait", sem.name, val))
        self.q[eng].append(lambda e, sem=sem, val=val: e.wait_ge(sem, val))

    def check_deadlock(self):
        pos = {e: 0 for e in self.ENGINES}
        cnt = {}
        progress = True
        while progress:
            progress = False
            for e in self.ENGINES:
                while pos[e] < len(self.trace[e]):
                    kind, sem, val = self.trace[e][pos[e]]
                    if kind == "wait" and cnt.get(sem, 0) < val:
                        break
                    if kind == "inc":
                        cnt[sem] = cnt.get(sem, 0) + val
                    pos[e] += 1
                    progress = True
        stuck = {e: (pos[e], len(self.trace[e])) for e in self.ENGINES
                 if pos[e] < len(self.trace[e])}
        if stuck:
            detail = []
            for e, (p, n) in stuck.items():
                kind, sem, val = self.trace[e][p]
                detail.append(f"{e}@{p}/{n}: waiting {sem}>={val} (has {cnt.get(sem, 0)})")
            raise RuntimeError("DEADLOCK: " + "; ".join(detail))

    def n(self, sem):
        return self.cnt.get(sem.name, 0)

    def replay(self, nc):
        with nc.Block() as block:
            for eng in self.ENGINES:
                lst = self.q[eng]
                if not lst:
                    continue
                deco = getattr(block, eng)

                def body(e, lst=lst):
                    for fn in lst:
                        fn(e)
                deco(body)


_PROG_CACHE = {}


def cached_build(key, fn):
    if key not in _PROG_CACHE:
        _PROG_CACHE[key] = fn()
    return _PROG_CACHE[key]


def build_l0_attn(variant='full'):
    nc = bass.Bass("TRN2", target_bir_lowering=False)
    xT = nc.dram_tensor("xT", [D, S], F32, kind="ExternalInput").ap()
    wqT = nc.dram_tensor("wqT", [D, 256], F32, kind="ExternalInput").ap()
    wkT = nc.dram_tensor("wkT", [D, 256], F32, kind="ExternalInput").ap()
    wvT = nc.dram_tensor("wvT", [D, 256], F32, kind="ExternalInput").ap()
    cosr = nc.dram_tensor("cosr", [128, S], F32, kind="ExternalInput").ap()
    sinr = nc.dram_tensor("sinr", [128, S], F32, kind="ExternalInput").ap()
    oT = nc.dram_tensor("oT", [256, S], F32, kind="ExternalOutput").ap()

    TC = 512
    NTC = S // TC
    KD = D // 128
    NV = S // 128

    sch = Sched()
    with ExitStack() as ctx:
        e = ctx.enter_context
        qtb = [e(nc.sbuf_tensor(f"qtb{i}", [128, S], BF16)) for i in range(2)]
        ktb = [e(nc.sbuf_tensor(f"ktb{i}", [128, S], BF16)) for i in range(2)]
        vpt = e(nc.sbuf_tensor("vpt", [128, NV * 260], BF16))
        ones128 = e(nc.sbuf_tensor("ones128", [128, 1], F32))
        onesb = e(nc.sbuf_tensor("onesb", [1, 64], F32))
        negm0 = e(nc.sbuf_tensor("negm0", [128, 1], F32))
        onesr = e(nc.sbuf_tensor("onesr", [1, 128], F32))
        nrm = e(nc.sbuf_tensor("nrm", [1, 4], F32))
        rmx = e(nc.sbuf_tensor("rmx", [1, 2], F32))
        ld = e(nc.semaphore("ld"))
        vs = e(nc.semaphore("vs"))
        ts = e(nc.semaphore("ts"))
        as_ = e(nc.semaphore("as"))
        gs = e(nc.semaphore("gs"))
        sto = e(nc.semaphore("sto"))

        shuf_mask = []
        for i in range(0, 32, 2):
            shuf_mask += [i + 1, i]

        with ExitStack() as proj:
            p = proj.enter_context
            xts = p(nc.sbuf_tensor("xts", [128, KD * TC], F32))
            xtsb = p(nc.sbuf_tensor("xtsb", [128, KD * TC], BF16))
            cosc = p(nc.sbuf_tensor("cosc", [128, TC], F32))
            sinc = p(nc.sbuf_tensor("sinc", [128, TC], F32))
            wstg = p(nc.sbuf_tensor("wstg", [128, KD * 256], F32))
            wq3b = p(nc.sbuf_tensor("wq3b", [128, KD * 256 * 3], BF16))
            qf = p(nc.sbuf_tensor("qf", [128, TC], F32))
            shf = p(nc.sbuf_tensor("shf", [128, TC], F32))
            am = p(nc.sbuf_tensor("am", [128, TC], F32))
            m2 = p(nc.sbuf_tensor("m2", [128, TC], F32))
            nsq = p(nc.sbuf_tensor("nsq", [128, TC], F32))
            psQK = [p(nc.psum_tensor(f"psQK{i}", [128, TC], F32)) for i in range(4)]
            psV = p(nc.psum_tensor("psV", [128, 256], F32))
            psN = p(nc.psum_tensor("psN", [1, TC], F32))

            for wi, w in enumerate((wqT, wkT, wvT)):
                n_ld = sch.emit("sync", lambda e, w=w: e.dma_start(
                    out=wstg[:].rearrange("p (k m) -> p k m", k=KD),
                    in_=w.rearrange("(k p) m -> p k m", p=128)),
                    inc=(ld, 16))
                sch.wait("vector", ld, n_ld)
                n_cast = sch.emit("vector", lambda e, wi=wi: e.tensor_copy(
                    out=wq3b[:, wi * KD * 256:(wi + 1) * KD * 256], in_=wstg[:]),
                    inc=(vs, 1))
                sch.wait("sync", vs, n_cast)
            sch.emit("vector", lambda e: e.memset(ones128[:], 1.0))
            sch.emit("vector", lambda e: e.memset(onesb[:], 1.0))
            sch.emit("vector", lambda e: e.memset(onesr[:], 1.0))
            sch.emit("vector", lambda e: e.memset(nrm[:], 0.0))

            def set_vones(e):
                return e.memset(
                    vpt[:].rearrange("p (t h c) -> p t h c", h=4, c=65)[:, :, :, 64:65],
                    1.0)
            n_init = sch.emit("vector", set_vones, inc=(vs, 1))
            sch.wait("tensor", vs, n_init)

            if variant == "projW":
                sch.wait("sync", vs, sch.n(vs))
                n_dbg = sch.emit("sync", lambda e: e.dma_start(
                    out=oT[0:128, 0:4],
                    in_=wstg[:, 0:4], allow_slow_non_contiguous=True), inc=(sto, 16))
                sch.wait("sync", sto, n_dbg)
                sch.check_deadlock()
                sch.replay(nc)
                return nc
            nchunks = NTC
            do_qk = do_v = do_rope = do_norm = True
            if variant.startswith("projC"):
                parts = variant[5:].split("-")
                nchunks = int(parts[0])
                if len(parts) > 1:
                    do_qk = "qk" in parts
                    do_v = "v" in parts
                    do_rope = "rope" in parts
                    do_norm = "norm" in parts
            tsN_prev = 0
            red_prev = 0
            for tc in range(nchunks):
                nl = sch.emit("sync", lambda e, tc=tc: e.dma_start(
                    out=xts[:].rearrange("p (k t) -> p k t", k=KD),
                    in_=xT[:, tc * TC:(tc + 1) * TC].rearrange(
                        "(k p) t -> p k t", p=128)), inc=(ld, 16))
                nl = sch.emit("sync", lambda e, tc=tc: e.dma_start(
                    out=cosc[:], in_=cosr[:, tc * TC:(tc + 1) * TC]), inc=(ld, 16))
                nl = sch.emit("sync", lambda e, tc=tc: e.dma_start(
                    out=sinc[:], in_=sinr[:, tc * TC:(tc + 1) * TC]), inc=(ld, 16))
                sch.wait("vector", ld, nl)
                n_xc = sch.emit("vector", lambda e: e.tensor_copy(
                    out=xtsb[:], in_=xts[:]), inc=(vs, 1))
                sch.wait("tensor", vs, n_xc)
                sch.wait("tensor", ld, nl)

                tsQK = {}
                for wi in range(2 if do_qk else 0):
                    for m in range(2):
                        for k in range(KD):
                            def mm(e, wi=wi, m=m, k=k):
                                return e.matmul(
                                    psQK[wi * 2 + m][:],
                                    wq3b[:, wi * KD * 256 + k * 256 + m * 128:
                                         wi * KD * 256 + k * 256 + m * 128 + 128],
                                    xtsb[:, k * TC:(k + 1) * TC],
                                    start=(k == 0), stop=(k == KD - 1))
                            r = sch.emit("tensor", mm,
                                         inc=(ts, 1) if k == KD - 1 else None)
                        tsQK[(wi, m)] = r

                for sv in range(4 if do_v else 0):
                    for k in range(KD):
                        def mmv(e, sv=sv, k=k):
                            return e.matmul(
                                psV[:],
                                xtsb[:, k * TC + sv * 128:k * TC + sv * 128 + 128],
                                wq3b[:, 2 * KD * 256 + k * 256:
                                     2 * KD * 256 + (k + 1) * 256],
                                start=(k == 0), stop=(k == KD - 1))
                        r = sch.emit("tensor", mmv,
                                     inc=(ts, 1) if k == KD - 1 else None)
                    sch.wait("vector", ts, r)

                    def evv(e, tc=tc, sv=sv):
                        tchunk = tc * 4 + sv
                        dst = vpt[:, tchunk * 260:tchunk * 260 + 260].rearrange(
                            "p (h c) -> p h c", h=4)[:, :, 0:64]
                        return e.tensor_copy(
                            out=dst, in_=psV[:].rearrange("p (h c) -> p h c", h=4))
                    n_evv = sch.emit("vector", evv, inc=(vs, 1))
                    if sv < 3:
                        sch.wait("tensor", vs, n_evv)

                for wi in range(2 if (do_rope or do_norm) else 0):
                    dstt = qtb if wi == 0 else ktb
                    for m in range(2):
                        bank = psQK[wi * 2 + m]
                        if not do_norm:
                            pass
                        sch.wait("scalar", ts, tsQK.get((wi, m), 0))
                        sch.wait("scalar", ts, tsN_prev)
                        if do_norm:
                            aN = sch.emit("scalar", lambda e, bank=bank: e.activation(
                                out=nsq[:], in_=bank[:], func=AF.Square), inc=(as_, 1))
                            sch.wait("tensor", as_, aN)
                            if m == 0:
                                sch.wait("tensor", vs, red_prev)
                            tsN_prev = sch.emit("tensor", lambda e, m=m: e.matmul(
                                psN[:], ones128[:], nsq[:],
                                start=(m == 0), stop=(m == 1)), inc=(ts, 1))
                        if not do_rope:
                            continue
                        sch.wait("vector", ts, tsQK.get((wi, m), 0))
                        if do_norm:
                            sch.wait("vector", as_, aN)
                        sch.emit("vector", lambda e, bank=bank: e.tensor_copy(
                            out=qf[:], in_=bank[:]))
                        sch.emit("vector", lambda e: e.stream_shuffle(
                            out=shf[:], in_=qf[:], mask=shuf_mask))
                        sch.emit("vector", lambda e, bank=bank: e.tensor_mul(
                            out=am[:], in0=bank[:], in1=cosc[:]))
                        sch.emit("vector", lambda e: e.tensor_mul(
                            out=m2[:], in0=shf[:], in1=sinc[:]))
                        n_rope = sch.emit(
                            "vector", lambda e, dstt=dstt, m=m, tc=tc: e.tensor_add(
                                out=dstt[m][:, tc * TC:(tc + 1) * TC],
                                in0=am[:], in1=m2[:]), inc=(vs, 1))
                        if m == 1 and do_norm:
                            sch.wait("vector", ts, tsN_prev)
                            n_rx = sch.emit("vector", lambda e, wi=wi: e.tensor_reduce(
                                out=rmx[:, wi:wi + 1], in_=psN[:],
                                axis=AXL.X, op=AX.max), inc=(vs, 1))
                            sch.wait("vector", vs, n_rx)
                            red_prev = sch.emit(
                                "vector", lambda e, wi=wi: e.tensor_tensor(
                                    out=nrm[:, wi:wi + 1], in0=nrm[:, wi:wi + 1],
                                    in1=rmx[:, wi:wi + 1], op=AX.max), inc=(vs, 1))
                sch.wait("sync", vs, sch.n(vs))

            sch.emit("vector", lambda e: e.tensor_mul(
                out=nrm[:, 2:3], in0=nrm[:, 0:1], in1=nrm[:, 1:2]), inc=(vs, 1))
            sch.wait("scalar", vs, sch.n(vs))
            aS = sch.emit("scalar", lambda e: e.activation(
                out=nrm[:, 3:4], in_=nrm[:, 2:3], func=AF.Sqrt), inc=(as_, 1))
            sch.wait("tensor", as_, aS)
            psM = nc.psum_tensor("psM", [128, 1], F32)
            psMh = psM.__enter__()
            nG = sch.emit("tensor", lambda e: e.matmul(
                psMh[:], onesr[:], nrm[:, 3:4], start=True, stop=True), inc=(ts, 1))
            sch.wait("vector", ts, nG)
            n_m0 = sch.emit("vector", lambda e: e.tensor_scalar_mul(
                out=negm0[:], in0=psMh[:], scalar1=-0.125), inc=(vs, 1))
            psM.__exit__(None, None, None)

        if variant == "proj":
            sch.wait("sync", vs, n_m0)
            n_dbg = sch.emit("sync", lambda e: e.dma_start(
                out=oT[0:128, 0:1], in_=negm0[:],
                allow_slow_non_contiguous=True), inc=(sto, 16))
            sch.wait("sync", sto, n_dbg)
            sch.check_deadlock()
            sch.replay(nc)
            return nc
        with ExitStack() as fl:
            p = fl.enter_context
            pbuf = [p(nc.sbuf_tensor(f"pbuf{i}", [128, TC], BF16)) for i in range(2)]
            rb = p(nc.sbuf_tensor("rb", [64, TC], F32))
            rl = p(nc.sbuf_tensor("rl", [1, TC], F32))
            onorm = [p(nc.sbuf_tensor(f"onorm{i}", [64, TC], F32)) for i in range(2)]
            psS = [p(nc.psum_tensor(f"psS{i}", [128, TC], F32)) for i in range(2)]
            psO = p(nc.psum_tensor("psO", [65, TC], F32))
            psB = p(nc.psum_tensor("psB", [64, TC], F32))

            sch.wait("tensor", vs, n_m0)
            sch.wait("scalar", vs, n_m0)

            last_exp = [0, 0]
            last_use_p = [0, 0]
            last_norm = 0
            last_sto = [0, 0]
            qi = 0
            nheads = 1 if variant == "flash1" else 4
            nqt_lim = 1 if variant == "flash1" else NTC
            for h in range(nheads):
                hh, hm = h // 2, h % 2
                qsrc = qtb[hh]
                ksrc = ktb[hh]
                for qt in range(nqt_lim):
                    nkc = 4 * qt + 4
                    for kc in range(nkc):
                        sb = kc % 2
                        diag = kc >= 4 * qt
                        sch.wait("tensor", as_, last_exp[sb])
                        if kc == 0:
                            sch.wait("tensor", vs, last_norm)
                        tS = sch.emit("tensor", lambda e, hm=hm, kc=kc, qt=qt, sb=sb,
                                      qsrc=qsrc, ksrc=ksrc: e.matmul(
                            psS[sb][:],
                            ksrc[hm * 64:hm * 64 + 64, kc * 128:(kc + 1) * 128],
                            qsrc[hm * 64:hm * 64 + 64, qt * TC:(qt + 1) * TC],
                            start=True, stop=True), inc=(ts, 1))
                        sch.wait("scalar", ts, tS)
                        sch.wait("scalar", ts, last_use_p[sb])
                        aE = sch.emit("scalar", lambda e, sb=sb: e.activation(
                            out=pbuf[sb][:], in_=psS[sb][:], func=AF.Exp,
                            bias=negm0[:], scale=0.125), inc=(as_, 1))
                        last_exp[sb] = aE
                        if diag:
                            sch.wait("gpsimd", as_, aE)
                            gM = sch.emit("gpsimd", lambda e, sb=sb, qt=qt, kc=kc:
                                          e.affine_select(
                                out=pbuf[sb][:], in_=pbuf[sb][:],
                                pattern=[[1, TC]], compare_op=AX.is_ge, fill=0.0,
                                base=qt * TC - kc * 128,
                                channel_multiplier=-1), inc=(gs, 1))
                            sch.wait("tensor", gs, gM)
                        else:
                            sch.wait("tensor", as_, aE)
                        tPV = sch.emit(
                            "tensor", lambda e, h=h, kc=kc, sb=sb, nkc=nkc: e.matmul(
                                psO[:],
                                vpt[:, kc * 260 + h * 65:kc * 260 + h * 65 + 65],
                                pbuf[sb][:],
                                start=(kc == 0), stop=(kc == nkc - 1)), inc=(ts, 1))
                        last_use_p[sb] = tPV
                    ob = qi % 2
                    qi += 1
                    sch.wait("vector", ts, tPV)
                    nR = sch.emit("vector", lambda e: e.reciprocal(
                        out=rl[:], in_=psO[64:65, :]), inc=(vs, 1))
                    sch.wait("tensor", vs, nR)
                    tB = sch.emit("tensor", lambda e: e.matmul(
                        psB[:], onesb[:], rl[:], start=True, stop=True), inc=(ts, 1))
                    sch.wait("vector", ts, tB)
                    sch.emit("vector", lambda e: e.tensor_copy(out=rb[:], in_=psB[:]))
                    sch.wait("vector", sto, last_sto[ob])
                    last_norm = sch.emit("vector", lambda e, ob=ob: e.tensor_mul(
                        out=onorm[ob][:], in0=psO[0:64, :], in1=rb[:]), inc=(vs, 1))
                    sch.wait("sync", vs, last_norm)
                    last_sto[ob] = sch.emit(
                        "sync", lambda e, h=h, qt=qt, ob=ob: e.dma_start(
                            out=oT[h * 64:h * 64 + 64, qt * TC:(qt + 1) * TC],
                            in_=onorm[ob][:]), inc=(sto, 16))
            sch.wait("sync", sto, sch.n(sto))
        sch.check_deadlock()
        sch.replay(nc)
    return nc


def rope_tables():
    inv_freq = (1.0 / (10000.0 ** (np.arange(0, HD, 2, dtype=np.float32) / HD))
                ).astype(np.float32)
    pos = np.arange(S, dtype=np.float32)
    freqs = pos[:, None] * inv_freq[None, :]
    emb = np.repeat(freqs, 2, axis=-1)
    cos_t = np.cos(emb).astype(np.float32)
    sin_t = np.sin(emb).astype(np.float32)
    sign = np.where(np.arange(HD) % 2 == 0, -1.0, 1.0).astype(np.float32)
    sinp = sin_t * sign[None, :]
    cosr = np.ascontiguousarray(np.tile(cos_t.T, (2, 1)))
    sinr = np.ascontiguousarray(np.tile(sinp.T, (2, 1)))
    return cosr, sinr


def run_l0_attn(x, Wq, Wk, Wv):
    nc = cached_build("l0_attn", build_l0_attn)
    cosr, sinr = rope_tables()
    in_maps = []
    for c in range(NCORE):
        b = c // 4
        h0 = 4 * (c % 4)
        sl = slice(64 * h0, 64 * h0 + 256)
        in_maps.append({
            "xT": np.ascontiguousarray(x[b].T),
            "wqT": np.ascontiguousarray(Wq.T[:, sl]),
            "wkT": np.ascontiguousarray(Wk.T[:, sl]),
            "wvT": np.ascontiguousarray(Wv.T[:, sl]),
            "cosr": cosr, "sinr": sinr,
        })
    res = run_bass_kernel_spmd(nc, in_maps, core_ids=list(range(NCORE)))
    oT = np.zeros((BT, D, S), np.float32)
    for c in range(NCORE):
        b = c // 4
        h0 = 4 * (c % 4)
        oT[b, 64 * h0:64 * h0 + 256] = res.results[c]["oT"]
    return oT


TB = 2048
TSUB = 512


def build_l0_rest():
    nc = bass.Bass("TRN2", target_bir_lowering=False)
    xTs = nc.dram_tensor("xTs", [D, TB], F32, kind="ExternalInput").ap()
    oTs = nc.dram_tensor("oTs", [D, TB], F32, kind="ExternalInput").ap()
    woT = nc.dram_tensor("woT", [D, D], F32, kind="ExternalInput").ap()
    fker = nc.dram_tensor("fker", [D, D], BF16, kind="ExternalInput").ap()
    fscale = nc.dram_tensor("fscale", [1, 2], F32, kind="ExternalInput").ap()
    fbias = nc.dram_tensor("fbias", [D], BF16, kind="ExternalInput").ap()
    x2T = nc.dram_tensor("x2T", [D, TB], F32, kind="ExternalOutput").ap()
    x2Tb = nc.dram_tensor("x2Tb", [D, TB], BF16, kind="ExternalOutput").ap()

    KD = D // 128
    NS = TB // TSUB

    sch = Sched()
    with ExitStack() as ctx:
        e = ctx.enter_context
        wob = e(nc.sbuf_tensor("wob", [128, KD * D], BF16))
        kq8 = e(nc.sbuf_tensor("kq8", [128, KD * D], FP8))
        fkb = e(nc.sbuf_tensor("fkb", [128, KD * D], BF16))
        biasT = e(nc.sbuf_tensor("biasT", [128, KD], BF16))
        biasF = e(nc.sbuf_tensor("biasF", [128, KD], F32))
        wstg = e(nc.sbuf_tensor("wstg", [128, KD * D], F32))
        scl = e(nc.sbuf_tensor("scl", [1, 4], F32))
        eps1 = e(nc.sbuf_tensor("eps1", [1, 1], F32))
        sclb = e(nc.sbuf_tensor("sclb", [128, 3], F32))
        ones128 = e(nc.sbuf_tensor("ones128", [128, 1], F32))
        onesr = e(nc.sbuf_tensor("onesr", [1, 128], F32))
        xs = e(nc.sbuf_tensor("xs", [128, KD * TSUB], F32))
        x1 = e(nc.sbuf_tensor("x1", [128, KD * TSUB], F32))
        x1b = e(nc.sbuf_tensor("x1b", [128, KD * TSUB], BF16))
        xq8t = e(nc.sbuf_tensor("xq8t", [128, KD * TSUB], FP8))
        ybf = e(nc.sbuf_tensor("ybf", [128, KD * TSUB], BF16))
        sq = e(nc.sbuf_tensor("sq", [128, TSUB], F32))
        stat = e(nc.sbuf_tensor("stat", [1, 4 * TSUB], F32))
        d1 = e(nc.sbuf_tensor("d1", [128, TSUB], F32))
        rbm = e(nc.sbuf_tensor("rbm", [128, TSUB], F32))
        psA = e(nc.psum_tensor("psA", [128, TSUB], F32))
        psSum = e(nc.psum_tensor("psSum", [1, TSUB], F32))
        psSq = e(nc.psum_tensor("psSq", [1, TSUB], F32))
        psBm = e(nc.psum_tensor("psBm", [128, TSUB], F32))
        psBr = e(nc.psum_tensor("psBr", [128, TSUB], F32))
        psS2 = e(nc.psum_tensor("psS2", [128, 3], F32))
        ld = e(nc.semaphore("ld"))
        vs = e(nc.semaphore("vs"))
        ts = e(nc.semaphore("ts"))
        as_ = e(nc.semaphore("as"))
        sto = e(nc.semaphore("sto"))

        n1 = sch.emit("sync", lambda e: e.dma_start(
            out=wstg[:].rearrange("p (k m) -> p k m", k=KD),
            in_=woT.rearrange("(k p) m -> p k m", p=128)), inc=(ld, 16))
        sch.wait("vector", ld, n1)
        n_wo = sch.emit("vector", lambda e: e.tensor_copy(out=wob[:], in_=wstg[:]),
                        inc=(vs, 1))
        n2 = sch.emit("sync", lambda e: e.dma_start(
            out=fkb[:].rearrange("p (k m) -> p k m", k=KD),
            in_=fker.rearrange("(k p) m -> p k m", p=128)), inc=(ld, 16))
        n3 = sch.emit("sync", lambda e: e.dma_start(
            out=biasT[:], in_=fbias.rearrange("(m p) -> p m", p=128),
            allow_slow_non_contiguous=True), inc=(ld, 16))
        n4 = sch.emit("sync", lambda e: e.dma_start(out=scl[:, 0:2], in_=fscale),
                      inc=(ld, 16))
        sch.wait("vector", ld, n4)
        sch.emit("vector", lambda e: e.tensor_copy(out=biasF[:], in_=biasT[:]))
        sch.emit("vector", lambda e: e.memset(ones128[:], 1.0))
        sch.emit("vector", lambda e: e.memset(onesr[:], 1.0))
        sch.emit("vector", lambda e: e.memset(eps1[:], 1e-5))
        n_rc = sch.emit("vector", lambda e: e.reciprocal(
            out=scl[:, 2:4], in_=scl[:, 0:2]), inc=(vs, 1))
        sch.wait("tensor", vs, n_rc)
        tb1 = sch.emit("tensor", lambda e: e.matmul(
            psS2[:, 0:2], onesr[:], scl[:, 2:4], start=True, stop=True), inc=(ts, 1))
        sch.wait("vector", ts, tb1)
        n_s2 = sch.emit("vector", lambda e: e.tensor_copy(
            out=sclb[:, 0:2], in_=psS2[:, 0:2]), inc=(vs, 1))
        n_sp = sch.emit("vector", lambda e: e.tensor_mul(
            out=scl[:, 0:1], in0=scl[:, 0:1], in1=scl[:, 1:2]), inc=(vs, 1))
        sch.wait("tensor", vs, n_sp)
        tb2 = sch.emit("tensor", lambda e: e.matmul(
            psS2[:, 2:3], onesr[:], scl[:, 0:1], start=True, stop=True), inc=(ts, 1))
        sch.wait("vector", ts, tb2)
        n_s3 = sch.emit("vector", lambda e: e.tensor_copy(
            out=sclb[:, 2:3], in_=psS2[:, 2:3]), inc=(vs, 1))
        n_kq = sch.emit("vector", lambda e: e.tensor_scalar(
            out=kq8[:], in0=fkb[:], scalar1=sclb[:, 1:2], scalar2=None,
            op0=AX.mult), inc=(vs, 1))
        sch.wait("tensor", vs, n_kq)

        def emit_ln(tsub, src, dst, dstb=None, gcol=None, bcol=None):
            sch.wait("scalar", vs, sch.n(vs))
            aN_l = None
            for k in range(KD):
                sch.emit("tensor", lambda e, k=k: e.matmul(
                    psSum[:], ones128[:], src[:, k * TSUB:(k + 1) * TSUB],
                    start=(k == 0), stop=(k == KD - 1)))
                if aN_l is not None:
                    sch.wait("scalar", ts, aN_l)
                aSq = sch.emit("scalar", lambda e, k=k: e.activation(
                    out=sq[:], in_=src[:, k * TSUB:(k + 1) * TSUB],
                    func=AF.Square), inc=(as_, 1))
                sch.wait("tensor", as_, aSq)
                aN_l = sch.emit("tensor", lambda e, k=k: e.matmul(
                    psSq[:], ones128[:], sq[:],
                    start=(k == 0), stop=(k == KD - 1)), inc=(ts, 1))
            sch.wait("vector", ts, aN_l)
            T_ = TSUB
            sch.emit("vector", lambda e: e.tensor_scalar_mul(
                out=stat[:, 0:T_], in0=psSum[:], scalar1=1.0 / D))
            sch.emit("vector", lambda e: e.tensor_mul(
                out=stat[:, T_:2 * T_], in0=stat[:, 0:T_], in1=stat[:, 0:T_]))
            n_var = sch.emit("vector", lambda e: e.scalar_tensor_tensor(
                out=stat[:, 2 * T_:3 * T_], in0=psSq[:], scalar=1.0 / D,
                in1=stat[:, T_:2 * T_], op0=AX.mult, op1=AX.subtract), inc=(vs, 1))
            sch.wait("scalar", vs, n_var)
            a_sd = sch.emit("scalar", lambda e: e.activation(
                out=stat[:, 3 * T_:4 * T_], in_=stat[:, 2 * T_:3 * T_],
                func=AF.Sqrt, bias=eps1[:]), inc=(as_, 1))
            sch.wait("vector", as_, a_sd)
            n_rs = sch.emit("vector", lambda e: e.reciprocal(
                out=stat[:, 2 * T_:3 * T_], in_=stat[:, 3 * T_:4 * T_]), inc=(vs, 1))
            sch.wait("tensor", vs, n_rs)
            t_bc = sch.emit("tensor", lambda e: e.matmul(
                psBm[:], onesr[:], stat[:, 0:T_], start=True, stop=True))
            t_bc = sch.emit("tensor", lambda e: e.matmul(
                psBr[:], onesr[:], stat[:, 2 * T_:3 * T_], start=True, stop=True),
                inc=(ts, 1))
            sch.wait("vector", ts, t_bc)
            n_rb = sch.emit("vector", lambda e: e.tensor_copy(
                out=rbm[:], in_=psBr[:]), inc=(vs, 1))
            nv = n_rb
            for k in range(KD):
                sch.emit("vector", lambda e, k=k: e.tensor_sub(
                    out=d1[:], in0=src[:, k * TSUB:(k + 1) * TSUB], in1=psBm[:]))
                if gcol is None:
                    nv = sch.emit("vector", lambda e, k=k: e.tensor_mul(
                        out=dst[:, k * TSUB:(k + 1) * TSUB], in0=d1[:], in1=rbm[:]),
                        inc=(vs, 1))
                else:
                    sch.emit("vector", lambda e, k=k: e.tensor_mul(
                        out=d1[:], in0=d1[:], in1=rbm[:]))
                    nv = sch.emit("vector", lambda e, k=k: e.tensor_scalar(
                        out=dst[:, k * TSUB:(k + 1) * TSUB], in0=d1[:],
                        scalar1=gcol[:, k:k + 1], scalar2=bcol[:, k:k + 1],
                        op0=AX.mult, op1=AX.add), inc=(vs, 1))
                if dstb is not None:
                    nv = sch.emit("vector", lambda e, k=k: e.tensor_copy(
                        out=dstb[:, k * TSUB:(k + 1) * TSUB],
                        in_=dst[:, k * TSUB:(k + 1) * TSUB]), inc=(vs, 1))
            return nv

        for su in range(NS):
            t0 = su * TSUB
            nl = sch.emit("sync", lambda e, t0=t0: e.dma_start(
                out=xs[:].rearrange("p (k t) -> p k t", k=KD),
                in_=xTs[:, t0:t0 + TSUB].rearrange("(k p) t -> p k t", p=128)),
                inc=(ld, 16))
            nl = sch.emit("sync", lambda e, t0=t0: e.dma_start(
                out=x1[:].rearrange("p (k t) -> p k t", k=KD),
                in_=oTs[:, t0:t0 + TSUB].rearrange("(k p) t -> p k t", p=128)),
                inc=(ld, 16))
            sch.wait("vector", ld, nl)
            n_ob = sch.emit("vector", lambda e: e.tensor_copy(
                out=ybf[:], in_=x1[:]), inc=(vs, 1))
            sch.wait("tensor", vs, n_ob)
            for m in range(KD):
                for k in range(KD):
                    tA = sch.emit("tensor", lambda e, m=m, k=k: e.matmul(
                        psA[:], wob[:, k * D + m * 128:k * D + m * 128 + 128],
                        ybf[:, k * TSUB:(k + 1) * TSUB],
                        start=(k == 0), stop=(k == KD - 1)),
                        inc=(ts, 1) if k == KD - 1 else None)
                sch.wait("vector", ts, tA)
                n_res = sch.emit("vector", lambda e, m=m: e.tensor_add(
                    out=xs[:, m * TSUB:(m + 1) * TSUB],
                    in0=xs[:, m * TSUB:(m + 1) * TSUB], in1=psA[:]), inc=(vs, 1))
                sch.wait("tensor", vs, n_res)
            emit_ln(su, xs, x1, x1b)
            n_q = sch.emit("vector", lambda e: e.tensor_scalar(
                out=xq8t[:], in0=x1b[:], scalar1=sclb[:, 0:1], scalar2=None,
                op0=AX.mult), inc=(vs, 1))
            sch.wait("tensor", vs, n_q)
            for m in range(KD):
                for k in range(KD):
                    tF = sch.emit("tensor", lambda e, m=m, k=k: e.matmul(
                        psA[:], kq8[:, k * D + m * 128:k * D + m * 128 + 128],
                        xq8t[:, k * TSUB:(k + 1) * TSUB],
                        start=(k == 0), stop=(k == KD - 1)),
                        inc=(ts, 1) if k == KD - 1 else None)
                sch.wait("scalar", ts, tF)
                aY = sch.emit("scalar", lambda e, m=m: e.activation(
                    out=ybf[:, m * TSUB:(m + 1) * TSUB], in_=psA[:],
                    func=AF.Identity, scale=sclb[:, 2:3]), inc=(as_, 1))
                sch.wait("vector", as_, aY)
                sch.emit("vector", lambda e, m=m: e.tensor_scalar(
                    out=ybf[:, m * TSUB:(m + 1) * TSUB],
                    in0=ybf[:, m * TSUB:(m + 1) * TSUB],
                    scalar1=biasF[:, m:m + 1], scalar2=None, op0=AX.add))
                n_r2 = sch.emit("vector", lambda e, m=m: e.tensor_add(
                    out=x1[:, m * TSUB:(m + 1) * TSUB],
                    in0=x1[:, m * TSUB:(m + 1) * TSUB],
                    in1=ybf[:, m * TSUB:(m + 1) * TSUB]), inc=(vs, 1))
                sch.wait("tensor", vs, n_r2)
            emit_ln(su, x1, xs, x1b)
            n_last = sch.n(vs)
            sch.wait("sync", vs, n_last)
            n_st = sch.emit("sync", lambda e, t0=t0: e.dma_start(
                out=x2T[:, t0:t0 + TSUB].rearrange("(k p) t -> p k t", p=128),
                in_=xs[:].rearrange("p (k t) -> p k t", k=KD)), inc=(sto, 16))
            n_st = sch.emit("sync", lambda e, t0=t0: e.dma_start(
                out=x2Tb[:, t0:t0 + TSUB].rearrange("(k p) t -> p k t", p=128),
                in_=x1b[:].rearrange("p (k t) -> p k t", k=KD)), inc=(sto, 16))
            sch.wait("vector", sto, n_st)
        sch.wait("sync", sto, sch.n(sto))
        sch.check_deadlock()
        sch.replay(nc)
    return nc


def run_l0_rest(x, oT, Wo, fker, s_in, s_k, fbias):
    nc = cached_build("l0_rest", build_l0_rest)
    woT = np.ascontiguousarray(Wo.T)
    fscale = np.array([[s_in, s_k]], np.float32)
    in_maps = []
    for c in range(NCORE):
        sl = slice(c * 1024, (c + 1) * 1024)
        xTs = np.concatenate([x[0].T[:, sl], x[1].T[:, sl]], axis=1)
        oTs = np.concatenate([oT[0][:, sl], oT[1][:, sl]], axis=1)
        in_maps.append({
            "xTs": np.ascontiguousarray(xTs),
            "oTs": np.ascontiguousarray(oTs),
            "woT": woT, "fker": fker, "fscale": fscale, "fbias": fbias,
        })
    res = run_bass_kernel_spmd(nc, in_maps, core_ids=list(range(NCORE)))
    x2T = np.zeros((BT, D, S), np.float32)
    x2Tb = np.zeros((BT, D, S), bf)
    for c in range(NCORE):
        sl = slice(c * 1024, (c + 1) * 1024)
        x2T[0][:, sl] = res.results[c]["x2T"][:, :1024]
        x2T[1][:, sl] = res.results[c]["x2T"][:, 1024:]
        x2Tb[0][:, sl] = res.results[c]["x2Tb"][:, :1024]
        x2Tb[1][:, sl] = res.results[c]["x2Tb"][:, 1024:]
    return x2T, x2Tb


SK = 64
SL = 128
SB_ = 32
SLAB = SK * 3


def build_rkv():
    nc = bass.Bass("TRN2", target_bir_lowering=False)
    x2b = nc.dram_tensor("x2b", [D, NT], BF16, kind="ExternalInput").ap()
    w3T = nc.dram_tensor("w3T", [D, 384], F32, kind="ExternalInput").ap()
    yTb = nc.dram_tensor("yTb", [BT, 128, S], BF16, kind="ExternalOutput").ap()

    KD = D // 128
    STEPS = SL + SB_
    TC = 512
    NTC = S // TC

    sch = Sched()
    with ExitStack() as ctx:
        e = ctx.enter_context
        w3b = e(nc.sbuf_tensor("w3b", [128, KD * 384], BF16))
        ld = e(nc.semaphore("ld"))
        vs = e(nc.semaphore("vs"))
        ts = e(nc.semaphore("ts"))
        as_ = e(nc.semaphore("as"))
        sto = e(nc.semaphore("sto"))
        with nc.sbuf_tensor("wstg", [128, KD * 384], F32) as wstg:
            n1 = sch.emit("sync", lambda e: e.dma_start(
                out=wstg[:].rearrange("p (k m) -> p k m", k=KD),
                in_=w3T.rearrange("(k p) m -> p k m", p=128)), inc=(ld, 16))
            sch.wait("vector", ld, n1)
            n_w = sch.emit("vector", lambda e: e.tensor_copy(out=w3b[:], in_=wstg[:]),
                           inc=(vs, 1))
        xs = e(nc.sbuf_tensor("xs", [128, STEPS * SLAB], F32))
        st = e(nc.sbuf_tensor("st", [128, SLAB], F32))
        xcs = [e(nc.sbuf_tensor(f"xcs{i}", [128, KD * TC], BF16)) for i in range(2)]
        spr = e(nc.sbuf_tensor("spr", [128, S], BF16))
        mt = e(nc.sbuf_tensor("mt", [128, S], BF16))
        hh = e(nc.sbuf_tensor("hh", [128, S], BF16))
        c09 = e(nc.sbuf_tensor("c09", [128, 2048], F32))
        hcar = e(nc.sbuf_tensor("hcar", [128, 1], F32))
        psR = [e(nc.psum_tensor(f"psR{i}", [128, TC], F32)) for i in range(3)]

        sch.emit("vector", lambda e: e.memset(c09[:], 0.9))
        n_w = sch.emit("vector", lambda e: e.memset(hcar[:], 0.0), inc=(vs, 1))
        sch.wait("tensor", vs, n_w)
        sch.wait("sync", vs, n_w)

        n_store = 0
        for b in range(BT):
            nz = sch.emit("vector", lambda e: e.memset(
                xs[:].rearrange("p (sl k c) -> p sl k c", sl=STEPS, k=SK)
                [:, 0:SB_, 0:1, :], 0.0), inc=(vs, 1))
            for tc in range(NTC):
                buf = tc % 2
                nl = sch.emit("sync", lambda e, b=b, tc=tc, buf=buf: e.dma_start(
                    out=xcs[buf][:].rearrange("p (k t) -> p k t", k=KD),
                    in_=x2b[:, b * S + tc * TC:b * S + (tc + 1) * TC].rearrange(
                        "(k p) t -> p k t", p=128)), inc=(ld, 16))
                sch.wait("tensor", ld, nl)
                for c in range(3):
                    for k in range(KD):
                        tR = sch.emit("tensor", lambda e, c=c, k=k, buf=buf: e.matmul(
                            psR[c][:],
                            w3b[:, k * 384 + c * 128:k * 384 + c * 128 + 128],
                            xcs[buf][:, k * TC:(k + 1) * TC],
                            start=(k == 0), stop=(k == KD - 1)),
                            inc=(ts, 1) if k == KD - 1 else None)
                    sch.wait("scalar", ts, tR)
                    k0 = tc * 4
                    aE = sch.emit("scalar", lambda e, c=c, k0=k0: e.activation(
                        out=xs[:].rearrange(
                            "p (sl k c) -> p sl k c", sl=STEPS, k=SK)
                        [:, SB_:SB_ + SL, k0:k0 + 4, c:c + 1]
                        .rearrange("p sl k c -> p (k sl c)" if False else
                                   "p sl k c -> p k sl c"),
                        in_=psR[c][:].rearrange("p (ks r) -> p ks r", ks=4),
                        func=AF.Identity, scale=0.5), inc=(as_, 1))
                sch.wait("tensor", as_, sch.n(as_))
                if tc + 1 < NTC:
                    sch.wait("sync", as_, sch.n(as_) - 3)
            sch.wait("vector", as_, sch.n(as_))
            xr = xs[:].rearrange("p (sl k c) -> p sl k c", sl=STEPS, k=SK)
            n_wu = sch.emit("vector", lambda e, xr=xr: e.tensor_copy(
                out=xr[:, 0:SB_, 1:SK, :], in_=xr[:, SL:SL + SB_, 0:SK - 1, :]),
                inc=(vs, 1))
            n_st0 = sch.emit("vector", lambda e: e.memset(st[:], 0.0), inc=(vs, 1))
            for sl in range(STEPS):
                sch.emit("vector", lambda e, sl=sl: e.scalar_tensor_tensor(
                    out=xs[:, sl * SLAB:(sl + 1) * SLAB], in0=st[:], scalar=0.5,
                    in1=xs[:, sl * SLAB:(sl + 1) * SLAB],
                    op0=AX.mult, op1=AX.add))
                sch.emit("vector", lambda e, sl=sl: e.scalar_tensor_tensor(
                    out=st[:], in0=xs[:, sl * SLAB:(sl + 1) * SLAB], scalar=1.0,
                    in1=xs[:, sl * SLAB:(sl + 1) * SLAB],
                    op0=AX.is_lt, op1=AX.mult))
            def reg(c):
                return xs[:].rearrange(
                    "p (sl k c) -> p sl k c", sl=STEPS, k=SK)[
                    :, SB_:STEPS, :, c:c + 1].rearrange("p sl k c -> p k sl c")
            sch.emit("vector", lambda e: e.tensor_single_scalar(
                out=spr[:], in_=reg(0), scalar=1.0, op=AX.is_ge))
            sch.emit("vector", lambda e: e.tensor_tensor(
                out=mt[:], in0=reg(1), in1=reg(2), op=AX.min))
            sch.emit("vector", lambda e: e.tensor_single_scalar(
                out=mt[:], in_=mt[:], scalar=1.0, op=AX.is_ge))
            for ch in range(4):
                s0 = ch * 2048
                if ch == 0:
                    n_sc = sch.emit("vector", lambda e, s0=s0: e.tensor_tensor_scan(
                        out=hh[:, s0:s0 + 2048], data0=c09[:],
                        data1=mt[:, s0:s0 + 2048], initial=0.0,
                        op0=AX.mult, op1=AX.add), inc=(vs, 1))
                else:
                    n_sc = sch.emit("vector", lambda e, s0=s0: e.tensor_tensor_scan(
                        out=hh[:, s0:s0 + 2048], data0=c09[:],
                        data1=mt[:, s0:s0 + 2048], initial=hcar[:],
                        op0=AX.mult, op1=AX.add), inc=(vs, 1))
                sch.wait("vector", vs, n_sc)
                n_hc = sch.emit("vector", lambda e, s0=s0: e.tensor_copy(
                    out=hcar[:], in_=hh[:, s0 + 2047:s0 + 2048]), inc=(vs, 1))
                sch.wait("vector", vs, n_hc)
            sch.wait("vector", sto, n_store)
            n_y = sch.emit("vector", lambda e: e.tensor_mul(
                out=mt[:], in0=spr[:], in1=hh[:]), inc=(vs, 1))
            sch.wait("sync", vs, n_y)
            n_store = sch.emit("sync", lambda e, b=b: e.dma_start(
                out=yTb[b], in_=mt[:]), inc=(sto, 16))
        sch.wait("sync", sto, sch.n(sto))
        sch.check_deadlock()
        sch.replay(nc)
    return nc


def run_rkv(x2Tb, Wr, Wk, Wv):
    nc = cached_build("rkv", build_rkv)
    x2full = np.ascontiguousarray(
        np.concatenate([x2Tb[0], x2Tb[1]], axis=1))
    in_maps = []
    for c in range(NCORE):
        ds = slice(c * 128, (c + 1) * 128)
        w3 = np.concatenate(
            [Wr.T[:, ds], Wk.T[:, ds], Wv.T[:, ds]], axis=1)
        in_maps.append({"x2b": x2full, "w3T": np.ascontiguousarray(w3)})
    res = run_bass_kernel_spmd(nc, in_maps, core_ids=list(range(NCORE)))
    yT = np.zeros((BT, D, S), bf)
    for c in range(NCORE):
        ds = slice(c * 128, (c + 1) * 128)
        yT[:, ds] = res.results[c]["yTb"]
    return yT


def build_postrkv():
    nc = bass.Bass("TRN2", target_bir_lowering=False)
    x2Ts = nc.dram_tensor("x2Ts", [D, TB], F32, kind="ExternalInput").ap()
    yTs = nc.dram_tensor("yTs", [D, TB], BF16, kind="ExternalInput").ap()
    woT = nc.dram_tensor("woT", [D, D], F32, kind="ExternalInput").ap()
    ln1g = nc.dram_tensor("ln1g", [D], F32, kind="ExternalInput").ap()
    ln1b = nc.dram_tensor("ln1b", [D], F32, kind="ExternalInput").ap()
    w1T = nc.dram_tensor("w1T", [D, 4096], F32, kind="ExternalInput").ap()
    b1v = nc.dram_tensor("b1v", [4096], F32, kind="ExternalInput").ap()
    x1T = nc.dram_tensor("x1T", [D, TB], F32, kind="ExternalOutput").ap()
    x1Tb = nc.dram_tensor("x1Tb", [D, TB], BF16, kind="ExternalOutput").ap()
    h1T = nc.dram_tensor("h1T", [4096, TB], F32, kind="ExternalOutput").ap()

    KD = D // 128
    NS = TB // TSUB

    sch = Sched()
    with ExitStack() as ctx:
        e = ctx.enter_context
        ld = e(nc.semaphore("ld"))
        vs = e(nc.semaphore("vs"))
        ts = e(nc.semaphore("ts"))
        as_ = e(nc.semaphore("as"))
        sto = e(nc.semaphore("sto"))
        wob = e(nc.sbuf_tensor("wob", [128, KD * D], BF16))
        w1b = e(nc.sbuf_tensor("w1b", [128, KD * 4096], BF16))
        gcol = e(nc.sbuf_tensor("gcol", [128, KD], F32))
        bcol = e(nc.sbuf_tensor("bcol", [128, KD], F32))
        b1c = e(nc.sbuf_tensor("b1c", [128, 32], F32))
        eps1 = e(nc.sbuf_tensor("eps1", [1, 1], F32))
        ones128 = e(nc.sbuf_tensor("ones128", [128, 1], F32))
        onesr = e(nc.sbuf_tensor("onesr", [1, 128], F32))
        with nc.sbuf_tensor("wstg", [128, KD * D], F32) as wstg:
            n1 = sch.emit("sync", lambda e: e.dma_start(
                out=wstg[:].rearrange("p (k m) -> p k m", k=KD),
                in_=woT.rearrange("(k p) m -> p k m", p=128)), inc=(ld, 16))
            sch.wait("vector", ld, n1)
            n_c = sch.emit("vector", lambda e: e.tensor_copy(
                out=wob[:], in_=wstg[:]), inc=(vs, 1))
            sch.wait("sync", vs, n_c)
            for q in range(4):
                n1 = sch.emit("sync", lambda e, q=q: e.dma_start(
                    out=wstg[:].rearrange("p (k m) -> p k m", k=KD),
                    in_=w1T[:, q * 1024:(q + 1) * 1024].rearrange(
                        "(k p) m -> p k m", p=128)), inc=(ld, 16))
                sch.wait("vector", ld, n1)
                def cst(e, q=q):
                    dst = w1b[:].rearrange("p (k m) -> p k m", k=KD)[
                        :, :, q * 1024:(q + 1) * 1024]
                    return e.tensor_copy(
                        out=dst, in_=wstg[:].rearrange("p (k m) -> p k m", k=KD))
                n_c = sch.emit("vector", cst, inc=(vs, 1))
                sch.wait("sync", vs, n_c)
        xs = e(nc.sbuf_tensor("xs", [128, KD * TSUB], F32))
        x1t = e(nc.sbuf_tensor("x1t", [128, KD * TSUB], F32))
        x1bt = e(nc.sbuf_tensor("x1bt", [128, KD * TSUB], BF16))
        ybt = e(nc.sbuf_tensor("ybt", [128, KD * TSUB], BF16))
        h1s = e(nc.sbuf_tensor("h1s", [128, 8 * TSUB], F32))
        sq = e(nc.sbuf_tensor("sq", [128, TSUB], F32))
        stat = e(nc.sbuf_tensor("stat", [1, 4 * TSUB], F32))
        d1 = e(nc.sbuf_tensor("d1", [128, TSUB], F32))
        rbm = e(nc.sbuf_tensor("rbm", [128, TSUB], F32))
        psA = e(nc.psum_tensor("psA", [128, TSUB], F32))
        psSum = e(nc.psum_tensor("psSum", [1, TSUB], F32))
        psSq = e(nc.psum_tensor("psSq", [1, TSUB], F32))
        psBm = e(nc.psum_tensor("psBm", [128, TSUB], F32))
        psBr = e(nc.psum_tensor("psBr", [128, TSUB], F32))

        nl = sch.emit("sync", lambda e: e.dma_start(
            out=gcol[:], in_=ln1g.rearrange("(m p) -> p m", p=128),
            allow_slow_non_contiguous=True), inc=(ld, 16))
        nl = sch.emit("sync", lambda e: e.dma_start(
            out=bcol[:], in_=ln1b.rearrange("(m p) -> p m", p=128),
            allow_slow_non_contiguous=True), inc=(ld, 16))
        nl = sch.emit("sync", lambda e: e.dma_start(
            out=b1c[:], in_=b1v.rearrange("(m p) -> p m", p=128),
            allow_slow_non_contiguous=True), inc=(ld, 16))
        sch.wait("vector", ld, nl)
        sch.emit("vector", lambda e: e.memset(eps1[:], 1e-5))
        sch.emit("vector", lambda e: e.memset(ones128[:], 1.0))
        n_i = sch.emit("vector", lambda e: e.memset(onesr[:], 1.0), inc=(vs, 1))
        n_b = sch.emit("vector", lambda e: e.tensor_scalar_mul(
            out=b1c[:], in0=b1c[:], scalar1=0.5), inc=(vs, 1))
        sch.wait("tensor", vs, n_b)
        sch.wait("scalar", vs, n_b)

        def emit_ln(src, dst, dstb=None, g=None, b=None):
            sch.wait("scalar", vs, sch.n(vs))
            aN_l = None
            for k in range(KD):
                sch.emit("tensor", lambda e, k=k: e.matmul(
                    psSum[:], ones128[:], src[:, k * TSUB:(k + 1) * TSUB],
                    start=(k == 0), stop=(k == KD - 1)))
                if aN_l is not None:
                    sch.wait("scalar", ts, aN_l)
                aSq = sch.emit("scalar", lambda e, k=k: e.activation(
                    out=sq[:], in_=src[:, k * TSUB:(k + 1) * TSUB],
                    func=AF.Square), inc=(as_, 1))
                sch.wait("tensor", as_, aSq)
                aN_l = sch.emit("tensor", lambda e, k=k: e.matmul(
                    psSq[:], ones128[:], sq[:],
                    start=(k == 0), stop=(k == KD - 1)), inc=(ts, 1))
            sch.wait("vector", ts, aN_l)
            T_ = TSUB
            sch.emit("vector", lambda e: e.tensor_scalar_mul(
                out=stat[:, 0:T_], in0=psSum[:], scalar1=1.0 / D))
            sch.emit("vector", lambda e: e.tensor_mul(
                out=stat[:, T_:2 * T_], in0=stat[:, 0:T_], in1=stat[:, 0:T_]))
            n_var = sch.emit("vector", lambda e: e.scalar_tensor_tensor(
                out=stat[:, 2 * T_:3 * T_], in0=psSq[:], scalar=1.0 / D,
                in1=stat[:, T_:2 * T_], op0=AX.mult, op1=AX.subtract), inc=(vs, 1))
            sch.wait("scalar", vs, n_var)
            a_sd = sch.emit("scalar", lambda e: e.activation(
                out=stat[:, 3 * T_:4 * T_], in_=stat[:, 2 * T_:3 * T_],
                func=AF.Sqrt, bias=eps1[:]), inc=(as_, 1))
            sch.wait("vector", as_, a_sd)
            n_rs = sch.emit("vector", lambda e: e.reciprocal(
                out=stat[:, 2 * T_:3 * T_], in_=stat[:, 3 * T_:4 * T_]), inc=(vs, 1))
            sch.wait("tensor", vs, n_rs)
            sch.emit("tensor", lambda e: e.matmul(
                psBm[:], onesr[:], stat[:, 0:T_], start=True, stop=True))
            t_bc = sch.emit("tensor", lambda e: e.matmul(
                psBr[:], onesr[:], stat[:, 2 * T_:3 * T_], start=True, stop=True),
                inc=(ts, 1))
            sch.wait("vector", ts, t_bc)
            n_rb = sch.emit("vector", lambda e: e.tensor_copy(
                out=rbm[:], in_=psBr[:]), inc=(vs, 1))
            nv = n_rb
            for k in range(KD):
                sch.emit("vector", lambda e, k=k: e.tensor_sub(
                    out=d1[:], in0=src[:, k * TSUB:(k + 1) * TSUB], in1=psBm[:]))
                if g is None:
                    nv = sch.emit("vector", lambda e, k=k: e.tensor_mul(
                        out=dst[:, k * TSUB:(k + 1) * TSUB], in0=d1[:], in1=rbm[:]),
                        inc=(vs, 1))
                else:
                    sch.emit("vector", lambda e, k=k: e.tensor_mul(
                        out=d1[:], in0=d1[:], in1=rbm[:]))
                    nv = sch.emit("vector", lambda e, k=k: e.tensor_scalar(
                        out=dst[:, k * TSUB:(k + 1) * TSUB], in0=d1[:],
                        scalar1=g[:, k:k + 1], scalar2=b[:, k:k + 1],
                        op0=AX.mult, op1=AX.add), inc=(vs, 1))
                if dstb is not None:
                    nv = sch.emit("vector", lambda e, k=k: e.tensor_copy(
                        out=dstb[:, k * TSUB:(k + 1) * TSUB],
                        in_=dst[:, k * TSUB:(k + 1) * TSUB]), inc=(vs, 1))
            return nv

        n_sto_h = 0
        for su in range(NS):
            t0 = su * TSUB
            nl = sch.emit("sync", lambda e, t0=t0: e.dma_start(
                out=xs[:].rearrange("p (k t) -> p k t", k=KD),
                in_=x2Ts[:, t0:t0 + TSUB].rearrange("(k p) t -> p k t", p=128)),
                inc=(ld, 16))
            nl = sch.emit("sync", lambda e, t0=t0: e.dma_start(
                out=ybt[:].rearrange("p (k t) -> p k t", k=KD),
                in_=yTs[:, t0:t0 + TSUB].rearrange("(k p) t -> p k t", p=128)),
                inc=(ld, 16))
            sch.wait("tensor", ld, nl)
            sch.wait("vector", ld, nl)
            for m in range(KD):
                for k in range(KD):
                    tA = sch.emit("tensor", lambda e, m=m, k=k: e.matmul(
                        psA[:], wob[:, k * D + m * 128:k * D + m * 128 + 128],
                        ybt[:, k * TSUB:(k + 1) * TSUB],
                        start=(k == 0), stop=(k == KD - 1)),
                        inc=(ts, 1) if k == KD - 1 else None)
                sch.wait("vector", ts, tA)
                n_res = sch.emit("vector", lambda e, m=m: e.tensor_add(
                    out=xs[:, m * TSUB:(m + 1) * TSUB],
                    in0=xs[:, m * TSUB:(m + 1) * TSUB], in1=psA[:]), inc=(vs, 1))
                sch.wait("tensor", vs, n_res)
            emit_ln(xs, x1t, x1bt, gcol, bcol)
            n_lnv = sch.n(vs)
            sch.wait("sync", vs, n_lnv)
            n_st = sch.emit("sync", lambda e, t0=t0: e.dma_start(
                out=x1T[:, t0:t0 + TSUB].rearrange("(k p) t -> p k t", p=128),
                in_=x1t[:].rearrange("p (k t) -> p k t", k=KD)), inc=(sto, 16))
            n_st = sch.emit("sync", lambda e, t0=t0: e.dma_start(
                out=x1Tb[:, t0:t0 + TSUB].rearrange("(k p) t -> p k t", p=128),
                in_=x1bt[:].rearrange("p (k t) -> p k t", k=KD)), inc=(sto, 16))
            sch.wait("tensor", vs, n_lnv)
            for mg in range(4):
                sch.wait("tensor", sto, n_sto_h)
                for mm in range(8):
                    m = mg * 8 + mm
                    for k in range(KD):
                        tH = sch.emit("tensor", lambda e, m=m, k=k: e.matmul(
                            psA[:], w1b[:, k * 4096 + m * 128:k * 4096 + m * 128 + 128],
                            x1bt[:, k * TSUB:(k + 1) * TSUB],
                            start=(k == 0), stop=(k == KD - 1)),
                            inc=(ts, 1) if k == KD - 1 else None)
                    sch.wait("scalar", ts, tH)
                    aH = sch.emit("scalar", lambda e, m=m, mm=mm: e.activation(
                        out=h1s[:, mm * TSUB:(mm + 1) * TSUB], in_=psA[:],
                        func=AF.Identity, scale=0.5, bias=b1c[:, m:m + 1]),
                        inc=(as_, 1))
                    sch.wait("tensor", as_, aH)
                sch.wait("sync", as_, sch.n(as_))
                n_sto_h = sch.emit("sync", lambda e, t0=t0, mg=mg: e.dma_start(
                    out=h1T[mg * 1024:(mg + 1) * 1024, t0:t0 + TSUB].rearrange(
                        "(k p) t -> p k t", p=128),
                    in_=h1s[:].rearrange("p (k t) -> p k t", k=KD)), inc=(sto, 16))
            sch.wait("vector", sto, n_st)
        sch.wait("sync", sto, sch.n(sto))
        sch.check_deadlock()
        sch.replay(nc)
    return nc


def build_lif(skw, cw, nsec):
    steps = SL + SB_
    slabw = skw * cw
    nc = bass.Bass("TRN2", target_bir_lowering=False)
    hs = nc.dram_tensor("hs", [nsec, 128, steps * slabw], F32,
                        kind="ExternalInput").ap()
    sp = nc.dram_tensor("sp", [nsec, 128, steps * slabw], F32,
                        kind="ExternalOutput").ap()
    sch = Sched()
    with ExitStack() as ctx:
        e = ctx.enter_context
        xs = e(nc.sbuf_tensor("xs", [128, steps * slabw], F32))
        st = e(nc.sbuf_tensor("st", [128, slabw], F32))
        ld = e(nc.semaphore("ld"))
        vs = e(nc.semaphore("vs"))
        sto = e(nc.semaphore("sto"))
        n_stod = 0
        for sec in range(nsec):
            sch.wait("sync", sto, n_stod)
            nl = sch.emit("sync", lambda e, sec=sec: e.dma_start(
                out=xs[:], in_=hs[sec]), inc=(ld, 16))
            sch.wait("vector", ld, nl)
            n0 = sch.emit("vector", lambda e: e.memset(st[:], 0.0), inc=(vs, 1))
            for sl in range(steps):
                sch.emit("vector", lambda e, sl=sl: e.scalar_tensor_tensor(
                    out=xs[:, sl * slabw:(sl + 1) * slabw], in0=st[:], scalar=0.5,
                    in1=xs[:, sl * slabw:(sl + 1) * slabw], op0=AX.mult, op1=AX.add))
                sch.emit("vector", lambda e, sl=sl: e.scalar_tensor_tensor(
                    out=st[:], in0=xs[:, sl * slabw:(sl + 1) * slabw], scalar=1.0,
                    in1=xs[:, sl * slabw:(sl + 1) * slabw],
                    op0=AX.is_lt, op1=AX.mult))
            n_sp = sch.emit("vector", lambda e: e.tensor_single_scalar(
                out=xs[:], in_=xs[:], scalar=1.0, op=AX.is_ge), inc=(vs, 1))
            sch.wait("sync", vs, n_sp)
            n_stod = sch.emit("sync", lambda e, sec=sec: e.dma_start(
                out=sp[sec], in_=xs[:]), inc=(sto, 16))
            sch.wait("vector", sto, n_stod)
        sch.wait("sync", sto, sch.n(sto))
        sch.check_deadlock()
        sch.replay(nc)
    return nc


def scan_pack(h_cw, skw, nsec):
    cw, _, T = h_cw.shape
    steps = SL + SB_
    L = T // (nsec * skw)
    assert L == SL
    out = np.zeros((nsec, 128, steps, skw, cw), np.float32)
    hp = h_cw.transpose(1, 0, 2)
    for sec in range(nsec):
        for k in range(skw):
            g0 = sec * skw * SL + k * SL - SB_
            n0 = max(0, -g0)
            seg = hp[:, :, g0 + n0:g0 + steps]
            out[sec, :, n0:n0 + seg.shape[2], k, :] = seg.transpose(0, 2, 1)
    return np.ascontiguousarray(out.reshape(nsec, 128, steps * skw * cw))


def scan_unpack(sp, skw, nsec, cw):
    steps = SL + SB_
    v = sp.reshape(nsec, 128, steps, skw, cw)[:, :, SB_:, :, :]
    v = v.transpose(4, 1, 0, 3, 2)
    return np.ascontiguousarray(v.reshape(cw, 128, nsec * skw * SL))


def build_w2():
    nc = bass.Bass("TRN2", target_bir_lowering=False)
    s1Ts = nc.dram_tensor("s1Ts", [4096, TB], BF16, kind="ExternalInput").ap()
    w2T = nc.dram_tensor("w2T", [4096, D], F32, kind="ExternalInput").ap()
    b2v = nc.dram_tensor("b2v", [D], F32, kind="ExternalInput").ap()
    h2T = nc.dram_tensor("h2T", [D, TB], F32, kind="ExternalOutput").ap()
    KF = 4096 // 128
    NS = TB // TSUB
    sch = Sched()
    with ExitStack() as ctx:
        e = ctx.enter_context
        ld = e(nc.semaphore("ld"))
        vs = e(nc.semaphore("vs"))
        ts = e(nc.semaphore("ts"))
        as_ = e(nc.semaphore("as"))
        sto = e(nc.semaphore("sto"))
        w2b = e(nc.sbuf_tensor("w2b", [128, KF * D], BF16))
        b2c = e(nc.sbuf_tensor("b2c", [128, 8], F32))
        with nc.sbuf_tensor("wstg", [128, KF * 256], F32) as wstg:
            for q in range(4):
                n1 = sch.emit("sync", lambda e, q=q: e.dma_start(
                    out=wstg[:].rearrange("p (k m) -> p k m", k=KF),
                    in_=w2T[:, q * 256:(q + 1) * 256].rearrange(
                        "(k p) m -> p k m", p=128)), inc=(ld, 16))
                sch.wait("vector", ld, n1)
                def cst(e, q=q):
                    dst = w2b[:].rearrange("p (k m) -> p k m", k=KF)[
                        :, :, q * 256:(q + 1) * 256]
                    return e.tensor_copy(
                        out=dst, in_=wstg[:].rearrange("p (k m) -> p k m", k=KF))
                n_c = sch.emit("vector", cst, inc=(vs, 1))
                sch.wait("sync", vs, n_c)
        s1t = e(nc.sbuf_tensor("s1t", [128, KF * TSUB], BF16))
        h2s = e(nc.sbuf_tensor("h2s", [128, 8 * TSUB], F32))
        psA = e(nc.psum_tensor("psA", [128, TSUB], F32))
        nl = sch.emit("sync", lambda e: e.dma_start(
            out=b2c[:], in_=b2v.rearrange("(m p) -> p m", p=128),
            allow_slow_non_contiguous=True), inc=(ld, 16))
        sch.wait("vector", ld, nl)
        n_b = sch.emit("vector", lambda e: e.tensor_scalar_mul(
            out=b2c[:], in0=b2c[:], scalar1=0.5), inc=(vs, 1))
        sch.wait("scalar", vs, n_b)
        n_sto = 0
        for su in range(NS):
            t0 = su * TSUB
            nl = sch.emit("sync", lambda e, t0=t0: e.dma_start(
                out=s1t[:].rearrange("p (k t) -> p k t", k=KF),
                in_=s1Ts[:, t0:t0 + TSUB].rearrange("(k p) t -> p k t", p=128)),
                inc=(ld, 16))
            sch.wait("tensor", ld, nl)
            for m in range(8):
                for k in range(KF):
                    tH = sch.emit("tensor", lambda e, m=m, k=k: e.matmul(
                        psA[:], w2b[:, k * D + m * 128:k * D + m * 128 + 128],
                        s1t[:, k * TSUB:(k + 1) * TSUB],
                        start=(k == 0), stop=(k == KF - 1)),
                        inc=(ts, 1) if k == KF - 1 else None)
                sch.wait("scalar", ts, tH)
                aH = sch.emit("scalar", lambda e, m=m: e.activation(
                    out=h2s[:, m * TSUB:(m + 1) * TSUB], in_=psA[:],
                    func=AF.Identity, scale=0.5, bias=b2c[:, m:m + 1]),
                    inc=(as_, 1))
                sch.wait("tensor", as_, aH)
            sch.wait("sync", as_, sch.n(as_))
            sch.wait("sync", sto, n_sto)
            n_sto = sch.emit("sync", lambda e, t0=t0: e.dma_start(
                out=h2T[:, t0:t0 + TSUB].rearrange("(k p) t -> p k t", p=128),
                in_=h2s[:].rearrange("p (k t) -> p k t", k=8)), inc=(sto, 16))
        sch.wait("sync", sto, sch.n(sto))
        sch.check_deadlock()
        sch.replay(nc)
    return nc


def run_postrkv(x2T, yT, Wo, ln1g, ln1b, W1, b1):
    nc = cached_build("postrkv", build_postrkv)
    woT = np.ascontiguousarray(Wo.T)
    w1Tc = np.ascontiguousarray(W1.T)
    in_maps = []
    for c in range(NCORE):
        sl = slice(c * 1024, (c + 1) * 1024)
        in_maps.append({
            "x2Ts": np.ascontiguousarray(
                np.concatenate([x2T[0][:, sl], x2T[1][:, sl]], axis=1)),
            "yTs": np.ascontiguousarray(
                np.concatenate([yT[0][:, sl], yT[1][:, sl]], axis=1)),
            "woT": woT, "ln1g": ln1g, "ln1b": ln1b, "w1T": w1Tc, "b1v": b1,
        })
    res = run_bass_kernel_spmd(nc, in_maps, core_ids=list(range(NCORE)))
    x1T = np.zeros((BT, D, S), np.float32)
    x1Tb = np.zeros((BT, D, S), bf)
    h1T = np.zeros((BT, 4096, S), np.float32)
    for c in range(NCORE):
        sl = slice(c * 1024, (c + 1) * 1024)
        r = res.results[c]
        x1T[0][:, sl] = r["x1T"][:, :1024]; x1T[1][:, sl] = r["x1T"][:, 1024:]
        x1Tb[0][:, sl] = r["x1Tb"][:, :1024]; x1Tb[1][:, sl] = r["x1Tb"][:, 1024:]
        h1T[0][:, sl] = r["h1T"][:, :1024]; h1T[1][:, sl] = r["h1T"][:, 1024:]
    return x1T, x1Tb, h1T


def run_lif_phase(h_T, skw, nsec, cw):
    nc = cached_build(("lif", skw, cw, nsec), lambda: build_lif(skw, cw, nsec))
    B, E, T = h_T.shape
    fpc = E // NCORE
    assert fpc * B == 128 * cw
    in_maps = []
    for c in range(NCORE):
        rows = h_T[:, c * fpc:(c + 1) * fpc, :].reshape(B * fpc, T)
        h_cw = rows.reshape(cw, 128, T)
        in_maps.append({"hs": scan_pack(h_cw, skw, nsec)})
    res = run_bass_kernel_spmd(nc, in_maps, core_ids=list(range(NCORE)))
    out = np.zeros((B, E, T), np.float32)
    for c in range(NCORE):
        spk = scan_unpack(np.asarray(res.results[c]["sp"]), skw, nsec, cw)
        out[:, c * fpc:(c + 1) * fpc, :] = spk.reshape(B, fpc, T)
    return out


def run_w2(s1T, W2, b2):
    nc = cached_build("w2", build_w2)
    w2Tc = np.ascontiguousarray(W2.T)
    in_maps = []
    for c in range(NCORE):
        sl = slice(c * 1024, (c + 1) * 1024)
        in_maps.append({
            "s1Ts": np.ascontiguousarray(np.concatenate(
                [s1T[0][:, sl], s1T[1][:, sl]], axis=1)).astype(bf),
            "w2T": w2Tc, "b2v": b2,
        })
    res = run_bass_kernel_spmd(nc, in_maps, core_ids=list(range(NCORE)))
    h2T = np.zeros((BT, D, S), np.float32)
    for c in range(NCORE):
        sl = slice(c * 1024, (c + 1) * 1024)
        h2T[0][:, sl] = res.results[c]["h2T"][:, :1024]
        h2T[1][:, sl] = res.results[c]["h2T"][:, 1024:]
    return h2T


def make_ln(sch, t, KD=8):
    def emit_ln(src, dst, dstb=None, g=None, b=None):
        vs, ts, as_ = t["vs"], t["ts"], t["as_"]
        TSU = t["tsub"]
        sch.wait("scalar", vs, sch.n(vs))
        aN_l = None
        for k in range(KD):
            sch.emit("tensor", lambda e, k=k: e.matmul(
                t["psSum"][:], t["ones128"][:], src[:, k * TSU:(k + 1) * TSU],
                start=(k == 0), stop=(k == KD - 1)))
            if aN_l is not None:
                sch.wait("scalar", ts, aN_l)
            aSq = sch.emit("scalar", lambda e, k=k: e.activation(
                out=t["sq"][:], in_=src[:, k * TSU:(k + 1) * TSU],
                func=AF.Square), inc=(as_, 1))
            sch.wait("tensor", as_, aSq)
            aN_l = sch.emit("tensor", lambda e, k=k: e.matmul(
                t["psSq"][:], t["ones128"][:], t["sq"][:],
                start=(k == 0), stop=(k == KD - 1)), inc=(ts, 1))
        sch.wait("vector", ts, aN_l)
        T_ = TSU
        stat = t["stat"]
        sch.emit("vector", lambda e: e.tensor_scalar_mul(
            out=stat[:, 0:T_], in0=t["psSum"][:], scalar1=1.0 / D))
        sch.emit("vector", lambda e: e.tensor_mul(
            out=stat[:, T_:2 * T_], in0=stat[:, 0:T_], in1=stat[:, 0:T_]))
        n_var = sch.emit("vector", lambda e: e.scalar_tensor_tensor(
            out=stat[:, 2 * T_:3 * T_], in0=t["psSq"][:], scalar=1.0 / D,
            in1=stat[:, T_:2 * T_], op0=AX.mult, op1=AX.subtract), inc=(vs, 1))
        sch.wait("scalar", vs, n_var)
        a_sd = sch.emit("scalar", lambda e: e.activation(
            out=stat[:, 3 * T_:4 * T_], in_=stat[:, 2 * T_:3 * T_],
            func=AF.Sqrt, bias=t["eps1"][:]), inc=(as_, 1))
        sch.wait("vector", as_, a_sd)
        n_rs = sch.emit("vector", lambda e: e.reciprocal(
            out=stat[:, 2 * T_:3 * T_], in_=stat[:, 3 * T_:4 * T_]), inc=(vs, 1))
        sch.wait("tensor", vs, n_rs)
        sch.emit("tensor", lambda e: e.matmul(
            t["psBm"][:], t["onesr"][:], stat[:, 0:T_], start=True, stop=True))
        t_bc = sch.emit("tensor", lambda e: e.matmul(
            t["psBr"][:], t["onesr"][:], stat[:, 2 * T_:3 * T_],
            start=True, stop=True), inc=(ts, 1))
        sch.wait("vector", ts, t_bc)
        n_rb = sch.emit("vector", lambda e: e.tensor_copy(
            out=t["rbm"][:], in_=t["psBr"][:]), inc=(vs, 1))
        nv = n_rb
        for k in range(KD):
            sch.emit("vector", lambda e, k=k: e.tensor_sub(
                out=t["d1"][:], in0=src[:, k * TSU:(k + 1) * TSU],
                in1=t["psBm"][:]))
            if g is None:
                nv = sch.emit("vector", lambda e, k=k: e.tensor_mul(
                    out=dst[:, k * TSU:(k + 1) * TSU], in0=t["d1"][:],
                    in1=t["rbm"][:]), inc=(vs, 1))
            else:
                sch.emit("vector", lambda e, k=k: e.tensor_mul(
                    out=t["d1"][:], in0=t["d1"][:], in1=t["rbm"][:]))
                nv = sch.emit("vector", lambda e, k=k: e.tensor_scalar(
                    out=dst[:, k * TSU:(k + 1) * TSU], in0=t["d1"][:],
                    scalar1=g[:, k:k + 1], scalar2=b[:, k:k + 1],
                    op0=AX.mult, op1=AX.add), inc=(vs, 1))
            if dstb is not None:
                nv = sch.emit("vector", lambda e, k=k: e.tensor_copy(
                    out=dstb[:, k * TSU:(k + 1) * TSU],
                    in_=dst[:, k * TSU:(k + 1) * TSU]), inc=(vs, 1))
        return nv
    return emit_ln


def ln_tiles(nc, e, sch, tsub):
    t = {"tsub": tsub}
    t["ones128"] = e(nc.sbuf_tensor("ones128", [128, 1], F32))
    t["onesr"] = e(nc.sbuf_tensor("onesr", [1, 128], F32))
    t["eps1"] = e(nc.sbuf_tensor("eps1", [1, 1], F32))
    t["sq"] = e(nc.sbuf_tensor("sq", [128, tsub], F32))
    t["stat"] = e(nc.sbuf_tensor("stat", [1, 4 * tsub], F32))
    t["d1"] = e(nc.sbuf_tensor("d1", [128, tsub], F32))
    t["rbm"] = e(nc.sbuf_tensor("rbm", [128, tsub], F32))
    t["psSum"] = e(nc.psum_tensor("psSum", [1, tsub], F32))
    t["psSq"] = e(nc.psum_tensor("psSq", [1, tsub], F32))
    t["psBm"] = e(nc.psum_tensor("psBm", [128, tsub], F32))
    t["psBr"] = e(nc.psum_tensor("psBr", [128, tsub], F32))
    return t


def ln_init(sch, t, ld, vs):
    sch.emit("vector", lambda e: e.memset(t["ones128"][:], 1.0))
    sch.emit("vector", lambda e: e.memset(t["onesr"][:], 1.0))
    return sch.emit("vector", lambda e: e.memset(t["eps1"][:], 1e-5), inc=(vs, 1))


def build_h1():
    nc = bass.Bass("TRN2", target_bir_lowering=False)
    x1Ts = nc.dram_tensor("x1Ts", [D, TB], F32, kind="ExternalInput").ap()
    fspTs = nc.dram_tensor("fspTs", [D, TB], BF16, kind="ExternalInput").ap()
    x2Ts = nc.dram_tensor("x2Ts", [D, TB], F32, kind="ExternalInput").ap()
    g2 = nc.dram_tensor("g2", [D], F32, kind="ExternalInput").ap()
    b2a = nc.dram_tensor("b2a", [D], F32, kind="ExternalInput").ap()
    x3T = nc.dram_tensor("x3T", [D, TB], F32, kind="ExternalOutput").ap()
    x3Tb = nc.dram_tensor("x3Tb", [D, TB], BF16, kind="ExternalOutput").ap()
    KD = 8
    NS = TB // TSUB
    sch = Sched()
    with ExitStack() as ctx:
        e = ctx.enter_context
        ld = e(nc.semaphore("ld"))
        vs = e(nc.semaphore("vs"))
        ts = e(nc.semaphore("ts"))
        as_ = e(nc.semaphore("as"))
        sto = e(nc.semaphore("sto"))
        t = ln_tiles(nc, e, sch, TSUB)
        t["vs"], t["ts"], t["as_"] = vs, ts, as_
        gcol = e(nc.sbuf_tensor("gcol", [128, KD], F32))
        bcol = e(nc.sbuf_tensor("bcol", [128, KD], F32))
        r1 = e(nc.sbuf_tensor("r1", [128, KD * TSUB], F32))
        r2 = e(nc.sbuf_tensor("r2", [128, KD * TSUB], F32))
        fsb = e(nc.sbuf_tensor("fsb", [128, KD * TSUB], BF16))
        x3b = e(nc.sbuf_tensor("x3b", [128, KD * TSUB], BF16))
        emit_ln = make_ln(sch, t)
        nl = sch.emit("sync", lambda e: e.dma_start(
            out=gcol[:], in_=g2.rearrange("(m p) -> p m", p=128),
            allow_slow_non_contiguous=True), inc=(ld, 16))
        nl = sch.emit("sync", lambda e: e.dma_start(
            out=bcol[:], in_=b2a.rearrange("(m p) -> p m", p=128),
            allow_slow_non_contiguous=True), inc=(ld, 16))
        sch.wait("vector", ld, nl)
        ln_init(sch, t, ld, vs)
        n_sto = 0
        for su in range(NS):
            t0 = su * TSUB
            sch.wait("sync", sto, n_sto)
            nl = sch.emit("sync", lambda e, t0=t0: e.dma_start(
                out=r1[:].rearrange("p (k t) -> p k t", k=KD),
                in_=x1Ts[:, t0:t0 + TSUB].rearrange("(k p) t -> p k t", p=128)),
                inc=(ld, 16))
            nl = sch.emit("sync", lambda e, t0=t0: e.dma_start(
                out=fsb[:].rearrange("p (k t) -> p k t", k=KD),
                in_=fspTs[:, t0:t0 + TSUB].rearrange("(k p) t -> p k t", p=128)),
                inc=(ld, 16))
            nl = sch.emit("sync", lambda e, t0=t0: e.dma_start(
                out=r2[:].rearrange("p (k t) -> p k t", k=KD),
                in_=x2Ts[:, t0:t0 + TSUB].rearrange("(k p) t -> p k t", p=128)),
                inc=(ld, 16))
            sch.wait("vector", ld, nl)
            n_a = sch.emit("vector", lambda e: e.tensor_add(
                out=r1[:], in0=r1[:], in1=fsb[:]), inc=(vs, 1))
            sch.wait("tensor", vs, n_a)
            emit_ln(r1, r1, None, gcol, bcol)
            n_a2 = sch.emit("vector", lambda e: e.tensor_add(
                out=r2[:], in0=r2[:], in1=r1[:]), inc=(vs, 1))
            sch.wait("tensor", vs, n_a2)
            emit_ln(r2, r2, x3b)
            n_f = sch.n(vs)
            sch.wait("sync", vs, n_f)
            n_sto = sch.emit("sync", lambda e, t0=t0: e.dma_start(
                out=x3T[:, t0:t0 + TSUB].rearrange("(k p) t -> p k t", p=128),
                in_=r2[:].rearrange("p (k t) -> p k t", k=KD)), inc=(sto, 16))
            n_sto = sch.emit("sync", lambda e, t0=t0: e.dma_start(
                out=x3Tb[:, t0:t0 + TSUB].rearrange("(k p) t -> p k t", p=128),
                in_=x3b[:].rearrange("p (k t) -> p k t", k=KD)), inc=(sto, 16))
        sch.wait("sync", sto, sch.n(sto))
        sch.check_deadlock()
        sch.replay(nc)
    return nc


def run_h1(x1T, fspT, x2T, g2, b2a):
    nc = cached_build("h1", build_h1)
    in_maps = []
    for c in range(NCORE):
        sl = slice(c * 1024, (c + 1) * 1024)
        in_maps.append({
            "x1Ts": np.ascontiguousarray(np.concatenate(
                [x1T[0][:, sl], x1T[1][:, sl]], axis=1)),
            "fspTs": np.ascontiguousarray(np.concatenate(
                [fspT[0][:, sl], fspT[1][:, sl]], axis=1)).astype(bf),
            "x2Ts": np.ascontiguousarray(np.concatenate(
                [x2T[0][:, sl], x2T[1][:, sl]], axis=1)),
            "g2": g2, "b2a": b2a,
        })
    res = run_bass_kernel_spmd(nc, in_maps, core_ids=list(range(NCORE)))
    x3T = np.zeros((BT, D, S), np.float32)
    x3Tb = np.zeros((BT, D, S), bf)
    for c in range(NCORE):
        sl = slice(c * 1024, (c + 1) * 1024)
        r = res.results[c]
        x3T[0][:, sl] = r["x3T"][:, :1024]; x3T[1][:, sl] = r["x3T"][:, 1024:]
        x3Tb[0][:, sl] = r["x3Tb"][:, :1024]; x3Tb[1][:, sl] = r["x3Tb"][:, 1024:]
    return x3T, x3Tb


def build_h2(variant='full'):
    nc = bass.Bass("TRN2", target_bir_lowering=False)
    x3Ts = nc.dram_tensor("x3Ts", [D, TB], F32, kind="ExternalInput").ap()
    x3Tbs = nc.dram_tensor("x3Tbs", [D, TB], BF16, kind="ExternalInput").ap()
    ipwT = nc.dram_tensor("ipwT", [D, 3 * D], F32, kind="ExternalInput").ap()
    ipb = nc.dram_tensor("ipb", [3 * D], F32, kind="ExternalInput").ap()
    opwT = nc.dram_tensor("opwT", [D, D], F32, kind="ExternalInput").ap()
    opb = nc.dram_tensor("opb", [D], F32, kind="ExternalInput").ap()
    f1g = nc.dram_tensor("f1g", [D], F32, kind="ExternalInput").ap()
    f1b = nc.dram_tensor("f1b", [D], F32, kind="ExternalInput").ap()
    wg = nc.dram_tensor("wg", [D, 8], F32, kind="ExternalInput").ap()
    y1T = nc.dram_tensor("y1T", [D, TB], F32, kind="ExternalOutput").ap()
    y1rows = nc.dram_tensor("y1rows", [TB, D], BF16, kind="ExternalOutput").ap()
    idxr = nc.dram_tensor("idxr", [1, TB], F32, kind="ExternalOutput").ap()
    gvr = nc.dram_tensor("gvr", [1, TB], F32, kind="ExternalOutput").ap()
    KD = 8
    HN = 4
    sch = Sched()
    with ExitStack() as ctx:
        e = ctx.enter_context
        ld = e(nc.semaphore("ld"))
        vs = e(nc.semaphore("vs"))
        ts = e(nc.semaphore("ts"))
        as_ = e(nc.semaphore("as"))
        gs = e(nc.semaphore("gs"))
        sto = e(nc.semaphore("sto"))
        t = ln_tiles(nc, e, sch, TSUB)
        t["vs"], t["ts"], t["as_"] = vs, ts, as_
        obf = e(nc.sbuf_tensor("obf", [128, KD * TB], BF16))
        ipbc = e(nc.sbuf_tensor("ipbc", [128, 24], F32))
        iota8 = e(nc.sbuf_tensor("iota8", [8, 1], F32))
        ones8 = e(nc.sbuf_tensor("ones8", [8, 1], F32))
        ones8r = e(nc.sbuf_tensor("ones8r", [1, 8], F32))
        identb = e(nc.sbuf_tensor("identb", [128, 128], BF16))
        emit_ln = make_ln(sch, t)
        ln_init(sch, t, ld, vs)
        sch.emit("vector", lambda e: e.memset(ones8[:], 1.0))
        sch.emit("vector", lambda e: e.memset(ones8r[:], 1.0))
        n_i8 = sch.emit("gpsimd", lambda e: e.iota(
            out=iota8[:], pattern=[[0, 1]], base=0, channel_multiplier=1,
            allow_small_or_imprecise_dtypes=True), inc=(gs, 1))
        n_id = sch.emit("gpsimd", lambda e: e.iota(
            out=identb[:], pattern=[[-1, 128]], base=0, channel_multiplier=1,
            allow_small_or_imprecise_dtypes=True), inc=(gs, 1))
        sch.wait("vector", gs, n_id)
        n_id2 = sch.emit("vector", lambda e: e.tensor_single_scalar(
            out=identb[:], in_=identb[:], scalar=0.0, op=AX.is_equal), inc=(vs, 1))
        sch.wait("tensor", vs, n_id2)
        nl = sch.emit("sync", lambda e: e.dma_start(
            out=ipbc[:], in_=ipb.rearrange("(m p) -> p m", p=128),
            allow_slow_non_contiguous=True), inc=(ld, 16))
        sch.wait("scalar", ld, nl)

        with ExitStack() as pa:
            ea = pa.enter_context
            qkvb = ea(nc.sbuf_tensor("qkvb", [128, 24 * TB], BF16))
            with ExitStack() as p1:
                e1 = p1.enter_context
                x3bb = e1(nc.sbuf_tensor("x3bb", [128, KD * TB], BF16))
                wstg = e1(nc.sbuf_tensor("wstg", [128, KD * 128], F32))
                wmb = e1(nc.sbuf_tensor("wmb", [128, KD * 128], BF16))
                psQ = e1(nc.psum_tensor("psQ", [128, TSUB], F32))
                nlx = sch.emit("sync", lambda e: e.dma_start(
                    out=x3bb[:].rearrange("p (k t) -> p k t", k=KD),
                    in_=x3Tbs.rearrange("(k p) t -> p k t", p=128)), inc=(ld, 16))
                for m in range(24):
                    nl2 = sch.emit("sync", lambda e, m=m: e.dma_start(
                        out=wstg[:].rearrange("p (k t) -> p k t", k=KD),
                        in_=ipwT[:, m * 128:(m + 1) * 128].rearrange(
                            "(k p) t -> p k t", p=128)), inc=(ld, 16))
                    sch.wait("vector", ld, nl2)
                    n_w = sch.emit("vector", lambda e: e.tensor_copy(
                        out=wmb[:], in_=wstg[:]), inc=(vs, 1))
                    sch.wait("tensor", vs, n_w)
                    sch.wait("tensor", ld, nl2)
                    for tsu in range(TB // TSUB):
                        for k in range(KD):
                            tQ = sch.emit("tensor",
                                          lambda e, k=k, tsu=tsu: e.matmul(
                                psQ[:], wmb[:, k * 128:(k + 1) * 128],
                                x3bb[:, k * TB + tsu * TSUB:
                                     k * TB + (tsu + 1) * TSUB],
                                start=(k == 0), stop=(k == KD - 1)),
                                inc=(ts, 1) if k == KD - 1 else None)
                        sch.wait("scalar", ts, tQ)
                        aQ = sch.emit("scalar", lambda e, m=m, tsu=tsu: e.activation(
                            out=qkvb[:, m * TB + tsu * TSUB:
                                     m * TB + (tsu + 1) * TSUB],
                            in_=psQ[:], func=AF.Identity,
                            bias=ipbc[:, m:m + 1]), inc=(as_, 1))
                        sch.wait("tensor", as_, aQ)
                    sch.wait("sync", as_, sch.n(as_))
            if variant == "qkv":
                sch.wait("sync", as_, sch.n(as_))
                nst = sch.emit("sync", lambda e: e.dma_start(
                    out=y1T[:, 0:TB], in_=qkvb[:, 0:TB].rearrange(
                        "p t -> p t") if False else qkvb[:, 0:TB],
                    allow_slow_non_contiguous=True), inc=(sto, 16)) if False else None
                sch.check_deadlock()
                sch.replay(nc)
                return nc
            with ExitStack() as p2:
                e2 = p2.enter_context
                scS = [e2(nc.sbuf_tensor(f"scS{ij}", [4, TB], F32))
                       for ij in range(4)]
                A1 = e2(nc.sbuf_tensor("A1", [4, TB], BF16))
                A0 = e2(nc.sbuf_tensor("A0", [4, TB], BF16))
                ebuf = e2(nc.sbuf_tensor("ebuf", [128, TSUB], BF16))
                ebf2 = e2(nc.sbuf_tensor("ebf2", [128, TSUB], F32))
                ind = e2(nc.sbuf_tensor("ind", [128, 4 * HN], BF16))
                sel = e2(nc.sbuf_tensor("sel", [4, 128 * HN], BF16))
                ps4 = e2(nc.psum_tensor("ps4", [4, TSUB], F32))
                psB0 = e2(nc.psum_tensor("psB0", [128, TSUB], F32))
                psB1 = e2(nc.psum_tensor("psB1", [128, TSUB], F32))
                sch.wait("vector", ts, sch.n(ts))
                sch.wait("vector", as_, sch.n(as_))
                sch.wait("gpsimd", ts, sch.n(ts))
                sch.wait("gpsimd", as_, sch.n(as_))
                n_in = sch.emit("vector", lambda e: e.memset(ind[:], 0.0),
                                inc=(vs, 1))
                for q in range(HN):
                    n_in = sch.emit("vector", lambda e, q=q: e.memset(
                        ind[:, q * 4 + q:q * 4 + q + 1], 1.0), inc=(vs, 1))
                n_io = sch.emit("gpsimd", lambda e: e.iota(
                    out=sel[:].rearrange("p (h m) -> p h m", h=HN),
                    pattern=[[1, HN], [0, 128]], base=0, channel_multiplier=-1,
                    allow_small_or_imprecise_dtypes=True), inc=(gs, 1))
                sch.wait("vector", gs, n_io)
                n_in = sch.emit("vector", lambda e: e.tensor_single_scalar(
                    out=sel[:], in_=sel[:], scalar=0.0, op=AX.is_equal),
                    inc=(vs, 1))
                sch.wait("tensor", vs, n_in)
                for ij in range(4):
                    i, j = ij // 2, ij % 2
                    for tsu in range(2):
                        cq = i * 1024 + tsu * TSUB
                        ck = j * 1024 + tsu * TSUB
                        for ch in range(KD):
                            q = ch // 2
                            n_e = sch.emit("vector",
                                           lambda e, ch=ch, cq=cq, ck=ck:
                                           e.tensor_mul(
                                out=ebuf[:],
                                in0=qkvb[:, ch * TB + cq:ch * TB + cq + TSUB],
                                in1=qkvb[:, (8 + ch) * TB + ck:
                                         (8 + ch) * TB + ck + TSUB]),
                                inc=(vs, 1))
                            sch.wait("tensor", vs, n_e)
                            tS = sch.emit("tensor",
                                          lambda e, ch=ch, q=q: e.matmul(
                                ps4[:], ind[:, q * 4:(q + 1) * 4], ebuf[:],
                                start=(ch == 0), stop=(ch == KD - 1)),
                                inc=(ts, 1))
                            sch.wait("vector", ts, tS)
                        n_sc = sch.emit("vector",
                                        lambda e, ij=ij, cq=cq: e.tensor_copy(
                            out=scS[ij][:, cq:cq + TSUB], in_=ps4[:]),
                            inc=(vs, 1))
                        sch.wait("tensor", vs, n_sc)
                for i in range(2):
                    n_d = sch.emit("vector", lambda e, i=i: e.tensor_sub(
                        out=A1[:, i * 1024:(i + 1) * 1024],
                        in0=scS[i * 2 + 1][:, i * 1024:(i + 1) * 1024],
                        in1=scS[i * 2 + 0][:, i * 1024:(i + 1) * 1024]),
                        inc=(vs, 1))
                sch.wait("scalar", vs, sch.n(vs))
                a_sg = sch.emit("scalar", lambda e: e.activation(
                    out=A1[:], in_=A1[:], func=AF.Sigmoid, scale=1.0 / 16),
                    inc=(as_, 1))
                sch.wait("vector", as_, a_sg)
                n_a0 = sch.emit("vector", lambda e: e.tensor_scalar(
                    out=A0[:], in0=A1[:], scalar1=-1.0, scalar2=1.0,
                    op0=AX.mult, op1=AX.add), inc=(vs, 1))
                sch.wait("tensor", vs, n_a0)
                for ch in range(KD):
                    h = ch // 2
                    for i in range(2):
                        for tsu in range(2):
                            c0 = i * 1024 + tsu * TSUB
                            cn = tsu * TSUB
                            sch.emit("tensor", lambda e, h=h, c0=c0: e.matmul(
                                psB0[:], sel[:, h * 128:(h + 1) * 128],
                                A0[:, c0:c0 + TSUB], start=True, stop=True))
                            tB1 = sch.emit("tensor",
                                           lambda e, h=h, c0=c0: e.matmul(
                                psB1[:], sel[:, h * 128:(h + 1) * 128],
                                A1[:, c0:c0 + TSUB], start=True, stop=True),
                                inc=(ts, 1))
                            sch.wait("vector", ts, tB1)
                            sch.emit("vector", lambda e, ch=ch, cn=cn: e.tensor_mul(
                                out=ebf2[:],
                                in0=qkvb[:, (16 + ch) * TB + cn:
                                         (16 + ch) * TB + cn + TSUB],
                                in1=psB0[:]))
                            sch.emit("vector", lambda e, ch=ch, cn=cn, c0=c0:
                                     e.tensor_mul(
                                out=obf[:, ch * TB + c0:ch * TB + c0 + TSUB],
                                in0=qkvb[:, (16 + ch) * TB + 1024 + cn:
                                         (16 + ch) * TB + 1024 + cn + TSUB],
                                in1=psB1[:]))
                            n_o = sch.emit("vector", lambda e, ch=ch, c0=c0:
                                           e.tensor_add(
                                out=obf[:, ch * TB + c0:ch * TB + c0 + TSUB],
                                in0=obf[:, ch * TB + c0:ch * TB + c0 + TSUB],
                                in1=ebf2[:]), inc=(vs, 1))
                            sch.wait("tensor", vs, n_o)
        if variant == "mha":
            sch.wait("sync", vs, sch.n(vs))
            sch.wait("sync", ts, sch.n(ts))
            sch.check_deadlock()
            sch.replay(nc)
            return nc
        with ExitStack() as p3:
            e3 = p3.enter_context
            opwb = e3(nc.sbuf_tensor("opwb", [128, KD * D], BF16))
            opbc = e3(nc.sbuf_tensor("opbc", [128, KD], F32))
            fgc = e3(nc.sbuf_tensor("fgc", [128, KD], F32))
            fbc = e3(nc.sbuf_tensor("fbc", [128, KD], F32))
            wgb = e3(nc.sbuf_tensor("wgb", [128, KD * 8], BF16))
            x3s = e3(nc.sbuf_tensor("x3s", [128, KD * TSUB], F32))
            y1s = e3(nc.sbuf_tensor("y1s", [128, KD * TSUB], F32))
            y1bs = e3(nc.sbuf_tensor("y1bs", [128, KD * TSUB], BF16))
            rowst = e3(nc.sbuf_tensor("rowst", [128, KD * 128], BF16))
            lgs = e3(nc.sbuf_tensor("lgs", [8, TB], F32))
            mrow = e3(nc.sbuf_tensor("mrow", [4, TB], F32))
            m2r = e3(nc.sbuf_tensor("m2r", [2, TB], F32))
            mx = e3(nc.sbuf_tensor("mx", [1, TB], F32))
            idxrow = e3(nc.sbuf_tensor("idxrow", [1, TB], F32))
            gvrow = e3(nc.sbuf_tensor("gvrow", [1, TB], F32))
            e8 = e3(nc.sbuf_tensor("e8", [8, TSUB], F32))
            psA = e3(nc.psum_tensor("psA", [128, TSUB], F32))
            psG = e3(nc.psum_tensor("psG", [8, TSUB], F32))
            psTr = e3(nc.psum_tensor("psTr", [128, 128], BF16))
            sch.wait("sync", vs, sch.n(vs))
            sch.wait("sync", ts, sch.n(ts))
            sch.wait("sync", as_, sch.n(as_))
            with nc.sbuf_tensor("wstg2", [128, KD * D], F32) as wstg2:
                nl = sch.emit("sync", lambda e: e.dma_start(
                    out=wstg2[:].rearrange("p (k m) -> p k m", k=KD),
                    in_=opwT.rearrange("(k p) m -> p k m", p=128)), inc=(ld, 16))
                sch.wait("vector", ld, nl)
                n_w = sch.emit("vector", lambda e: e.tensor_copy(
                    out=opwb[:], in_=wstg2[:]), inc=(vs, 1))
                sch.wait("sync", vs, n_w)
                nl = sch.emit("sync", lambda e: e.dma_start(
                    out=wstg2[:, 0:KD * 8].rearrange("p (k m) -> p k m", k=KD),
                    in_=wg.rearrange("(k p) m -> p k m", p=128)), inc=(ld, 16))
                sch.wait("vector", ld, nl)
                n_w = sch.emit("vector", lambda e: e.tensor_copy(
                    out=wgb[:], in_=wstg2[:, 0:KD * 8]), inc=(vs, 1))
                sch.wait("sync", vs, n_w)
            nl = sch.emit("sync", lambda e: e.dma_start(
                out=opbc[:], in_=opb.rearrange("(m p) -> p m", p=128),
                allow_slow_non_contiguous=True), inc=(ld, 16))
            nl = sch.emit("sync", lambda e: e.dma_start(
                out=fgc[:], in_=f1g.rearrange("(m p) -> p m", p=128),
                allow_slow_non_contiguous=True), inc=(ld, 16))
            nl = sch.emit("sync", lambda e: e.dma_start(
                out=fbc[:], in_=f1b.rearrange("(m p) -> p m", p=128),
                allow_slow_non_contiguous=True), inc=(ld, 16))
            sch.wait("vector", ld, nl)
            sch.wait("tensor", vs, sch.n(vs))
            n_sto = 0
            for su in range(TB // TSUB):
                t0 = su * TSUB
                sch.wait("sync", sto, n_sto)
                nl = sch.emit("sync", lambda e, t0=t0: e.dma_start(
                    out=x3s[:].rearrange("p (k t) -> p k t", k=KD),
                    in_=x3Ts[:, t0:t0 + TSUB].rearrange(
                        "(k p) t -> p k t", p=128)), inc=(ld, 16))
                sch.wait("vector", ld, nl)
                for m in range(KD):
                    for k in range(KD):
                        tA = sch.emit("tensor", lambda e, m=m, k=k, t0=t0: e.matmul(
                            psA[:], opwb[:, k * D + m * 128:k * D + m * 128 + 128],
                            obf[:, k * TB + t0:k * TB + t0 + TSUB],
                            start=(k == 0), stop=(k == KD - 1)),
                            inc=(ts, 1) if k == KD - 1 else None)
                    sch.wait("vector", ts, tA)
                    sch.emit("vector", lambda e, m=m: e.tensor_scalar(
                        out=t["d1"][:], in0=psA[:], scalar1=opbc[:, m:m + 1],
                        scalar2=None, op0=AX.add))
                    n_res = sch.emit("vector", lambda e, m=m: e.tensor_add(
                        out=x3s[:, m * TSUB:(m + 1) * TSUB],
                        in0=x3s[:, m * TSUB:(m + 1) * TSUB],
                        in1=t["d1"][:]), inc=(vs, 1))
                    sch.wait("tensor", vs, n_res)
                emit_ln(x3s, y1s, y1bs, fgc, fbc)
                n_f = sch.n(vs)
                sch.wait("sync", vs, n_f)
                n_sto = sch.emit("sync", lambda e, t0=t0: e.dma_start(
                    out=y1T[:, t0:t0 + TSUB].rearrange("(k p) t -> p k t", p=128),
                    in_=y1s[:].rearrange("p (k t) -> p k t", k=KD)), inc=(sto, 16))
                sch.wait("tensor", vs, n_f)
                for tt in range(TSUB // 128):
                    sch.wait("vector", sto, sch.n(sto))
                    for k in range(KD):
                        tT = sch.emit("tensor", lambda e, tt=tt, k=k: e.transpose(
                            psTr[:], y1bs[:, k * TSUB + tt * 128:
                                          k * TSUB + tt * 128 + 128],
                            identb[:]), inc=(ts, 1))
                        sch.wait("vector", ts, tT)
                        n_rc = sch.emit("vector", lambda e, k=k: e.tensor_copy(
                            out=rowst[:, k * 128:(k + 1) * 128], in_=psTr[:]),
                            inc=(vs, 1))
                        sch.wait("tensor", vs, n_rc)
                    sch.wait("sync", vs, sch.n(vs))
                    n_sto = sch.emit("sync", lambda e, t0=t0, tt=tt: e.dma_start(
                        out=y1rows[t0 + tt * 128:t0 + tt * 128 + 128, :],
                        in_=rowst[:]), inc=(sto, 16))
                for k in range(KD):
                    tG = sch.emit("tensor", lambda e, k=k: e.matmul(
                        psG[:], wgb[:, k * 8:(k + 1) * 8],
                        y1bs[:, k * TSUB:(k + 1) * TSUB],
                        start=(k == 0), stop=(k == KD - 1)),
                        inc=(ts, 1) if k == KD - 1 else None)
                sch.wait("vector", ts, tG)
                n_g = sch.emit("vector", lambda e, t0=t0: e.tensor_copy(
                    out=lgs[:, t0:t0 + TSUB], in_=psG[:]), inc=(vs, 1))
                sch.wait("tensor", vs, n_g)
            if variant == "proj":
                sch.wait("sync", sto, sch.n(sto))
                sch.check_deadlock()
                sch.replay(nc)
                return nc
            sch.wait("sync", vs, sch.n(vs))
            nd = sch.emit("sync", lambda e: e.dma_start(
                out=mrow[:], in_=lgs[4:8, :]), inc=(ld, 16))
            sch.wait("vector", ld, nd)
            n_m = sch.emit("vector", lambda e: e.tensor_tensor(
                out=mrow[:], in0=mrow[:], in1=lgs[0:4, :], op=AX.max), inc=(vs, 1))
            sch.wait("sync", vs, n_m)
            nd = sch.emit("sync", lambda e: e.dma_start(
                out=m2r[:], in_=mrow[2:4, :]), inc=(ld, 16))
            sch.wait("vector", ld, nd)
            n_m = sch.emit("vector", lambda e: e.tensor_tensor(
                out=m2r[:], in0=m2r[:], in1=mrow[0:2, :], op=AX.max), inc=(vs, 1))
            sch.wait("sync", vs, n_m)
            nd = sch.emit("sync", lambda e: e.dma_start(
                out=mx[:], in_=m2r[1:2, :]), inc=(ld, 16))
            sch.wait("vector", ld, nd)
            n_m = sch.emit("vector", lambda e: e.tensor_tensor(
                out=mx[:], in0=mx[:], in1=m2r[0:1, :], op=AX.max), inc=(vs, 1))
            sch.wait("tensor", vs, n_m)
            for su in range(TB // TSUB):
                t0 = su * TSUB
                tBm = sch.emit("tensor", lambda e, t0=t0: e.matmul(
                    psA[0:8, :], ones8r[:], mx[:, t0:t0 + TSUB],
                    start=True, stop=True), inc=(ts, 1))
                sch.wait("vector", ts, tBm)
                n_d8 = sch.emit("vector", lambda e, t0=t0: e.tensor_sub(
                    out=e8[:], in0=lgs[:, t0:t0 + TSUB], in1=psA[0:8, :]),
                    inc=(vs, 1))
                sch.wait("vector", vs, n_d8)
                n_eq = sch.emit("vector", lambda e: e.tensor_single_scalar(
                    out=e8[:], in_=e8[:], scalar=0.0, op=AX.is_equal), inc=(vs, 1))
                sch.wait("tensor", vs, n_eq)
                tI = sch.emit("tensor", lambda e: e.matmul(
                    psA[0:1, :], iota8[:], e8[:], start=True, stop=True),
                    inc=(ts, 1))
                sch.wait("vector", ts, tI)
                n_ix = sch.emit("vector", lambda e, t0=t0: e.tensor_copy(
                    out=idxrow[:, t0:t0 + TSUB], in_=psA[0:1, :]), inc=(vs, 1))
                sch.wait("tensor", vs, n_ix)
                tBm = sch.emit("tensor", lambda e, t0=t0: e.matmul(
                    psA[0:8, :], ones8r[:], mx[:, t0:t0 + TSUB],
                    start=True, stop=True), inc=(ts, 1))
                sch.wait("vector", ts, tBm)
                n_d8 = sch.emit("vector", lambda e, t0=t0: e.tensor_sub(
                    out=e8[:], in0=lgs[:, t0:t0 + TSUB], in1=psA[0:8, :]),
                    inc=(vs, 1))
                sch.wait("scalar", vs, n_d8)
                a_e = sch.emit("scalar", lambda e: e.activation(
                    out=e8[:], in_=e8[:], func=AF.Exp), inc=(as_, 1))
                sch.wait("tensor", as_, a_e)
                tE = sch.emit("tensor", lambda e: e.matmul(
                    psA[0:1, :], ones8[:], e8[:], start=True, stop=True),
                    inc=(ts, 1))
                sch.wait("vector", ts, tE)
                n_gv = sch.emit("vector", lambda e, t0=t0: e.reciprocal(
                    out=gvrow[:, t0:t0 + TSUB], in_=psA[0:1, :]), inc=(vs, 1))
                sch.wait("tensor", vs, n_gv)
            n_f = sch.n(vs)
            sch.wait("sync", vs, n_f)
            n_sto = sch.emit("sync", lambda e: e.dma_start(
                out=idxr, in_=idxrow[:]), inc=(sto, 16))
            n_sto = sch.emit("sync", lambda e: e.dma_start(
                out=gvr, in_=gvrow[:]), inc=(sto, 16))
            sch.wait("sync", sto, sch.n(sto))
        sch.check_deadlock()
        sch.replay(nc)
    return nc


def run_h2(x3T, x3Tb, ipw, ipb, opw, opb, f1g, f1b, wg):
    nc = cached_build("h2", build_h2)
    ipwT = np.ascontiguousarray(ipw.T)
    opwT = np.ascontiguousarray(opw.T)
    in_maps = []
    for c in range(NCORE):
        sl = slice(c * 1024, (c + 1) * 1024)
        in_maps.append({
            "x3Ts": np.ascontiguousarray(np.concatenate(
                [x3T[0][:, sl], x3T[1][:, sl]], axis=1)),
            "x3Tbs": np.ascontiguousarray(np.concatenate(
                [x3Tb[0][:, sl], x3Tb[1][:, sl]], axis=1)),
            "ipwT": ipwT, "ipb": ipb, "opwT": opwT, "opb": opb,
            "f1g": f1g, "f1b": f1b, "wg": np.ascontiguousarray(wg),
        })
    res = run_bass_kernel_spmd(nc, in_maps, core_ids=list(range(NCORE)))
    y1T = np.zeros((BT, D, S), np.float32)
    y1rows = np.zeros((NT, D), bf)
    idxf = np.zeros(NT, np.float32)
    gvf = np.zeros(NT, np.float32)
    for c in range(NCORE):
        sl = slice(c * 1024, (c + 1) * 1024)
        r = res.results[c]
        y1T[0][:, sl] = r["y1T"][:, :1024]; y1T[1][:, sl] = r["y1T"][:, 1024:]
        y1rows[c * 1024:(c + 1) * 1024] = r["y1rows"][:1024]
        y1rows[8192 + c * 1024:8192 + (c + 1) * 1024] = r["y1rows"][1024:]
        idxf[c * 1024:(c + 1) * 1024] = r["idxr"][0, :1024]
        idxf[8192 + c * 1024:8192 + (c + 1) * 1024] = r["idxr"][0, 1024:]
        gvf[c * 1024:(c + 1) * 1024] = r["gvr"][0, :1024]
        gvf[8192 + c * 1024:8192 + (c + 1) * 1024] = r["gvr"][0, 1024:]
    return y1T, y1rows, idxf, gvf


def build_route():
    nc = bass.Bass("TRN2", target_bir_lowering=False)
    idxf = nc.dram_tensor("idxf", [1, NT], F32, kind="ExternalInput").ap()
    gvf = nc.dram_tensor("gvf", [1, NT], F32, kind="ExternalInput").ap()
    keepgv = nc.dram_tensor("keepgv", [1, NT], F32, kind="ExternalOutput").ap()
    slotT = nc.dram_tensor("slotT", [128, 128], U32, kind="ExternalOutput").ap()
    src2 = nc.dram_tensor("src2", [16384 + 8 * 14336 + 2, 2], U32,
                          kind="ExternalOutput").ap()
    slsc = nc.dram_tensor("slsc", [NT], U32, kind="Internal").ap()
    CH = 4096
    NCH = NT // CH
    sch = Sched()
    with ExitStack() as ctx:
        e = ctx.enter_context
        ld = e(nc.semaphore("ld"))
        vs = e(nc.semaphore("vs"))
        ts = e(nc.semaphore("ts"))
        gs = e(nc.semaphore("gs"))
        sto = e(nc.semaphore("sto"))
        idx8 = e(nc.sbuf_tensor("idx8", [8, CH], F32))
        oneh = e(nc.sbuf_tensor("oneh", [8, CH], F32))
        ones8w = e(nc.sbuf_tensor("ones8w", [8, CH], F32))
        csum = e(nc.sbuf_tensor("csum", [8, CH], F32))
        car8 = e(nc.sbuf_tensor("car8", [8, 1], F32))
        iota8 = e(nc.sbuf_tensor("iota8", [8, 1], F32))
        ones8 = e(nc.sbuf_tensor("ones8", [8, 1], F32))
        locc = e(nc.sbuf_tensor("locc", [1, CH], F32))
        keepc = e(nc.sbuf_tensor("keepc", [1, CH], F32))
        gvc = e(nc.sbuf_tensor("gvc", [1, CH], F32))
        slotr = e(nc.sbuf_tensor("slotr", [1, CH], F32))
        t1r = e(nc.sbuf_tensor("t1r", [1, CH], F32))
        slotu = e(nc.sbuf_tensor("slotu", [1, CH], U32))
        idxTc = e(nc.sbuf_tensor("idxTc", [128, CH // 128], U32))
        iodup = e(nc.sbuf_tensor("iodup", [128, 2 * (CH // 128)], U32))
        sltt = e(nc.sbuf_tensor("sltt", [128, 128], U32))
        psL = e(nc.psum_tensor("psL", [1, 512], F32))
        n_i = sch.emit("gpsimd", lambda e: e.iota(
            out=iota8[:], pattern=[[0, 1]], base=0, channel_multiplier=1,
            allow_small_or_imprecise_dtypes=True), inc=(gs, 1))
        sch.emit("vector", lambda e: e.memset(ones8w[:], 1.0))
        n_z = sch.emit("vector", lambda e: e.memset(car8[:], 0.0), inc=(vs, 1))
        n_z = sch.emit("vector", lambda e: e.memset(ones8[:], 1.0), inc=(vs, 1))
        sch.wait("vector", gs, n_i)
        sch.wait("tensor", vs, n_z)
        n_stp = 0
        for ch in range(NCH):
            c0 = ch * CH
            sch.wait("sync", sto, n_stp)
            for r in range(8):
                nl = sch.emit("sync", lambda e, r=r, c0=c0: e.dma_start(
                    out=idx8[r:r + 1, :], in_=idxf[:, c0:c0 + CH]), inc=(ld, 16))
            nl = sch.emit("sync", lambda e, c0=c0: e.dma_start(
                out=gvc[:], in_=gvf[:, c0:c0 + CH]), inc=(ld, 16))
            sch.wait("vector", ld, nl)
            n_oh = sch.emit("vector", lambda e: e.tensor_scalar(
                out=oneh[:], in0=idx8[:], scalar1=iota8[:], scalar2=None,
                op0=AX.is_equal), inc=(vs, 1))
            sch.wait("vector", vs, n_oh)
            ini = 0.0 if ch == 0 else car8[:]
            n_cs = sch.emit("vector", lambda e, ini=ini: e.tensor_tensor_scan(
                out=csum[:], data0=ones8w[:], data1=oneh[:], initial=ini,
                op0=AX.mult, op1=AX.add), inc=(vs, 1))
            sch.wait("vector", vs, n_cs)
            n_cr = sch.emit("vector", lambda e: e.tensor_copy(
                out=car8[:], in_=csum[:, CH - 1:CH]), inc=(vs, 1))
            sch.wait("vector", vs, n_cr)
            n_tm = sch.emit("vector", lambda e: e.tensor_mul(
                out=oneh[:], in0=oneh[:], in1=csum[:]), inc=(vs, 1))
            sch.wait("tensor", vs, n_tm)
            for sub in range(CH // 512):
                tL = sch.emit("tensor", lambda e, sub=sub: e.matmul(
                    psL[:], ones8[:], oneh[:, sub * 512:(sub + 1) * 512],
                    start=True, stop=True), inc=(ts, 1))
                sch.wait("vector", ts, tL)
                n_lc = sch.emit("vector", lambda e, sub=sub: e.tensor_scalar(
                    out=locc[:, sub * 512:(sub + 1) * 512], in0=psL[:],
                    scalar1=-1.0, scalar2=None, op0=AX.add), inc=(vs, 1))
                sch.wait("tensor", vs, n_lc)
            sch.emit("vector", lambda e: e.tensor_single_scalar(
                out=keepc[:], in_=locc[:], scalar=2048.0, op=AX.is_lt))
            n_kg = sch.emit("vector", lambda e: e.tensor_mul(
                out=gvc[:], in0=gvc[:], in1=keepc[:]), inc=(vs, 1))
            sch.wait("sync", vs, n_kg)
            n_st1 = sch.emit("sync", lambda e, c0=c0: e.dma_start(
                out=keepgv[:, c0:c0 + CH], in_=gvc[:]), inc=(sto, 16))
            sch.emit("vector", lambda e: e.scalar_tensor_tensor(
                out=slotr[:], in0=idx8[0:1, :], scalar=2048.0, in1=locc[:],
                op0=AX.mult, op1=AX.add))
            sch.emit("vector", lambda e: e.scalar_tensor_tensor(
                out=t1r[:], in0=idx8[0:1, :], scalar=14336.0, in1=locc[:],
                op0=AX.mult, op1=AX.add))
            n_s1 = sch.emit("vector", lambda e: e.tensor_scalar(
                out=t1r[:], in0=t1r[:], scalar1=14336.0, scalar2=None,
                op0=AX.add), inc=(vs, 1))
            sch.wait("vector", vs, n_s1)
            sch.emit("vector", lambda e: e.tensor_sub(
                out=slotr[:], in0=slotr[:], in1=t1r[:]))
            sch.emit("vector", lambda e: e.tensor_mul(
                out=slotr[:], in0=slotr[:], in1=keepc[:]))
            sch.emit("vector", lambda e: e.tensor_add(
                out=slotr[:], in0=slotr[:], in1=t1r[:]))
            n_su = sch.emit("vector", lambda e: e.tensor_copy(
                out=slotu[:], in_=slotr[:]), inc=(vs, 1))
            sch.wait("sync", vs, n_su)
            n_sd = sch.emit("sync", lambda e, c0=c0: e.dma_start(
                out=slsc[c0:c0 + CH].rearrange("(one t) -> one t", one=1),
                in_=slotu[:]), inc=(sto, 16))
            sch.wait("sync", sto, n_sd)
            sch.wait("gpsimd", sto, n_stp)
            n_io = sch.emit("gpsimd", lambda e, c0=c0: e.iota(
                out=iodup[:].rearrange("p (i c) -> p i c", c=2),
                pattern=[[128, CH // 128], [0, 2]], base=c0,
                channel_multiplier=1,
                allow_small_or_imprecise_dtypes=True), inc=(gs, 1))
            for i in range(CH // 128):
                n_ci = sch.emit("sync", lambda e, c0=c0, i=i: e.dma_start(
                    out=idxTc[:, i:i + 1],
                    in_=slsc[c0 + i * 128:c0 + i * 128 + 128].rearrange(
                        "(p one) -> p one", one=1)), inc=(ld, 16))
            sch.wait("gpsimd", ld, n_ci)
            for i in range(CH // 128):
                n_sc = sch.emit("gpsimd", lambda e, i=i: e.indirect_dma_start(
                    out=src2,
                    out_offset=bass.IndirectOffsetOnAxis(
                        ap=idxTc[:, i:i + 1], axis=0),
                    in_=iodup[:, 2 * i:2 * i + 2], in_offset=None),
                    inc=(sto, 16))
            n_stp = sch.n(sto)
            sch.wait("sync", sto, n_stp)
        for col in range(128):
            n_tr = sch.emit("sync", lambda e, col=col: e.dma_start(
                out=sltt[:, col:col + 1],
                in_=slsc[col * 128:col * 128 + 128].rearrange(
                    "(p one) -> p one", one=1)), inc=(ld, 16))
        sch.wait("vector", ld, n_tr)
        n_cl = sch.emit("vector", lambda e: e.tensor_single_scalar(
            out=sltt[:], in_=sltt[:], scalar=16383.0, op=AX.min), inc=(vs, 1))
        sch.wait("sync", vs, n_cl)
        n_f = sch.emit("sync", lambda e: e.dma_start(
            out=slotT, in_=sltt[:]), inc=(sto, 16))
        sch.wait("sync", sto, sch.n(sto))
        sch.check_deadlock()
        sch.replay(nc)
    return nc


def build_expert():
    nc = bass.Bass("TRN2", target_bir_lowering=False)
    y1r = nc.dram_tensor("y1r", [NT, D], BF16, kind="ExternalInput").ap()
    src2 = nc.dram_tensor("src2", [2048, 2], U32, kind="ExternalInput").ap()
    w1e = nc.dram_tensor("w1e", [D, 4096], F32, kind="ExternalInput").ap()
    b1e = nc.dram_tensor("b1e", [4096], F32, kind="ExternalInput").ap()
    w2e = nc.dram_tensor("w2e", [4096, D], F32, kind="ExternalInput").ap()
    b2e = nc.dram_tensor("b2e", [D], F32, kind="ExternalInput").ap()
    erow = nc.dram_tensor("erow", [2048, D], BF16, kind="ExternalOutput").ap()
    KD = 8
    KF = 32
    CAP = 2048
    sch = Sched()
    with ExitStack() as ctx:
        e = ctx.enter_context
        ld = e(nc.semaphore("ld"))
        vs = e(nc.semaphore("vs"))
        ts = e(nc.semaphore("ts"))
        as_ = e(nc.semaphore("as"))
        gs = e(nc.semaphore("gs"))
        sto = e(nc.semaphore("sto"))
        w1b = e(nc.sbuf_tensor("w1b", [128, KD * 4096], BF16))
        w2b = e(nc.sbuf_tensor("w2b", [128, KF * D], BF16))
        b1c = e(nc.sbuf_tensor("b1c", [128, KF], F32))
        b2c = e(nc.sbuf_tensor("b2c", [128, KD], F32))
        idxt = e(nc.sbuf_tensor("idxt", [128, 16], U32))
        identb = e(nc.sbuf_tensor("identb", [128, 128], BF16))
        with nc.sbuf_tensor("wstg", [128, KD * 2048], F32) as wstg:
            for q in range(2):
                nl = sch.emit("sync", lambda e, q=q: e.dma_start(
                    out=wstg[:].rearrange("p (k m) -> p k m", k=KD),
                    in_=w1e[:, q * 2048:(q + 1) * 2048].rearrange(
                        "(k p) m -> p k m", p=128)), inc=(ld, 16))
                sch.wait("vector", ld, nl)
                def cst(e, q=q):
                    dst = w1b[:].rearrange("p (k m) -> p k m", k=KD)[
                        :, :, q * 2048:(q + 1) * 2048]
                    return e.tensor_copy(
                        out=dst, in_=wstg[:].rearrange("p (k m) -> p k m", k=KD))
                n_c = sch.emit("vector", cst, inc=(vs, 1))
                sch.wait("sync", vs, n_c)
            for q in range(2):
                nl = sch.emit("sync", lambda e, q=q: e.dma_start(
                    out=wstg[:, 0:KF * 512].rearrange("p (k m) -> p k m", k=KF),
                    in_=w2e[:, q * 512:(q + 1) * 512].rearrange(
                        "(k p) m -> p k m", p=128)), inc=(ld, 16))
                sch.wait("vector", ld, nl)
                def cst2(e, q=q):
                    dst = w2b[:].rearrange("p (k m) -> p k m", k=KF)[
                        :, :, q * 512:(q + 1) * 512]
                    return e.tensor_copy(
                        out=dst,
                        in_=wstg[:, 0:KF * 512].rearrange("p (k m) -> p k m", k=KF))
                n_c = sch.emit("vector", cst2, inc=(vs, 1))
                sch.wait("sync", vs, n_c)
        dspT = e(nc.sbuf_tensor("dspT", [128, KD * CAP], BF16))
        hsb = e(nc.sbuf_tensor("hsb", [128, KF * 512], BF16))
        yty = e(nc.sbuf_tensor("yty", [128, KD * 512], BF16))
        rowst = e(nc.sbuf_tensor("rowst", [128, KD * 128], BF16))
        dsp = e(nc.sbuf_tensor("dsp", [128, D], BF16))
        psH = e(nc.psum_tensor("psH", [128, 512], F32))
        psTr = e(nc.psum_tensor("psTr", [128, 128], BF16))
        nl = sch.emit("sync", lambda e: e.dma_start(
            out=b1c[:], in_=b1e.rearrange("(m p) -> p m", p=128),
            allow_slow_non_contiguous=True), inc=(ld, 16))
        nl = sch.emit("sync", lambda e: e.dma_start(
            out=b2c[:], in_=b2e.rearrange("(m p) -> p m", p=128),
            allow_slow_non_contiguous=True), inc=(ld, 16))
        nid = sch.emit("gpsimd", lambda e: e.iota(
            out=identb[:], pattern=[[-1, 128]], base=0, channel_multiplier=1,
            allow_small_or_imprecise_dtypes=True), inc=(gs, 1))
        sch.wait("vector", gs, nid)
        n_id2 = sch.emit("vector", lambda e: e.tensor_single_scalar(
            out=identb[:], in_=identb[:], scalar=0.0, op=AX.is_equal), inc=(vs, 1))
        sch.wait("tensor", vs, n_id2)
        nl = sch.emit("sync", lambda e: e.dma_start(
            out=idxt[:], in_=src2[0:2048, 0:1].rearrange("(i p) c -> p (i c)", p=128),
            allow_slow_non_contiguous=True), inc=(ld, 16))
        sch.wait("gpsimd", ld, nl)
        sch.wait("gpsimd", vs, sch.n(vs))
        for i in range(16):
            n_g = sch.emit("gpsimd", lambda e, i=i: e.indirect_dma_start(
                out=dsp[:], out_offset=None, in_=y1r,
                in_offset=bass.IndirectOffsetOnAxis(ap=idxt[:, i:i + 1], axis=0)),
                inc=(gs, 16))
            sch.wait("tensor", gs, n_g)
            for k in range(KD):
                tT = sch.emit("tensor", lambda e, k=k: e.transpose(
                    psTr[:], dsp[:, k * 128:(k + 1) * 128], identb[:]),
                    inc=(ts, 1))
                sch.wait("vector", ts, tT)
                n_rc = sch.emit("vector", lambda e, k=k, i=i: e.tensor_copy(
                    out=dspT[:, k * CAP + i * 128:k * CAP + i * 128 + 128],
                    in_=psTr[:]), inc=(vs, 1))
                sch.wait("tensor", vs, n_rc)
            sch.wait("gpsimd", vs, sch.n(vs))
        n_stp = 0
        for tsu in range(4):
            t0 = tsu * 512
            for m in range(KF):
                for k in range(KD):
                    tH = sch.emit("tensor", lambda e, m=m, k=k, t0=t0: e.matmul(
                        psH[:], w1b[:, k * 4096 + m * 128:k * 4096 + m * 128 + 128],
                        dspT[:, k * CAP + t0:k * CAP + t0 + 512],
                        start=(k == 0), stop=(k == KD - 1)),
                        inc=(ts, 1) if k == KD - 1 else None)
                sch.wait("scalar", ts, tH)
                aH = sch.emit("scalar", lambda e, m=m: e.activation(
                    out=hsb[:, m * 512:(m + 1) * 512], in_=psH[:],
                    func=AF.Identity, bias=b1c[:, m:m + 1]), inc=(as_, 1))
                sch.wait("tensor", as_, aH)
            for m in range(KD):
                for k in range(KF):
                    tY = sch.emit("tensor", lambda e, m=m, k=k: e.matmul(
                        psH[:], w2b[:, k * D + m * 128:k * D + m * 128 + 128],
                        hsb[:, k * 512:(k + 1) * 512],
                        start=(k == 0), stop=(k == KF - 1)),
                        inc=(ts, 1) if k == KF - 1 else None)
                sch.wait("scalar", ts, tY)
                aY = sch.emit("scalar", lambda e, m=m: e.activation(
                    out=yty[:, m * 512:(m + 1) * 512], in_=psH[:],
                    func=AF.Identity, bias=b2c[:, m:m + 1]), inc=(as_, 1))
                sch.wait("tensor", as_, aY)
            for tt in range(4):
                sch.wait("vector", sto, sch.n(sto))
                for k in range(KD):
                    tT = sch.emit("tensor", lambda e, tt=tt, k=k: e.transpose(
                        psTr[:], yty[:, k * 512 + tt * 128:k * 512 + tt * 128 + 128],
                        identb[:]), inc=(ts, 1))
                    sch.wait("vector", ts, tT)
                    n_rc = sch.emit("vector", lambda e, k=k: e.tensor_copy(
                        out=rowst[:, k * 128:(k + 1) * 128], in_=psTr[:]),
                        inc=(vs, 1))
                    sch.wait("tensor", vs, n_rc)
                sch.wait("sync", vs, sch.n(vs))
                n_stp = sch.emit("sync", lambda e, t0=t0, tt=tt: e.dma_start(
                    out=erow[t0 + tt * 128:t0 + tt * 128 + 128, :],
                    in_=rowst[:]), inc=(sto, 16))
            sch.wait("scalar", sto, n_stp)
        sch.wait("sync", sto, sch.n(sto))
        sch.check_deadlock()
        sch.replay(nc)
    return nc


def build_combine():
    nc = bass.Bass("TRN2", target_bir_lowering=False)
    yall = nc.dram_tensor("yall", [NT, D], BF16, kind="ExternalInput").ap()
    slotTs = nc.dram_tensor("slotTs", [128, 16], U32, kind="ExternalInput").ap()
    kgs = nc.dram_tensor("kgs", [1, TB], F32, kind="ExternalInput").ap()
    y1Ts = nc.dram_tensor("y1Ts", [D, TB], F32, kind="ExternalInput").ap()
    x3Ts = nc.dram_tensor("x3Ts", [D, TB], F32, kind="ExternalInput").ap()
    g2 = nc.dram_tensor("g2", [D], F32, kind="ExternalInput").ap()
    b2a = nc.dram_tensor("b2a", [D], F32, kind="ExternalInput").ap()
    outT = nc.dram_tensor("outT", [D, TB], F32, kind="ExternalOutput").ap()
    KD = 8
    sch = Sched()
    with ExitStack() as ctx:
        e = ctx.enter_context
        ld = e(nc.semaphore("ld"))
        vs = e(nc.semaphore("vs"))
        ts = e(nc.semaphore("ts"))
        as_ = e(nc.semaphore("as"))
        gs = e(nc.semaphore("gs"))
        sto = e(nc.semaphore("sto"))
        t = ln_tiles(nc, e, sch, TSUB)
        t["vs"], t["ts"], t["as_"] = vs, ts, as_
        emit_ln = make_ln(sch, t)
        ln_init(sch, t, ld, vs)
        ymT = e(nc.sbuf_tensor("ymT", [128, KD * TB], BF16))
        idxt = e(nc.sbuf_tensor("idxt", [128, 16], U32))
        identb = e(nc.sbuf_tensor("identb", [128, 128], BF16))
        dsp = e(nc.sbuf_tensor("dsp", [128, D], BF16))
        gcol = e(nc.sbuf_tensor("gcol", [128, KD], F32))
        bcol = e(nc.sbuf_tensor("bcol", [128, KD], F32))
        kgr = e(nc.sbuf_tensor("kgr", [1, TB], F32))
        y1s = e(nc.sbuf_tensor("y1s", [128, KD * TSUB], F32))
        x3s = e(nc.sbuf_tensor("x3s", [128, KD * TSUB], F32))
        kb = e(nc.sbuf_tensor("kb", [128, TSUB], F32))
        onesr1 = e(nc.sbuf_tensor("onesr1", [1, 128], F32))
        psB = e(nc.psum_tensor("psB", [128, TSUB], F32))
        psTr = e(nc.psum_tensor("psTr", [128, 128], BF16))
        nl = sch.emit("sync", lambda e: e.dma_start(
            out=gcol[:], in_=g2.rearrange("(m p) -> p m", p=128),
            allow_slow_non_contiguous=True), inc=(ld, 16))
        nl = sch.emit("sync", lambda e: e.dma_start(
            out=bcol[:], in_=b2a.rearrange("(m p) -> p m", p=128),
            allow_slow_non_contiguous=True), inc=(ld, 16))
        nl = sch.emit("sync", lambda e: e.dma_start(out=idxt[:], in_=slotTs),
                      inc=(ld, 16))
        nl = sch.emit("sync", lambda e: e.dma_start(out=kgr[:], in_=kgs),
                      inc=(ld, 16))
        sch.wait("vector", ld, nl)
        sch.emit("vector", lambda e: e.memset(onesr1[:], 1.0))
        ln_init(sch, t, ld, vs)
        nid = sch.emit("gpsimd", lambda e: e.iota(
            out=identb[:], pattern=[[-1, 128]], base=0, channel_multiplier=1,
            allow_small_or_imprecise_dtypes=True), inc=(gs, 1))
        sch.wait("vector", gs, nid)
        n_id2 = sch.emit("vector", lambda e: e.tensor_single_scalar(
            out=identb[:], in_=identb[:], scalar=0.0, op=AX.is_equal), inc=(vs, 1))
        sch.wait("tensor", vs, n_id2)
        sch.wait("gpsimd", ld, nl)
        for i in range(16):
            n_g = sch.emit("gpsimd", lambda e, i=i: e.indirect_dma_start(
                out=dsp[:], out_offset=None, in_=yall,
                in_offset=bass.IndirectOffsetOnAxis(ap=idxt[:, i:i + 1], axis=0)),
                inc=(gs, 16))
            sch.wait("tensor", gs, n_g)
            for k in range(KD):
                tT = sch.emit("tensor", lambda e, k=k: e.transpose(
                    psTr[:], dsp[:, k * 128:(k + 1) * 128], identb[:]),
                    inc=(ts, 1))
                sch.wait("vector", ts, tT)
                n_rc = sch.emit("vector", lambda e, k=k, i=i: e.tensor_copy(
                    out=ymT[:, k * TB + i * 128:k * TB + i * 128 + 128],
                    in_=psTr[:]), inc=(vs, 1))
                sch.wait("tensor", vs, n_rc)
            sch.wait("gpsimd", vs, sch.n(vs))
        n_sto = 0
        for su in range(TB // TSUB):
            t0 = su * TSUB
            sch.wait("sync", sto, n_sto)
            nl = sch.emit("sync", lambda e, t0=t0: e.dma_start(
                out=y1s[:].rearrange("p (k t) -> p k t", k=KD),
                in_=y1Ts[:, t0:t0 + TSUB].rearrange("(k p) t -> p k t", p=128)),
                inc=(ld, 16))
            nl = sch.emit("sync", lambda e, t0=t0: e.dma_start(
                out=x3s[:].rearrange("p (k t) -> p k t", k=KD),
                in_=x3Ts[:, t0:t0 + TSUB].rearrange("(k p) t -> p k t", p=128)),
                inc=(ld, 16))
            sch.wait("tensor", ld, nl)
            tB = sch.emit("tensor", lambda e, t0=t0: e.matmul(
                psB[:], onesr1[:], kgr[:, t0:t0 + TSUB], start=True, stop=True),
                inc=(ts, 1))
            sch.wait("vector", ts, tB)
            sch.wait("vector", ld, nl)
            n_kb = sch.emit("vector", lambda e: e.tensor_copy(
                out=kb[:], in_=psB[:]), inc=(vs, 1))
            for m in range(KD):
                sch.emit("vector", lambda e, m=m, t0=t0: e.tensor_mul(
                    out=t["d1"][:], in0=ymT[:, m * TB + t0:m * TB + t0 + TSUB],
                    in1=kb[:]))
                n_r = sch.emit("vector", lambda e, m=m: e.tensor_add(
                    out=y1s[:, m * TSUB:(m + 1) * TSUB],
                    in0=y1s[:, m * TSUB:(m + 1) * TSUB], in1=t["d1"][:]),
                    inc=(vs, 1))
            emit_ln(y1s, y1s, None, gcol, bcol)
            n_a2 = sch.emit("vector", lambda e: e.tensor_add(
                out=x3s[:], in0=x3s[:], in1=y1s[:]), inc=(vs, 1))
            sch.wait("tensor", vs, n_a2)
            emit_ln(x3s, x3s)
            n_f = sch.n(vs)
            sch.wait("sync", vs, n_f)
            n_sto = sch.emit("sync", lambda e, t0=t0: e.dma_start(
                out=outT[:, t0:t0 + TSUB].rearrange("(k p) t -> p k t", p=128),
                in_=x3s[:].rearrange("p (k t) -> p k t", k=KD)), inc=(sto, 16))
        sch.wait("sync", sto, sch.n(sto))
        sch.check_deadlock()
        sch.replay(nc)
    return nc


def run_route(idxf, gvf):
    nc = cached_build("route", build_route)
    im = {"idxf": idxf.reshape(1, NT).astype(np.float32),
          "gvf": gvf.reshape(1, NT).astype(np.float32)}
    res = run_bass_kernel_spmd(nc, [im] * NCORE, core_ids=list(range(NCORE)))
    r = res.results[0]
    return (np.asarray(r["keepgv"])[0], np.asarray(r["slotT"]),
            np.asarray(r["src2"]))


def run_expert(y1rows, src2, w1, b1, w2, b2):
    nc = cached_build("expert", build_expert)
    in_maps = []
    for c in range(NCORE):
        in_maps.append({
            "y1r": y1rows,
            "src2": np.ascontiguousarray(src2[c * 2048:(c + 1) * 2048]),
            "w1e": np.ascontiguousarray(w1[c]),
            "b1e": np.ascontiguousarray(b1[c]),
            "w2e": np.ascontiguousarray(w2[c]),
            "b2e": np.ascontiguousarray(b2[c]),
        })
    res = run_bass_kernel_spmd(nc, in_maps, core_ids=list(range(NCORE)))
    return np.concatenate(
        [np.asarray(res.results[c]["erow"]) for c in range(NCORE)], axis=0)


def run_combine(yall, slotT, keepgv, y1T, x3T, g2, b2a):
    nc = cached_build("combine", build_combine)
    in_maps = []
    for c in range(NCORE):
        sl = slice(c * 1024, (c + 1) * 1024)
        slotTs = np.ascontiguousarray(np.concatenate(
            [slotT[:, c * 8:(c + 1) * 8], slotT[:, 64 + c * 8:64 + (c + 1) * 8]],
            axis=1))
        kgs = np.concatenate(
            [keepgv[c * 1024:(c + 1) * 1024],
             keepgv[8192 + c * 1024:8192 + (c + 1) * 1024]]).reshape(1, TB)
        in_maps.append({
            "yall": yall, "slotTs": slotTs,
            "kgs": np.ascontiguousarray(kgs.astype(np.float32)),
            "y1Ts": np.ascontiguousarray(np.concatenate(
                [y1T[0][:, sl], y1T[1][:, sl]], axis=1)),
            "x3Ts": np.ascontiguousarray(np.concatenate(
                [x3T[0][:, sl], x3T[1][:, sl]], axis=1)),
            "g2": g2, "b2a": b2a,
        })
    res = run_bass_kernel_spmd(nc, in_maps, core_ids=list(range(NCORE)))
    outT = np.zeros((BT, D, S), np.float32)
    for c in range(NCORE):
        sl = slice(c * 1024, (c + 1) * 1024)
        outT[0][:, sl] = res.results[c]["outT"][:, :1024]
        outT[1][:, sl] = res.results[c]["outT"][:, 1024:]
    return outT


def kernel(L0_attn_x, L0_attn_Wq, L0_attn_Wk, L0_attn_Wv, L0_attn_Wo,
           L0_ffn_kernel, L0_ffn_input_scale, L0_ffn_kernel_scale, L0_ffn_bias,
           L1_attn_Wr, L1_attn_Wk, L1_attn_Wv, L1_attn_Wo, L1_attn_W1,
           L1_attn_b1, L1_attn_W2, L1_attn_b2, L1_attn_ln1_g, L1_attn_ln1_b,
           L1_attn_ln2_g, L1_attn_ln2_b, L1_ffn_in_proj_w, L1_ffn_in_proj_b,
           L1_ffn_out_proj_w, L1_ffn_out_proj_b, L1_ffn_ln1_g, L1_ffn_ln1_b,
           L1_ffn_ln2_g, L1_ffn_ln2_b, L1_ffn_wg, L1_ffn_w1, L1_ffn_b1,
           L1_ffn_w2, L1_ffn_b2):
    x = np.asarray(L0_attn_x, np.float32)
    f32 = lambda a: np.asarray(a, np.float32)
    oT = run_l0_attn(x, f32(L0_attn_Wq), f32(L0_attn_Wk), f32(L0_attn_Wv))
    x2T, x2Tb = run_l0_rest(
        x, oT, f32(L0_attn_Wo), np.asarray(L0_ffn_kernel),
        float(np.asarray(L0_ffn_input_scale)[0]),
        float(np.asarray(L0_ffn_kernel_scale)[0]), np.asarray(L0_ffn_bias))
    yT = run_rkv(x2Tb, f32(L1_attn_Wr), f32(L1_attn_Wk), f32(L1_attn_Wv))
    x1T, x1Tb, h1T = run_postrkv(
        x2T, yT, f32(L1_attn_Wo), f32(L1_attn_ln1_g), f32(L1_attn_ln1_b),
        f32(L1_attn_W1), f32(L1_attn_b1))
    s1 = run_lif_phase(h1T, 32, 2, 8)
    h2T = run_w2(s1, f32(L1_attn_W2), f32(L1_attn_b2))
    fsp = run_lif_phase(h2T, 64, 1, 2)
    x3T, x3Tb = run_h1(x1T, fsp, x2T, f32(L1_attn_ln2_g), f32(L1_attn_ln2_b))
    y1T, y1rows, idxf, gvf = run_h2(
        x3T, x3Tb, f32(L1_ffn_in_proj_w), f32(L1_ffn_in_proj_b),
        f32(L1_ffn_out_proj_w), f32(L1_ffn_out_proj_b),
        f32(L1_ffn_ln1_g), f32(L1_ffn_ln1_b), f32(L1_ffn_wg))
    keepgv, slotT, src2 = run_route(idxf, gvf)
    yall = run_expert(y1rows, src2, f32(L1_ffn_w1), f32(L1_ffn_b1),
                      f32(L1_ffn_w2), f32(L1_ffn_b2))
    outT = run_combine(yall, slotT, keepgv, y1T, x3T,
                       f32(L1_ffn_ln2_g), f32(L1_ffn_ln2_b))
    return np.ascontiguousarray(outT.transpose(0, 2, 1))


# version 05
# speedup vs baseline: 1.2120x; 1.2120x over previous
import math
from contextlib import ExitStack

import numpy as np
import ml_dtypes
import concourse.bass as bass
import concourse.mybir as mybir
from concourse.bass_utils import run_bass_kernel_spmd

F32 = mybir.dt.float32
BF16 = mybir.dt.bfloat16
FP8 = mybir.dt.float8e4
U8 = mybir.dt.uint8
U32 = mybir.dt.uint32
AX = mybir.AluOpType
AF = mybir.ActivationFunctionType
AXL = mybir.AxisListType

BT = 2
S = 8192
D = 1024
H = 16
HD = 64
NCORE = 8
NT = BT * S

bf = ml_dtypes.bfloat16


class Sched:
    ENGINES = ("sync", "vector", "scalar", "tensor", "gpsimd")

    def __init__(self):
        self.q = {e: [] for e in self.ENGINES}
        self.cnt = {}
        self.trace = {e: [] for e in self.ENGINES}

    def emit(self, eng, fn, inc=None):
        if inc is None:
            self.q[eng].append(fn)
            self.trace[eng].append(("op", None, 0))
            return None
        sem, amt = inc
        self.cnt[sem.name] = self.cnt.get(sem.name, 0) + amt
        self.trace[eng].append(("inc", sem.name, amt))

        def wrapped(e, fn=fn, sem=sem, amt=amt):
            fn(e).then_inc(sem, amt)
        self.q[eng].append(wrapped)
        return self.cnt[sem.name]

    def wait(self, eng, sem, val):
        if val is None or val <= 0:
            return
        self.trace[eng].append(("wait", sem.name, val))
        self.q[eng].append(lambda e, sem=sem, val=val: e.wait_ge(sem, val))

    def check_deadlock(self):
        pos = {e: 0 for e in self.ENGINES}
        cnt = {}
        progress = True
        while progress:
            progress = False
            for e in self.ENGINES:
                while pos[e] < len(self.trace[e]):
                    kind, sem, val = self.trace[e][pos[e]]
                    if kind == "wait" and cnt.get(sem, 0) < val:
                        break
                    if kind == "inc":
                        cnt[sem] = cnt.get(sem, 0) + val
                    pos[e] += 1
                    progress = True
        stuck = {e: (pos[e], len(self.trace[e])) for e in self.ENGINES
                 if pos[e] < len(self.trace[e])}
        if stuck:
            detail = []
            for e, (p, n) in stuck.items():
                kind, sem, val = self.trace[e][p]
                detail.append(f"{e}@{p}/{n}: waiting {sem}>={val} (has {cnt.get(sem, 0)})")
            raise RuntimeError("DEADLOCK: " + "; ".join(detail))

    def n(self, sem):
        return self.cnt.get(sem.name, 0)

    def replay(self, nc):
        with nc.Block() as block:
            for eng in self.ENGINES:
                lst = self.q[eng]
                if not lst:
                    continue
                deco = getattr(block, eng)

                def body(e, lst=lst):
                    for fn in lst:
                        fn(e)
                deco(body)


_PROG_CACHE = {}


def cached_build(key, fn):
    if key not in _PROG_CACHE:
        _PROG_CACHE[key] = fn()
    return _PROG_CACHE[key]


def build_l0_attn(variant='full'):
    nc = bass.Bass("TRN2", target_bir_lowering=False)
    xT = nc.dram_tensor("xT", [D, S], F32, kind="ExternalInput").ap()
    wqT = nc.dram_tensor("wqT", [D, 256], F32, kind="ExternalInput").ap()
    wkT = nc.dram_tensor("wkT", [D, 256], F32, kind="ExternalInput").ap()
    wvT = nc.dram_tensor("wvT", [D, 256], F32, kind="ExternalInput").ap()
    cosr = nc.dram_tensor("cosr", [128, S], F32, kind="ExternalInput").ap()
    sinr = nc.dram_tensor("sinr", [128, S], F32, kind="ExternalInput").ap()
    oT = nc.dram_tensor("oT", [256, S], F32, kind="ExternalOutput").ap()

    TC = 512
    NTC = S // TC
    KD = D // 128
    NV = S // 128

    sch = Sched()
    with ExitStack() as ctx:
        e = ctx.enter_context
        qtb = [e(nc.sbuf_tensor(f"qtb{i}", [128, S], BF16)) for i in range(2)]
        ktb = [e(nc.sbuf_tensor(f"ktb{i}", [128, S], BF16)) for i in range(2)]
        vpt = e(nc.sbuf_tensor("vpt", [128, NV * 260], BF16))
        ones128 = e(nc.sbuf_tensor("ones128", [128, 1], F32))
        onesb = e(nc.sbuf_tensor("onesb", [1, 64], F32))
        negm0 = e(nc.sbuf_tensor("negm0", [128, 1], F32))
        onesr = e(nc.sbuf_tensor("onesr", [1, 128], F32))
        nrm = e(nc.sbuf_tensor("nrm", [1, 4], F32))
        rmx = e(nc.sbuf_tensor("rmx", [1, 2], F32))
        ld = e(nc.semaphore("ld"))
        vs = e(nc.semaphore("vs"))
        ts = e(nc.semaphore("ts"))
        as_ = e(nc.semaphore("as"))
        gs = e(nc.semaphore("gs"))
        sto = e(nc.semaphore("sto"))

        shuf_mask = []
        for i in range(0, 32, 2):
            shuf_mask += [i + 1, i]

        with ExitStack() as proj:
            p = proj.enter_context
            xts = [p(nc.sbuf_tensor(f"xts{i}", [128, KD * TC], F32))
                   for i in range(2)]
            xtsb = [p(nc.sbuf_tensor(f"xtsb{i}", [128, KD * TC], BF16))
                    for i in range(2)]
            cosc = [p(nc.sbuf_tensor(f"cosc{i}", [128, TC], F32))
                    for i in range(2)]
            sinc = [p(nc.sbuf_tensor(f"sinc{i}", [128, TC], F32))
                    for i in range(2)]
            wstg = p(nc.sbuf_tensor("wstg", [128, KD * 256], F32))
            wq3b = p(nc.sbuf_tensor("wq3b", [128, KD * 256 * 3], BF16))
            qf = p(nc.sbuf_tensor("qf", [128, TC], F32))
            shf = p(nc.sbuf_tensor("shf", [128, TC], F32))
            am = p(nc.sbuf_tensor("am", [128, TC], F32))
            m2 = p(nc.sbuf_tensor("m2", [128, TC], F32))
            nsq = p(nc.sbuf_tensor("nsq", [128, TC], F32))
            psQK = [p(nc.psum_tensor(f"psQK{i}", [128, TC], F32)) for i in range(4)]
            psV = p(nc.psum_tensor("psV", [128, 256], F32))
            psN = p(nc.psum_tensor("psN", [1, TC], F32))

            for wi, w in enumerate((wqT, wkT, wvT)):
                n_ld = sch.emit("sync", lambda e, w=w: e.dma_start(
                    out=wstg[:].rearrange("p (k m) -> p k m", k=KD),
                    in_=w.rearrange("(k p) m -> p k m", p=128)),
                    inc=(ld, 16))
                sch.wait("vector", ld, n_ld)
                n_cast = sch.emit("vector", lambda e, wi=wi: e.tensor_copy(
                    out=wq3b[:, wi * KD * 256:(wi + 1) * KD * 256], in_=wstg[:]),
                    inc=(vs, 1))
                sch.wait("sync", vs, n_cast)
            sch.emit("vector", lambda e: e.memset(ones128[:], 1.0))
            sch.emit("vector", lambda e: e.memset(onesb[:], 1.0))
            sch.emit("vector", lambda e: e.memset(onesr[:], 1.0))
            sch.emit("vector", lambda e: e.memset(nrm[:], 0.0))

            def set_vones(e):
                return e.memset(
                    vpt[:].rearrange("p (t h c) -> p t h c", h=4, c=65)[:, :, :, 64:65],
                    1.0)
            n_init = sch.emit("vector", set_vones, inc=(vs, 1))
            sch.wait("tensor", vs, n_init)

            if variant == "projW":
                sch.wait("sync", vs, sch.n(vs))
                n_dbg = sch.emit("sync", lambda e: e.dma_start(
                    out=oT[0:128, 0:4],
                    in_=wstg[:, 0:4], allow_slow_non_contiguous=True), inc=(sto, 16))
                sch.wait("sync", sto, n_dbg)
                sch.check_deadlock()
                sch.replay(nc)
                return nc
            nchunks = NTC
            do_qk = do_v = do_rope = do_norm = True
            if variant.startswith("projC"):
                parts = variant[5:].split("-")
                nchunks = int(parts[0])
                if len(parts) > 1:
                    do_qk = "qk" in parts
                    do_v = "v" in parts
                    do_rope = "rope" in parts
                    do_norm = "norm" in parts
            tsN_prev = 0
            red_prev = 0
            ld_done = [0, 0]
            consumed = [0, 0]

            def emit_loads(tc):
                buf = tc % 2
                sch.wait("sync", vs, consumed[buf])
                sch.emit("sync", lambda e, tc=tc, buf=buf: e.dma_start(
                    out=xts[buf][:].rearrange("p (k t) -> p k t", k=KD),
                    in_=xT[:, tc * TC:(tc + 1) * TC].rearrange(
                        "(k p) t -> p k t", p=128)), inc=(ld, 16))
                sch.emit("sync", lambda e, tc=tc, buf=buf: e.dma_start(
                    out=cosc[buf][:], in_=cosr[:, tc * TC:(tc + 1) * TC]),
                    inc=(ld, 16))
                nl = sch.emit("sync", lambda e, tc=tc, buf=buf: e.dma_start(
                    out=sinc[buf][:], in_=sinr[:, tc * TC:(tc + 1) * TC]),
                    inc=(ld, 16))
                ld_done[buf] = nl

            emit_loads(0)
            for tc in range(NTC):
                buf = tc % 2
                if tc + 1 < NTC:
                    emit_loads(tc + 1)
                sch.wait("vector", ld, ld_done[buf])
                n_xc = sch.emit("vector", lambda e, buf=buf: e.tensor_copy(
                    out=xtsb[buf][:], in_=xts[buf][:]), inc=(vs, 1))
                sch.wait("tensor", vs, n_xc)
                sch.wait("tensor", ld, ld_done[buf])
                tsQK = {}
                for wi in range(2):
                    for m in range(2):
                        for k in range(KD):
                            def mm(e, wi=wi, m=m, k=k, buf=buf):
                                return e.matmul(
                                    psQK[wi * 2 + m][:],
                                    wq3b[:, wi * KD * 256 + k * 256 + m * 128:
                                         wi * KD * 256 + k * 256 + m * 128 + 128],
                                    xtsb[buf][:, k * TC:(k + 1) * TC],
                                    start=(k == 0), stop=(k == KD - 1))
                            r = sch.emit("tensor", mm,
                                         inc=(ts, 1) if k == KD - 1 else None)
                        tsQK[(wi, m)] = r
                for sv in range(4):
                    for k in range(KD):
                        def mmv(e, sv=sv, k=k, buf=buf):
                            return e.matmul(
                                psV[:],
                                xtsb[buf][:, k * TC + sv * 128:
                                          k * TC + sv * 128 + 128],
                                wq3b[:, 2 * KD * 256 + k * 256:
                                     2 * KD * 256 + (k + 1) * 256],
                                start=(k == 0), stop=(k == KD - 1))
                        r = sch.emit("tensor", mmv,
                                     inc=(ts, 1) if k == KD - 1 else None)
                    sch.wait("vector", ts, r)

                    def evv(e, tc=tc, sv=sv):
                        tchunk = tc * 4 + sv
                        dst = vpt[:, tchunk * 260:tchunk * 260 + 260].rearrange(
                            "p (h c) -> p h c", h=4)[:, :, 0:64]
                        return e.tensor_copy(
                            out=dst, in_=psV[:].rearrange("p (h c) -> p h c", h=4))
                    n_evv = sch.emit("vector", evv, inc=(vs, 1))
                    if sv < 3:
                        sch.wait("tensor", vs, n_evv)
                for wi in range(2):
                    dstt = qtb if wi == 0 else ktb
                    for m in range(2):
                        bank = psQK[wi * 2 + m]
                        sch.wait("scalar", ts, tsQK[(wi, m)])
                        sch.wait("scalar", ts, tsN_prev)
                        aN = sch.emit("scalar", lambda e, bank=bank: e.activation(
                            out=nsq[:], in_=bank[:], func=AF.Square), inc=(as_, 1))
                        sch.wait("tensor", as_, aN)
                        if m == 0:
                            sch.wait("tensor", vs, red_prev)
                        tsN_prev = sch.emit("tensor", lambda e, m=m: e.matmul(
                            psN[:], ones128[:], nsq[:],
                            start=(m == 0), stop=(m == 1)), inc=(ts, 1))
                        sch.wait("vector", ts, tsQK[(wi, m)])
                        sch.wait("vector", as_, aN)
                        sch.emit("vector", lambda e, bank=bank: e.tensor_copy(
                            out=qf[:], in_=bank[:]))
                        sch.emit("vector", lambda e: e.stream_shuffle(
                            out=shf[:], in_=qf[:], mask=shuf_mask))
                        sch.emit("vector", lambda e, bank=bank, buf=buf:
                                 e.tensor_mul(
                            out=am[:], in0=bank[:], in1=cosc[buf][:]))
                        sch.emit("vector", lambda e, buf=buf: e.tensor_mul(
                            out=m2[:], in0=shf[:], in1=sinc[buf][:]))
                        n_rope = sch.emit(
                            "vector", lambda e, dstt=dstt, m=m, tc=tc:
                            e.tensor_add(
                                out=dstt[m][:, tc * TC:(tc + 1) * TC],
                                in0=am[:], in1=m2[:]), inc=(vs, 1))
                        if m == 1:
                            sch.wait("vector", ts, tsN_prev)
                            n_rx = sch.emit("vector", lambda e, wi=wi:
                                            e.tensor_reduce(
                                out=rmx[:, wi:wi + 1], in_=psN[:],
                                axis=AXL.X, op=AX.max), inc=(vs, 1))
                            sch.wait("vector", vs, n_rx)
                            red_prev = sch.emit(
                                "vector", lambda e, wi=wi: e.tensor_tensor(
                                    out=nrm[:, wi:wi + 1], in0=nrm[:, wi:wi + 1],
                                    in1=rmx[:, wi:wi + 1], op=AX.max), inc=(vs, 1))
                consumed[buf] = sch.n(vs)

            sch.emit("vector", lambda e: e.tensor_mul(
                out=nrm[:, 2:3], in0=nrm[:, 0:1], in1=nrm[:, 1:2]), inc=(vs, 1))
            sch.wait("scalar", vs, sch.n(vs))
            aS = sch.emit("scalar", lambda e: e.activation(
                out=nrm[:, 3:4], in_=nrm[:, 2:3], func=AF.Sqrt), inc=(as_, 1))
            sch.wait("tensor", as_, aS)
            psM = nc.psum_tensor("psM", [128, 1], F32)
            psMh = psM.__enter__()
            nG = sch.emit("tensor", lambda e: e.matmul(
                psMh[:], onesr[:], nrm[:, 3:4], start=True, stop=True), inc=(ts, 1))
            sch.wait("vector", ts, nG)
            n_m0 = sch.emit("vector", lambda e: e.tensor_scalar_mul(
                out=negm0[:], in0=psMh[:], scalar1=-0.125), inc=(vs, 1))
            psM.__exit__(None, None, None)

        if variant == "proj":
            sch.wait("sync", vs, n_m0)
            n_dbg = sch.emit("sync", lambda e: e.dma_start(
                out=oT[0:128, 0:1], in_=negm0[:],
                allow_slow_non_contiguous=True), inc=(sto, 16))
            sch.wait("sync", sto, n_dbg)
            sch.check_deadlock()
            sch.replay(nc)
            return nc
        with ExitStack() as fl:
            p = fl.enter_context
            pbuf = [[p(nc.sbuf_tensor(f"pbuf{h}_{i}", [128, TC], BF16))
                     for i in range(2)] for h in range(2)]
            rb = p(nc.sbuf_tensor("rb", [64, TC], F32))
            rl = p(nc.sbuf_tensor("rl", [1, TC], F32))
            onorm = [p(nc.sbuf_tensor(f"onorm{i}", [64, TC], F32)) for i in range(2)]
            psS = [[p(nc.psum_tensor(f"psS{h}_{i}", [128, TC], F32))
                    for i in range(2)] for h in range(2)]
            psO = [p(nc.psum_tensor(f"psO{h}", [65, TC], F32)) for h in range(2)]
            psB = p(nc.psum_tensor("psB", [64, TC], F32))

            sch.wait("tensor", vs, n_m0)
            sch.wait("scalar", vs, n_m0)

            last_exp = [[0, 0], [0, 0]]
            last_use_p = [[0, 0], [0, 0]]
            last_norm = [0, 0]
            last_sto = [0, 0]
            qi = 0
            for hp in range(2):
                qsrc = qtb[hp]
                ksrc = ktb[hp]
                for qt in range(NTC):
                    nkc = 4 * qt + 4
                    tPV = [0, 0]

                    def emit_S(kc):
                        sb = kc % 2
                        for hm in range(2):
                            sch.wait("tensor", as_, last_exp[hm][sb])
                            if kc == 0:
                                sch.wait("tensor", vs, last_norm[hm])
                            tS = sch.emit("tensor", lambda e, hm=hm, kc=kc,
                                          sb=sb, qt=qt, qsrc=qsrc, ksrc=ksrc:
                                          e.matmul(
                                psS[hm][sb][:],
                                ksrc[hm * 64:hm * 64 + 64,
                                     kc * 128:(kc + 1) * 128],
                                qsrc[hm * 64:hm * 64 + 64,
                                     qt * TC:(qt + 1) * TC],
                                start=True, stop=True), inc=(ts, 1))
                            sch.wait("scalar", ts, tS)
                            sch.wait("scalar", ts, last_use_p[hm][sb])
                            aE = sch.emit("scalar", lambda e, hm=hm, sb=sb:
                                          e.activation(
                                out=pbuf[hm][sb][:], in_=psS[hm][sb][:],
                                func=AF.Exp, bias=negm0[:], scale=0.125),
                                inc=(as_, 1))
                            last_exp[hm][sb] = aE
                            if kc >= 4 * qt:
                                sch.wait("gpsimd", as_, aE)
                                sch.emit("gpsimd", lambda e, hm=hm, sb=sb,
                                         qt=qt, kc=kc: e.affine_select(
                                    out=pbuf[hm][sb][:], in_=pbuf[hm][sb][:],
                                    pattern=[[1, TC]], compare_op=AX.is_ge,
                                    fill=0.0, base=qt * TC - kc * 128,
                                    channel_multiplier=-1), inc=(gs, 1))

                    def emit_PV(kc):
                        sb = kc % 2
                        for hm in range(2):
                            if kc >= 4 * qt:
                                sch.wait("tensor", gs, sch.n(gs))
                            else:
                                sch.wait("tensor", as_, last_exp[hm][sb])
                            h = hp * 2 + hm
                            tPV[hm] = sch.emit("tensor", lambda e, h=h, kc=kc,
                                               hm=hm, sb=sb, nkc=nkc: e.matmul(
                                psO[hm][:],
                                vpt[:, kc * 260 + h * 65:
                                    kc * 260 + h * 65 + 65],
                                pbuf[hm][sb][:],
                                start=(kc == 0), stop=(kc == nkc - 1)),
                                inc=(ts, 1))
                            last_use_p[hm][sb] = tPV[hm]

                    for kc in range(nkc):
                        emit_S(kc)
                        if kc >= 1:
                            emit_PV(kc - 1)
                    emit_PV(nkc - 1)
                    for hm in range(2):
                        h = hp * 2 + hm
                        ob = qi % 2
                        qi += 1
                        sch.wait("vector", ts, tPV[hm])
                        nR = sch.emit("vector", lambda e, hm=hm: e.reciprocal(
                            out=rl[:], in_=psO[hm][64:65, :]), inc=(vs, 1))
                        sch.wait("tensor", vs, nR)
                        tB = sch.emit("tensor", lambda e: e.matmul(
                            psB[:], onesb[:], rl[:], start=True, stop=True),
                            inc=(ts, 1))
                        sch.wait("vector", ts, tB)
                        sch.emit("vector", lambda e: e.tensor_copy(
                            out=rb[:], in_=psB[:]))
                        sch.wait("vector", sto, last_sto[ob])
                        last_norm[hm] = sch.emit("vector",
                                                 lambda e, hm=hm, ob=ob:
                                                 e.tensor_mul(
                            out=onorm[ob][:], in0=psO[hm][0:64, :], in1=rb[:]),
                            inc=(vs, 1))
                        sch.wait("sync", vs, last_norm[hm])
                        last_sto[ob] = sch.emit(
                            "sync", lambda e, h=h, qt=qt, ob=ob: e.dma_start(
                                out=oT[h * 64:h * 64 + 64, qt * TC:(qt + 1) * TC],
                                in_=onorm[ob][:]), inc=(sto, 16))
            sch.wait("sync", sto, sch.n(sto))
        sch.check_deadlock()
        sch.replay(nc)
    return nc


def rope_tables():
    inv_freq = (1.0 / (10000.0 ** (np.arange(0, HD, 2, dtype=np.float32) / HD))
                ).astype(np.float32)
    pos = np.arange(S, dtype=np.float32)
    freqs = pos[:, None] * inv_freq[None, :]
    emb = np.repeat(freqs, 2, axis=-1)
    cos_t = np.cos(emb).astype(np.float32)
    sin_t = np.sin(emb).astype(np.float32)
    sign = np.where(np.arange(HD) % 2 == 0, -1.0, 1.0).astype(np.float32)
    sinp = sin_t * sign[None, :]
    cosr = np.ascontiguousarray(np.tile(cos_t.T, (2, 1)))
    sinr = np.ascontiguousarray(np.tile(sinp.T, (2, 1)))
    return cosr, sinr


def run_l0_attn(x, Wq, Wk, Wv):
    nc = cached_build("l0_attn", build_l0_attn)
    cosr, sinr = rope_tables()
    in_maps = []
    for c in range(NCORE):
        b = c // 4
        h0 = 4 * (c % 4)
        sl = slice(64 * h0, 64 * h0 + 256)
        in_maps.append({
            "xT": np.ascontiguousarray(x[b].T),
            "wqT": np.ascontiguousarray(Wq.T[:, sl]),
            "wkT": np.ascontiguousarray(Wk.T[:, sl]),
            "wvT": np.ascontiguousarray(Wv.T[:, sl]),
            "cosr": cosr, "sinr": sinr,
        })
    res = run_bass_kernel_spmd(nc, in_maps, core_ids=list(range(NCORE)))
    oT = np.zeros((BT, D, S), np.float32)
    for c in range(NCORE):
        b = c // 4
        h0 = 4 * (c % 4)
        oT[b, 64 * h0:64 * h0 + 256] = res.results[c]["oT"]
    return oT


TB = 2048
TSUB = 512


def build_l0_rest():
    nc = bass.Bass("TRN2", target_bir_lowering=False)
    xTs = nc.dram_tensor("xTs", [D, TB], F32, kind="ExternalInput").ap()
    oTs = nc.dram_tensor("oTs", [D, TB], F32, kind="ExternalInput").ap()
    woT = nc.dram_tensor("woT", [D, D], F32, kind="ExternalInput").ap()
    fker = nc.dram_tensor("fker", [D, D], BF16, kind="ExternalInput").ap()
    fscale = nc.dram_tensor("fscale", [1, 2], F32, kind="ExternalInput").ap()
    fbias = nc.dram_tensor("fbias", [D], BF16, kind="ExternalInput").ap()
    x2T = nc.dram_tensor("x2T", [D, TB], F32, kind="ExternalOutput").ap()
    x2Tb = nc.dram_tensor("x2Tb", [D, TB], BF16, kind="ExternalOutput").ap()

    KD = D // 128
    NS = TB // TSUB

    sch = Sched()
    with ExitStack() as ctx:
        e = ctx.enter_context
        wob = e(nc.sbuf_tensor("wob", [128, KD * D], BF16))
        kq8 = e(nc.sbuf_tensor("kq8", [128, KD * D], FP8))
        fkb = e(nc.sbuf_tensor("fkb", [128, KD * D], BF16))
        biasT = e(nc.sbuf_tensor("biasT", [128, KD], BF16))
        biasF = e(nc.sbuf_tensor("biasF", [128, KD], F32))
        wstg = e(nc.sbuf_tensor("wstg", [128, KD * D], F32))
        scl = e(nc.sbuf_tensor("scl", [1, 4], F32))
        eps1 = e(nc.sbuf_tensor("eps1", [1, 1], F32))
        sclb = e(nc.sbuf_tensor("sclb", [128, 3], F32))
        ones128 = e(nc.sbuf_tensor("ones128", [128, 1], F32))
        onesr = e(nc.sbuf_tensor("onesr", [1, 128], F32))
        xs = e(nc.sbuf_tensor("xs", [128, KD * TSUB], F32))
        x1 = e(nc.sbuf_tensor("x1", [128, KD * TSUB], F32))
        x1b = e(nc.sbuf_tensor("x1b", [128, KD * TSUB], BF16))
        xq8t = e(nc.sbuf_tensor("xq8t", [128, KD * TSUB], FP8))
        ybf = e(nc.sbuf_tensor("ybf", [128, KD * TSUB], BF16))
        sq = e(nc.sbuf_tensor("sq", [128, TSUB], F32))
        stat = e(nc.sbuf_tensor("stat", [1, 4 * TSUB], F32))
        d1 = e(nc.sbuf_tensor("d1", [128, TSUB], F32))
        rbm = e(nc.sbuf_tensor("rbm", [128, TSUB], F32))
        psA2 = [e(nc.psum_tensor(f"psA{i}", [128, TSUB], F32)) for i in range(2)]
        psSum = e(nc.psum_tensor("psSum", [1, TSUB], F32))
        psSq = e(nc.psum_tensor("psSq", [1, TSUB], F32))
        psBm = e(nc.psum_tensor("psBm", [128, TSUB], F32))
        psBr = e(nc.psum_tensor("psBr", [128, TSUB], F32))
        psS2 = e(nc.psum_tensor("psS2", [128, 3], F32))
        ld = e(nc.semaphore("ld"))
        vs = e(nc.semaphore("vs"))
        ts = e(nc.semaphore("ts"))
        as_ = e(nc.semaphore("as"))
        sto = e(nc.semaphore("sto"))

        n1 = sch.emit("sync", lambda e: e.dma_start(
            out=wstg[:].rearrange("p (k m) -> p k m", k=KD),
            in_=woT.rearrange("(k p) m -> p k m", p=128)), inc=(ld, 16))
        sch.wait("vector", ld, n1)
        n_wo = sch.emit("vector", lambda e: e.tensor_copy(out=wob[:], in_=wstg[:]),
                        inc=(vs, 1))
        n2 = sch.emit("sync", lambda e: e.dma_start(
            out=fkb[:].rearrange("p (k m) -> p k m", k=KD),
            in_=fker.rearrange("(k p) m -> p k m", p=128)), inc=(ld, 16))
        n3 = sch.emit("sync", lambda e: e.dma_start(
            out=biasT[:], in_=fbias.rearrange("(m p) -> p m", p=128),
            allow_slow_non_contiguous=True), inc=(ld, 16))
        n4 = sch.emit("sync", lambda e: e.dma_start(out=scl[:, 0:2], in_=fscale),
                      inc=(ld, 16))
        sch.wait("vector", ld, n4)
        sch.emit("vector", lambda e: e.tensor_copy(out=biasF[:], in_=biasT[:]))
        sch.emit("vector", lambda e: e.memset(ones128[:], 1.0))
        sch.emit("vector", lambda e: e.memset(onesr[:], 1.0))
        sch.emit("vector", lambda e: e.memset(eps1[:], 1e-5))
        n_rc = sch.emit("vector", lambda e: e.reciprocal(
            out=scl[:, 2:4], in_=scl[:, 0:2]), inc=(vs, 1))
        sch.wait("tensor", vs, n_rc)
        tb1 = sch.emit("tensor", lambda e: e.matmul(
            psS2[:, 0:2], onesr[:], scl[:, 2:4], start=True, stop=True), inc=(ts, 1))
        sch.wait("vector", ts, tb1)
        n_s2 = sch.emit("vector", lambda e: e.tensor_copy(
            out=sclb[:, 0:2], in_=psS2[:, 0:2]), inc=(vs, 1))
        n_sp = sch.emit("vector", lambda e: e.tensor_mul(
            out=scl[:, 0:1], in0=scl[:, 0:1], in1=scl[:, 1:2]), inc=(vs, 1))
        sch.wait("tensor", vs, n_sp)
        tb2 = sch.emit("tensor", lambda e: e.matmul(
            psS2[:, 2:3], onesr[:], scl[:, 0:1], start=True, stop=True), inc=(ts, 1))
        sch.wait("vector", ts, tb2)
        n_s3 = sch.emit("vector", lambda e: e.tensor_copy(
            out=sclb[:, 2:3], in_=psS2[:, 2:3]), inc=(vs, 1))
        n_kq = sch.emit("vector", lambda e: e.tensor_scalar(
            out=kq8[:], in0=fkb[:], scalar1=sclb[:, 1:2], scalar2=None,
            op0=AX.mult), inc=(vs, 1))
        sch.wait("tensor", vs, n_kq)

        def emit_ln(tsub, src, dst, dstb=None, gcol=None, bcol=None):
            sch.wait("scalar", vs, sch.n(vs))
            aN_l = None
            for k in range(KD):
                sch.emit("tensor", lambda e, k=k: e.matmul(
                    psSum[:], ones128[:], src[:, k * TSUB:(k + 1) * TSUB],
                    start=(k == 0), stop=(k == KD - 1)))
                if aN_l is not None:
                    sch.wait("scalar", ts, aN_l)
                aSq = sch.emit("scalar", lambda e, k=k: e.activation(
                    out=sq[:], in_=src[:, k * TSUB:(k + 1) * TSUB],
                    func=AF.Square), inc=(as_, 1))
                sch.wait("tensor", as_, aSq)
                aN_l = sch.emit("tensor", lambda e, k=k: e.matmul(
                    psSq[:], ones128[:], sq[:],
                    start=(k == 0), stop=(k == KD - 1)), inc=(ts, 1))
            sch.wait("vector", ts, aN_l)
            T_ = TSUB
            sch.emit("vector", lambda e: e.tensor_scalar_mul(
                out=stat[:, 0:T_], in0=psSum[:], scalar1=1.0 / D))
            sch.emit("vector", lambda e: e.tensor_mul(
                out=stat[:, T_:2 * T_], in0=stat[:, 0:T_], in1=stat[:, 0:T_]))
            n_var = sch.emit("vector", lambda e: e.scalar_tensor_tensor(
                out=stat[:, 2 * T_:3 * T_], in0=psSq[:], scalar=1.0 / D,
                in1=stat[:, T_:2 * T_], op0=AX.mult, op1=AX.subtract), inc=(vs, 1))
            sch.wait("scalar", vs, n_var)
            a_sd = sch.emit("scalar", lambda e: e.activation(
                out=stat[:, 3 * T_:4 * T_], in_=stat[:, 2 * T_:3 * T_],
                func=AF.Sqrt, bias=eps1[:]), inc=(as_, 1))
            sch.wait("vector", as_, a_sd)
            n_rs = sch.emit("vector", lambda e: e.reciprocal(
                out=stat[:, 2 * T_:3 * T_], in_=stat[:, 3 * T_:4 * T_]), inc=(vs, 1))
            sch.wait("tensor", vs, n_rs)
            t_bc = sch.emit("tensor", lambda e: e.matmul(
                psBm[:], onesr[:], stat[:, 0:T_], start=True, stop=True))
            t_bc = sch.emit("tensor", lambda e: e.matmul(
                psBr[:], onesr[:], stat[:, 2 * T_:3 * T_], start=True, stop=True),
                inc=(ts, 1))
            sch.wait("vector", ts, t_bc)
            n_rb = sch.emit("vector", lambda e: e.tensor_copy(
                out=rbm[:], in_=psBr[:]), inc=(vs, 1))
            nv = n_rb
            for k in range(KD):
                sch.emit("vector", lambda e, k=k: e.tensor_sub(
                    out=d1[:], in0=src[:, k * TSUB:(k + 1) * TSUB], in1=psBm[:]))
                if gcol is None:
                    nv = sch.emit("vector", lambda e, k=k: e.tensor_mul(
                        out=dst[:, k * TSUB:(k + 1) * TSUB], in0=d1[:], in1=rbm[:]),
                        inc=(vs, 1))
                else:
                    sch.emit("vector", lambda e, k=k: e.tensor_mul(
                        out=d1[:], in0=d1[:], in1=rbm[:]))
                    nv = sch.emit("vector", lambda e, k=k: e.tensor_scalar(
                        out=dst[:, k * TSUB:(k + 1) * TSUB], in0=d1[:],
                        scalar1=gcol[:, k:k + 1], scalar2=bcol[:, k:k + 1],
                        op0=AX.mult, op1=AX.add), inc=(vs, 1))
                if dstb is not None:
                    nv = sch.emit("vector", lambda e, k=k: e.tensor_copy(
                        out=dstb[:, k * TSUB:(k + 1) * TSUB],
                        in_=dst[:, k * TSUB:(k + 1) * TSUB]), inc=(vs, 1))
            return nv

        for su in range(NS):
            t0 = su * TSUB
            nl = sch.emit("sync", lambda e, t0=t0: e.dma_start(
                out=xs[:].rearrange("p (k t) -> p k t", k=KD),
                in_=xTs[:, t0:t0 + TSUB].rearrange("(k p) t -> p k t", p=128)),
                inc=(ld, 16))
            nl = sch.emit("sync", lambda e, t0=t0: e.dma_start(
                out=x1[:].rearrange("p (k t) -> p k t", k=KD),
                in_=oTs[:, t0:t0 + TSUB].rearrange("(k p) t -> p k t", p=128)),
                inc=(ld, 16))
            sch.wait("vector", ld, nl)
            n_ob = sch.emit("vector", lambda e: e.tensor_copy(
                out=ybf[:], in_=x1[:]), inc=(vs, 1))
            sch.wait("tensor", vs, n_ob)
            ev_res = [0, 0]
            for m in range(KD):
                pb = m % 2
                sch.wait("tensor", vs, ev_res[pb])
                for k in range(KD):
                    tA = sch.emit("tensor", lambda e, m=m, k=k, pb=pb: e.matmul(
                        psA2[pb][:], wob[:, k * D + m * 128:k * D + m * 128 + 128],
                        ybf[:, k * TSUB:(k + 1) * TSUB],
                        start=(k == 0), stop=(k == KD - 1)),
                        inc=(ts, 1) if k == KD - 1 else None)
                sch.wait("vector", ts, tA)
                ev_res[pb] = sch.emit("vector", lambda e, m=m, pb=pb: e.tensor_add(
                    out=xs[:, m * TSUB:(m + 1) * TSUB],
                    in0=xs[:, m * TSUB:(m + 1) * TSUB], in1=psA2[pb][:]),
                    inc=(vs, 1))
            sch.wait("tensor", vs, max(ev_res))
            emit_ln(su, xs, x1, x1b)
            n_q = sch.emit("vector", lambda e: e.tensor_scalar(
                out=xq8t[:], in0=x1b[:], scalar1=sclb[:, 0:1], scalar2=None,
                op0=AX.mult), inc=(vs, 1))
            sch.wait("tensor", vs, n_q)
            ev_y = [0, 0]
            for m in range(KD):
                pb = m % 2
                sch.wait("tensor", as_, ev_y[pb])
                for k in range(KD):
                    tF = sch.emit("tensor", lambda e, m=m, k=k, pb=pb: e.matmul(
                        psA2[pb][:], kq8[:, k * D + m * 128:k * D + m * 128 + 128],
                        xq8t[:, k * TSUB:(k + 1) * TSUB],
                        start=(k == 0), stop=(k == KD - 1)),
                        inc=(ts, 1) if k == KD - 1 else None)
                sch.wait("scalar", ts, tF)
                aY = sch.emit("scalar", lambda e, m=m, pb=pb: e.activation(
                    out=ybf[:, m * TSUB:(m + 1) * TSUB], in_=psA2[pb][:],
                    func=AF.Identity, scale=sclb[:, 2:3]), inc=(as_, 1))
                ev_y[pb] = aY
                sch.wait("vector", as_, aY)
                sch.emit("vector", lambda e, m=m: e.tensor_scalar(
                    out=ybf[:, m * TSUB:(m + 1) * TSUB],
                    in0=ybf[:, m * TSUB:(m + 1) * TSUB],
                    scalar1=biasF[:, m:m + 1], scalar2=None, op0=AX.add))
                n_r2 = sch.emit("vector", lambda e, m=m: e.tensor_add(
                    out=x1[:, m * TSUB:(m + 1) * TSUB],
                    in0=x1[:, m * TSUB:(m + 1) * TSUB],
                    in1=ybf[:, m * TSUB:(m + 1) * TSUB]), inc=(vs, 1))
            emit_ln(su, x1, xs, x1b)
            n_last = sch.n(vs)
            sch.wait("sync", vs, n_last)
            n_st = sch.emit("sync", lambda e, t0=t0: e.dma_start(
                out=x2T[:, t0:t0 + TSUB].rearrange("(k p) t -> p k t", p=128),
                in_=xs[:].rearrange("p (k t) -> p k t", k=KD)), inc=(sto, 16))
            n_st = sch.emit("sync", lambda e, t0=t0: e.dma_start(
                out=x2Tb[:, t0:t0 + TSUB].rearrange("(k p) t -> p k t", p=128),
                in_=x1b[:].rearrange("p (k t) -> p k t", k=KD)), inc=(sto, 16))
            sch.wait("vector", sto, n_st)
        sch.wait("sync", sto, sch.n(sto))
        sch.check_deadlock()
        sch.replay(nc)
    return nc


def run_l0_rest(x, oT, Wo, fker, s_in, s_k, fbias):
    nc = cached_build("l0_rest", build_l0_rest)
    woT = np.ascontiguousarray(Wo.T)
    fscale = np.array([[s_in, s_k]], np.float32)
    in_maps = []
    for c in range(NCORE):
        sl = slice(c * 1024, (c + 1) * 1024)
        xTs = np.concatenate([x[0].T[:, sl], x[1].T[:, sl]], axis=1)
        oTs = np.concatenate([oT[0][:, sl], oT[1][:, sl]], axis=1)
        in_maps.append({
            "xTs": np.ascontiguousarray(xTs),
            "oTs": np.ascontiguousarray(oTs),
            "woT": woT, "fker": fker, "fscale": fscale, "fbias": fbias,
        })
    res = run_bass_kernel_spmd(nc, in_maps, core_ids=list(range(NCORE)))
    x2T = np.zeros((BT, D, S), np.float32)
    x2Tb = np.zeros((BT, D, S), bf)
    for c in range(NCORE):
        sl = slice(c * 1024, (c + 1) * 1024)
        x2T[0][:, sl] = res.results[c]["x2T"][:, :1024]
        x2T[1][:, sl] = res.results[c]["x2T"][:, 1024:]
        x2Tb[0][:, sl] = res.results[c]["x2Tb"][:, :1024]
        x2Tb[1][:, sl] = res.results[c]["x2Tb"][:, 1024:]
    return x2T, x2Tb


SK = 64
SL = 128
SB_ = 32
SLAB = SK * 3


def build_rkv():
    nc = bass.Bass("TRN2", target_bir_lowering=False)
    x2b = nc.dram_tensor("x2b", [D, NT], BF16, kind="ExternalInput").ap()
    w3T = nc.dram_tensor("w3T", [D, 384], F32, kind="ExternalInput").ap()
    yTb = nc.dram_tensor("yTb", [BT, 128, S], BF16, kind="ExternalOutput").ap()

    KD = D // 128
    STEPS = SL + SB_
    TC = 512
    NTC = S // TC

    sch = Sched()
    with ExitStack() as ctx:
        e = ctx.enter_context
        w3b = e(nc.sbuf_tensor("w3b", [128, KD * 384], BF16))
        ld = e(nc.semaphore("ld"))
        vs = e(nc.semaphore("vs"))
        ts = e(nc.semaphore("ts"))
        as_ = e(nc.semaphore("as"))
        sto = e(nc.semaphore("sto"))
        with nc.sbuf_tensor("wstg", [128, KD * 384], F32) as wstg:
            n1 = sch.emit("sync", lambda e: e.dma_start(
                out=wstg[:].rearrange("p (k m) -> p k m", k=KD),
                in_=w3T.rearrange("(k p) m -> p k m", p=128)), inc=(ld, 16))
            sch.wait("vector", ld, n1)
            n_w = sch.emit("vector", lambda e: e.tensor_copy(out=w3b[:], in_=wstg[:]),
                           inc=(vs, 1))
        xs = e(nc.sbuf_tensor("xs", [128, STEPS * SLAB], F32))
        st = e(nc.sbuf_tensor("st", [128, SLAB], F32))
        xcs = [e(nc.sbuf_tensor(f"xcs{i}", [128, KD * TC], BF16)) for i in range(2)]
        spr = e(nc.sbuf_tensor("spr", [128, S], BF16))
        mt = e(nc.sbuf_tensor("mt", [128, S], BF16))
        hh = e(nc.sbuf_tensor("hh", [128, S], BF16))
        c09 = e(nc.sbuf_tensor("c09", [128, 2048], F32))
        hcar = e(nc.sbuf_tensor("hcar", [128, 1], F32))
        psR = [e(nc.psum_tensor(f"psR{i}", [128, TC], F32)) for i in range(3)]

        sch.emit("vector", lambda e: e.memset(c09[:], 0.9))
        n_w = sch.emit("vector", lambda e: e.memset(hcar[:], 0.0), inc=(vs, 1))
        sch.wait("tensor", vs, n_w)
        sch.wait("sync", vs, n_w)

        n_store = 0
        for b in range(BT):
            nz = sch.emit("vector", lambda e: e.memset(
                xs[:].rearrange("p (sl k c) -> p sl k c", sl=STEPS, k=SK)
                [:, 0:SB_, 0:1, :], 0.0), inc=(vs, 1))
            for tc in range(NTC):
                buf = tc % 2
                nl = sch.emit("sync", lambda e, b=b, tc=tc, buf=buf: e.dma_start(
                    out=xcs[buf][:].rearrange("p (k t) -> p k t", k=KD),
                    in_=x2b[:, b * S + tc * TC:b * S + (tc + 1) * TC].rearrange(
                        "(k p) t -> p k t", p=128)), inc=(ld, 16))
                sch.wait("tensor", ld, nl)
                for c in range(3):
                    for k in range(KD):
                        tR = sch.emit("tensor", lambda e, c=c, k=k, buf=buf: e.matmul(
                            psR[c][:],
                            w3b[:, k * 384 + c * 128:k * 384 + c * 128 + 128],
                            xcs[buf][:, k * TC:(k + 1) * TC],
                            start=(k == 0), stop=(k == KD - 1)),
                            inc=(ts, 1) if k == KD - 1 else None)
                    sch.wait("scalar", ts, tR)
                    k0 = tc * 4
                    aE = sch.emit("scalar", lambda e, c=c, k0=k0: e.activation(
                        out=xs[:].rearrange(
                            "p (sl k c) -> p sl k c", sl=STEPS, k=SK)
                        [:, SB_:SB_ + SL, k0:k0 + 4, c:c + 1]
                        .rearrange("p sl k c -> p (k sl c)" if False else
                                   "p sl k c -> p k sl c"),
                        in_=psR[c][:].rearrange("p (ks r) -> p ks r", ks=4),
                        func=AF.Identity, scale=0.5), inc=(as_, 1))
                sch.wait("tensor", as_, sch.n(as_))
                if tc + 1 < NTC:
                    sch.wait("sync", as_, sch.n(as_) - 3)
            sch.wait("vector", as_, sch.n(as_))
            xr = xs[:].rearrange("p (sl k c) -> p sl k c", sl=STEPS, k=SK)
            n_wu = sch.emit("vector", lambda e, xr=xr: e.tensor_copy(
                out=xr[:, 0:SB_, 1:SK, :], in_=xr[:, SL:SL + SB_, 0:SK - 1, :]),
                inc=(vs, 1))
            n_st0 = sch.emit("vector", lambda e: e.memset(st[:], 0.0), inc=(vs, 1))
            for sl in range(STEPS):
                sch.emit("vector", lambda e, sl=sl: e.scalar_tensor_tensor(
                    out=xs[:, sl * SLAB:(sl + 1) * SLAB], in0=st[:], scalar=0.5,
                    in1=xs[:, sl * SLAB:(sl + 1) * SLAB],
                    op0=AX.mult, op1=AX.add))
                sch.emit("vector", lambda e, sl=sl: e.scalar_tensor_tensor(
                    out=st[:], in0=xs[:, sl * SLAB:(sl + 1) * SLAB], scalar=1.0,
                    in1=xs[:, sl * SLAB:(sl + 1) * SLAB],
                    op0=AX.is_lt, op1=AX.mult))
            def reg(c):
                return xs[:].rearrange(
                    "p (sl k c) -> p sl k c", sl=STEPS, k=SK)[
                    :, SB_:STEPS, :, c:c + 1].rearrange("p sl k c -> p k sl c")
            sch.emit("vector", lambda e: e.tensor_single_scalar(
                out=spr[:], in_=reg(0), scalar=1.0, op=AX.is_ge))
            sch.emit("vector", lambda e: e.tensor_tensor(
                out=mt[:], in0=reg(1), in1=reg(2), op=AX.min))
            sch.emit("vector", lambda e: e.tensor_single_scalar(
                out=mt[:], in_=mt[:], scalar=1.0, op=AX.is_ge))
            for ch in range(4):
                s0 = ch * 2048
                if ch == 0:
                    n_sc = sch.emit("vector", lambda e, s0=s0: e.tensor_tensor_scan(
                        out=hh[:, s0:s0 + 2048], data0=c09[:],
                        data1=mt[:, s0:s0 + 2048], initial=0.0,
                        op0=AX.mult, op1=AX.add), inc=(vs, 1))
                else:
                    n_sc = sch.emit("vector", lambda e, s0=s0: e.tensor_tensor_scan(
                        out=hh[:, s0:s0 + 2048], data0=c09[:],
                        data1=mt[:, s0:s0 + 2048], initial=hcar[:],
                        op0=AX.mult, op1=AX.add), inc=(vs, 1))
                sch.wait("vector", vs, n_sc)
                n_hc = sch.emit("vector", lambda e, s0=s0: e.tensor_copy(
                    out=hcar[:], in_=hh[:, s0 + 2047:s0 + 2048]), inc=(vs, 1))
                sch.wait("vector", vs, n_hc)
            sch.wait("vector", sto, n_store)
            n_y = sch.emit("vector", lambda e: e.tensor_mul(
                out=mt[:], in0=spr[:], in1=hh[:]), inc=(vs, 1))
            sch.wait("sync", vs, n_y)
            n_store = sch.emit("sync", lambda e, b=b: e.dma_start(
                out=yTb[b], in_=mt[:]), inc=(sto, 16))
        sch.wait("sync", sto, sch.n(sto))
        sch.check_deadlock()
        sch.replay(nc)
    return nc


def run_rkv(x2Tb, Wr, Wk, Wv):
    nc = cached_build("rkv", build_rkv)
    x2full = np.ascontiguousarray(
        np.concatenate([x2Tb[0], x2Tb[1]], axis=1))
    in_maps = []
    for c in range(NCORE):
        ds = slice(c * 128, (c + 1) * 128)
        w3 = np.concatenate(
            [Wr.T[:, ds], Wk.T[:, ds], Wv.T[:, ds]], axis=1)
        in_maps.append({"x2b": x2full, "w3T": np.ascontiguousarray(w3)})
    res = run_bass_kernel_spmd(nc, in_maps, core_ids=list(range(NCORE)))
    yT = np.zeros((BT, D, S), bf)
    for c in range(NCORE):
        ds = slice(c * 128, (c + 1) * 128)
        yT[:, ds] = res.results[c]["yTb"]
    return yT


def build_postrkv():
    nc = bass.Bass("TRN2", target_bir_lowering=False)
    x2Ts = nc.dram_tensor("x2Ts", [D, TB], F32, kind="ExternalInput").ap()
    yTs = nc.dram_tensor("yTs", [D, TB], BF16, kind="ExternalInput").ap()
    woT = nc.dram_tensor("woT", [D, D], F32, kind="ExternalInput").ap()
    ln1g = nc.dram_tensor("ln1g", [D], F32, kind="ExternalInput").ap()
    ln1b = nc.dram_tensor("ln1b", [D], F32, kind="ExternalInput").ap()
    w1T = nc.dram_tensor("w1T", [D, 4096], F32, kind="ExternalInput").ap()
    b1v = nc.dram_tensor("b1v", [4096], F32, kind="ExternalInput").ap()
    x1T = nc.dram_tensor("x1T", [D, TB], F32, kind="ExternalOutput").ap()
    x1Tb = nc.dram_tensor("x1Tb", [D, TB], BF16, kind="ExternalOutput").ap()
    h1T = nc.dram_tensor("h1T", [4096, TB], F32, kind="ExternalOutput").ap()

    KD = D // 128
    NS = TB // TSUB

    sch = Sched()
    with ExitStack() as ctx:
        e = ctx.enter_context
        ld = e(nc.semaphore("ld"))
        vs = e(nc.semaphore("vs"))
        ts = e(nc.semaphore("ts"))
        as_ = e(nc.semaphore("as"))
        sto = e(nc.semaphore("sto"))
        wob = e(nc.sbuf_tensor("wob", [128, KD * D], BF16))
        w1b = e(nc.sbuf_tensor("w1b", [128, KD * 4096], BF16))
        gcol = e(nc.sbuf_tensor("gcol", [128, KD], F32))
        bcol = e(nc.sbuf_tensor("bcol", [128, KD], F32))
        b1c = e(nc.sbuf_tensor("b1c", [128, 32], F32))
        eps1 = e(nc.sbuf_tensor("eps1", [1, 1], F32))
        ones128 = e(nc.sbuf_tensor("ones128", [128, 1], F32))
        onesr = e(nc.sbuf_tensor("onesr", [1, 128], F32))
        with nc.sbuf_tensor("wstg", [128, KD * D], F32) as wstg:
            n1 = sch.emit("sync", lambda e: e.dma_start(
                out=wstg[:].rearrange("p (k m) -> p k m", k=KD),
                in_=woT.rearrange("(k p) m -> p k m", p=128)), inc=(ld, 16))
            sch.wait("vector", ld, n1)
            n_c = sch.emit("vector", lambda e: e.tensor_copy(
                out=wob[:], in_=wstg[:]), inc=(vs, 1))
            sch.wait("sync", vs, n_c)
            for q in range(4):
                n1 = sch.emit("sync", lambda e, q=q: e.dma_start(
                    out=wstg[:].rearrange("p (k m) -> p k m", k=KD),
                    in_=w1T[:, q * 1024:(q + 1) * 1024].rearrange(
                        "(k p) m -> p k m", p=128)), inc=(ld, 16))
                sch.wait("vector", ld, n1)
                def cst(e, q=q):
                    dst = w1b[:].rearrange("p (k m) -> p k m", k=KD)[
                        :, :, q * 1024:(q + 1) * 1024]
                    return e.tensor_copy(
                        out=dst, in_=wstg[:].rearrange("p (k m) -> p k m", k=KD))
                n_c = sch.emit("vector", cst, inc=(vs, 1))
                sch.wait("sync", vs, n_c)
        xs = e(nc.sbuf_tensor("xs", [128, KD * TSUB], F32))
        x1t = e(nc.sbuf_tensor("x1t", [128, KD * TSUB], F32))
        x1bt = e(nc.sbuf_tensor("x1bt", [128, KD * TSUB], BF16))
        ybt = e(nc.sbuf_tensor("ybt", [128, KD * TSUB], BF16))
        h1s = e(nc.sbuf_tensor("h1s", [128, 8 * TSUB], F32))
        sq = e(nc.sbuf_tensor("sq", [128, TSUB], F32))
        stat = e(nc.sbuf_tensor("stat", [1, 4 * TSUB], F32))
        d1 = e(nc.sbuf_tensor("d1", [128, TSUB], F32))
        rbm = e(nc.sbuf_tensor("rbm", [128, TSUB], F32))
        psA2 = [e(nc.psum_tensor(f"psA{i}", [128, TSUB], F32)) for i in range(2)]
        psSum = e(nc.psum_tensor("psSum", [1, TSUB], F32))
        psSq = e(nc.psum_tensor("psSq", [1, TSUB], F32))
        psBm = e(nc.psum_tensor("psBm", [128, TSUB], F32))
        psBr = e(nc.psum_tensor("psBr", [128, TSUB], F32))

        nl = sch.emit("sync", lambda e: e.dma_start(
            out=gcol[:], in_=ln1g.rearrange("(m p) -> p m", p=128),
            allow_slow_non_contiguous=True), inc=(ld, 16))
        nl = sch.emit("sync", lambda e: e.dma_start(
            out=bcol[:], in_=ln1b.rearrange("(m p) -> p m", p=128),
            allow_slow_non_contiguous=True), inc=(ld, 16))
        nl = sch.emit("sync", lambda e: e.dma_start(
            out=b1c[:], in_=b1v.rearrange("(m p) -> p m", p=128),
            allow_slow_non_contiguous=True), inc=(ld, 16))
        sch.wait("vector", ld, nl)
        sch.emit("vector", lambda e: e.memset(eps1[:], 1e-5))
        sch.emit("vector", lambda e: e.memset(ones128[:], 1.0))
        n_i = sch.emit("vector", lambda e: e.memset(onesr[:], 1.0), inc=(vs, 1))
        n_b = sch.emit("vector", lambda e: e.tensor_scalar_mul(
            out=b1c[:], in0=b1c[:], scalar1=0.5), inc=(vs, 1))
        sch.wait("tensor", vs, n_b)
        sch.wait("scalar", vs, n_b)

        def emit_ln(src, dst, dstb=None, g=None, b=None):
            sch.wait("scalar", vs, sch.n(vs))
            aN_l = None
            for k in range(KD):
                sch.emit("tensor", lambda e, k=k: e.matmul(
                    psSum[:], ones128[:], src[:, k * TSUB:(k + 1) * TSUB],
                    start=(k == 0), stop=(k == KD - 1)))
                if aN_l is not None:
                    sch.wait("scalar", ts, aN_l)
                aSq = sch.emit("scalar", lambda e, k=k: e.activation(
                    out=sq[:], in_=src[:, k * TSUB:(k + 1) * TSUB],
                    func=AF.Square), inc=(as_, 1))
                sch.wait("tensor", as_, aSq)
                aN_l = sch.emit("tensor", lambda e, k=k: e.matmul(
                    psSq[:], ones128[:], sq[:],
                    start=(k == 0), stop=(k == KD - 1)), inc=(ts, 1))
            sch.wait("vector", ts, aN_l)
            T_ = TSUB
            sch.emit("vector", lambda e: e.tensor_scalar_mul(
                out=stat[:, 0:T_], in0=psSum[:], scalar1=1.0 / D))
            sch.emit("vector", lambda e: e.tensor_mul(
                out=stat[:, T_:2 * T_], in0=stat[:, 0:T_], in1=stat[:, 0:T_]))
            n_var = sch.emit("vector", lambda e: e.scalar_tensor_tensor(
                out=stat[:, 2 * T_:3 * T_], in0=psSq[:], scalar=1.0 / D,
                in1=stat[:, T_:2 * T_], op0=AX.mult, op1=AX.subtract), inc=(vs, 1))
            sch.wait("scalar", vs, n_var)
            a_sd = sch.emit("scalar", lambda e: e.activation(
                out=stat[:, 3 * T_:4 * T_], in_=stat[:, 2 * T_:3 * T_],
                func=AF.Sqrt, bias=eps1[:]), inc=(as_, 1))
            sch.wait("vector", as_, a_sd)
            n_rs = sch.emit("vector", lambda e: e.reciprocal(
                out=stat[:, 2 * T_:3 * T_], in_=stat[:, 3 * T_:4 * T_]), inc=(vs, 1))
            sch.wait("tensor", vs, n_rs)
            sch.emit("tensor", lambda e: e.matmul(
                psBm[:], onesr[:], stat[:, 0:T_], start=True, stop=True))
            t_bc = sch.emit("tensor", lambda e: e.matmul(
                psBr[:], onesr[:], stat[:, 2 * T_:3 * T_], start=True, stop=True),
                inc=(ts, 1))
            sch.wait("vector", ts, t_bc)
            n_rb = sch.emit("vector", lambda e: e.tensor_copy(
                out=rbm[:], in_=psBr[:]), inc=(vs, 1))
            nv = n_rb
            for k in range(KD):
                sch.emit("vector", lambda e, k=k: e.tensor_sub(
                    out=d1[:], in0=src[:, k * TSUB:(k + 1) * TSUB], in1=psBm[:]))
                if g is None:
                    nv = sch.emit("vector", lambda e, k=k: e.tensor_mul(
                        out=dst[:, k * TSUB:(k + 1) * TSUB], in0=d1[:], in1=rbm[:]),
                        inc=(vs, 1))
                else:
                    sch.emit("vector", lambda e, k=k: e.tensor_mul(
                        out=d1[:], in0=d1[:], in1=rbm[:]))
                    nv = sch.emit("vector", lambda e, k=k: e.tensor_scalar(
                        out=dst[:, k * TSUB:(k + 1) * TSUB], in0=d1[:],
                        scalar1=g[:, k:k + 1], scalar2=b[:, k:k + 1],
                        op0=AX.mult, op1=AX.add), inc=(vs, 1))
                if dstb is not None:
                    nv = sch.emit("vector", lambda e, k=k: e.tensor_copy(
                        out=dstb[:, k * TSUB:(k + 1) * TSUB],
                        in_=dst[:, k * TSUB:(k + 1) * TSUB]), inc=(vs, 1))
            return nv

        n_sto_h = 0
        for su in range(NS):
            t0 = su * TSUB
            nl = sch.emit("sync", lambda e, t0=t0: e.dma_start(
                out=xs[:].rearrange("p (k t) -> p k t", k=KD),
                in_=x2Ts[:, t0:t0 + TSUB].rearrange("(k p) t -> p k t", p=128)),
                inc=(ld, 16))
            nl = sch.emit("sync", lambda e, t0=t0: e.dma_start(
                out=ybt[:].rearrange("p (k t) -> p k t", k=KD),
                in_=yTs[:, t0:t0 + TSUB].rearrange("(k p) t -> p k t", p=128)),
                inc=(ld, 16))
            sch.wait("tensor", ld, nl)
            sch.wait("vector", ld, nl)
            ev_res = [0, 0]
            for m in range(KD):
                pb = m % 2
                sch.wait("tensor", vs, ev_res[pb])
                for k in range(KD):
                    tA = sch.emit("tensor", lambda e, m=m, k=k, pb=pb: e.matmul(
                        psA2[pb][:], wob[:, k * D + m * 128:k * D + m * 128 + 128],
                        ybt[:, k * TSUB:(k + 1) * TSUB],
                        start=(k == 0), stop=(k == KD - 1)),
                        inc=(ts, 1) if k == KD - 1 else None)
                sch.wait("vector", ts, tA)
                ev_res[pb] = sch.emit("vector", lambda e, m=m, pb=pb: e.tensor_add(
                    out=xs[:, m * TSUB:(m + 1) * TSUB],
                    in0=xs[:, m * TSUB:(m + 1) * TSUB], in1=psA2[pb][:]),
                    inc=(vs, 1))
            sch.wait("tensor", vs, max(ev_res))
            emit_ln(xs, x1t, x1bt, gcol, bcol)
            n_lnv = sch.n(vs)
            sch.wait("sync", vs, n_lnv)
            n_st = sch.emit("sync", lambda e, t0=t0: e.dma_start(
                out=x1T[:, t0:t0 + TSUB].rearrange("(k p) t -> p k t", p=128),
                in_=x1t[:].rearrange("p (k t) -> p k t", k=KD)), inc=(sto, 16))
            n_st = sch.emit("sync", lambda e, t0=t0: e.dma_start(
                out=x1Tb[:, t0:t0 + TSUB].rearrange("(k p) t -> p k t", p=128),
                in_=x1bt[:].rearrange("p (k t) -> p k t", k=KD)), inc=(sto, 16))
            sch.wait("tensor", vs, n_lnv)
            ev_h = [0, 0]
            for mg in range(4):
                sch.wait("tensor", sto, n_sto_h)
                for mm in range(8):
                    m = mg * 8 + mm
                    pb = m % 2
                    sch.wait("tensor", as_, ev_h[pb])
                    for k in range(KD):
                        tH = sch.emit("tensor", lambda e, m=m, k=k, pb=pb: e.matmul(
                            psA2[pb][:],
                            w1b[:, k * 4096 + m * 128:k * 4096 + m * 128 + 128],
                            x1bt[:, k * TSUB:(k + 1) * TSUB],
                            start=(k == 0), stop=(k == KD - 1)),
                            inc=(ts, 1) if k == KD - 1 else None)
                    sch.wait("scalar", ts, tH)
                    ev_h[pb] = sch.emit("scalar", lambda e, m=m, mm=mm, pb=pb:
                                        e.activation(
                        out=h1s[:, mm * TSUB:(mm + 1) * TSUB], in_=psA2[pb][:],
                        func=AF.Identity, scale=0.5, bias=b1c[:, m:m + 1]),
                        inc=(as_, 1))
                sch.wait("sync", as_, sch.n(as_))
                n_sto_h = sch.emit("sync", lambda e, t0=t0, mg=mg: e.dma_start(
                    out=h1T[mg * 1024:(mg + 1) * 1024, t0:t0 + TSUB].rearrange(
                        "(k p) t -> p k t", p=128),
                    in_=h1s[:].rearrange("p (k t) -> p k t", k=KD)), inc=(sto, 16))
            sch.wait("vector", sto, n_st)
        sch.wait("sync", sto, sch.n(sto))
        sch.check_deadlock()
        sch.replay(nc)
    return nc


def build_lif(skw, cw, nsec):
    steps = SL + SB_
    slabw = skw * cw
    nc = bass.Bass("TRN2", target_bir_lowering=False)
    hs = nc.dram_tensor("hs", [nsec, 128, steps * slabw], F32,
                        kind="ExternalInput").ap()
    sp = nc.dram_tensor("sp", [nsec, 128, steps * slabw], F32,
                        kind="ExternalOutput").ap()
    sch = Sched()
    with ExitStack() as ctx:
        e = ctx.enter_context
        xs = e(nc.sbuf_tensor("xs", [128, steps * slabw], F32))
        st = e(nc.sbuf_tensor("st", [128, slabw], F32))
        ld = e(nc.semaphore("ld"))
        vs = e(nc.semaphore("vs"))
        sto = e(nc.semaphore("sto"))
        n_stod = 0
        for sec in range(nsec):
            sch.wait("sync", sto, n_stod)
            nl = sch.emit("sync", lambda e, sec=sec: e.dma_start(
                out=xs[:], in_=hs[sec]), inc=(ld, 16))
            sch.wait("vector", ld, nl)
            n0 = sch.emit("vector", lambda e: e.memset(st[:], 0.0), inc=(vs, 1))
            for sl in range(steps):
                sch.emit("vector", lambda e, sl=sl: e.scalar_tensor_tensor(
                    out=xs[:, sl * slabw:(sl + 1) * slabw], in0=st[:], scalar=0.5,
                    in1=xs[:, sl * slabw:(sl + 1) * slabw], op0=AX.mult, op1=AX.add))
                sch.emit("vector", lambda e, sl=sl: e.scalar_tensor_tensor(
                    out=st[:], in0=xs[:, sl * slabw:(sl + 1) * slabw], scalar=1.0,
                    in1=xs[:, sl * slabw:(sl + 1) * slabw],
                    op0=AX.is_lt, op1=AX.mult))
            n_sp = sch.emit("vector", lambda e: e.tensor_single_scalar(
                out=xs[:], in_=xs[:], scalar=1.0, op=AX.is_ge), inc=(vs, 1))
            sch.wait("sync", vs, n_sp)
            n_stod = sch.emit("sync", lambda e, sec=sec: e.dma_start(
                out=sp[sec], in_=xs[:]), inc=(sto, 16))
            sch.wait("vector", sto, n_stod)
        sch.wait("sync", sto, sch.n(sto))
        sch.check_deadlock()
        sch.replay(nc)
    return nc


def scan_pack(h_cw, skw, nsec):
    cw, _, T = h_cw.shape
    steps = SL + SB_
    L = T // (nsec * skw)
    assert L == SL
    out = np.zeros((nsec, 128, steps, skw, cw), np.float32)
    hp = h_cw.transpose(1, 0, 2)
    for sec in range(nsec):
        for k in range(skw):
            g0 = sec * skw * SL + k * SL - SB_
            n0 = max(0, -g0)
            seg = hp[:, :, g0 + n0:g0 + steps]
            out[sec, :, n0:n0 + seg.shape[2], k, :] = seg.transpose(0, 2, 1)
    return np.ascontiguousarray(out.reshape(nsec, 128, steps * skw * cw))


def scan_unpack(sp, skw, nsec, cw):
    steps = SL + SB_
    v = sp.reshape(nsec, 128, steps, skw, cw)[:, :, SB_:, :, :]
    v = v.transpose(4, 1, 0, 3, 2)
    return np.ascontiguousarray(v.reshape(cw, 128, nsec * skw * SL))


def build_w2():
    nc = bass.Bass("TRN2", target_bir_lowering=False)
    s1Ts = nc.dram_tensor("s1Ts", [4096, TB], BF16, kind="ExternalInput").ap()
    w2T = nc.dram_tensor("w2T", [4096, D], F32, kind="ExternalInput").ap()
    b2v = nc.dram_tensor("b2v", [D], F32, kind="ExternalInput").ap()
    h2T = nc.dram_tensor("h2T", [D, TB], F32, kind="ExternalOutput").ap()
    KF = 4096 // 128
    NS = TB // TSUB
    sch = Sched()
    with ExitStack() as ctx:
        e = ctx.enter_context
        ld = e(nc.semaphore("ld"))
        vs = e(nc.semaphore("vs"))
        ts = e(nc.semaphore("ts"))
        as_ = e(nc.semaphore("as"))
        sto = e(nc.semaphore("sto"))
        w2b = e(nc.sbuf_tensor("w2b", [128, KF * D], BF16))
        b2c = e(nc.sbuf_tensor("b2c", [128, 8], F32))
        with nc.sbuf_tensor("wstg", [128, KF * 256], F32) as wstg:
            for q in range(4):
                n1 = sch.emit("sync", lambda e, q=q: e.dma_start(
                    out=wstg[:].rearrange("p (k m) -> p k m", k=KF),
                    in_=w2T[:, q * 256:(q + 1) * 256].rearrange(
                        "(k p) m -> p k m", p=128)), inc=(ld, 16))
                sch.wait("vector", ld, n1)
                def cst(e, q=q):
                    dst = w2b[:].rearrange("p (k m) -> p k m", k=KF)[
                        :, :, q * 256:(q + 1) * 256]
                    return e.tensor_copy(
                        out=dst, in_=wstg[:].rearrange("p (k m) -> p k m", k=KF))
                n_c = sch.emit("vector", cst, inc=(vs, 1))
                sch.wait("sync", vs, n_c)
        s1t = e(nc.sbuf_tensor("s1t", [128, KF * TSUB], BF16))
        h2s = e(nc.sbuf_tensor("h2s", [128, 8 * TSUB], F32))
        psA2 = [e(nc.psum_tensor(f"psA{i}", [128, TSUB], F32)) for i in range(2)]
        nl = sch.emit("sync", lambda e: e.dma_start(
            out=b2c[:], in_=b2v.rearrange("(m p) -> p m", p=128),
            allow_slow_non_contiguous=True), inc=(ld, 16))
        sch.wait("vector", ld, nl)
        n_b = sch.emit("vector", lambda e: e.tensor_scalar_mul(
            out=b2c[:], in0=b2c[:], scalar1=0.5), inc=(vs, 1))
        sch.wait("scalar", vs, n_b)
        n_sto = 0
        for su in range(NS):
            t0 = su * TSUB
            nl = sch.emit("sync", lambda e, t0=t0: e.dma_start(
                out=s1t[:].rearrange("p (k t) -> p k t", k=KF),
                in_=s1Ts[:, t0:t0 + TSUB].rearrange("(k p) t -> p k t", p=128)),
                inc=(ld, 16))
            sch.wait("tensor", ld, nl)
            ev_h = [0, 0]
            for m in range(8):
                pb = m % 2
                sch.wait("tensor", as_, ev_h[pb])
                for k in range(KF):
                    tH = sch.emit("tensor", lambda e, m=m, k=k, pb=pb: e.matmul(
                        psA2[pb][:], w2b[:, k * D + m * 128:k * D + m * 128 + 128],
                        s1t[:, k * TSUB:(k + 1) * TSUB],
                        start=(k == 0), stop=(k == KF - 1)),
                        inc=(ts, 1) if k == KF - 1 else None)
                sch.wait("scalar", ts, tH)
                ev_h[pb] = sch.emit("scalar", lambda e, m=m, pb=pb: e.activation(
                    out=h2s[:, m * TSUB:(m + 1) * TSUB], in_=psA2[pb][:],
                    func=AF.Identity, scale=0.5, bias=b2c[:, m:m + 1]),
                    inc=(as_, 1))
            sch.wait("sync", as_, sch.n(as_))
            sch.wait("sync", sto, n_sto)
            n_sto = sch.emit("sync", lambda e, t0=t0: e.dma_start(
                out=h2T[:, t0:t0 + TSUB].rearrange("(k p) t -> p k t", p=128),
                in_=h2s[:].rearrange("p (k t) -> p k t", k=8)), inc=(sto, 16))
        sch.wait("sync", sto, sch.n(sto))
        sch.check_deadlock()
        sch.replay(nc)
    return nc


def run_postrkv(x2T, yT, Wo, ln1g, ln1b, W1, b1):
    nc = cached_build("postrkv", build_postrkv)
    woT = np.ascontiguousarray(Wo.T)
    w1Tc = np.ascontiguousarray(W1.T)
    in_maps = []
    for c in range(NCORE):
        sl = slice(c * 1024, (c + 1) * 1024)
        in_maps.append({
            "x2Ts": np.ascontiguousarray(
                np.concatenate([x2T[0][:, sl], x2T[1][:, sl]], axis=1)),
            "yTs": np.ascontiguousarray(
                np.concatenate([yT[0][:, sl], yT[1][:, sl]], axis=1)),
            "woT": woT, "ln1g": ln1g, "ln1b": ln1b, "w1T": w1Tc, "b1v": b1,
        })
    res = run_bass_kernel_spmd(nc, in_maps, core_ids=list(range(NCORE)))
    x1T = np.zeros((BT, D, S), np.float32)
    x1Tb = np.zeros((BT, D, S), bf)
    h1T = np.zeros((BT, 4096, S), np.float32)
    for c in range(NCORE):
        sl = slice(c * 1024, (c + 1) * 1024)
        r = res.results[c]
        x1T[0][:, sl] = r["x1T"][:, :1024]; x1T[1][:, sl] = r["x1T"][:, 1024:]
        x1Tb[0][:, sl] = r["x1Tb"][:, :1024]; x1Tb[1][:, sl] = r["x1Tb"][:, 1024:]
        h1T[0][:, sl] = r["h1T"][:, :1024]; h1T[1][:, sl] = r["h1T"][:, 1024:]
    return x1T, x1Tb, h1T


def run_lif_phase(h_T, skw, nsec, cw):
    nc = cached_build(("lif", skw, cw, nsec), lambda: build_lif(skw, cw, nsec))
    B, E, T = h_T.shape
    fpc = E // NCORE
    assert fpc * B == 128 * cw
    in_maps = []
    for c in range(NCORE):
        rows = h_T[:, c * fpc:(c + 1) * fpc, :].reshape(B * fpc, T)
        h_cw = rows.reshape(cw, 128, T)
        in_maps.append({"hs": scan_pack(h_cw, skw, nsec)})
    res = run_bass_kernel_spmd(nc, in_maps, core_ids=list(range(NCORE)))
    out = np.zeros((B, E, T), np.float32)
    for c in range(NCORE):
        spk = scan_unpack(np.asarray(res.results[c]["sp"]), skw, nsec, cw)
        out[:, c * fpc:(c + 1) * fpc, :] = spk.reshape(B, fpc, T)
    return out


def run_w2(s1T, W2, b2):
    nc = cached_build("w2", build_w2)
    w2Tc = np.ascontiguousarray(W2.T)
    in_maps = []
    for c in range(NCORE):
        sl = slice(c * 1024, (c + 1) * 1024)
        in_maps.append({
            "s1Ts": np.ascontiguousarray(np.concatenate(
                [s1T[0][:, sl], s1T[1][:, sl]], axis=1)).astype(bf),
            "w2T": w2Tc, "b2v": b2,
        })
    res = run_bass_kernel_spmd(nc, in_maps, core_ids=list(range(NCORE)))
    h2T = np.zeros((BT, D, S), np.float32)
    for c in range(NCORE):
        sl = slice(c * 1024, (c + 1) * 1024)
        h2T[0][:, sl] = res.results[c]["h2T"][:, :1024]
        h2T[1][:, sl] = res.results[c]["h2T"][:, 1024:]
    return h2T


def make_ln(sch, t, KD=8):
    def emit_ln(src, dst, dstb=None, g=None, b=None):
        vs, ts, as_ = t["vs"], t["ts"], t["as_"]
        TSU = t["tsub"]
        sch.wait("scalar", vs, sch.n(vs))
        aN_l = None
        for k in range(KD):
            sch.emit("tensor", lambda e, k=k: e.matmul(
                t["psSum"][:], t["ones128"][:], src[:, k * TSU:(k + 1) * TSU],
                start=(k == 0), stop=(k == KD - 1)))
            if aN_l is not None:
                sch.wait("scalar", ts, aN_l)
            aSq = sch.emit("scalar", lambda e, k=k: e.activation(
                out=t["sq"][:], in_=src[:, k * TSU:(k + 1) * TSU],
                func=AF.Square), inc=(as_, 1))
            sch.wait("tensor", as_, aSq)
            aN_l = sch.emit("tensor", lambda e, k=k: e.matmul(
                t["psSq"][:], t["ones128"][:], t["sq"][:],
                start=(k == 0), stop=(k == KD - 1)), inc=(ts, 1))
        sch.wait("vector", ts, aN_l)
        T_ = TSU
        stat = t["stat"]
        sch.emit("vector", lambda e: e.tensor_scalar_mul(
            out=stat[:, 0:T_], in0=t["psSum"][:], scalar1=1.0 / D))
        sch.emit("vector", lambda e: e.tensor_mul(
            out=stat[:, T_:2 * T_], in0=stat[:, 0:T_], in1=stat[:, 0:T_]))
        n_var = sch.emit("vector", lambda e: e.scalar_tensor_tensor(
            out=stat[:, 2 * T_:3 * T_], in0=t["psSq"][:], scalar=1.0 / D,
            in1=stat[:, T_:2 * T_], op0=AX.mult, op1=AX.subtract), inc=(vs, 1))
        sch.wait("scalar", vs, n_var)
        a_sd = sch.emit("scalar", lambda e: e.activation(
            out=stat[:, 3 * T_:4 * T_], in_=stat[:, 2 * T_:3 * T_],
            func=AF.Sqrt, bias=t["eps1"][:]), inc=(as_, 1))
        sch.wait("vector", as_, a_sd)
        n_rs = sch.emit("vector", lambda e: e.reciprocal(
            out=stat[:, 2 * T_:3 * T_], in_=stat[:, 3 * T_:4 * T_]), inc=(vs, 1))
        sch.wait("tensor", vs, n_rs)
        sch.emit("tensor", lambda e: e.matmul(
            t["psBm"][:], t["onesr"][:], stat[:, 0:T_], start=True, stop=True))
        t_bc = sch.emit("tensor", lambda e: e.matmul(
            t["psBr"][:], t["onesr"][:], stat[:, 2 * T_:3 * T_],
            start=True, stop=True), inc=(ts, 1))
        sch.wait("vector", ts, t_bc)
        n_rb = sch.emit("vector", lambda e: e.tensor_copy(
            out=t["rbm"][:], in_=t["psBr"][:]), inc=(vs, 1))
        nv = n_rb
        for k in range(KD):
            sch.emit("vector", lambda e, k=k: e.tensor_sub(
                out=t["d1"][:], in0=src[:, k * TSU:(k + 1) * TSU],
                in1=t["psBm"][:]))
            if g is None:
                nv = sch.emit("vector", lambda e, k=k: e.tensor_mul(
                    out=dst[:, k * TSU:(k + 1) * TSU], in0=t["d1"][:],
                    in1=t["rbm"][:]), inc=(vs, 1))
            else:
                sch.emit("vector", lambda e, k=k: e.tensor_mul(
                    out=t["d1"][:], in0=t["d1"][:], in1=t["rbm"][:]))
                nv = sch.emit("vector", lambda e, k=k: e.tensor_scalar(
                    out=dst[:, k * TSU:(k + 1) * TSU], in0=t["d1"][:],
                    scalar1=g[:, k:k + 1], scalar2=b[:, k:k + 1],
                    op0=AX.mult, op1=AX.add), inc=(vs, 1))
            if dstb is not None:
                nv = sch.emit("vector", lambda e, k=k: e.tensor_copy(
                    out=dstb[:, k * TSU:(k + 1) * TSU],
                    in_=dst[:, k * TSU:(k + 1) * TSU]), inc=(vs, 1))
        return nv
    return emit_ln


def ln_tiles(nc, e, sch, tsub):
    t = {"tsub": tsub}
    t["ones128"] = e(nc.sbuf_tensor("ones128", [128, 1], F32))
    t["onesr"] = e(nc.sbuf_tensor("onesr", [1, 128], F32))
    t["eps1"] = e(nc.sbuf_tensor("eps1", [1, 1], F32))
    t["sq"] = e(nc.sbuf_tensor("sq", [128, tsub], F32))
    t["stat"] = e(nc.sbuf_tensor("stat", [1, 4 * tsub], F32))
    t["d1"] = e(nc.sbuf_tensor("d1", [128, tsub], F32))
    t["rbm"] = e(nc.sbuf_tensor("rbm", [128, tsub], F32))
    t["psSum"] = e(nc.psum_tensor("psSum", [1, tsub], F32))
    t["psSq"] = e(nc.psum_tensor("psSq", [1, tsub], F32))
    t["psBm"] = e(nc.psum_tensor("psBm", [128, tsub], F32))
    t["psBr"] = e(nc.psum_tensor("psBr", [128, tsub], F32))
    return t


def ln_init(sch, t, ld, vs):
    sch.emit("vector", lambda e: e.memset(t["ones128"][:], 1.0))
    sch.emit("vector", lambda e: e.memset(t["onesr"][:], 1.0))
    return sch.emit("vector", lambda e: e.memset(t["eps1"][:], 1e-5), inc=(vs, 1))


def build_h1():
    nc = bass.Bass("TRN2", target_bir_lowering=False)
    x1Ts = nc.dram_tensor("x1Ts", [D, TB], F32, kind="ExternalInput").ap()
    fspTs = nc.dram_tensor("fspTs", [D, TB], BF16, kind="ExternalInput").ap()
    x2Ts = nc.dram_tensor("x2Ts", [D, TB], F32, kind="ExternalInput").ap()
    g2 = nc.dram_tensor("g2", [D], F32, kind="ExternalInput").ap()
    b2a = nc.dram_tensor("b2a", [D], F32, kind="ExternalInput").ap()
    x3T = nc.dram_tensor("x3T", [D, TB], F32, kind="ExternalOutput").ap()
    x3Tb = nc.dram_tensor("x3Tb", [D, TB], BF16, kind="ExternalOutput").ap()
    KD = 8
    NS = TB // TSUB
    sch = Sched()
    with ExitStack() as ctx:
        e = ctx.enter_context
        ld = e(nc.semaphore("ld"))
        vs = e(nc.semaphore("vs"))
        ts = e(nc.semaphore("ts"))
        as_ = e(nc.semaphore("as"))
        sto = e(nc.semaphore("sto"))
        t = ln_tiles(nc, e, sch, TSUB)
        t["vs"], t["ts"], t["as_"] = vs, ts, as_
        gcol = e(nc.sbuf_tensor("gcol", [128, KD], F32))
        bcol = e(nc.sbuf_tensor("bcol", [128, KD], F32))
        r1 = e(nc.sbuf_tensor("r1", [128, KD * TSUB], F32))
        r2 = e(nc.sbuf_tensor("r2", [128, KD * TSUB], F32))
        fsb = e(nc.sbuf_tensor("fsb", [128, KD * TSUB], BF16))
        x3b = e(nc.sbuf_tensor("x3b", [128, KD * TSUB], BF16))
        emit_ln = make_ln(sch, t)
        nl = sch.emit("sync", lambda e: e.dma_start(
            out=gcol[:], in_=g2.rearrange("(m p) -> p m", p=128),
            allow_slow_non_contiguous=True), inc=(ld, 16))
        nl = sch.emit("sync", lambda e: e.dma_start(
            out=bcol[:], in_=b2a.rearrange("(m p) -> p m", p=128),
            allow_slow_non_contiguous=True), inc=(ld, 16))
        sch.wait("vector", ld, nl)
        ln_init(sch, t, ld, vs)
        n_sto = 0
        for su in range(NS):
            t0 = su * TSUB
            sch.wait("sync", sto, n_sto)
            nl = sch.emit("sync", lambda e, t0=t0: e.dma_start(
                out=r1[:].rearrange("p (k t) -> p k t", k=KD),
                in_=x1Ts[:, t0:t0 + TSUB].rearrange("(k p) t -> p k t", p=128)),
                inc=(ld, 16))
            nl = sch.emit("sync", lambda e, t0=t0: e.dma_start(
                out=fsb[:].rearrange("p (k t) -> p k t", k=KD),
                in_=fspTs[:, t0:t0 + TSUB].rearrange("(k p) t -> p k t", p=128)),
                inc=(ld, 16))
            nl = sch.emit("sync", lambda e, t0=t0: e.dma_start(
                out=r2[:].rearrange("p (k t) -> p k t", k=KD),
                in_=x2Ts[:, t0:t0 + TSUB].rearrange("(k p) t -> p k t", p=128)),
                inc=(ld, 16))
            sch.wait("vector", ld, nl)
            n_a = sch.emit("vector", lambda e: e.tensor_add(
                out=r1[:], in0=r1[:], in1=fsb[:]), inc=(vs, 1))
            sch.wait("tensor", vs, n_a)
            emit_ln(r1, r1, None, gcol, bcol)
            n_a2 = sch.emit("vector", lambda e: e.tensor_add(
                out=r2[:], in0=r2[:], in1=r1[:]), inc=(vs, 1))
            sch.wait("tensor", vs, n_a2)
            emit_ln(r2, r2, x3b)
            n_f = sch.n(vs)
            sch.wait("sync", vs, n_f)
            n_sto = sch.emit("sync", lambda e, t0=t0: e.dma_start(
                out=x3T[:, t0:t0 + TSUB].rearrange("(k p) t -> p k t", p=128),
                in_=r2[:].rearrange("p (k t) -> p k t", k=KD)), inc=(sto, 16))
            n_sto = sch.emit("sync", lambda e, t0=t0: e.dma_start(
                out=x3Tb[:, t0:t0 + TSUB].rearrange("(k p) t -> p k t", p=128),
                in_=x3b[:].rearrange("p (k t) -> p k t", k=KD)), inc=(sto, 16))
        sch.wait("sync", sto, sch.n(sto))
        sch.check_deadlock()
        sch.replay(nc)
    return nc


def run_h1(x1T, fspT, x2T, g2, b2a):
    nc = cached_build("h1", build_h1)
    in_maps = []
    for c in range(NCORE):
        sl = slice(c * 1024, (c + 1) * 1024)
        in_maps.append({
            "x1Ts": np.ascontiguousarray(np.concatenate(
                [x1T[0][:, sl], x1T[1][:, sl]], axis=1)),
            "fspTs": np.ascontiguousarray(np.concatenate(
                [fspT[0][:, sl], fspT[1][:, sl]], axis=1)).astype(bf),
            "x2Ts": np.ascontiguousarray(np.concatenate(
                [x2T[0][:, sl], x2T[1][:, sl]], axis=1)),
            "g2": g2, "b2a": b2a,
        })
    res = run_bass_kernel_spmd(nc, in_maps, core_ids=list(range(NCORE)))
    x3T = np.zeros((BT, D, S), np.float32)
    x3Tb = np.zeros((BT, D, S), bf)
    for c in range(NCORE):
        sl = slice(c * 1024, (c + 1) * 1024)
        r = res.results[c]
        x3T[0][:, sl] = r["x3T"][:, :1024]; x3T[1][:, sl] = r["x3T"][:, 1024:]
        x3Tb[0][:, sl] = r["x3Tb"][:, :1024]; x3Tb[1][:, sl] = r["x3Tb"][:, 1024:]
    return x3T, x3Tb


def build_h2(variant='full'):
    nc = bass.Bass("TRN2", target_bir_lowering=False)
    x3Ts = nc.dram_tensor("x3Ts", [D, TB], F32, kind="ExternalInput").ap()
    x3Tbs = nc.dram_tensor("x3Tbs", [D, TB], BF16, kind="ExternalInput").ap()
    ipwT = nc.dram_tensor("ipwT", [D, 3 * D], F32, kind="ExternalInput").ap()
    ipb = nc.dram_tensor("ipb", [3 * D], F32, kind="ExternalInput").ap()
    opwT = nc.dram_tensor("opwT", [D, D], F32, kind="ExternalInput").ap()
    opb = nc.dram_tensor("opb", [D], F32, kind="ExternalInput").ap()
    f1g = nc.dram_tensor("f1g", [D], F32, kind="ExternalInput").ap()
    f1b = nc.dram_tensor("f1b", [D], F32, kind="ExternalInput").ap()
    wg = nc.dram_tensor("wg", [D, 8], F32, kind="ExternalInput").ap()
    y1T = nc.dram_tensor("y1T", [D, TB], F32, kind="ExternalOutput").ap()
    y1rows = nc.dram_tensor("y1rows", [TB, D], BF16, kind="ExternalOutput").ap()
    idxr = nc.dram_tensor("idxr", [1, TB], F32, kind="ExternalOutput").ap()
    gvr = nc.dram_tensor("gvr", [1, TB], F32, kind="ExternalOutput").ap()
    KD = 8
    HN = 4
    sch = Sched()
    with ExitStack() as ctx:
        e = ctx.enter_context
        ld = e(nc.semaphore("ld"))
        vs = e(nc.semaphore("vs"))
        ts = e(nc.semaphore("ts"))
        as_ = e(nc.semaphore("as"))
        gs = e(nc.semaphore("gs"))
        sto = e(nc.semaphore("sto"))
        t = ln_tiles(nc, e, sch, TSUB)
        t["vs"], t["ts"], t["as_"] = vs, ts, as_
        obf = e(nc.sbuf_tensor("obf", [128, KD * TB], BF16))
        ipbc = e(nc.sbuf_tensor("ipbc", [128, 24], F32))
        iota8 = e(nc.sbuf_tensor("iota8", [8, 1], F32))
        ones8 = e(nc.sbuf_tensor("ones8", [8, 1], F32))
        ones8r = e(nc.sbuf_tensor("ones8r", [1, 8], F32))
        identb = e(nc.sbuf_tensor("identb", [128, 128], BF16))
        emit_ln = make_ln(sch, t)
        ln_init(sch, t, ld, vs)
        sch.emit("vector", lambda e: e.memset(ones8[:], 1.0))
        sch.emit("vector", lambda e: e.memset(ones8r[:], 1.0))
        n_i8 = sch.emit("gpsimd", lambda e: e.iota(
            out=iota8[:], pattern=[[0, 1]], base=0, channel_multiplier=1,
            allow_small_or_imprecise_dtypes=True), inc=(gs, 1))
        n_id = sch.emit("gpsimd", lambda e: e.iota(
            out=identb[:], pattern=[[-1, 128]], base=0, channel_multiplier=1,
            allow_small_or_imprecise_dtypes=True), inc=(gs, 1))
        sch.wait("vector", gs, n_id)
        n_id2 = sch.emit("vector", lambda e: e.tensor_single_scalar(
            out=identb[:], in_=identb[:], scalar=0.0, op=AX.is_equal), inc=(vs, 1))
        sch.wait("tensor", vs, n_id2)
        nl = sch.emit("sync", lambda e: e.dma_start(
            out=ipbc[:], in_=ipb.rearrange("(m p) -> p m", p=128),
            allow_slow_non_contiguous=True), inc=(ld, 16))
        sch.wait("scalar", ld, nl)

        with ExitStack() as pa:
            ea = pa.enter_context
            qkvb = ea(nc.sbuf_tensor("qkvb", [128, 24 * TB], BF16))
            with ExitStack() as p1:
                e1 = p1.enter_context
                x3bb = e1(nc.sbuf_tensor("x3bb", [128, KD * TB], BF16))
                wstg = e1(nc.sbuf_tensor("wstg", [128, KD * 128], F32))
                wmb = e1(nc.sbuf_tensor("wmb", [128, KD * 128], BF16))
                psQ = e1(nc.psum_tensor("psQ", [128, TSUB], F32))
                nlx = sch.emit("sync", lambda e: e.dma_start(
                    out=x3bb[:].rearrange("p (k t) -> p k t", k=KD),
                    in_=x3Tbs.rearrange("(k p) t -> p k t", p=128)), inc=(ld, 16))
                for m in range(24):
                    nl2 = sch.emit("sync", lambda e, m=m: e.dma_start(
                        out=wstg[:].rearrange("p (k t) -> p k t", k=KD),
                        in_=ipwT[:, m * 128:(m + 1) * 128].rearrange(
                            "(k p) t -> p k t", p=128)), inc=(ld, 16))
                    sch.wait("vector", ld, nl2)
                    n_w = sch.emit("vector", lambda e: e.tensor_copy(
                        out=wmb[:], in_=wstg[:]), inc=(vs, 1))
                    sch.wait("tensor", vs, n_w)
                    sch.wait("tensor", ld, nl2)
                    for tsu in range(TB // TSUB):
                        for k in range(KD):
                            tQ = sch.emit("tensor",
                                          lambda e, k=k, tsu=tsu: e.matmul(
                                psQ[:], wmb[:, k * 128:(k + 1) * 128],
                                x3bb[:, k * TB + tsu * TSUB:
                                     k * TB + (tsu + 1) * TSUB],
                                start=(k == 0), stop=(k == KD - 1)),
                                inc=(ts, 1) if k == KD - 1 else None)
                        sch.wait("scalar", ts, tQ)
                        aQ = sch.emit("scalar", lambda e, m=m, tsu=tsu: e.activation(
                            out=qkvb[:, m * TB + tsu * TSUB:
                                     m * TB + (tsu + 1) * TSUB],
                            in_=psQ[:], func=AF.Identity,
                            bias=ipbc[:, m:m + 1]), inc=(as_, 1))
                        sch.wait("tensor", as_, aQ)
                    sch.wait("sync", as_, sch.n(as_))
            if variant == "qkv":
                sch.wait("sync", as_, sch.n(as_))
                nst = sch.emit("sync", lambda e: e.dma_start(
                    out=y1T[:, 0:TB], in_=qkvb[:, 0:TB].rearrange(
                        "p t -> p t") if False else qkvb[:, 0:TB],
                    allow_slow_non_contiguous=True), inc=(sto, 16)) if False else None
                sch.check_deadlock()
                sch.replay(nc)
                return nc
            with ExitStack() as p2:
                e2 = p2.enter_context
                scS = [e2(nc.sbuf_tensor(f"scS{ij}", [4, TB], F32))
                       for ij in range(4)]
                A1 = e2(nc.sbuf_tensor("A1", [4, TB], BF16))
                A0 = e2(nc.sbuf_tensor("A0", [4, TB], BF16))
                ebuf = e2(nc.sbuf_tensor("ebuf", [128, TSUB], BF16))
                ebf2 = e2(nc.sbuf_tensor("ebf2", [128, TSUB], F32))
                ind = e2(nc.sbuf_tensor("ind", [128, 4 * HN], BF16))
                sel = e2(nc.sbuf_tensor("sel", [4, 128 * HN], BF16))
                ps4 = e2(nc.psum_tensor("ps4", [4, TSUB], F32))
                psB0 = e2(nc.psum_tensor("psB0", [128, TSUB], F32))
                psB1 = e2(nc.psum_tensor("psB1", [128, TSUB], F32))
                sch.wait("vector", ts, sch.n(ts))
                sch.wait("vector", as_, sch.n(as_))
                sch.wait("gpsimd", ts, sch.n(ts))
                sch.wait("gpsimd", as_, sch.n(as_))
                n_in = sch.emit("vector", lambda e: e.memset(ind[:], 0.0),
                                inc=(vs, 1))
                for q in range(HN):
                    n_in = sch.emit("vector", lambda e, q=q: e.memset(
                        ind[:, q * 4 + q:q * 4 + q + 1], 1.0), inc=(vs, 1))
                n_io = sch.emit("gpsimd", lambda e: e.iota(
                    out=sel[:].rearrange("p (h m) -> p h m", h=HN),
                    pattern=[[1, HN], [0, 128]], base=0, channel_multiplier=-1,
                    allow_small_or_imprecise_dtypes=True), inc=(gs, 1))
                sch.wait("vector", gs, n_io)
                n_in = sch.emit("vector", lambda e: e.tensor_single_scalar(
                    out=sel[:], in_=sel[:], scalar=0.0, op=AX.is_equal),
                    inc=(vs, 1))
                sch.wait("tensor", vs, n_in)
                for ij in range(4):
                    i, j = ij // 2, ij % 2
                    for tsu in range(2):
                        cq = i * 1024 + tsu * TSUB
                        ck = j * 1024 + tsu * TSUB
                        for ch in range(KD):
                            q = ch // 2
                            n_e = sch.emit("vector",
                                           lambda e, ch=ch, cq=cq, ck=ck:
                                           e.tensor_mul(
                                out=ebuf[:],
                                in0=qkvb[:, ch * TB + cq:ch * TB + cq + TSUB],
                                in1=qkvb[:, (8 + ch) * TB + ck:
                                         (8 + ch) * TB + ck + TSUB]),
                                inc=(vs, 1))
                            sch.wait("tensor", vs, n_e)
                            tS = sch.emit("tensor",
                                          lambda e, ch=ch, q=q: e.matmul(
                                ps4[:], ind[:, q * 4:(q + 1) * 4], ebuf[:],
                                start=(ch == 0), stop=(ch == KD - 1)),
                                inc=(ts, 1))
                            sch.wait("vector", ts, tS)
                        n_sc = sch.emit("vector",
                                        lambda e, ij=ij, cq=cq: e.tensor_copy(
                            out=scS[ij][:, cq:cq + TSUB], in_=ps4[:]),
                            inc=(vs, 1))
                        sch.wait("tensor", vs, n_sc)
                for i in range(2):
                    n_d = sch.emit("vector", lambda e, i=i: e.tensor_sub(
                        out=A1[:, i * 1024:(i + 1) * 1024],
                        in0=scS[i * 2 + 1][:, i * 1024:(i + 1) * 1024],
                        in1=scS[i * 2 + 0][:, i * 1024:(i + 1) * 1024]),
                        inc=(vs, 1))
                sch.wait("scalar", vs, sch.n(vs))
                a_sg = sch.emit("scalar", lambda e: e.activation(
                    out=A1[:], in_=A1[:], func=AF.Sigmoid, scale=1.0 / 16),
                    inc=(as_, 1))
                sch.wait("vector", as_, a_sg)
                n_a0 = sch.emit("vector", lambda e: e.tensor_scalar(
                    out=A0[:], in0=A1[:], scalar1=-1.0, scalar2=1.0,
                    op0=AX.mult, op1=AX.add), inc=(vs, 1))
                sch.wait("tensor", vs, n_a0)
                for ch in range(KD):
                    h = ch // 2
                    for i in range(2):
                        for tsu in range(2):
                            c0 = i * 1024 + tsu * TSUB
                            cn = tsu * TSUB
                            sch.emit("tensor", lambda e, h=h, c0=c0: e.matmul(
                                psB0[:], sel[:, h * 128:(h + 1) * 128],
                                A0[:, c0:c0 + TSUB], start=True, stop=True))
                            tB1 = sch.emit("tensor",
                                           lambda e, h=h, c0=c0: e.matmul(
                                psB1[:], sel[:, h * 128:(h + 1) * 128],
                                A1[:, c0:c0 + TSUB], start=True, stop=True),
                                inc=(ts, 1))
                            sch.wait("vector", ts, tB1)
                            sch.emit("vector", lambda e, ch=ch, cn=cn: e.tensor_mul(
                                out=ebf2[:],
                                in0=qkvb[:, (16 + ch) * TB + cn:
                                         (16 + ch) * TB + cn + TSUB],
                                in1=psB0[:]))
                            sch.emit("vector", lambda e, ch=ch, cn=cn, c0=c0:
                                     e.tensor_mul(
                                out=obf[:, ch * TB + c0:ch * TB + c0 + TSUB],
                                in0=qkvb[:, (16 + ch) * TB + 1024 + cn:
                                         (16 + ch) * TB + 1024 + cn + TSUB],
                                in1=psB1[:]))
                            n_o = sch.emit("vector", lambda e, ch=ch, c0=c0:
                                           e.tensor_add(
                                out=obf[:, ch * TB + c0:ch * TB + c0 + TSUB],
                                in0=obf[:, ch * TB + c0:ch * TB + c0 + TSUB],
                                in1=ebf2[:]), inc=(vs, 1))
                            sch.wait("tensor", vs, n_o)
        if variant == "mha":
            sch.wait("sync", vs, sch.n(vs))
            sch.wait("sync", ts, sch.n(ts))
            sch.check_deadlock()
            sch.replay(nc)
            return nc
        with ExitStack() as p3:
            e3 = p3.enter_context
            opwb = e3(nc.sbuf_tensor("opwb", [128, KD * D], BF16))
            opbc = e3(nc.sbuf_tensor("opbc", [128, KD], F32))
            fgc = e3(nc.sbuf_tensor("fgc", [128, KD], F32))
            fbc = e3(nc.sbuf_tensor("fbc", [128, KD], F32))
            wgb = e3(nc.sbuf_tensor("wgb", [128, KD * 8], BF16))
            x3s = e3(nc.sbuf_tensor("x3s", [128, KD * TSUB], F32))
            y1s = e3(nc.sbuf_tensor("y1s", [128, KD * TSUB], F32))
            y1bs = e3(nc.sbuf_tensor("y1bs", [128, KD * TSUB], BF16))
            rowst = e3(nc.sbuf_tensor("rowst", [128, KD * 128], BF16))
            lgs = e3(nc.sbuf_tensor("lgs", [8, TB], F32))
            mrow = e3(nc.sbuf_tensor("mrow", [4, TB], F32))
            m2r = e3(nc.sbuf_tensor("m2r", [2, TB], F32))
            mx = e3(nc.sbuf_tensor("mx", [1, TB], F32))
            idxrow = e3(nc.sbuf_tensor("idxrow", [1, TB], F32))
            gvrow = e3(nc.sbuf_tensor("gvrow", [1, TB], F32))
            e8 = e3(nc.sbuf_tensor("e8", [8, TSUB], F32))
            psA = e3(nc.psum_tensor("psA", [128, TSUB], F32))
            psG = e3(nc.psum_tensor("psG", [8, TSUB], F32))
            psTr = e3(nc.psum_tensor("psTr", [128, 128], BF16))
            sch.wait("sync", vs, sch.n(vs))
            sch.wait("sync", ts, sch.n(ts))
            sch.wait("sync", as_, sch.n(as_))
            with nc.sbuf_tensor("wstg2", [128, KD * D], F32) as wstg2:
                nl = sch.emit("sync", lambda e: e.dma_start(
                    out=wstg2[:].rearrange("p (k m) -> p k m", k=KD),
                    in_=opwT.rearrange("(k p) m -> p k m", p=128)), inc=(ld, 16))
                sch.wait("vector", ld, nl)
                n_w = sch.emit("vector", lambda e: e.tensor_copy(
                    out=opwb[:], in_=wstg2[:]), inc=(vs, 1))
                sch.wait("sync", vs, n_w)
                nl = sch.emit("sync", lambda e: e.dma_start(
                    out=wstg2[:, 0:KD * 8].rearrange("p (k m) -> p k m", k=KD),
                    in_=wg.rearrange("(k p) m -> p k m", p=128)), inc=(ld, 16))
                sch.wait("vector", ld, nl)
                n_w = sch.emit("vector", lambda e: e.tensor_copy(
                    out=wgb[:], in_=wstg2[:, 0:KD * 8]), inc=(vs, 1))
                sch.wait("sync", vs, n_w)
            nl = sch.emit("sync", lambda e: e.dma_start(
                out=opbc[:], in_=opb.rearrange("(m p) -> p m", p=128),
                allow_slow_non_contiguous=True), inc=(ld, 16))
            nl = sch.emit("sync", lambda e: e.dma_start(
                out=fgc[:], in_=f1g.rearrange("(m p) -> p m", p=128),
                allow_slow_non_contiguous=True), inc=(ld, 16))
            nl = sch.emit("sync", lambda e: e.dma_start(
                out=fbc[:], in_=f1b.rearrange("(m p) -> p m", p=128),
                allow_slow_non_contiguous=True), inc=(ld, 16))
            sch.wait("vector", ld, nl)
            sch.wait("tensor", vs, sch.n(vs))
            n_sto = 0
            for su in range(TB // TSUB):
                t0 = su * TSUB
                sch.wait("sync", sto, n_sto)
                nl = sch.emit("sync", lambda e, t0=t0: e.dma_start(
                    out=x3s[:].rearrange("p (k t) -> p k t", k=KD),
                    in_=x3Ts[:, t0:t0 + TSUB].rearrange(
                        "(k p) t -> p k t", p=128)), inc=(ld, 16))
                sch.wait("vector", ld, nl)
                for m in range(KD):
                    for k in range(KD):
                        tA = sch.emit("tensor", lambda e, m=m, k=k, t0=t0: e.matmul(
                            psA[:], opwb[:, k * D + m * 128:k * D + m * 128 + 128],
                            obf[:, k * TB + t0:k * TB + t0 + TSUB],
                            start=(k == 0), stop=(k == KD - 1)),
                            inc=(ts, 1) if k == KD - 1 else None)
                    sch.wait("vector", ts, tA)
                    sch.emit("vector", lambda e, m=m: e.tensor_scalar(
                        out=t["d1"][:], in0=psA[:], scalar1=opbc[:, m:m + 1],
                        scalar2=None, op0=AX.add))
                    n_res = sch.emit("vector", lambda e, m=m: e.tensor_add(
                        out=x3s[:, m * TSUB:(m + 1) * TSUB],
                        in0=x3s[:, m * TSUB:(m + 1) * TSUB],
                        in1=t["d1"][:]), inc=(vs, 1))
                    sch.wait("tensor", vs, n_res)
                emit_ln(x3s, y1s, y1bs, fgc, fbc)
                n_f = sch.n(vs)
                sch.wait("sync", vs, n_f)
                n_sto = sch.emit("sync", lambda e, t0=t0: e.dma_start(
                    out=y1T[:, t0:t0 + TSUB].rearrange("(k p) t -> p k t", p=128),
                    in_=y1s[:].rearrange("p (k t) -> p k t", k=KD)), inc=(sto, 16))
                sch.wait("tensor", vs, n_f)
                for tt in range(TSUB // 128):
                    sch.wait("vector", sto, sch.n(sto))
                    for k in range(KD):
                        tT = sch.emit("tensor", lambda e, tt=tt, k=k: e.transpose(
                            psTr[:], y1bs[:, k * TSUB + tt * 128:
                                          k * TSUB + tt * 128 + 128],
                            identb[:]), inc=(ts, 1))
                        sch.wait("vector", ts, tT)
                        n_rc = sch.emit("vector", lambda e, k=k: e.tensor_copy(
                            out=rowst[:, k * 128:(k + 1) * 128], in_=psTr[:]),
                            inc=(vs, 1))
                        sch.wait("tensor", vs, n_rc)
                    sch.wait("sync", vs, sch.n(vs))
                    n_sto = sch.emit("sync", lambda e, t0=t0, tt=tt: e.dma_start(
                        out=y1rows[t0 + tt * 128:t0 + tt * 128 + 128, :],
                        in_=rowst[:]), inc=(sto, 16))
                for k in range(KD):
                    tG = sch.emit("tensor", lambda e, k=k: e.matmul(
                        psG[:], wgb[:, k * 8:(k + 1) * 8],
                        y1bs[:, k * TSUB:(k + 1) * TSUB],
                        start=(k == 0), stop=(k == KD - 1)),
                        inc=(ts, 1) if k == KD - 1 else None)
                sch.wait("vector", ts, tG)
                n_g = sch.emit("vector", lambda e, t0=t0: e.tensor_copy(
                    out=lgs[:, t0:t0 + TSUB], in_=psG[:]), inc=(vs, 1))
                sch.wait("tensor", vs, n_g)
            if variant == "proj":
                sch.wait("sync", sto, sch.n(sto))
                sch.check_deadlock()
                sch.replay(nc)
                return nc
            sch.wait("sync", vs, sch.n(vs))
            nd = sch.emit("sync", lambda e: e.dma_start(
                out=mrow[:], in_=lgs[4:8, :]), inc=(ld, 16))
            sch.wait("vector", ld, nd)
            n_m = sch.emit("vector", lambda e: e.tensor_tensor(
                out=mrow[:], in0=mrow[:], in1=lgs[0:4, :], op=AX.max), inc=(vs, 1))
            sch.wait("sync", vs, n_m)
            nd = sch.emit("sync", lambda e: e.dma_start(
                out=m2r[:], in_=mrow[2:4, :]), inc=(ld, 16))
            sch.wait("vector", ld, nd)
            n_m = sch.emit("vector", lambda e: e.tensor_tensor(
                out=m2r[:], in0=m2r[:], in1=mrow[0:2, :], op=AX.max), inc=(vs, 1))
            sch.wait("sync", vs, n_m)
            nd = sch.emit("sync", lambda e: e.dma_start(
                out=mx[:], in_=m2r[1:2, :]), inc=(ld, 16))
            sch.wait("vector", ld, nd)
            n_m = sch.emit("vector", lambda e: e.tensor_tensor(
                out=mx[:], in0=mx[:], in1=m2r[0:1, :], op=AX.max), inc=(vs, 1))
            sch.wait("tensor", vs, n_m)
            for su in range(TB // TSUB):
                t0 = su * TSUB
                tBm = sch.emit("tensor", lambda e, t0=t0: e.matmul(
                    psA[0:8, :], ones8r[:], mx[:, t0:t0 + TSUB],
                    start=True, stop=True), inc=(ts, 1))
                sch.wait("vector", ts, tBm)
                n_d8 = sch.emit("vector", lambda e, t0=t0: e.tensor_sub(
                    out=e8[:], in0=lgs[:, t0:t0 + TSUB], in1=psA[0:8, :]),
                    inc=(vs, 1))
                sch.wait("vector", vs, n_d8)
                n_eq = sch.emit("vector", lambda e: e.tensor_single_scalar(
                    out=e8[:], in_=e8[:], scalar=0.0, op=AX.is_equal), inc=(vs, 1))
                sch.wait("tensor", vs, n_eq)
                tI = sch.emit("tensor", lambda e: e.matmul(
                    psA[0:1, :], iota8[:], e8[:], start=True, stop=True),
                    inc=(ts, 1))
                sch.wait("vector", ts, tI)
                n_ix = sch.emit("vector", lambda e, t0=t0: e.tensor_copy(
                    out=idxrow[:, t0:t0 + TSUB], in_=psA[0:1, :]), inc=(vs, 1))
                sch.wait("tensor", vs, n_ix)
                tBm = sch.emit("tensor", lambda e, t0=t0: e.matmul(
                    psA[0:8, :], ones8r[:], mx[:, t0:t0 + TSUB],
                    start=True, stop=True), inc=(ts, 1))
                sch.wait("vector", ts, tBm)
                n_d8 = sch.emit("vector", lambda e, t0=t0: e.tensor_sub(
                    out=e8[:], in0=lgs[:, t0:t0 + TSUB], in1=psA[0:8, :]),
                    inc=(vs, 1))
                sch.wait("scalar", vs, n_d8)
                a_e = sch.emit("scalar", lambda e: e.activation(
                    out=e8[:], in_=e8[:], func=AF.Exp), inc=(as_, 1))
                sch.wait("tensor", as_, a_e)
                tE = sch.emit("tensor", lambda e: e.matmul(
                    psA[0:1, :], ones8[:], e8[:], start=True, stop=True),
                    inc=(ts, 1))
                sch.wait("vector", ts, tE)
                n_gv = sch.emit("vector", lambda e, t0=t0: e.reciprocal(
                    out=gvrow[:, t0:t0 + TSUB], in_=psA[0:1, :]), inc=(vs, 1))
                sch.wait("tensor", vs, n_gv)
            n_f = sch.n(vs)
            sch.wait("sync", vs, n_f)
            n_sto = sch.emit("sync", lambda e: e.dma_start(
                out=idxr, in_=idxrow[:]), inc=(sto, 16))
            n_sto = sch.emit("sync", lambda e: e.dma_start(
                out=gvr, in_=gvrow[:]), inc=(sto, 16))
            sch.wait("sync", sto, sch.n(sto))
        sch.check_deadlock()
        sch.replay(nc)
    return nc


def run_h2(x3T, x3Tb, ipw, ipb, opw, opb, f1g, f1b, wg):
    nc = cached_build("h2", build_h2)
    ipwT = np.ascontiguousarray(ipw.T)
    opwT = np.ascontiguousarray(opw.T)
    in_maps = []
    for c in range(NCORE):
        sl = slice(c * 1024, (c + 1) * 1024)
        in_maps.append({
            "x3Ts": np.ascontiguousarray(np.concatenate(
                [x3T[0][:, sl], x3T[1][:, sl]], axis=1)),
            "x3Tbs": np.ascontiguousarray(np.concatenate(
                [x3Tb[0][:, sl], x3Tb[1][:, sl]], axis=1)),
            "ipwT": ipwT, "ipb": ipb, "opwT": opwT, "opb": opb,
            "f1g": f1g, "f1b": f1b, "wg": np.ascontiguousarray(wg),
        })
    res = run_bass_kernel_spmd(nc, in_maps, core_ids=list(range(NCORE)))
    y1T = np.zeros((BT, D, S), np.float32)
    y1rows = np.zeros((NT, D), bf)
    idxf = np.zeros(NT, np.float32)
    gvf = np.zeros(NT, np.float32)
    for c in range(NCORE):
        sl = slice(c * 1024, (c + 1) * 1024)
        r = res.results[c]
        y1T[0][:, sl] = r["y1T"][:, :1024]; y1T[1][:, sl] = r["y1T"][:, 1024:]
        y1rows[c * 1024:(c + 1) * 1024] = r["y1rows"][:1024]
        y1rows[8192 + c * 1024:8192 + (c + 1) * 1024] = r["y1rows"][1024:]
        idxf[c * 1024:(c + 1) * 1024] = r["idxr"][0, :1024]
        idxf[8192 + c * 1024:8192 + (c + 1) * 1024] = r["idxr"][0, 1024:]
        gvf[c * 1024:(c + 1) * 1024] = r["gvr"][0, :1024]
        gvf[8192 + c * 1024:8192 + (c + 1) * 1024] = r["gvr"][0, 1024:]
    return y1T, y1rows, idxf, gvf


def build_route():
    nc = bass.Bass("TRN2", target_bir_lowering=False)
    idxf = nc.dram_tensor("idxf", [1, NT], F32, kind="ExternalInput").ap()
    gvf = nc.dram_tensor("gvf", [1, NT], F32, kind="ExternalInput").ap()
    keepgv = nc.dram_tensor("keepgv", [1, NT], F32, kind="ExternalOutput").ap()
    slotT = nc.dram_tensor("slotT", [128, 128], U32, kind="ExternalOutput").ap()
    src2 = nc.dram_tensor("src2", [16384 + 8 * 14336 + 2, 2], U32,
                          kind="ExternalOutput").ap()
    slsc = nc.dram_tensor("slsc", [NT], U32, kind="Internal").ap()
    CH = 4096
    NCH = NT // CH
    sch = Sched()
    with ExitStack() as ctx:
        e = ctx.enter_context
        ld = e(nc.semaphore("ld"))
        vs = e(nc.semaphore("vs"))
        ts = e(nc.semaphore("ts"))
        gs = e(nc.semaphore("gs"))
        sto = e(nc.semaphore("sto"))
        idx8 = e(nc.sbuf_tensor("idx8", [8, CH], F32))
        oneh = e(nc.sbuf_tensor("oneh", [8, CH], F32))
        ones8w = e(nc.sbuf_tensor("ones8w", [8, CH], F32))
        csum = e(nc.sbuf_tensor("csum", [8, CH], F32))
        car8 = e(nc.sbuf_tensor("car8", [8, 1], F32))
        iota8 = e(nc.sbuf_tensor("iota8", [8, 1], F32))
        ones8 = e(nc.sbuf_tensor("ones8", [8, 1], F32))
        locc = e(nc.sbuf_tensor("locc", [1, CH], F32))
        keepc = e(nc.sbuf_tensor("keepc", [1, CH], F32))
        gvc = e(nc.sbuf_tensor("gvc", [1, CH], F32))
        slotr = e(nc.sbuf_tensor("slotr", [1, CH], F32))
        t1r = e(nc.sbuf_tensor("t1r", [1, CH], F32))
        slotu = e(nc.sbuf_tensor("slotu", [1, CH], U32))
        idxTc = e(nc.sbuf_tensor("idxTc", [128, CH // 128], U32))
        iodup = e(nc.sbuf_tensor("iodup", [128, 2 * (CH // 128)], U32))
        sltt = e(nc.sbuf_tensor("sltt", [128, 128], U32))
        psL = e(nc.psum_tensor("psL", [1, 512], F32))
        n_i = sch.emit("gpsimd", lambda e: e.iota(
            out=iota8[:], pattern=[[0, 1]], base=0, channel_multiplier=1,
            allow_small_or_imprecise_dtypes=True), inc=(gs, 1))
        sch.emit("vector", lambda e: e.memset(ones8w[:], 1.0))
        n_z = sch.emit("vector", lambda e: e.memset(car8[:], 0.0), inc=(vs, 1))
        n_z = sch.emit("vector", lambda e: e.memset(ones8[:], 1.0), inc=(vs, 1))
        sch.wait("vector", gs, n_i)
        sch.wait("tensor", vs, n_z)
        n_stp = 0
        for ch in range(NCH):
            c0 = ch * CH
            sch.wait("sync", sto, n_stp)
            for r in range(8):
                nl = sch.emit("sync", lambda e, r=r, c0=c0: e.dma_start(
                    out=idx8[r:r + 1, :], in_=idxf[:, c0:c0 + CH]), inc=(ld, 16))
            nl = sch.emit("sync", lambda e, c0=c0: e.dma_start(
                out=gvc[:], in_=gvf[:, c0:c0 + CH]), inc=(ld, 16))
            sch.wait("vector", ld, nl)
            n_oh = sch.emit("vector", lambda e: e.tensor_scalar(
                out=oneh[:], in0=idx8[:], scalar1=iota8[:], scalar2=None,
                op0=AX.is_equal), inc=(vs, 1))
            sch.wait("vector", vs, n_oh)
            ini = 0.0 if ch == 0 else car8[:]
            n_cs = sch.emit("vector", lambda e, ini=ini: e.tensor_tensor_scan(
                out=csum[:], data0=ones8w[:], data1=oneh[:], initial=ini,
                op0=AX.mult, op1=AX.add), inc=(vs, 1))
            sch.wait("vector", vs, n_cs)
            n_cr = sch.emit("vector", lambda e: e.tensor_copy(
                out=car8[:], in_=csum[:, CH - 1:CH]), inc=(vs, 1))
            sch.wait("vector", vs, n_cr)
            n_tm = sch.emit("vector", lambda e: e.tensor_mul(
                out=oneh[:], in0=oneh[:], in1=csum[:]), inc=(vs, 1))
            sch.wait("tensor", vs, n_tm)
            for sub in range(CH // 512):
                tL = sch.emit("tensor", lambda e, sub=sub: e.matmul(
                    psL[:], ones8[:], oneh[:, sub * 512:(sub + 1) * 512],
                    start=True, stop=True), inc=(ts, 1))
                sch.wait("vector", ts, tL)
                n_lc = sch.emit("vector", lambda e, sub=sub: e.tensor_scalar(
                    out=locc[:, sub * 512:(sub + 1) * 512], in0=psL[:],
                    scalar1=-1.0, scalar2=None, op0=AX.add), inc=(vs, 1))
                sch.wait("tensor", vs, n_lc)
            sch.emit("vector", lambda e: e.tensor_single_scalar(
                out=keepc[:], in_=locc[:], scalar=2048.0, op=AX.is_lt))
            n_kg = sch.emit("vector", lambda e: e.tensor_mul(
                out=gvc[:], in0=gvc[:], in1=keepc[:]), inc=(vs, 1))
            sch.wait("sync", vs, n_kg)
            n_st1 = sch.emit("sync", lambda e, c0=c0: e.dma_start(
                out=keepgv[:, c0:c0 + CH], in_=gvc[:]), inc=(sto, 16))
            sch.emit("vector", lambda e: e.scalar_tensor_tensor(
                out=slotr[:], in0=idx8[0:1, :], scalar=2048.0, in1=locc[:],
                op0=AX.mult, op1=AX.add))
            sch.emit("vector", lambda e: e.scalar_tensor_tensor(
                out=t1r[:], in0=idx8[0:1, :], scalar=14336.0, in1=locc[:],
                op0=AX.mult, op1=AX.add))
            n_s1 = sch.emit("vector", lambda e: e.tensor_scalar(
                out=t1r[:], in0=t1r[:], scalar1=14336.0, scalar2=None,
                op0=AX.add), inc=(vs, 1))
            sch.wait("vector", vs, n_s1)
            sch.emit("vector", lambda e: e.tensor_sub(
                out=slotr[:], in0=slotr[:], in1=t1r[:]))
            sch.emit("vector", lambda e: e.tensor_mul(
                out=slotr[:], in0=slotr[:], in1=keepc[:]))
            sch.emit("vector", lambda e: e.tensor_add(
                out=slotr[:], in0=slotr[:], in1=t1r[:]))
            n_su = sch.emit("vector", lambda e: e.tensor_copy(
                out=slotu[:], in_=slotr[:]), inc=(vs, 1))
            sch.wait("sync", vs, n_su)
            n_sd = sch.emit("sync", lambda e, c0=c0: e.dma_start(
                out=slsc[c0:c0 + CH].rearrange("(one t) -> one t", one=1),
                in_=slotu[:]), inc=(sto, 16))
            sch.wait("sync", sto, n_sd)
            sch.wait("gpsimd", sto, n_stp)
            n_io = sch.emit("gpsimd", lambda e, c0=c0: e.iota(
                out=iodup[:].rearrange("p (i c) -> p i c", c=2),
                pattern=[[128, CH // 128], [0, 2]], base=c0,
                channel_multiplier=1,
                allow_small_or_imprecise_dtypes=True), inc=(gs, 1))
            for i in range(CH // 128):
                n_ci = sch.emit("sync", lambda e, c0=c0, i=i: e.dma_start(
                    out=idxTc[:, i:i + 1],
                    in_=slsc[c0 + i * 128:c0 + i * 128 + 128].rearrange(
                        "(p one) -> p one", one=1)), inc=(ld, 16))
            sch.wait("gpsimd", ld, n_ci)
            for i in range(CH // 128):
                n_sc = sch.emit("gpsimd", lambda e, i=i: e.indirect_dma_start(
                    out=src2,
                    out_offset=bass.IndirectOffsetOnAxis(
                        ap=idxTc[:, i:i + 1], axis=0),
                    in_=iodup[:, 2 * i:2 * i + 2], in_offset=None),
                    inc=(sto, 16))
            n_stp = sch.n(sto)
            sch.wait("sync", sto, n_stp)
        for col in range(128):
            n_tr = sch.emit("sync", lambda e, col=col: e.dma_start(
                out=sltt[:, col:col + 1],
                in_=slsc[col * 128:col * 128 + 128].rearrange(
                    "(p one) -> p one", one=1)), inc=(ld, 16))
        sch.wait("vector", ld, n_tr)
        n_cl = sch.emit("vector", lambda e: e.tensor_single_scalar(
            out=sltt[:], in_=sltt[:], scalar=16383.0, op=AX.min), inc=(vs, 1))
        sch.wait("sync", vs, n_cl)
        n_f = sch.emit("sync", lambda e: e.dma_start(
            out=slotT, in_=sltt[:]), inc=(sto, 16))
        sch.wait("sync", sto, sch.n(sto))
        sch.check_deadlock()
        sch.replay(nc)
    return nc


def build_expert():
    nc = bass.Bass("TRN2", target_bir_lowering=False)
    y1r = nc.dram_tensor("y1r", [NT, D], BF16, kind="ExternalInput").ap()
    src2 = nc.dram_tensor("src2", [2048, 2], U32, kind="ExternalInput").ap()
    w1e = nc.dram_tensor("w1e", [D, 4096], F32, kind="ExternalInput").ap()
    b1e = nc.dram_tensor("b1e", [4096], F32, kind="ExternalInput").ap()
    w2e = nc.dram_tensor("w2e", [4096, D], F32, kind="ExternalInput").ap()
    b2e = nc.dram_tensor("b2e", [D], F32, kind="ExternalInput").ap()
    erow = nc.dram_tensor("erow", [2048, D], BF16, kind="ExternalOutput").ap()
    KD = 8
    KF = 32
    CAP = 2048
    sch = Sched()
    with ExitStack() as ctx:
        e = ctx.enter_context
        ld = e(nc.semaphore("ld"))
        vs = e(nc.semaphore("vs"))
        ts = e(nc.semaphore("ts"))
        as_ = e(nc.semaphore("as"))
        gs = e(nc.semaphore("gs"))
        sto = e(nc.semaphore("sto"))
        w1b = e(nc.sbuf_tensor("w1b", [128, KD * 4096], BF16))
        w2b = e(nc.sbuf_tensor("w2b", [128, KF * D], BF16))
        b1c = e(nc.sbuf_tensor("b1c", [128, KF], F32))
        b2c = e(nc.sbuf_tensor("b2c", [128, KD], F32))
        idxt = e(nc.sbuf_tensor("idxt", [128, 16], U32))
        identb = e(nc.sbuf_tensor("identb", [128, 128], BF16))
        with nc.sbuf_tensor("wstg", [128, KD * 2048], F32) as wstg:
            for q in range(2):
                nl = sch.emit("sync", lambda e, q=q: e.dma_start(
                    out=wstg[:].rearrange("p (k m) -> p k m", k=KD),
                    in_=w1e[:, q * 2048:(q + 1) * 2048].rearrange(
                        "(k p) m -> p k m", p=128)), inc=(ld, 16))
                sch.wait("vector", ld, nl)
                def cst(e, q=q):
                    dst = w1b[:].rearrange("p (k m) -> p k m", k=KD)[
                        :, :, q * 2048:(q + 1) * 2048]
                    return e.tensor_copy(
                        out=dst, in_=wstg[:].rearrange("p (k m) -> p k m", k=KD))
                n_c = sch.emit("vector", cst, inc=(vs, 1))
                sch.wait("sync", vs, n_c)
            for q in range(2):
                nl = sch.emit("sync", lambda e, q=q: e.dma_start(
                    out=wstg[:, 0:KF * 512].rearrange("p (k m) -> p k m", k=KF),
                    in_=w2e[:, q * 512:(q + 1) * 512].rearrange(
                        "(k p) m -> p k m", p=128)), inc=(ld, 16))
                sch.wait("vector", ld, nl)
                def cst2(e, q=q):
                    dst = w2b[:].rearrange("p (k m) -> p k m", k=KF)[
                        :, :, q * 512:(q + 1) * 512]
                    return e.tensor_copy(
                        out=dst,
                        in_=wstg[:, 0:KF * 512].rearrange("p (k m) -> p k m", k=KF))
                n_c = sch.emit("vector", cst2, inc=(vs, 1))
                sch.wait("sync", vs, n_c)
        dspT = e(nc.sbuf_tensor("dspT", [128, KD * CAP], BF16))
        hsb = e(nc.sbuf_tensor("hsb", [128, KF * 512], BF16))
        yty = e(nc.sbuf_tensor("yty", [128, KD * 512], BF16))
        rowst = e(nc.sbuf_tensor("rowst", [128, KD * 128], BF16))
        dsp = e(nc.sbuf_tensor("dsp", [128, D], BF16))
        psH2 = [e(nc.psum_tensor(f"psH{i}", [128, 512], F32)) for i in range(2)]
        psTr = e(nc.psum_tensor("psTr", [128, 128], BF16))
        nl = sch.emit("sync", lambda e: e.dma_start(
            out=b1c[:], in_=b1e.rearrange("(m p) -> p m", p=128),
            allow_slow_non_contiguous=True), inc=(ld, 16))
        nl = sch.emit("sync", lambda e: e.dma_start(
            out=b2c[:], in_=b2e.rearrange("(m p) -> p m", p=128),
            allow_slow_non_contiguous=True), inc=(ld, 16))
        nid = sch.emit("gpsimd", lambda e: e.iota(
            out=identb[:], pattern=[[-1, 128]], base=0, channel_multiplier=1,
            allow_small_or_imprecise_dtypes=True), inc=(gs, 1))
        sch.wait("vector", gs, nid)
        n_id2 = sch.emit("vector", lambda e: e.tensor_single_scalar(
            out=identb[:], in_=identb[:], scalar=0.0, op=AX.is_equal), inc=(vs, 1))
        sch.wait("tensor", vs, n_id2)
        nl = sch.emit("sync", lambda e: e.dma_start(
            out=idxt[:], in_=src2[0:2048, 0:1].rearrange("(i p) c -> p (i c)", p=128),
            allow_slow_non_contiguous=True), inc=(ld, 16))
        sch.wait("gpsimd", ld, nl)
        sch.wait("gpsimd", vs, sch.n(vs))
        for i in range(16):
            n_g = sch.emit("gpsimd", lambda e, i=i: e.indirect_dma_start(
                out=dsp[:], out_offset=None, in_=y1r,
                in_offset=bass.IndirectOffsetOnAxis(ap=idxt[:, i:i + 1], axis=0)),
                inc=(gs, 16))
            sch.wait("tensor", gs, n_g)
            for k in range(KD):
                tT = sch.emit("tensor", lambda e, k=k: e.transpose(
                    psTr[:], dsp[:, k * 128:(k + 1) * 128], identb[:]),
                    inc=(ts, 1))
                sch.wait("vector", ts, tT)
                n_rc = sch.emit("vector", lambda e, k=k, i=i: e.tensor_copy(
                    out=dspT[:, k * CAP + i * 128:k * CAP + i * 128 + 128],
                    in_=psTr[:]), inc=(vs, 1))
                sch.wait("tensor", vs, n_rc)
            sch.wait("gpsimd", vs, sch.n(vs))
        n_stp = 0
        for tsu in range(4):
            t0 = tsu * 512
            ev_h = [0, 0]
            for m in range(KF):
                pb = m % 2
                sch.wait("tensor", as_, ev_h[pb])
                for k in range(KD):
                    tH = sch.emit("tensor", lambda e, m=m, k=k, t0=t0, pb=pb:
                                  e.matmul(
                        psH2[pb][:],
                        w1b[:, k * 4096 + m * 128:k * 4096 + m * 128 + 128],
                        dspT[:, k * CAP + t0:k * CAP + t0 + 512],
                        start=(k == 0), stop=(k == KD - 1)),
                        inc=(ts, 1) if k == KD - 1 else None)
                sch.wait("scalar", ts, tH)
                ev_h[pb] = sch.emit("scalar", lambda e, m=m, pb=pb: e.activation(
                    out=hsb[:, m * 512:(m + 1) * 512], in_=psH2[pb][:],
                    func=AF.Identity, bias=b1c[:, m:m + 1]), inc=(as_, 1))
            sch.wait("tensor", as_, max(ev_h))
            ev_y = [0, 0]
            for m in range(KD):
                pb = m % 2
                sch.wait("tensor", as_, ev_y[pb])
                for k in range(KF):
                    tY = sch.emit("tensor", lambda e, m=m, k=k, pb=pb: e.matmul(
                        psH2[pb][:], w2b[:, k * D + m * 128:k * D + m * 128 + 128],
                        hsb[:, k * 512:(k + 1) * 512],
                        start=(k == 0), stop=(k == KF - 1)),
                        inc=(ts, 1) if k == KF - 1 else None)
                sch.wait("scalar", ts, tY)
                ev_y[pb] = sch.emit("scalar", lambda e, m=m, pb=pb: e.activation(
                    out=yty[:, m * 512:(m + 1) * 512], in_=psH2[pb][:],
                    func=AF.Identity, bias=b2c[:, m:m + 1]), inc=(as_, 1))
                sch.wait("tensor", as_, ev_y[pb]) if False else None
            sch.wait("vector", as_, max(ev_y))
            for tt in range(4):
                sch.wait("vector", sto, sch.n(sto))
                for k in range(KD):
                    tT = sch.emit("tensor", lambda e, tt=tt, k=k: e.transpose(
                        psTr[:], yty[:, k * 512 + tt * 128:k * 512 + tt * 128 + 128],
                        identb[:]), inc=(ts, 1))
                    sch.wait("vector", ts, tT)
                    n_rc = sch.emit("vector", lambda e, k=k: e.tensor_copy(
                        out=rowst[:, k * 128:(k + 1) * 128], in_=psTr[:]),
                        inc=(vs, 1))
                    sch.wait("tensor", vs, n_rc)
                sch.wait("sync", vs, sch.n(vs))
                n_stp = sch.emit("sync", lambda e, t0=t0, tt=tt: e.dma_start(
                    out=erow[t0 + tt * 128:t0 + tt * 128 + 128, :],
                    in_=rowst[:]), inc=(sto, 16))
            sch.wait("scalar", sto, n_stp)
        sch.wait("sync", sto, sch.n(sto))
        sch.check_deadlock()
        sch.replay(nc)
    return nc


def build_combine():
    nc = bass.Bass("TRN2", target_bir_lowering=False)
    yall = nc.dram_tensor("yall", [NT, D], BF16, kind="ExternalInput").ap()
    slotTs = nc.dram_tensor("slotTs", [128, 16], U32, kind="ExternalInput").ap()
    kgs = nc.dram_tensor("kgs", [1, TB], F32, kind="ExternalInput").ap()
    y1Ts = nc.dram_tensor("y1Ts", [D, TB], F32, kind="ExternalInput").ap()
    x3Ts = nc.dram_tensor("x3Ts", [D, TB], F32, kind="ExternalInput").ap()
    g2 = nc.dram_tensor("g2", [D], F32, kind="ExternalInput").ap()
    b2a = nc.dram_tensor("b2a", [D], F32, kind="ExternalInput").ap()
    outT = nc.dram_tensor("outT", [D, TB], F32, kind="ExternalOutput").ap()
    KD = 8
    sch = Sched()
    with ExitStack() as ctx:
        e = ctx.enter_context
        ld = e(nc.semaphore("ld"))
        vs = e(nc.semaphore("vs"))
        ts = e(nc.semaphore("ts"))
        as_ = e(nc.semaphore("as"))
        gs = e(nc.semaphore("gs"))
        sto = e(nc.semaphore("sto"))
        t = ln_tiles(nc, e, sch, TSUB)
        t["vs"], t["ts"], t["as_"] = vs, ts, as_
        emit_ln = make_ln(sch, t)
        ln_init(sch, t, ld, vs)
        ymT = e(nc.sbuf_tensor("ymT", [128, KD * TB], BF16))
        idxt = e(nc.sbuf_tensor("idxt", [128, 16], U32))
        identb = e(nc.sbuf_tensor("identb", [128, 128], BF16))
        dsp = e(nc.sbuf_tensor("dsp", [128, D], BF16))
        gcol = e(nc.sbuf_tensor("gcol", [128, KD], F32))
        bcol = e(nc.sbuf_tensor("bcol", [128, KD], F32))
        kgr = e(nc.sbuf_tensor("kgr", [1, TB], F32))
        y1s = e(nc.sbuf_tensor("y1s", [128, KD * TSUB], F32))
        x3s = e(nc.sbuf_tensor("x3s", [128, KD * TSUB], F32))
        kb = e(nc.sbuf_tensor("kb", [128, TSUB], F32))
        onesr1 = e(nc.sbuf_tensor("onesr1", [1, 128], F32))
        psB = e(nc.psum_tensor("psB", [128, TSUB], F32))
        psTr = e(nc.psum_tensor("psTr", [128, 128], BF16))
        nl = sch.emit("sync", lambda e: e.dma_start(
            out=gcol[:], in_=g2.rearrange("(m p) -> p m", p=128),
            allow_slow_non_contiguous=True), inc=(ld, 16))
        nl = sch.emit("sync", lambda e: e.dma_start(
            out=bcol[:], in_=b2a.rearrange("(m p) -> p m", p=128),
            allow_slow_non_contiguous=True), inc=(ld, 16))
        nl = sch.emit("sync", lambda e: e.dma_start(out=idxt[:], in_=slotTs),
                      inc=(ld, 16))
        nl = sch.emit("sync", lambda e: e.dma_start(out=kgr[:], in_=kgs),
                      inc=(ld, 16))
        sch.wait("vector", ld, nl)
        sch.emit("vector", lambda e: e.memset(onesr1[:], 1.0))
        ln_init(sch, t, ld, vs)
        nid = sch.emit("gpsimd", lambda e: e.iota(
            out=identb[:], pattern=[[-1, 128]], base=0, channel_multiplier=1,
            allow_small_or_imprecise_dtypes=True), inc=(gs, 1))
        sch.wait("vector", gs, nid)
        n_id2 = sch.emit("vector", lambda e: e.tensor_single_scalar(
            out=identb[:], in_=identb[:], scalar=0.0, op=AX.is_equal), inc=(vs, 1))
        sch.wait("tensor", vs, n_id2)
        sch.wait("gpsimd", ld, nl)
        for i in range(16):
            n_g = sch.emit("gpsimd", lambda e, i=i: e.indirect_dma_start(
                out=dsp[:], out_offset=None, in_=yall,
                in_offset=bass.IndirectOffsetOnAxis(ap=idxt[:, i:i + 1], axis=0)),
                inc=(gs, 16))
            sch.wait("tensor", gs, n_g)
            for k in range(KD):
                tT = sch.emit("tensor", lambda e, k=k: e.transpose(
                    psTr[:], dsp[:, k * 128:(k + 1) * 128], identb[:]),
                    inc=(ts, 1))
                sch.wait("vector", ts, tT)
                n_rc = sch.emit("vector", lambda e, k=k, i=i: e.tensor_copy(
                    out=ymT[:, k * TB + i * 128:k * TB + i * 128 + 128],
                    in_=psTr[:]), inc=(vs, 1))
                sch.wait("tensor", vs, n_rc)
            sch.wait("gpsimd", vs, sch.n(vs))
        n_sto = 0
        for su in range(TB // TSUB):
            t0 = su * TSUB
            sch.wait("sync", sto, n_sto)
            nl = sch.emit("sync", lambda e, t0=t0: e.dma_start(
                out=y1s[:].rearrange("p (k t) -> p k t", k=KD),
                in_=y1Ts[:, t0:t0 + TSUB].rearrange("(k p) t -> p k t", p=128)),
                inc=(ld, 16))
            nl = sch.emit("sync", lambda e, t0=t0: e.dma_start(
                out=x3s[:].rearrange("p (k t) -> p k t", k=KD),
                in_=x3Ts[:, t0:t0 + TSUB].rearrange("(k p) t -> p k t", p=128)),
                inc=(ld, 16))
            sch.wait("tensor", ld, nl)
            tB = sch.emit("tensor", lambda e, t0=t0: e.matmul(
                psB[:], onesr1[:], kgr[:, t0:t0 + TSUB], start=True, stop=True),
                inc=(ts, 1))
            sch.wait("vector", ts, tB)
            sch.wait("vector", ld, nl)
            n_kb = sch.emit("vector", lambda e: e.tensor_copy(
                out=kb[:], in_=psB[:]), inc=(vs, 1))
            for m in range(KD):
                sch.emit("vector", lambda e, m=m, t0=t0: e.tensor_mul(
                    out=t["d1"][:], in0=ymT[:, m * TB + t0:m * TB + t0 + TSUB],
                    in1=kb[:]))
                n_r = sch.emit("vector", lambda e, m=m: e.tensor_add(
                    out=y1s[:, m * TSUB:(m + 1) * TSUB],
                    in0=y1s[:, m * TSUB:(m + 1) * TSUB], in1=t["d1"][:]),
                    inc=(vs, 1))
            emit_ln(y1s, y1s, None, gcol, bcol)
            n_a2 = sch.emit("vector", lambda e: e.tensor_add(
                out=x3s[:], in0=x3s[:], in1=y1s[:]), inc=(vs, 1))
            sch.wait("tensor", vs, n_a2)
            emit_ln(x3s, x3s)
            n_f = sch.n(vs)
            sch.wait("sync", vs, n_f)
            n_sto = sch.emit("sync", lambda e, t0=t0: e.dma_start(
                out=outT[:, t0:t0 + TSUB].rearrange("(k p) t -> p k t", p=128),
                in_=x3s[:].rearrange("p (k t) -> p k t", k=KD)), inc=(sto, 16))
        sch.wait("sync", sto, sch.n(sto))
        sch.check_deadlock()
        sch.replay(nc)
    return nc


def run_route(idxf, gvf):
    nc = cached_build("route", build_route)
    im = {"idxf": idxf.reshape(1, NT).astype(np.float32),
          "gvf": gvf.reshape(1, NT).astype(np.float32)}
    res = run_bass_kernel_spmd(nc, [im] * NCORE, core_ids=list(range(NCORE)))
    r = res.results[0]
    return (np.asarray(r["keepgv"])[0], np.asarray(r["slotT"]),
            np.asarray(r["src2"]))


def run_expert(y1rows, src2, w1, b1, w2, b2):
    nc = cached_build("expert", build_expert)
    in_maps = []
    for c in range(NCORE):
        in_maps.append({
            "y1r": y1rows,
            "src2": np.ascontiguousarray(src2[c * 2048:(c + 1) * 2048]),
            "w1e": np.ascontiguousarray(w1[c]),
            "b1e": np.ascontiguousarray(b1[c]),
            "w2e": np.ascontiguousarray(w2[c]),
            "b2e": np.ascontiguousarray(b2[c]),
        })
    res = run_bass_kernel_spmd(nc, in_maps, core_ids=list(range(NCORE)))
    return np.concatenate(
        [np.asarray(res.results[c]["erow"]) for c in range(NCORE)], axis=0)


def run_combine(yall, slotT, keepgv, y1T, x3T, g2, b2a):
    nc = cached_build("combine", build_combine)
    in_maps = []
    for c in range(NCORE):
        sl = slice(c * 1024, (c + 1) * 1024)
        slotTs = np.ascontiguousarray(np.concatenate(
            [slotT[:, c * 8:(c + 1) * 8], slotT[:, 64 + c * 8:64 + (c + 1) * 8]],
            axis=1))
        kgs = np.concatenate(
            [keepgv[c * 1024:(c + 1) * 1024],
             keepgv[8192 + c * 1024:8192 + (c + 1) * 1024]]).reshape(1, TB)
        in_maps.append({
            "yall": yall, "slotTs": slotTs,
            "kgs": np.ascontiguousarray(kgs.astype(np.float32)),
            "y1Ts": np.ascontiguousarray(np.concatenate(
                [y1T[0][:, sl], y1T[1][:, sl]], axis=1)),
            "x3Ts": np.ascontiguousarray(np.concatenate(
                [x3T[0][:, sl], x3T[1][:, sl]], axis=1)),
            "g2": g2, "b2a": b2a,
        })
    res = run_bass_kernel_spmd(nc, in_maps, core_ids=list(range(NCORE)))
    outT = np.zeros((BT, D, S), np.float32)
    for c in range(NCORE):
        sl = slice(c * 1024, (c + 1) * 1024)
        outT[0][:, sl] = res.results[c]["outT"][:, :1024]
        outT[1][:, sl] = res.results[c]["outT"][:, 1024:]
    return outT


def kernel(L0_attn_x, L0_attn_Wq, L0_attn_Wk, L0_attn_Wv, L0_attn_Wo,
           L0_ffn_kernel, L0_ffn_input_scale, L0_ffn_kernel_scale, L0_ffn_bias,
           L1_attn_Wr, L1_attn_Wk, L1_attn_Wv, L1_attn_Wo, L1_attn_W1,
           L1_attn_b1, L1_attn_W2, L1_attn_b2, L1_attn_ln1_g, L1_attn_ln1_b,
           L1_attn_ln2_g, L1_attn_ln2_b, L1_ffn_in_proj_w, L1_ffn_in_proj_b,
           L1_ffn_out_proj_w, L1_ffn_out_proj_b, L1_ffn_ln1_g, L1_ffn_ln1_b,
           L1_ffn_ln2_g, L1_ffn_ln2_b, L1_ffn_wg, L1_ffn_w1, L1_ffn_b1,
           L1_ffn_w2, L1_ffn_b2):
    x = np.asarray(L0_attn_x, np.float32)
    f32 = lambda a: np.asarray(a, np.float32)
    oT = run_l0_attn(x, f32(L0_attn_Wq), f32(L0_attn_Wk), f32(L0_attn_Wv))
    x2T, x2Tb = run_l0_rest(
        x, oT, f32(L0_attn_Wo), np.asarray(L0_ffn_kernel),
        float(np.asarray(L0_ffn_input_scale)[0]),
        float(np.asarray(L0_ffn_kernel_scale)[0]), np.asarray(L0_ffn_bias))
    yT = run_rkv(x2Tb, f32(L1_attn_Wr), f32(L1_attn_Wk), f32(L1_attn_Wv))
    x1T, x1Tb, h1T = run_postrkv(
        x2T, yT, f32(L1_attn_Wo), f32(L1_attn_ln1_g), f32(L1_attn_ln1_b),
        f32(L1_attn_W1), f32(L1_attn_b1))
    s1 = run_lif_phase(h1T, 32, 2, 8)
    h2T = run_w2(s1, f32(L1_attn_W2), f32(L1_attn_b2))
    fsp = run_lif_phase(h2T, 64, 1, 2)
    x3T, x3Tb = run_h1(x1T, fsp, x2T, f32(L1_attn_ln2_g), f32(L1_attn_ln2_b))
    y1T, y1rows, idxf, gvf = run_h2(
        x3T, x3Tb, f32(L1_ffn_in_proj_w), f32(L1_ffn_in_proj_b),
        f32(L1_ffn_out_proj_w), f32(L1_ffn_out_proj_b),
        f32(L1_ffn_ln1_g), f32(L1_ffn_ln1_b), f32(L1_ffn_wg))
    keepgv, slotT, src2 = run_route(idxf, gvf)
    yall = run_expert(y1rows, src2, f32(L1_ffn_w1), f32(L1_ffn_b1),
                      f32(L1_ffn_w2), f32(L1_ffn_b2))
    outT = run_combine(yall, slotT, keepgv, y1T, x3T,
                       f32(L1_ffn_ln2_g), f32(L1_ffn_ln2_b))
    return np.ascontiguousarray(outT.transpose(0, 2, 1))


# version 07
# speedup vs baseline: 1.4613x; 1.2057x over previous
import math
from contextlib import ExitStack

import numpy as np
import ml_dtypes
import concourse.bass as bass
import concourse.mybir as mybir
from concourse.bass_utils import run_bass_kernel_spmd

F32 = mybir.dt.float32
BF16 = mybir.dt.bfloat16
FP8 = mybir.dt.float8e4
U8 = mybir.dt.uint8
U32 = mybir.dt.uint32
AX = mybir.AluOpType
AF = mybir.ActivationFunctionType
AXL = mybir.AxisListType

BT = 2
S = 8192
D = 1024
H = 16
HD = 64
NCORE = 8
NT = BT * S

bf = ml_dtypes.bfloat16


class Sched:
    ENGINES = ("sync", "vector", "scalar", "tensor", "gpsimd")

    def __init__(self):
        self.q = {e: [] for e in self.ENGINES}
        self.cnt = {}
        self.trace = {e: [] for e in self.ENGINES}

    def emit(self, eng, fn, inc=None):
        if inc is None:
            self.q[eng].append(fn)
            self.trace[eng].append(("op", None, 0))
            return None
        sem, amt = inc
        self.cnt[sem.name] = self.cnt.get(sem.name, 0) + amt
        self.trace[eng].append(("inc", sem.name, amt))

        def wrapped(e, fn=fn, sem=sem, amt=amt):
            fn(e).then_inc(sem, amt)
        self.q[eng].append(wrapped)
        return self.cnt[sem.name]

    def wait(self, eng, sem, val):
        if val is None or val <= 0:
            return
        self.trace[eng].append(("wait", sem.name, val))
        self.q[eng].append(lambda e, sem=sem, val=val: e.wait_ge(sem, val))

    def check_deadlock(self):
        pos = {e: 0 for e in self.ENGINES}
        cnt = {}
        progress = True
        while progress:
            progress = False
            for e in self.ENGINES:
                while pos[e] < len(self.trace[e]):
                    kind, sem, val = self.trace[e][pos[e]]
                    if kind == "wait" and cnt.get(sem, 0) < val:
                        break
                    if kind == "inc":
                        cnt[sem] = cnt.get(sem, 0) + val
                    pos[e] += 1
                    progress = True
        stuck = {e: (pos[e], len(self.trace[e])) for e in self.ENGINES
                 if pos[e] < len(self.trace[e])}
        if stuck:
            detail = []
            for e, (p, n) in stuck.items():
                kind, sem, val = self.trace[e][p]
                detail.append(f"{e}@{p}/{n}: waiting {sem}>={val} (has {cnt.get(sem, 0)})")
            raise RuntimeError("DEADLOCK: " + "; ".join(detail))

    def n(self, sem):
        return self.cnt.get(sem.name, 0)

    def replay(self, nc):
        with nc.Block() as block:
            for eng in self.ENGINES:
                lst = self.q[eng]
                if not lst:
                    continue
                deco = getattr(block, eng)

                def body(e, lst=lst):
                    for fn in lst:
                        fn(e)
                deco(body)


_PROG_CACHE = {}


def cached_build(key, fn):
    if key not in _PROG_CACHE:
        _PROG_CACHE[key] = fn()
    return _PROG_CACHE[key]


def build_l0_attn(variant='full'):
    nc = bass.Bass("TRN2", target_bir_lowering=False)
    xT = nc.dram_tensor("xT", [D, S], F32, kind="ExternalInput").ap()
    wqT = nc.dram_tensor("wqT", [D, 256], F32, kind="ExternalInput").ap()
    wkT = nc.dram_tensor("wkT", [D, 256], F32, kind="ExternalInput").ap()
    wvT = nc.dram_tensor("wvT", [D, 256], F32, kind="ExternalInput").ap()
    cosr = nc.dram_tensor("cosr", [128, S], F32, kind="ExternalInput").ap()
    sinr = nc.dram_tensor("sinr", [128, S], F32, kind="ExternalInput").ap()
    oT = nc.dram_tensor("oT", [256, S], F32, kind="ExternalOutput").ap()

    TC = 512
    NTC = S // TC
    KD = D // 128
    NV = S // 128

    sch = Sched()
    with ExitStack() as ctx:
        e = ctx.enter_context
        qtb = [e(nc.sbuf_tensor(f"qtb{i}", [128, S], BF16)) for i in range(2)]
        ktb = [e(nc.sbuf_tensor(f"ktb{i}", [128, S], BF16)) for i in range(2)]
        vpt = e(nc.sbuf_tensor("vpt", [128, NV * 260], BF16))
        ones128 = e(nc.sbuf_tensor("ones128", [128, 1], F32))
        onesb = e(nc.sbuf_tensor("onesb", [1, 64], F32))
        negm0 = e(nc.sbuf_tensor("negm0", [128, 1], F32))
        onesr = e(nc.sbuf_tensor("onesr", [1, 128], F32))
        nrm = e(nc.sbuf_tensor("nrm", [1, 4], F32))
        rmx = e(nc.sbuf_tensor("rmx", [1, 2], F32))
        ld = e(nc.semaphore("ld"))
        vs = e(nc.semaphore("vs"))
        ts = e(nc.semaphore("ts"))
        as_ = e(nc.semaphore("as"))
        gs = e(nc.semaphore("gs"))
        sto = e(nc.semaphore("sto"))

        shuf_mask = []
        for i in range(0, 32, 2):
            shuf_mask += [i + 1, i]

        with ExitStack() as proj:
            p = proj.enter_context
            xts = [p(nc.sbuf_tensor(f"xts{i}", [128, KD * TC], F32))
                   for i in range(2)]
            xtsb = [p(nc.sbuf_tensor(f"xtsb{i}", [128, KD * TC], BF16))
                    for i in range(2)]
            cosc = [p(nc.sbuf_tensor(f"cosc{i}", [128, TC], F32))
                    for i in range(2)]
            sinc = [p(nc.sbuf_tensor(f"sinc{i}", [128, TC], F32))
                    for i in range(2)]
            wstg = p(nc.sbuf_tensor("wstg", [128, KD * 256], F32))
            wq3b = p(nc.sbuf_tensor("wq3b", [128, KD * 256 * 3], BF16))
            qf = p(nc.sbuf_tensor("qf", [128, TC], F32))
            shf = p(nc.sbuf_tensor("shf", [128, TC], F32))
            am = p(nc.sbuf_tensor("am", [128, TC], F32))
            m2 = p(nc.sbuf_tensor("m2", [128, TC], F32))
            nsq = p(nc.sbuf_tensor("nsq", [128, TC], F32))
            psQK = [p(nc.psum_tensor(f"psQK{i}", [128, TC], F32)) for i in range(4)]
            psV = p(nc.psum_tensor("psV", [128, 256], F32))
            psN = p(nc.psum_tensor("psN", [1, TC], F32))

            for wi, w in enumerate((wqT, wkT, wvT)):
                n_ld = sch.emit("sync", lambda e, w=w: e.dma_start(
                    out=wstg[:].rearrange("p (k m) -> p k m", k=KD),
                    in_=w.rearrange("(k p) m -> p k m", p=128)),
                    inc=(ld, 16))
                sch.wait("vector", ld, n_ld)
                n_cast = sch.emit("vector", lambda e, wi=wi: e.tensor_copy(
                    out=wq3b[:, wi * KD * 256:(wi + 1) * KD * 256], in_=wstg[:]),
                    inc=(vs, 1))
                sch.wait("sync", vs, n_cast)
            sch.emit("vector", lambda e: e.memset(ones128[:], 1.0))
            sch.emit("vector", lambda e: e.memset(onesb[:], 1.0))
            sch.emit("vector", lambda e: e.memset(onesr[:], 1.0))
            sch.emit("vector", lambda e: e.memset(nrm[:], 0.0))

            def set_vones(e):
                return e.memset(
                    vpt[:].rearrange("p (t h c) -> p t h c", h=4, c=65)[:, :, :, 64:65],
                    1.0)
            n_init = sch.emit("vector", set_vones, inc=(vs, 1))
            sch.wait("tensor", vs, n_init)

            if variant == "projW":
                sch.wait("sync", vs, sch.n(vs))
                n_dbg = sch.emit("sync", lambda e: e.dma_start(
                    out=oT[0:128, 0:4],
                    in_=wstg[:, 0:4], allow_slow_non_contiguous=True), inc=(sto, 16))
                sch.wait("sync", sto, n_dbg)
                sch.check_deadlock()
                sch.replay(nc)
                return nc
            nchunks = NTC
            do_qk = do_v = do_rope = do_norm = True
            if variant.startswith("projC"):
                parts = variant[5:].split("-")
                nchunks = int(parts[0])
                if len(parts) > 1:
                    do_qk = "qk" in parts
                    do_v = "v" in parts
                    do_rope = "rope" in parts
                    do_norm = "norm" in parts
            tsN_prev = 0
            red_prev = 0
            ld_done = [0, 0]
            consumed = [0, 0]

            def emit_loads(tc):
                buf = tc % 2
                sch.wait("sync", vs, consumed[buf])
                sch.emit("sync", lambda e, tc=tc, buf=buf: e.dma_start(
                    out=xts[buf][:].rearrange("p (k t) -> p k t", k=KD),
                    in_=xT[:, tc * TC:(tc + 1) * TC].rearrange(
                        "(k p) t -> p k t", p=128)), inc=(ld, 16))
                sch.emit("sync", lambda e, tc=tc, buf=buf: e.dma_start(
                    out=cosc[buf][:], in_=cosr[:, tc * TC:(tc + 1) * TC]),
                    inc=(ld, 16))
                nl = sch.emit("sync", lambda e, tc=tc, buf=buf: e.dma_start(
                    out=sinc[buf][:], in_=sinr[:, tc * TC:(tc + 1) * TC]),
                    inc=(ld, 16))
                ld_done[buf] = nl

            emit_loads(0)
            for tc in range(NTC):
                buf = tc % 2
                if tc + 1 < NTC:
                    emit_loads(tc + 1)
                sch.wait("vector", ld, ld_done[buf])
                n_xc = sch.emit("vector", lambda e, buf=buf: e.tensor_copy(
                    out=xtsb[buf][:], in_=xts[buf][:]), inc=(vs, 1))
                sch.wait("tensor", vs, n_xc)
                sch.wait("tensor", ld, ld_done[buf])
                tsQK = {}
                for wi in range(2):
                    for m in range(2):
                        for k in range(KD):
                            def mm(e, wi=wi, m=m, k=k, buf=buf):
                                return e.matmul(
                                    psQK[wi * 2 + m][:],
                                    wq3b[:, wi * KD * 256 + k * 256 + m * 128:
                                         wi * KD * 256 + k * 256 + m * 128 + 128],
                                    xtsb[buf][:, k * TC:(k + 1) * TC],
                                    start=(k == 0), stop=(k == KD - 1))
                            r = sch.emit("tensor", mm,
                                         inc=(ts, 1) if k == KD - 1 else None)
                        tsQK[(wi, m)] = r
                for sv in range(4):
                    for k in range(KD):
                        def mmv(e, sv=sv, k=k, buf=buf):
                            return e.matmul(
                                psV[:],
                                xtsb[buf][:, k * TC + sv * 128:
                                          k * TC + sv * 128 + 128],
                                wq3b[:, 2 * KD * 256 + k * 256:
                                     2 * KD * 256 + (k + 1) * 256],
                                start=(k == 0), stop=(k == KD - 1))
                        r = sch.emit("tensor", mmv,
                                     inc=(ts, 1) if k == KD - 1 else None)
                    sch.wait("vector", ts, r)

                    def evv(e, tc=tc, sv=sv):
                        tchunk = tc * 4 + sv
                        dst = vpt[:, tchunk * 260:tchunk * 260 + 260].rearrange(
                            "p (h c) -> p h c", h=4)[:, :, 0:64]
                        return e.tensor_copy(
                            out=dst, in_=psV[:].rearrange("p (h c) -> p h c", h=4))
                    n_evv = sch.emit("vector", evv, inc=(vs, 1))
                    if sv < 3:
                        sch.wait("tensor", vs, n_evv)
                for wi in range(2):
                    dstt = qtb if wi == 0 else ktb
                    for m in range(2):
                        bank = psQK[wi * 2 + m]
                        sch.wait("scalar", ts, tsQK[(wi, m)])
                        sch.wait("scalar", ts, tsN_prev)
                        aN = sch.emit("scalar", lambda e, bank=bank: e.activation(
                            out=nsq[:], in_=bank[:], func=AF.Square), inc=(as_, 1))
                        sch.wait("tensor", as_, aN)
                        if m == 0:
                            sch.wait("tensor", vs, red_prev)
                        tsN_prev = sch.emit("tensor", lambda e, m=m: e.matmul(
                            psN[:], ones128[:], nsq[:],
                            start=(m == 0), stop=(m == 1)), inc=(ts, 1))
                        sch.wait("vector", ts, tsQK[(wi, m)])
                        sch.wait("vector", as_, aN)
                        sch.emit("vector", lambda e, bank=bank: e.tensor_copy(
                            out=qf[:], in_=bank[:]))
                        sch.emit("vector", lambda e: e.stream_shuffle(
                            out=shf[:], in_=qf[:], mask=shuf_mask))
                        sch.emit("vector", lambda e, bank=bank, buf=buf:
                                 e.tensor_mul(
                            out=am[:], in0=bank[:], in1=cosc[buf][:]))
                        sch.emit("vector", lambda e, buf=buf: e.tensor_mul(
                            out=m2[:], in0=shf[:], in1=sinc[buf][:]))
                        n_rope = sch.emit(
                            "vector", lambda e, dstt=dstt, m=m, tc=tc:
                            e.tensor_add(
                                out=dstt[m][:, tc * TC:(tc + 1) * TC],
                                in0=am[:], in1=m2[:]), inc=(vs, 1))
                        if m == 1:
                            sch.wait("vector", ts, tsN_prev)
                            n_rx = sch.emit("vector", lambda e, wi=wi:
                                            e.tensor_reduce(
                                out=rmx[:, wi:wi + 1], in_=psN[:],
                                axis=AXL.X, op=AX.max), inc=(vs, 1))
                            sch.wait("vector", vs, n_rx)
                            red_prev = sch.emit(
                                "vector", lambda e, wi=wi: e.tensor_tensor(
                                    out=nrm[:, wi:wi + 1], in0=nrm[:, wi:wi + 1],
                                    in1=rmx[:, wi:wi + 1], op=AX.max), inc=(vs, 1))
                consumed[buf] = sch.n(vs)

            sch.emit("vector", lambda e: e.tensor_mul(
                out=nrm[:, 2:3], in0=nrm[:, 0:1], in1=nrm[:, 1:2]), inc=(vs, 1))
            sch.wait("scalar", vs, sch.n(vs))
            aS = sch.emit("scalar", lambda e: e.activation(
                out=nrm[:, 3:4], in_=nrm[:, 2:3], func=AF.Sqrt), inc=(as_, 1))
            sch.wait("tensor", as_, aS)
            psM = nc.psum_tensor("psM", [128, 1], F32)
            psMh = psM.__enter__()
            nG = sch.emit("tensor", lambda e: e.matmul(
                psMh[:], onesr[:], nrm[:, 3:4], start=True, stop=True), inc=(ts, 1))
            sch.wait("vector", ts, nG)
            n_m0 = sch.emit("vector", lambda e: e.tensor_scalar_mul(
                out=negm0[:], in0=psMh[:], scalar1=-0.125), inc=(vs, 1))
            psM.__exit__(None, None, None)

        if variant == "proj":
            sch.wait("sync", vs, n_m0)
            n_dbg = sch.emit("sync", lambda e: e.dma_start(
                out=oT[0:128, 0:1], in_=negm0[:],
                allow_slow_non_contiguous=True), inc=(sto, 16))
            sch.wait("sync", sto, n_dbg)
            sch.check_deadlock()
            sch.replay(nc)
            return nc
        with ExitStack() as fl:
            p = fl.enter_context
            pbuf = [[p(nc.sbuf_tensor(f"pbuf{h}_{i}", [128, TC], BF16))
                     for i in range(2)] for h in range(2)]
            rb = p(nc.sbuf_tensor("rb", [64, TC], F32))
            rl = p(nc.sbuf_tensor("rl", [1, TC], F32))
            onorm = [p(nc.sbuf_tensor(f"onorm{i}", [64, TC], F32)) for i in range(2)]
            psS = [[p(nc.psum_tensor(f"psS{h}_{i}", [128, TC], F32))
                    for i in range(2)] for h in range(2)]
            psO = [p(nc.psum_tensor(f"psO{h}", [65, TC], F32)) for h in range(2)]
            psB = p(nc.psum_tensor("psB", [64, TC], F32))

            sch.wait("tensor", vs, n_m0)
            sch.wait("scalar", vs, n_m0)

            last_exp = [[0, 0], [0, 0]]
            last_use_p = [[0, 0], [0, 0]]
            last_norm = [0, 0]
            last_sto = [0, 0]
            qi = 0
            for hp in range(2):
                qsrc = qtb[hp]
                ksrc = ktb[hp]
                for qt in range(NTC):
                    nkc = 4 * qt + 4
                    tPV = [0, 0]

                    def emit_S(kc):
                        sb = kc % 2
                        for hm in range(2):
                            sch.wait("tensor", as_, last_exp[hm][sb])
                            if kc == 0:
                                sch.wait("tensor", vs, last_norm[hm])
                            tS = sch.emit("tensor", lambda e, hm=hm, kc=kc,
                                          sb=sb, qt=qt, qsrc=qsrc, ksrc=ksrc:
                                          e.matmul(
                                psS[hm][sb][:],
                                ksrc[hm * 64:hm * 64 + 64,
                                     kc * 128:(kc + 1) * 128],
                                qsrc[hm * 64:hm * 64 + 64,
                                     qt * TC:(qt + 1) * TC],
                                start=True, stop=True), inc=(ts, 1))
                            sch.wait("scalar", ts, tS)
                            sch.wait("scalar", ts, last_use_p[hm][sb])
                            aE = sch.emit("scalar", lambda e, hm=hm, sb=sb:
                                          e.activation(
                                out=pbuf[hm][sb][:], in_=psS[hm][sb][:],
                                func=AF.Exp, bias=negm0[:], scale=0.125),
                                inc=(as_, 1))
                            last_exp[hm][sb] = aE
                            if kc >= 4 * qt:
                                sch.wait("gpsimd", as_, aE)
                                sch.emit("gpsimd", lambda e, hm=hm, sb=sb,
                                         qt=qt, kc=kc: e.affine_select(
                                    out=pbuf[hm][sb][:], in_=pbuf[hm][sb][:],
                                    pattern=[[1, TC]], compare_op=AX.is_ge,
                                    fill=0.0, base=qt * TC - kc * 128,
                                    channel_multiplier=-1), inc=(gs, 1))

                    def emit_PV(kc):
                        sb = kc % 2
                        for hm in range(2):
                            if kc >= 4 * qt:
                                sch.wait("tensor", gs, sch.n(gs))
                            else:
                                sch.wait("tensor", as_, last_exp[hm][sb])
                            h = hp * 2 + hm
                            tPV[hm] = sch.emit("tensor", lambda e, h=h, kc=kc,
                                               hm=hm, sb=sb, nkc=nkc: e.matmul(
                                psO[hm][:],
                                vpt[:, kc * 260 + h * 65:
                                    kc * 260 + h * 65 + 65],
                                pbuf[hm][sb][:],
                                start=(kc == 0), stop=(kc == nkc - 1)),
                                inc=(ts, 1))
                            last_use_p[hm][sb] = tPV[hm]

                    for kc in range(nkc):
                        emit_S(kc)
                        if kc >= 1:
                            emit_PV(kc - 1)
                    emit_PV(nkc - 1)
                    for hm in range(2):
                        h = hp * 2 + hm
                        ob = qi % 2
                        qi += 1
                        sch.wait("vector", ts, tPV[hm])
                        nR = sch.emit("vector", lambda e, hm=hm: e.reciprocal(
                            out=rl[:], in_=psO[hm][64:65, :]), inc=(vs, 1))
                        sch.wait("tensor", vs, nR)
                        tB = sch.emit("tensor", lambda e: e.matmul(
                            psB[:], onesb[:], rl[:], start=True, stop=True),
                            inc=(ts, 1))
                        sch.wait("vector", ts, tB)
                        sch.emit("vector", lambda e: e.tensor_copy(
                            out=rb[:], in_=psB[:]))
                        sch.wait("vector", sto, last_sto[ob])
                        last_norm[hm] = sch.emit("vector",
                                                 lambda e, hm=hm, ob=ob:
                                                 e.tensor_mul(
                            out=onorm[ob][:], in0=psO[hm][0:64, :], in1=rb[:]),
                            inc=(vs, 1))
                        sch.wait("sync", vs, last_norm[hm])
                        last_sto[ob] = sch.emit(
                            "sync", lambda e, h=h, qt=qt, ob=ob: e.dma_start(
                                out=oT[h * 64:h * 64 + 64, qt * TC:(qt + 1) * TC],
                                in_=onorm[ob][:]), inc=(sto, 16))
            sch.wait("sync", sto, sch.n(sto))
        sch.check_deadlock()
        sch.replay(nc)
    return nc


def rope_tables():
    inv_freq = (1.0 / (10000.0 ** (np.arange(0, HD, 2, dtype=np.float32) / HD))
                ).astype(np.float32)
    pos = np.arange(S, dtype=np.float32)
    freqs = pos[:, None] * inv_freq[None, :]
    emb = np.repeat(freqs, 2, axis=-1)
    cos_t = np.cos(emb).astype(np.float32)
    sin_t = np.sin(emb).astype(np.float32)
    sign = np.where(np.arange(HD) % 2 == 0, -1.0, 1.0).astype(np.float32)
    sinp = sin_t * sign[None, :]
    cosr = np.ascontiguousarray(np.tile(cos_t.T, (2, 1)))
    sinr = np.ascontiguousarray(np.tile(sinp.T, (2, 1)))
    return cosr, sinr


def run_l0_attn(x, Wq, Wk, Wv):
    nc = cached_build("l0_attn", build_l0_attn)
    cosr, sinr = rope_tables()
    in_maps = []
    for c in range(NCORE):
        b = c // 4
        h0 = 4 * (c % 4)
        sl = slice(64 * h0, 64 * h0 + 256)
        in_maps.append({
            "xT": np.ascontiguousarray(x[b].T),
            "wqT": np.ascontiguousarray(Wq.T[:, sl]),
            "wkT": np.ascontiguousarray(Wk.T[:, sl]),
            "wvT": np.ascontiguousarray(Wv.T[:, sl]),
            "cosr": cosr, "sinr": sinr,
        })
    res = run_bass_kernel_spmd(nc, in_maps, core_ids=list(range(NCORE)))
    oT = np.zeros((BT, D, S), np.float32)
    for c in range(NCORE):
        b = c // 4
        h0 = 4 * (c % 4)
        oT[b, 64 * h0:64 * h0 + 256] = res.results[c]["oT"]
    return oT


TB = 2048
TSUB = 512


def build_l0_rest():
    nc = bass.Bass("TRN2", target_bir_lowering=False)
    xTs = nc.dram_tensor("xTs", [D, TB], F32, kind="ExternalInput").ap()
    oTs = nc.dram_tensor("oTs", [D, TB], F32, kind="ExternalInput").ap()
    woT = nc.dram_tensor("woT", [D, D], F32, kind="ExternalInput").ap()
    fker = nc.dram_tensor("fker", [D, D], BF16, kind="ExternalInput").ap()
    fscale = nc.dram_tensor("fscale", [1, 2], F32, kind="ExternalInput").ap()
    fbias = nc.dram_tensor("fbias", [D], BF16, kind="ExternalInput").ap()
    x2T = nc.dram_tensor("x2T", [D, TB], F32, kind="ExternalOutput").ap()
    x2Tb = nc.dram_tensor("x2Tb", [D, TB], BF16, kind="ExternalOutput").ap()

    KD = D // 128
    NS = TB // TSUB

    sch = Sched()
    with ExitStack() as ctx:
        e = ctx.enter_context
        wob = e(nc.sbuf_tensor("wob", [128, KD * D], BF16))
        kq8 = e(nc.sbuf_tensor("kq8", [128, KD * D], FP8))
        fkb = e(nc.sbuf_tensor("fkb", [128, KD * D], BF16))
        biasT = e(nc.sbuf_tensor("biasT", [128, KD], BF16))
        biasF = e(nc.sbuf_tensor("biasF", [128, KD], F32))
        wstg = e(nc.sbuf_tensor("wstg", [128, KD * D], F32))
        scl = e(nc.sbuf_tensor("scl", [1, 4], F32))
        eps1 = e(nc.sbuf_tensor("eps1", [1, 1], F32))
        sclb = e(nc.sbuf_tensor("sclb", [128, 3], F32))
        ones128 = e(nc.sbuf_tensor("ones128", [128, 1], F32))
        onesr = e(nc.sbuf_tensor("onesr", [1, 128], F32))
        xs = e(nc.sbuf_tensor("xs", [128, KD * TSUB], F32))
        x1 = e(nc.sbuf_tensor("x1", [128, KD * TSUB], F32))
        x1b = e(nc.sbuf_tensor("x1b", [128, KD * TSUB], BF16))
        xq8t = e(nc.sbuf_tensor("xq8t", [128, KD * TSUB], FP8))
        ybf = e(nc.sbuf_tensor("ybf", [128, KD * TSUB], BF16))
        sq = e(nc.sbuf_tensor("sq", [128, TSUB], F32))
        stat = e(nc.sbuf_tensor("stat", [1, 4 * TSUB], F32))
        d1 = e(nc.sbuf_tensor("d1", [128, TSUB], F32))
        rbm = e(nc.sbuf_tensor("rbm", [128, TSUB], F32))
        psA2 = [e(nc.psum_tensor(f"psA{i}", [128, TSUB], F32)) for i in range(2)]
        psSum = e(nc.psum_tensor("psSum", [1, TSUB], F32))
        psSq = e(nc.psum_tensor("psSq", [1, TSUB], F32))
        psBm = e(nc.psum_tensor("psBm", [128, TSUB], F32))
        psBr = e(nc.psum_tensor("psBr", [128, TSUB], F32))
        psS2 = e(nc.psum_tensor("psS2", [128, 3], F32))
        ld = e(nc.semaphore("ld"))
        vs = e(nc.semaphore("vs"))
        ts = e(nc.semaphore("ts"))
        as_ = e(nc.semaphore("as"))
        sto = e(nc.semaphore("sto"))

        n1 = sch.emit("sync", lambda e: e.dma_start(
            out=wstg[:].rearrange("p (k m) -> p k m", k=KD),
            in_=woT.rearrange("(k p) m -> p k m", p=128)), inc=(ld, 16))
        sch.wait("vector", ld, n1)
        n_wo = sch.emit("vector", lambda e: e.tensor_copy(out=wob[:], in_=wstg[:]),
                        inc=(vs, 1))
        n2 = sch.emit("sync", lambda e: e.dma_start(
            out=fkb[:].rearrange("p (k m) -> p k m", k=KD),
            in_=fker.rearrange("(k p) m -> p k m", p=128)), inc=(ld, 16))
        n3 = sch.emit("sync", lambda e: e.dma_start(
            out=biasT[:], in_=fbias.rearrange("(m p) -> p m", p=128),
            allow_slow_non_contiguous=True), inc=(ld, 16))
        n4 = sch.emit("sync", lambda e: e.dma_start(out=scl[:, 0:2], in_=fscale),
                      inc=(ld, 16))
        sch.wait("vector", ld, n4)
        sch.emit("vector", lambda e: e.tensor_copy(out=biasF[:], in_=biasT[:]))
        sch.emit("vector", lambda e: e.memset(ones128[:], 1.0))
        sch.emit("vector", lambda e: e.memset(onesr[:], 1.0))
        sch.emit("vector", lambda e: e.memset(eps1[:], 1e-5))
        n_rc = sch.emit("vector", lambda e: e.reciprocal(
            out=scl[:, 2:4], in_=scl[:, 0:2]), inc=(vs, 1))
        sch.wait("tensor", vs, n_rc)
        tb1 = sch.emit("tensor", lambda e: e.matmul(
            psS2[:, 0:2], onesr[:], scl[:, 2:4], start=True, stop=True), inc=(ts, 1))
        sch.wait("vector", ts, tb1)
        n_s2 = sch.emit("vector", lambda e: e.tensor_copy(
            out=sclb[:, 0:2], in_=psS2[:, 0:2]), inc=(vs, 1))
        n_sp = sch.emit("vector", lambda e: e.tensor_mul(
            out=scl[:, 0:1], in0=scl[:, 0:1], in1=scl[:, 1:2]), inc=(vs, 1))
        sch.wait("tensor", vs, n_sp)
        tb2 = sch.emit("tensor", lambda e: e.matmul(
            psS2[:, 2:3], onesr[:], scl[:, 0:1], start=True, stop=True), inc=(ts, 1))
        sch.wait("vector", ts, tb2)
        n_s3 = sch.emit("vector", lambda e: e.tensor_copy(
            out=sclb[:, 2:3], in_=psS2[:, 2:3]), inc=(vs, 1))
        n_kq = sch.emit("vector", lambda e: e.tensor_scalar(
            out=kq8[:], in0=fkb[:], scalar1=sclb[:, 1:2], scalar2=None,
            op0=AX.mult), inc=(vs, 1))
        sch.wait("tensor", vs, n_kq)

        def emit_ln(tsub, src, dst, dstb=None, gcol=None, bcol=None):
            sch.wait("scalar", vs, sch.n(vs))
            aN_l = None
            for k in range(KD):
                sch.emit("tensor", lambda e, k=k: e.matmul(
                    psSum[:], ones128[:], src[:, k * TSUB:(k + 1) * TSUB],
                    start=(k == 0), stop=(k == KD - 1)))
                if aN_l is not None:
                    sch.wait("scalar", ts, aN_l)
                aSq = sch.emit("scalar", lambda e, k=k: e.activation(
                    out=sq[:], in_=src[:, k * TSUB:(k + 1) * TSUB],
                    func=AF.Square), inc=(as_, 1))
                sch.wait("tensor", as_, aSq)
                aN_l = sch.emit("tensor", lambda e, k=k: e.matmul(
                    psSq[:], ones128[:], sq[:],
                    start=(k == 0), stop=(k == KD - 1)), inc=(ts, 1))
            sch.wait("vector", ts, aN_l)
            T_ = TSUB
            sch.emit("vector", lambda e: e.tensor_scalar_mul(
                out=stat[:, 0:T_], in0=psSum[:], scalar1=1.0 / D))
            sch.emit("vector", lambda e: e.tensor_mul(
                out=stat[:, T_:2 * T_], in0=stat[:, 0:T_], in1=stat[:, 0:T_]))
            n_var = sch.emit("vector", lambda e: e.scalar_tensor_tensor(
                out=stat[:, 2 * T_:3 * T_], in0=psSq[:], scalar=1.0 / D,
                in1=stat[:, T_:2 * T_], op0=AX.mult, op1=AX.subtract), inc=(vs, 1))
            sch.wait("scalar", vs, n_var)
            a_sd = sch.emit("scalar", lambda e: e.activation(
                out=stat[:, 3 * T_:4 * T_], in_=stat[:, 2 * T_:3 * T_],
                func=AF.Sqrt, bias=eps1[:]), inc=(as_, 1))
            sch.wait("vector", as_, a_sd)
            n_rs = sch.emit("vector", lambda e: e.reciprocal(
                out=stat[:, 2 * T_:3 * T_], in_=stat[:, 3 * T_:4 * T_]), inc=(vs, 1))
            sch.wait("tensor", vs, n_rs)
            t_bc = sch.emit("tensor", lambda e: e.matmul(
                psBm[:], onesr[:], stat[:, 0:T_], start=True, stop=True))
            t_bc = sch.emit("tensor", lambda e: e.matmul(
                psBr[:], onesr[:], stat[:, 2 * T_:3 * T_], start=True, stop=True),
                inc=(ts, 1))
            sch.wait("vector", ts, t_bc)
            n_rb = sch.emit("vector", lambda e: e.tensor_copy(
                out=rbm[:], in_=psBr[:]), inc=(vs, 1))
            nv = n_rb
            for k in range(KD):
                sch.emit("vector", lambda e, k=k: e.tensor_sub(
                    out=d1[:], in0=src[:, k * TSUB:(k + 1) * TSUB], in1=psBm[:]))
                if gcol is None:
                    nv = sch.emit("vector", lambda e, k=k: e.tensor_mul(
                        out=dst[:, k * TSUB:(k + 1) * TSUB], in0=d1[:], in1=rbm[:]),
                        inc=(vs, 1))
                else:
                    sch.emit("vector", lambda e, k=k: e.tensor_mul(
                        out=d1[:], in0=d1[:], in1=rbm[:]))
                    nv = sch.emit("vector", lambda e, k=k: e.tensor_scalar(
                        out=dst[:, k * TSUB:(k + 1) * TSUB], in0=d1[:],
                        scalar1=gcol[:, k:k + 1], scalar2=bcol[:, k:k + 1],
                        op0=AX.mult, op1=AX.add), inc=(vs, 1))
                if dstb is not None:
                    nv = sch.emit("vector", lambda e, k=k: e.tensor_copy(
                        out=dstb[:, k * TSUB:(k + 1) * TSUB],
                        in_=dst[:, k * TSUB:(k + 1) * TSUB]), inc=(vs, 1))
            return nv

        for su in range(NS):
            t0 = su * TSUB
            nl = sch.emit("sync", lambda e, t0=t0: e.dma_start(
                out=xs[:].rearrange("p (k t) -> p k t", k=KD),
                in_=xTs[:, t0:t0 + TSUB].rearrange("(k p) t -> p k t", p=128)),
                inc=(ld, 16))
            nl = sch.emit("sync", lambda e, t0=t0: e.dma_start(
                out=x1[:].rearrange("p (k t) -> p k t", k=KD),
                in_=oTs[:, t0:t0 + TSUB].rearrange("(k p) t -> p k t", p=128)),
                inc=(ld, 16))
            sch.wait("vector", ld, nl)
            n_ob = sch.emit("vector", lambda e: e.tensor_copy(
                out=ybf[:], in_=x1[:]), inc=(vs, 1))
            sch.wait("tensor", vs, n_ob)
            ev_res = [0, 0]
            for m in range(KD):
                pb = m % 2
                sch.wait("tensor", vs, ev_res[pb])
                for k in range(KD):
                    tA = sch.emit("tensor", lambda e, m=m, k=k, pb=pb: e.matmul(
                        psA2[pb][:], wob[:, k * D + m * 128:k * D + m * 128 + 128],
                        ybf[:, k * TSUB:(k + 1) * TSUB],
                        start=(k == 0), stop=(k == KD - 1)),
                        inc=(ts, 1) if k == KD - 1 else None)
                sch.wait("vector", ts, tA)
                ev_res[pb] = sch.emit("vector", lambda e, m=m, pb=pb: e.tensor_add(
                    out=xs[:, m * TSUB:(m + 1) * TSUB],
                    in0=xs[:, m * TSUB:(m + 1) * TSUB], in1=psA2[pb][:]),
                    inc=(vs, 1))
            sch.wait("tensor", vs, max(ev_res))
            emit_ln(su, xs, x1, x1b)
            n_q = sch.emit("vector", lambda e: e.tensor_scalar(
                out=xq8t[:], in0=x1b[:], scalar1=sclb[:, 0:1], scalar2=None,
                op0=AX.mult), inc=(vs, 1))
            sch.wait("tensor", vs, n_q)
            ev_y = [0, 0]
            for m in range(KD):
                pb = m % 2
                sch.wait("tensor", as_, ev_y[pb])
                for k in range(KD):
                    tF = sch.emit("tensor", lambda e, m=m, k=k, pb=pb: e.matmul(
                        psA2[pb][:], kq8[:, k * D + m * 128:k * D + m * 128 + 128],
                        xq8t[:, k * TSUB:(k + 1) * TSUB],
                        start=(k == 0), stop=(k == KD - 1)),
                        inc=(ts, 1) if k == KD - 1 else None)
                sch.wait("scalar", ts, tF)
                aY = sch.emit("scalar", lambda e, m=m, pb=pb: e.activation(
                    out=ybf[:, m * TSUB:(m + 1) * TSUB], in_=psA2[pb][:],
                    func=AF.Identity, scale=sclb[:, 2:3]), inc=(as_, 1))
                ev_y[pb] = aY
                sch.wait("vector", as_, aY)
                sch.emit("vector", lambda e, m=m: e.tensor_scalar(
                    out=ybf[:, m * TSUB:(m + 1) * TSUB],
                    in0=ybf[:, m * TSUB:(m + 1) * TSUB],
                    scalar1=biasF[:, m:m + 1], scalar2=None, op0=AX.add))
                n_r2 = sch.emit("vector", lambda e, m=m: e.tensor_add(
                    out=x1[:, m * TSUB:(m + 1) * TSUB],
                    in0=x1[:, m * TSUB:(m + 1) * TSUB],
                    in1=ybf[:, m * TSUB:(m + 1) * TSUB]), inc=(vs, 1))
            emit_ln(su, x1, xs, x1b)
            n_last = sch.n(vs)
            sch.wait("sync", vs, n_last)
            n_st = sch.emit("sync", lambda e, t0=t0: e.dma_start(
                out=x2T[:, t0:t0 + TSUB].rearrange("(k p) t -> p k t", p=128),
                in_=xs[:].rearrange("p (k t) -> p k t", k=KD)), inc=(sto, 16))
            n_st = sch.emit("sync", lambda e, t0=t0: e.dma_start(
                out=x2Tb[:, t0:t0 + TSUB].rearrange("(k p) t -> p k t", p=128),
                in_=x1b[:].rearrange("p (k t) -> p k t", k=KD)), inc=(sto, 16))
            sch.wait("vector", sto, n_st)
        sch.wait("sync", sto, sch.n(sto))
        sch.check_deadlock()
        sch.replay(nc)
    return nc


def run_l0_rest(x, oT, Wo, fker, s_in, s_k, fbias):
    nc = cached_build("l0_rest", build_l0_rest)
    woT = np.ascontiguousarray(Wo.T)
    fscale = np.array([[s_in, s_k]], np.float32)
    in_maps = []
    for c in range(NCORE):
        sl = slice(c * 1024, (c + 1) * 1024)
        xTs = np.concatenate([x[0].T[:, sl], x[1].T[:, sl]], axis=1)
        oTs = np.concatenate([oT[0][:, sl], oT[1][:, sl]], axis=1)
        in_maps.append({
            "xTs": np.ascontiguousarray(xTs),
            "oTs": np.ascontiguousarray(oTs),
            "woT": woT, "fker": fker, "fscale": fscale, "fbias": fbias,
        })
    res = run_bass_kernel_spmd(nc, in_maps, core_ids=list(range(NCORE)))
    x2T = np.zeros((BT, D, S), np.float32)
    x2Tb = np.zeros((BT, D, S), bf)
    for c in range(NCORE):
        sl = slice(c * 1024, (c + 1) * 1024)
        x2T[0][:, sl] = res.results[c]["x2T"][:, :1024]
        x2T[1][:, sl] = res.results[c]["x2T"][:, 1024:]
        x2Tb[0][:, sl] = res.results[c]["x2Tb"][:, :1024]
        x2Tb[1][:, sl] = res.results[c]["x2Tb"][:, 1024:]
    return x2T, x2Tb


SK = 64
SL = 128
SB_ = 32
SLAB = SK * 3


def build_rkv():
    nc = bass.Bass("TRN2", target_bir_lowering=False)
    x2b = nc.dram_tensor("x2b", [D, NT], BF16, kind="ExternalInput").ap()
    w3T = nc.dram_tensor("w3T", [D, 384], F32, kind="ExternalInput").ap()
    yTb = nc.dram_tensor("yTb", [BT, 128, S], BF16, kind="ExternalOutput").ap()

    KD = D // 128
    STEPS = SL + SB_
    TC = 512
    NTC = S // TC

    sch = Sched()
    with ExitStack() as ctx:
        e = ctx.enter_context
        w3b = e(nc.sbuf_tensor("w3b", [128, KD * 384], BF16))
        ld = e(nc.semaphore("ld"))
        vs = e(nc.semaphore("vs"))
        ts = e(nc.semaphore("ts"))
        as_ = e(nc.semaphore("as"))
        sto = e(nc.semaphore("sto"))
        with nc.sbuf_tensor("wstg", [128, KD * 384], F32) as wstg:
            n1 = sch.emit("sync", lambda e: e.dma_start(
                out=wstg[:].rearrange("p (k m) -> p k m", k=KD),
                in_=w3T.rearrange("(k p) m -> p k m", p=128)), inc=(ld, 16))
            sch.wait("vector", ld, n1)
            n_w = sch.emit("vector", lambda e: e.tensor_copy(out=w3b[:], in_=wstg[:]),
                           inc=(vs, 1))
        xs = e(nc.sbuf_tensor("xs", [128, STEPS * SLAB], F32))
        st = e(nc.sbuf_tensor("st", [128, SLAB], F32))
        xcs = [e(nc.sbuf_tensor(f"xcs{i}", [128, KD * TC], BF16)) for i in range(2)]
        spr = e(nc.sbuf_tensor("spr", [128, S], BF16))
        mt = e(nc.sbuf_tensor("mt", [128, S], BF16))
        hh = e(nc.sbuf_tensor("hh", [128, S], BF16))
        c09 = e(nc.sbuf_tensor("c09", [128, 2048], F32))
        hcar = e(nc.sbuf_tensor("hcar", [128, 1], F32))
        psR = [e(nc.psum_tensor(f"psR{i}", [128, TC], F32)) for i in range(3)]

        sch.emit("vector", lambda e: e.memset(c09[:], 0.9))
        n_w = sch.emit("vector", lambda e: e.memset(hcar[:], 0.0), inc=(vs, 1))
        sch.wait("tensor", vs, n_w)
        sch.wait("sync", vs, n_w)

        n_store = 0
        for b in range(BT):
            nz = sch.emit("vector", lambda e: e.memset(
                xs[:].rearrange("p (sl k c) -> p sl k c", sl=STEPS, k=SK)
                [:, 0:SB_, 0:1, :], 0.0), inc=(vs, 1))
            for tc in range(NTC):
                buf = tc % 2
                nl = sch.emit("sync", lambda e, b=b, tc=tc, buf=buf: e.dma_start(
                    out=xcs[buf][:].rearrange("p (k t) -> p k t", k=KD),
                    in_=x2b[:, b * S + tc * TC:b * S + (tc + 1) * TC].rearrange(
                        "(k p) t -> p k t", p=128)), inc=(ld, 16))
                sch.wait("tensor", ld, nl)
                for c in range(3):
                    for k in range(KD):
                        tR = sch.emit("tensor", lambda e, c=c, k=k, buf=buf: e.matmul(
                            psR[c][:],
                            w3b[:, k * 384 + c * 128:k * 384 + c * 128 + 128],
                            xcs[buf][:, k * TC:(k + 1) * TC],
                            start=(k == 0), stop=(k == KD - 1)),
                            inc=(ts, 1) if k == KD - 1 else None)
                    sch.wait("scalar", ts, tR)
                    k0 = tc * 4
                    aE = sch.emit("scalar", lambda e, c=c, k0=k0: e.activation(
                        out=xs[:].rearrange(
                            "p (sl k c) -> p sl k c", sl=STEPS, k=SK)
                        [:, SB_:SB_ + SL, k0:k0 + 4, c:c + 1]
                        .rearrange("p sl k c -> p (k sl c)" if False else
                                   "p sl k c -> p k sl c"),
                        in_=psR[c][:].rearrange("p (ks r) -> p ks r", ks=4),
                        func=AF.Identity, scale=0.5), inc=(as_, 1))
                sch.wait("tensor", as_, sch.n(as_))
                if tc + 1 < NTC:
                    sch.wait("sync", as_, sch.n(as_) - 3)
            sch.wait("vector", as_, sch.n(as_))
            xr = xs[:].rearrange("p (sl k c) -> p sl k c", sl=STEPS, k=SK)
            n_wu = sch.emit("vector", lambda e, xr=xr: e.tensor_copy(
                out=xr[:, 0:SB_, 1:SK, :], in_=xr[:, SL:SL + SB_, 0:SK - 1, :]),
                inc=(vs, 1))
            n_st0 = sch.emit("vector", lambda e: e.memset(st[:], 0.0), inc=(vs, 1))
            for sl in range(STEPS):
                sch.emit("vector", lambda e, sl=sl: e.scalar_tensor_tensor(
                    out=xs[:, sl * SLAB:(sl + 1) * SLAB], in0=st[:], scalar=0.5,
                    in1=xs[:, sl * SLAB:(sl + 1) * SLAB],
                    op0=AX.mult, op1=AX.add))
                sch.emit("vector", lambda e, sl=sl: e.scalar_tensor_tensor(
                    out=st[:], in0=xs[:, sl * SLAB:(sl + 1) * SLAB], scalar=1.0,
                    in1=xs[:, sl * SLAB:(sl + 1) * SLAB],
                    op0=AX.is_lt, op1=AX.mult))
            def reg(c):
                return xs[:].rearrange(
                    "p (sl k c) -> p sl k c", sl=STEPS, k=SK)[
                    :, SB_:STEPS, :, c:c + 1].rearrange("p sl k c -> p k sl c")
            sch.emit("vector", lambda e: e.tensor_single_scalar(
                out=spr[:], in_=reg(0), scalar=1.0, op=AX.is_ge))
            sch.emit("vector", lambda e: e.tensor_tensor(
                out=mt[:], in0=reg(1), in1=reg(2), op=AX.min))
            sch.emit("vector", lambda e: e.tensor_single_scalar(
                out=mt[:], in_=mt[:], scalar=1.0, op=AX.is_ge))
            for ch in range(4):
                s0 = ch * 2048
                if ch == 0:
                    n_sc = sch.emit("vector", lambda e, s0=s0: e.tensor_tensor_scan(
                        out=hh[:, s0:s0 + 2048], data0=c09[:],
                        data1=mt[:, s0:s0 + 2048], initial=0.0,
                        op0=AX.mult, op1=AX.add), inc=(vs, 1))
                else:
                    n_sc = sch.emit("vector", lambda e, s0=s0: e.tensor_tensor_scan(
                        out=hh[:, s0:s0 + 2048], data0=c09[:],
                        data1=mt[:, s0:s0 + 2048], initial=hcar[:],
                        op0=AX.mult, op1=AX.add), inc=(vs, 1))
                sch.wait("vector", vs, n_sc)
                n_hc = sch.emit("vector", lambda e, s0=s0: e.tensor_copy(
                    out=hcar[:], in_=hh[:, s0 + 2047:s0 + 2048]), inc=(vs, 1))
                sch.wait("vector", vs, n_hc)
            sch.wait("vector", sto, n_store)
            n_y = sch.emit("vector", lambda e: e.tensor_mul(
                out=mt[:], in0=spr[:], in1=hh[:]), inc=(vs, 1))
            sch.wait("sync", vs, n_y)
            n_store = sch.emit("sync", lambda e, b=b: e.dma_start(
                out=yTb[b], in_=mt[:]), inc=(sto, 16))
        sch.wait("sync", sto, sch.n(sto))
        sch.check_deadlock()
        sch.replay(nc)
    return nc


def run_rkv(x2Tb, Wr, Wk, Wv):
    nc = cached_build("rkv", build_rkv)
    x2full = np.ascontiguousarray(
        np.concatenate([x2Tb[0], x2Tb[1]], axis=1))
    in_maps = []
    for c in range(NCORE):
        ds = slice(c * 128, (c + 1) * 128)
        w3 = np.concatenate(
            [Wr.T[:, ds], Wk.T[:, ds], Wv.T[:, ds]], axis=1)
        in_maps.append({"x2b": x2full, "w3T": np.ascontiguousarray(w3)})
    res = run_bass_kernel_spmd(nc, in_maps, core_ids=list(range(NCORE)))
    yT = np.zeros((BT, D, S), bf)
    for c in range(NCORE):
        ds = slice(c * 128, (c + 1) * 128)
        yT[:, ds] = res.results[c]["yTb"]
    return yT


def build_postrkv():
    nc = bass.Bass("TRN2", target_bir_lowering=False)
    x2Ts = nc.dram_tensor("x2Ts", [D, TB], F32, kind="ExternalInput").ap()
    yTs = nc.dram_tensor("yTs", [D, TB], BF16, kind="ExternalInput").ap()
    woT = nc.dram_tensor("woT", [D, D], F32, kind="ExternalInput").ap()
    ln1g = nc.dram_tensor("ln1g", [D], F32, kind="ExternalInput").ap()
    ln1b = nc.dram_tensor("ln1b", [D], F32, kind="ExternalInput").ap()
    w1T = nc.dram_tensor("w1T", [D, 4096], F32, kind="ExternalInput").ap()
    b1v = nc.dram_tensor("b1v", [4096], F32, kind="ExternalInput").ap()
    x1T = nc.dram_tensor("x1T", [D, TB], F32, kind="ExternalOutput").ap()
    x1Tb = nc.dram_tensor("x1Tb", [D, TB], BF16, kind="ExternalOutput").ap()
    h1T = nc.dram_tensor("h1T", [4096, TB], F32, kind="ExternalOutput").ap()

    KD = D // 128
    NS = TB // TSUB

    sch = Sched()
    with ExitStack() as ctx:
        e = ctx.enter_context
        ld = e(nc.semaphore("ld"))
        vs = e(nc.semaphore("vs"))
        ts = e(nc.semaphore("ts"))
        as_ = e(nc.semaphore("as"))
        sto = e(nc.semaphore("sto"))
        wob = e(nc.sbuf_tensor("wob", [128, KD * D], BF16))
        w1b = e(nc.sbuf_tensor("w1b", [128, KD * 4096], BF16))
        gcol = e(nc.sbuf_tensor("gcol", [128, KD], F32))
        bcol = e(nc.sbuf_tensor("bcol", [128, KD], F32))
        b1c = e(nc.sbuf_tensor("b1c", [128, 32], F32))
        eps1 = e(nc.sbuf_tensor("eps1", [1, 1], F32))
        ones128 = e(nc.sbuf_tensor("ones128", [128, 1], F32))
        onesr = e(nc.sbuf_tensor("onesr", [1, 128], F32))
        with nc.sbuf_tensor("wstg", [128, KD * D], F32) as wstg:
            n1 = sch.emit("sync", lambda e: e.dma_start(
                out=wstg[:].rearrange("p (k m) -> p k m", k=KD),
                in_=woT.rearrange("(k p) m -> p k m", p=128)), inc=(ld, 16))
            sch.wait("vector", ld, n1)
            n_c = sch.emit("vector", lambda e: e.tensor_copy(
                out=wob[:], in_=wstg[:]), inc=(vs, 1))
            sch.wait("sync", vs, n_c)
            for q in range(4):
                n1 = sch.emit("sync", lambda e, q=q: e.dma_start(
                    out=wstg[:].rearrange("p (k m) -> p k m", k=KD),
                    in_=w1T[:, q * 1024:(q + 1) * 1024].rearrange(
                        "(k p) m -> p k m", p=128)), inc=(ld, 16))
                sch.wait("vector", ld, n1)
                def cst(e, q=q):
                    dst = w1b[:].rearrange("p (k m) -> p k m", k=KD)[
                        :, :, q * 1024:(q + 1) * 1024]
                    return e.tensor_copy(
                        out=dst, in_=wstg[:].rearrange("p (k m) -> p k m", k=KD))
                n_c = sch.emit("vector", cst, inc=(vs, 1))
                sch.wait("sync", vs, n_c)
        xs = e(nc.sbuf_tensor("xs", [128, KD * TSUB], F32))
        x1t = e(nc.sbuf_tensor("x1t", [128, KD * TSUB], F32))
        x1bt = e(nc.sbuf_tensor("x1bt", [128, KD * TSUB], BF16))
        ybt = e(nc.sbuf_tensor("ybt", [128, KD * TSUB], BF16))
        h1s = e(nc.sbuf_tensor("h1s", [128, 8 * TSUB], F32))
        sq = e(nc.sbuf_tensor("sq", [128, TSUB], F32))
        stat = e(nc.sbuf_tensor("stat", [1, 4 * TSUB], F32))
        d1 = e(nc.sbuf_tensor("d1", [128, TSUB], F32))
        rbm = e(nc.sbuf_tensor("rbm", [128, TSUB], F32))
        psA2 = [e(nc.psum_tensor(f"psA{i}", [128, TSUB], F32)) for i in range(2)]
        psSum = e(nc.psum_tensor("psSum", [1, TSUB], F32))
        psSq = e(nc.psum_tensor("psSq", [1, TSUB], F32))
        psBm = e(nc.psum_tensor("psBm", [128, TSUB], F32))
        psBr = e(nc.psum_tensor("psBr", [128, TSUB], F32))

        nl = sch.emit("sync", lambda e: e.dma_start(
            out=gcol[:], in_=ln1g.rearrange("(m p) -> p m", p=128),
            allow_slow_non_contiguous=True), inc=(ld, 16))
        nl = sch.emit("sync", lambda e: e.dma_start(
            out=bcol[:], in_=ln1b.rearrange("(m p) -> p m", p=128),
            allow_slow_non_contiguous=True), inc=(ld, 16))
        nl = sch.emit("sync", lambda e: e.dma_start(
            out=b1c[:], in_=b1v.rearrange("(m p) -> p m", p=128),
            allow_slow_non_contiguous=True), inc=(ld, 16))
        sch.wait("vector", ld, nl)
        sch.emit("vector", lambda e: e.memset(eps1[:], 1e-5))
        sch.emit("vector", lambda e: e.memset(ones128[:], 1.0))
        n_i = sch.emit("vector", lambda e: e.memset(onesr[:], 1.0), inc=(vs, 1))
        n_b = sch.emit("vector", lambda e: e.tensor_scalar_mul(
            out=b1c[:], in0=b1c[:], scalar1=0.5), inc=(vs, 1))
        sch.wait("tensor", vs, n_b)
        sch.wait("scalar", vs, n_b)

        def emit_ln(src, dst, dstb=None, g=None, b=None):
            sch.wait("scalar", vs, sch.n(vs))
            aN_l = None
            for k in range(KD):
                sch.emit("tensor", lambda e, k=k: e.matmul(
                    psSum[:], ones128[:], src[:, k * TSUB:(k + 1) * TSUB],
                    start=(k == 0), stop=(k == KD - 1)))
                if aN_l is not None:
                    sch.wait("scalar", ts, aN_l)
                aSq = sch.emit("scalar", lambda e, k=k: e.activation(
                    out=sq[:], in_=src[:, k * TSUB:(k + 1) * TSUB],
                    func=AF.Square), inc=(as_, 1))
                sch.wait("tensor", as_, aSq)
                aN_l = sch.emit("tensor", lambda e, k=k: e.matmul(
                    psSq[:], ones128[:], sq[:],
                    start=(k == 0), stop=(k == KD - 1)), inc=(ts, 1))
            sch.wait("vector", ts, aN_l)
            T_ = TSUB
            sch.emit("vector", lambda e: e.tensor_scalar_mul(
                out=stat[:, 0:T_], in0=psSum[:], scalar1=1.0 / D))
            sch.emit("vector", lambda e: e.tensor_mul(
                out=stat[:, T_:2 * T_], in0=stat[:, 0:T_], in1=stat[:, 0:T_]))
            n_var = sch.emit("vector", lambda e: e.scalar_tensor_tensor(
                out=stat[:, 2 * T_:3 * T_], in0=psSq[:], scalar=1.0 / D,
                in1=stat[:, T_:2 * T_], op0=AX.mult, op1=AX.subtract), inc=(vs, 1))
            sch.wait("scalar", vs, n_var)
            a_sd = sch.emit("scalar", lambda e: e.activation(
                out=stat[:, 3 * T_:4 * T_], in_=stat[:, 2 * T_:3 * T_],
                func=AF.Sqrt, bias=eps1[:]), inc=(as_, 1))
            sch.wait("vector", as_, a_sd)
            n_rs = sch.emit("vector", lambda e: e.reciprocal(
                out=stat[:, 2 * T_:3 * T_], in_=stat[:, 3 * T_:4 * T_]), inc=(vs, 1))
            sch.wait("tensor", vs, n_rs)
            sch.emit("tensor", lambda e: e.matmul(
                psBm[:], onesr[:], stat[:, 0:T_], start=True, stop=True))
            t_bc = sch.emit("tensor", lambda e: e.matmul(
                psBr[:], onesr[:], stat[:, 2 * T_:3 * T_], start=True, stop=True),
                inc=(ts, 1))
            sch.wait("vector", ts, t_bc)
            n_rb = sch.emit("vector", lambda e: e.tensor_copy(
                out=rbm[:], in_=psBr[:]), inc=(vs, 1))
            nv = n_rb
            for k in range(KD):
                sch.emit("vector", lambda e, k=k: e.tensor_sub(
                    out=d1[:], in0=src[:, k * TSUB:(k + 1) * TSUB], in1=psBm[:]))
                if g is None:
                    nv = sch.emit("vector", lambda e, k=k: e.tensor_mul(
                        out=dst[:, k * TSUB:(k + 1) * TSUB], in0=d1[:], in1=rbm[:]),
                        inc=(vs, 1))
                else:
                    sch.emit("vector", lambda e, k=k: e.tensor_mul(
                        out=d1[:], in0=d1[:], in1=rbm[:]))
                    nv = sch.emit("vector", lambda e, k=k: e.tensor_scalar(
                        out=dst[:, k * TSUB:(k + 1) * TSUB], in0=d1[:],
                        scalar1=g[:, k:k + 1], scalar2=b[:, k:k + 1],
                        op0=AX.mult, op1=AX.add), inc=(vs, 1))
                if dstb is not None:
                    nv = sch.emit("vector", lambda e, k=k: e.tensor_copy(
                        out=dstb[:, k * TSUB:(k + 1) * TSUB],
                        in_=dst[:, k * TSUB:(k + 1) * TSUB]), inc=(vs, 1))
            return nv

        n_sto_h = 0
        for su in range(NS):
            t0 = su * TSUB
            nl = sch.emit("sync", lambda e, t0=t0: e.dma_start(
                out=xs[:].rearrange("p (k t) -> p k t", k=KD),
                in_=x2Ts[:, t0:t0 + TSUB].rearrange("(k p) t -> p k t", p=128)),
                inc=(ld, 16))
            nl = sch.emit("sync", lambda e, t0=t0: e.dma_start(
                out=ybt[:].rearrange("p (k t) -> p k t", k=KD),
                in_=yTs[:, t0:t0 + TSUB].rearrange("(k p) t -> p k t", p=128)),
                inc=(ld, 16))
            sch.wait("tensor", ld, nl)
            sch.wait("vector", ld, nl)
            ev_res = [0, 0]
            for m in range(KD):
                pb = m % 2
                sch.wait("tensor", vs, ev_res[pb])
                for k in range(KD):
                    tA = sch.emit("tensor", lambda e, m=m, k=k, pb=pb: e.matmul(
                        psA2[pb][:], wob[:, k * D + m * 128:k * D + m * 128 + 128],
                        ybt[:, k * TSUB:(k + 1) * TSUB],
                        start=(k == 0), stop=(k == KD - 1)),
                        inc=(ts, 1) if k == KD - 1 else None)
                sch.wait("vector", ts, tA)
                ev_res[pb] = sch.emit("vector", lambda e, m=m, pb=pb: e.tensor_add(
                    out=xs[:, m * TSUB:(m + 1) * TSUB],
                    in0=xs[:, m * TSUB:(m + 1) * TSUB], in1=psA2[pb][:]),
                    inc=(vs, 1))
            sch.wait("tensor", vs, max(ev_res))
            emit_ln(xs, x1t, x1bt, gcol, bcol)
            n_lnv = sch.n(vs)
            sch.wait("sync", vs, n_lnv)
            n_st = sch.emit("sync", lambda e, t0=t0: e.dma_start(
                out=x1T[:, t0:t0 + TSUB].rearrange("(k p) t -> p k t", p=128),
                in_=x1t[:].rearrange("p (k t) -> p k t", k=KD)), inc=(sto, 16))
            n_st = sch.emit("sync", lambda e, t0=t0: e.dma_start(
                out=x1Tb[:, t0:t0 + TSUB].rearrange("(k p) t -> p k t", p=128),
                in_=x1bt[:].rearrange("p (k t) -> p k t", k=KD)), inc=(sto, 16))
            sch.wait("tensor", vs, n_lnv)
            ev_h = [0, 0]
            for mg in range(4):
                sch.wait("tensor", sto, n_sto_h)
                for mm in range(8):
                    m = mg * 8 + mm
                    pb = m % 2
                    sch.wait("tensor", as_, ev_h[pb])
                    for k in range(KD):
                        tH = sch.emit("tensor", lambda e, m=m, k=k, pb=pb: e.matmul(
                            psA2[pb][:],
                            w1b[:, k * 4096 + m * 128:k * 4096 + m * 128 + 128],
                            x1bt[:, k * TSUB:(k + 1) * TSUB],
                            start=(k == 0), stop=(k == KD - 1)),
                            inc=(ts, 1) if k == KD - 1 else None)
                    sch.wait("scalar", ts, tH)
                    ev_h[pb] = sch.emit("scalar", lambda e, m=m, mm=mm, pb=pb:
                                        e.activation(
                        out=h1s[:, mm * TSUB:(mm + 1) * TSUB], in_=psA2[pb][:],
                        func=AF.Identity, scale=0.5, bias=b1c[:, m:m + 1]),
                        inc=(as_, 1))
                sch.wait("sync", as_, sch.n(as_))
                n_sto_h = sch.emit("sync", lambda e, t0=t0, mg=mg: e.dma_start(
                    out=h1T[mg * 1024:(mg + 1) * 1024, t0:t0 + TSUB].rearrange(
                        "(k p) t -> p k t", p=128),
                    in_=h1s[:].rearrange("p (k t) -> p k t", k=KD)), inc=(sto, 16))
            sch.wait("vector", sto, n_st)
        sch.wait("sync", sto, sch.n(sto))
        sch.check_deadlock()
        sch.replay(nc)
    return nc


def build_lif(skw, cw, nsec):
    steps = SL + SB_
    slabw = skw * cw
    nc = bass.Bass("TRN2", target_bir_lowering=False)
    hs = nc.dram_tensor("hs", [nsec, 128, steps * slabw], F32,
                        kind="ExternalInput").ap()
    sp = nc.dram_tensor("sp", [nsec, 128, steps * slabw], F32,
                        kind="ExternalOutput").ap()
    sch = Sched()
    with ExitStack() as ctx:
        e = ctx.enter_context
        xs = e(nc.sbuf_tensor("xs", [128, steps * slabw], F32))
        st = e(nc.sbuf_tensor("st", [128, slabw], F32))
        ld = e(nc.semaphore("ld"))
        vs = e(nc.semaphore("vs"))
        sto = e(nc.semaphore("sto"))
        n_stod = 0
        for sec in range(nsec):
            sch.wait("sync", sto, n_stod)
            nl = sch.emit("sync", lambda e, sec=sec: e.dma_start(
                out=xs[:], in_=hs[sec]), inc=(ld, 16))
            sch.wait("vector", ld, nl)
            n0 = sch.emit("vector", lambda e: e.memset(st[:], 0.0), inc=(vs, 1))
            for sl in range(steps):
                sch.emit("vector", lambda e, sl=sl: e.scalar_tensor_tensor(
                    out=xs[:, sl * slabw:(sl + 1) * slabw], in0=st[:], scalar=0.5,
                    in1=xs[:, sl * slabw:(sl + 1) * slabw], op0=AX.mult, op1=AX.add))
                sch.emit("vector", lambda e, sl=sl: e.scalar_tensor_tensor(
                    out=st[:], in0=xs[:, sl * slabw:(sl + 1) * slabw], scalar=1.0,
                    in1=xs[:, sl * slabw:(sl + 1) * slabw],
                    op0=AX.is_lt, op1=AX.mult))
            n_sp = sch.emit("vector", lambda e: e.tensor_single_scalar(
                out=xs[:], in_=xs[:], scalar=1.0, op=AX.is_ge), inc=(vs, 1))
            sch.wait("sync", vs, n_sp)
            n_stod = sch.emit("sync", lambda e, sec=sec: e.dma_start(
                out=sp[sec], in_=xs[:]), inc=(sto, 16))
            sch.wait("vector", sto, n_stod)
        sch.wait("sync", sto, sch.n(sto))
        sch.check_deadlock()
        sch.replay(nc)
    return nc


def scan_pack(h_cw, skw, nsec):
    cw, _, T = h_cw.shape
    steps = SL + SB_
    L = T // (nsec * skw)
    assert L == SL
    out = np.zeros((nsec, 128, steps, skw, cw), np.float32)
    hp = h_cw.transpose(1, 0, 2)
    for sec in range(nsec):
        for k in range(skw):
            g0 = sec * skw * SL + k * SL - SB_
            n0 = max(0, -g0)
            seg = hp[:, :, g0 + n0:g0 + steps]
            out[sec, :, n0:n0 + seg.shape[2], k, :] = seg.transpose(0, 2, 1)
    return np.ascontiguousarray(out.reshape(nsec, 128, steps * skw * cw))


def scan_unpack(sp, skw, nsec, cw):
    steps = SL + SB_
    v = sp.reshape(nsec, 128, steps, skw, cw)[:, :, SB_:, :, :]
    v = v.transpose(4, 1, 0, 3, 2)
    return np.ascontiguousarray(v.reshape(cw, 128, nsec * skw * SL))


def build_w2():
    nc = bass.Bass("TRN2", target_bir_lowering=False)
    s1Ts = nc.dram_tensor("s1Ts", [4096, TB], BF16, kind="ExternalInput").ap()
    w2T = nc.dram_tensor("w2T", [4096, D], F32, kind="ExternalInput").ap()
    b2v = nc.dram_tensor("b2v", [D], F32, kind="ExternalInput").ap()
    h2T = nc.dram_tensor("h2T", [D, TB], F32, kind="ExternalOutput").ap()
    KF = 4096 // 128
    NS = TB // TSUB
    sch = Sched()
    with ExitStack() as ctx:
        e = ctx.enter_context
        ld = e(nc.semaphore("ld"))
        vs = e(nc.semaphore("vs"))
        ts = e(nc.semaphore("ts"))
        as_ = e(nc.semaphore("as"))
        sto = e(nc.semaphore("sto"))
        w2b = e(nc.sbuf_tensor("w2b", [128, KF * D], BF16))
        b2c = e(nc.sbuf_tensor("b2c", [128, 8], F32))
        with nc.sbuf_tensor("wstg", [128, KF * 256], F32) as wstg:
            for q in range(4):
                n1 = sch.emit("sync", lambda e, q=q: e.dma_start(
                    out=wstg[:].rearrange("p (k m) -> p k m", k=KF),
                    in_=w2T[:, q * 256:(q + 1) * 256].rearrange(
                        "(k p) m -> p k m", p=128)), inc=(ld, 16))
                sch.wait("vector", ld, n1)
                def cst(e, q=q):
                    dst = w2b[:].rearrange("p (k m) -> p k m", k=KF)[
                        :, :, q * 256:(q + 1) * 256]
                    return e.tensor_copy(
                        out=dst, in_=wstg[:].rearrange("p (k m) -> p k m", k=KF))
                n_c = sch.emit("vector", cst, inc=(vs, 1))
                sch.wait("sync", vs, n_c)
        s1t = e(nc.sbuf_tensor("s1t", [128, KF * TSUB], BF16))
        h2s = e(nc.sbuf_tensor("h2s", [128, 8 * TSUB], F32))
        psA2 = [e(nc.psum_tensor(f"psA{i}", [128, TSUB], F32)) for i in range(2)]
        nl = sch.emit("sync", lambda e: e.dma_start(
            out=b2c[:], in_=b2v.rearrange("(m p) -> p m", p=128),
            allow_slow_non_contiguous=True), inc=(ld, 16))
        sch.wait("vector", ld, nl)
        n_b = sch.emit("vector", lambda e: e.tensor_scalar_mul(
            out=b2c[:], in0=b2c[:], scalar1=0.5), inc=(vs, 1))
        sch.wait("scalar", vs, n_b)
        n_sto = 0
        for su in range(NS):
            t0 = su * TSUB
            nl = sch.emit("sync", lambda e, t0=t0: e.dma_start(
                out=s1t[:].rearrange("p (k t) -> p k t", k=KF),
                in_=s1Ts[:, t0:t0 + TSUB].rearrange("(k p) t -> p k t", p=128)),
                inc=(ld, 16))
            sch.wait("tensor", ld, nl)
            ev_h = [0, 0]
            for m in range(8):
                pb = m % 2
                sch.wait("tensor", as_, ev_h[pb])
                for k in range(KF):
                    tH = sch.emit("tensor", lambda e, m=m, k=k, pb=pb: e.matmul(
                        psA2[pb][:], w2b[:, k * D + m * 128:k * D + m * 128 + 128],
                        s1t[:, k * TSUB:(k + 1) * TSUB],
                        start=(k == 0), stop=(k == KF - 1)),
                        inc=(ts, 1) if k == KF - 1 else None)
                sch.wait("scalar", ts, tH)
                ev_h[pb] = sch.emit("scalar", lambda e, m=m, pb=pb: e.activation(
                    out=h2s[:, m * TSUB:(m + 1) * TSUB], in_=psA2[pb][:],
                    func=AF.Identity, scale=0.5, bias=b2c[:, m:m + 1]),
                    inc=(as_, 1))
            sch.wait("sync", as_, sch.n(as_))
            sch.wait("sync", sto, n_sto)
            n_sto = sch.emit("sync", lambda e, t0=t0: e.dma_start(
                out=h2T[:, t0:t0 + TSUB].rearrange("(k p) t -> p k t", p=128),
                in_=h2s[:].rearrange("p (k t) -> p k t", k=8)), inc=(sto, 16))
        sch.wait("sync", sto, sch.n(sto))
        sch.check_deadlock()
        sch.replay(nc)
    return nc


def run_postrkv(x2T, yT, Wo, ln1g, ln1b, W1, b1):
    nc = cached_build("postrkv", build_postrkv)
    woT = np.ascontiguousarray(Wo.T)
    w1Tc = np.ascontiguousarray(W1.T)
    in_maps = []
    for c in range(NCORE):
        sl = slice(c * 1024, (c + 1) * 1024)
        in_maps.append({
            "x2Ts": np.ascontiguousarray(
                np.concatenate([x2T[0][:, sl], x2T[1][:, sl]], axis=1)),
            "yTs": np.ascontiguousarray(
                np.concatenate([yT[0][:, sl], yT[1][:, sl]], axis=1)),
            "woT": woT, "ln1g": ln1g, "ln1b": ln1b, "w1T": w1Tc, "b1v": b1,
        })
    res = run_bass_kernel_spmd(nc, in_maps, core_ids=list(range(NCORE)))
    x1T = np.zeros((BT, D, S), np.float32)
    x1Tb = np.zeros((BT, D, S), bf)
    h1T = np.zeros((BT, 4096, S), np.float32)
    for c in range(NCORE):
        sl = slice(c * 1024, (c + 1) * 1024)
        r = res.results[c]
        x1T[0][:, sl] = r["x1T"][:, :1024]; x1T[1][:, sl] = r["x1T"][:, 1024:]
        x1Tb[0][:, sl] = r["x1Tb"][:, :1024]; x1Tb[1][:, sl] = r["x1Tb"][:, 1024:]
        h1T[0][:, sl] = r["h1T"][:, :1024]; h1T[1][:, sl] = r["h1T"][:, 1024:]
    return x1T, x1Tb, h1T


def run_lif_phase(h_T, skw, nsec, cw):
    nc = cached_build(("lif", skw, cw, nsec), lambda: build_lif(skw, cw, nsec))
    B, E, T = h_T.shape
    fpc = E // NCORE
    assert fpc * B == 128 * cw
    in_maps = []
    for c in range(NCORE):
        rows = h_T[:, c * fpc:(c + 1) * fpc, :].reshape(B * fpc, T)
        h_cw = rows.reshape(cw, 128, T)
        in_maps.append({"hs": scan_pack(h_cw, skw, nsec)})
    res = run_bass_kernel_spmd(nc, in_maps, core_ids=list(range(NCORE)))
    out = np.zeros((B, E, T), np.float32)
    for c in range(NCORE):
        spk = scan_unpack(np.asarray(res.results[c]["sp"]), skw, nsec, cw)
        out[:, c * fpc:(c + 1) * fpc, :] = spk.reshape(B, fpc, T)
    return out


def run_w2(s1T, W2, b2):
    nc = cached_build("w2", build_w2)
    w2Tc = np.ascontiguousarray(W2.T)
    in_maps = []
    for c in range(NCORE):
        sl = slice(c * 1024, (c + 1) * 1024)
        in_maps.append({
            "s1Ts": np.ascontiguousarray(np.concatenate(
                [s1T[0][:, sl], s1T[1][:, sl]], axis=1)).astype(bf),
            "w2T": w2Tc, "b2v": b2,
        })
    res = run_bass_kernel_spmd(nc, in_maps, core_ids=list(range(NCORE)))
    h2T = np.zeros((BT, D, S), np.float32)
    for c in range(NCORE):
        sl = slice(c * 1024, (c + 1) * 1024)
        h2T[0][:, sl] = res.results[c]["h2T"][:, :1024]
        h2T[1][:, sl] = res.results[c]["h2T"][:, 1024:]
    return h2T


def make_ln(sch, t, KD=8):
    def emit_ln(src, dst, dstb=None, g=None, b=None):
        vs, ts, as_ = t["vs"], t["ts"], t["as_"]
        TSU = t["tsub"]
        sch.wait("scalar", vs, sch.n(vs))
        aN_l = None
        for k in range(KD):
            sch.emit("tensor", lambda e, k=k: e.matmul(
                t["psSum"][:], t["ones128"][:], src[:, k * TSU:(k + 1) * TSU],
                start=(k == 0), stop=(k == KD - 1)))
            if aN_l is not None:
                sch.wait("scalar", ts, aN_l)
            aSq = sch.emit("scalar", lambda e, k=k: e.activation(
                out=t["sq"][:], in_=src[:, k * TSU:(k + 1) * TSU],
                func=AF.Square), inc=(as_, 1))
            sch.wait("tensor", as_, aSq)
            aN_l = sch.emit("tensor", lambda e, k=k: e.matmul(
                t["psSq"][:], t["ones128"][:], t["sq"][:],
                start=(k == 0), stop=(k == KD - 1)), inc=(ts, 1))
        sch.wait("vector", ts, aN_l)
        T_ = TSU
        stat = t["stat"]
        sch.emit("vector", lambda e: e.tensor_scalar_mul(
            out=stat[:, 0:T_], in0=t["psSum"][:], scalar1=1.0 / D))
        sch.emit("vector", lambda e: e.tensor_mul(
            out=stat[:, T_:2 * T_], in0=stat[:, 0:T_], in1=stat[:, 0:T_]))
        n_var = sch.emit("vector", lambda e: e.scalar_tensor_tensor(
            out=stat[:, 2 * T_:3 * T_], in0=t["psSq"][:], scalar=1.0 / D,
            in1=stat[:, T_:2 * T_], op0=AX.mult, op1=AX.subtract), inc=(vs, 1))
        sch.wait("scalar", vs, n_var)
        a_sd = sch.emit("scalar", lambda e: e.activation(
            out=stat[:, 3 * T_:4 * T_], in_=stat[:, 2 * T_:3 * T_],
            func=AF.Sqrt, bias=t["eps1"][:]), inc=(as_, 1))
        sch.wait("vector", as_, a_sd)
        n_rs = sch.emit("vector", lambda e: e.reciprocal(
            out=stat[:, 2 * T_:3 * T_], in_=stat[:, 3 * T_:4 * T_]), inc=(vs, 1))
        sch.wait("tensor", vs, n_rs)
        sch.emit("tensor", lambda e: e.matmul(
            t["psBm"][:], t["onesr"][:], stat[:, 0:T_], start=True, stop=True))
        t_bc = sch.emit("tensor", lambda e: e.matmul(
            t["psBr"][:], t["onesr"][:], stat[:, 2 * T_:3 * T_],
            start=True, stop=True), inc=(ts, 1))
        sch.wait("vector", ts, t_bc)
        n_rb = sch.emit("vector", lambda e: e.tensor_copy(
            out=t["rbm"][:], in_=t["psBr"][:]), inc=(vs, 1))
        nv = n_rb
        for k in range(KD):
            sch.emit("vector", lambda e, k=k: e.tensor_sub(
                out=t["d1"][:], in0=src[:, k * TSU:(k + 1) * TSU],
                in1=t["psBm"][:]))
            if g is None:
                nv = sch.emit("vector", lambda e, k=k: e.tensor_mul(
                    out=dst[:, k * TSU:(k + 1) * TSU], in0=t["d1"][:],
                    in1=t["rbm"][:]), inc=(vs, 1))
            else:
                sch.emit("vector", lambda e, k=k: e.tensor_mul(
                    out=t["d1"][:], in0=t["d1"][:], in1=t["rbm"][:]))
                nv = sch.emit("vector", lambda e, k=k: e.tensor_scalar(
                    out=dst[:, k * TSU:(k + 1) * TSU], in0=t["d1"][:],
                    scalar1=g[:, k:k + 1], scalar2=b[:, k:k + 1],
                    op0=AX.mult, op1=AX.add), inc=(vs, 1))
            if dstb is not None:
                nv = sch.emit("vector", lambda e, k=k: e.tensor_copy(
                    out=dstb[:, k * TSU:(k + 1) * TSU],
                    in_=dst[:, k * TSU:(k + 1) * TSU]), inc=(vs, 1))
        return nv
    return emit_ln


def ln_tiles(nc, e, sch, tsub):
    t = {"tsub": tsub}
    t["ones128"] = e(nc.sbuf_tensor("ones128", [128, 1], F32))
    t["onesr"] = e(nc.sbuf_tensor("onesr", [1, 128], F32))
    t["eps1"] = e(nc.sbuf_tensor("eps1", [1, 1], F32))
    t["sq"] = e(nc.sbuf_tensor("sq", [128, tsub], F32))
    t["stat"] = e(nc.sbuf_tensor("stat", [1, 4 * tsub], F32))
    t["d1"] = e(nc.sbuf_tensor("d1", [128, tsub], F32))
    t["rbm"] = e(nc.sbuf_tensor("rbm", [128, tsub], F32))
    t["psSum"] = e(nc.psum_tensor("psSum", [1, tsub], F32))
    t["psSq"] = e(nc.psum_tensor("psSq", [1, tsub], F32))
    t["psBm"] = e(nc.psum_tensor("psBm", [128, tsub], F32))
    t["psBr"] = e(nc.psum_tensor("psBr", [128, tsub], F32))
    return t


def ln_init(sch, t, ld, vs):
    sch.emit("vector", lambda e: e.memset(t["ones128"][:], 1.0))
    sch.emit("vector", lambda e: e.memset(t["onesr"][:], 1.0))
    return sch.emit("vector", lambda e: e.memset(t["eps1"][:], 1e-5), inc=(vs, 1))


def build_h1():
    nc = bass.Bass("TRN2", target_bir_lowering=False)
    x1Ts = nc.dram_tensor("x1Ts", [D, TB], F32, kind="ExternalInput").ap()
    fspTs = nc.dram_tensor("fspTs", [D, TB], BF16, kind="ExternalInput").ap()
    x2Ts = nc.dram_tensor("x2Ts", [D, TB], F32, kind="ExternalInput").ap()
    g2 = nc.dram_tensor("g2", [D], F32, kind="ExternalInput").ap()
    b2a = nc.dram_tensor("b2a", [D], F32, kind="ExternalInput").ap()
    x3T = nc.dram_tensor("x3T", [D, TB], F32, kind="ExternalOutput").ap()
    x3Tb = nc.dram_tensor("x3Tb", [D, TB], BF16, kind="ExternalOutput").ap()
    KD = 8
    NS = TB // TSUB
    sch = Sched()
    with ExitStack() as ctx:
        e = ctx.enter_context
        ld = e(nc.semaphore("ld"))
        vs = e(nc.semaphore("vs"))
        ts = e(nc.semaphore("ts"))
        as_ = e(nc.semaphore("as"))
        sto = e(nc.semaphore("sto"))
        t = ln_tiles(nc, e, sch, TSUB)
        t["vs"], t["ts"], t["as_"] = vs, ts, as_
        gcol = e(nc.sbuf_tensor("gcol", [128, KD], F32))
        bcol = e(nc.sbuf_tensor("bcol", [128, KD], F32))
        r1 = e(nc.sbuf_tensor("r1", [128, KD * TSUB], F32))
        r2 = e(nc.sbuf_tensor("r2", [128, KD * TSUB], F32))
        fsb = e(nc.sbuf_tensor("fsb", [128, KD * TSUB], BF16))
        x3b = e(nc.sbuf_tensor("x3b", [128, KD * TSUB], BF16))
        emit_ln = make_ln(sch, t)
        nl = sch.emit("sync", lambda e: e.dma_start(
            out=gcol[:], in_=g2.rearrange("(m p) -> p m", p=128),
            allow_slow_non_contiguous=True), inc=(ld, 16))
        nl = sch.emit("sync", lambda e: e.dma_start(
            out=bcol[:], in_=b2a.rearrange("(m p) -> p m", p=128),
            allow_slow_non_contiguous=True), inc=(ld, 16))
        sch.wait("vector", ld, nl)
        ln_init(sch, t, ld, vs)
        n_sto = 0
        for su in range(NS):
            t0 = su * TSUB
            sch.wait("sync", sto, n_sto)
            nl = sch.emit("sync", lambda e, t0=t0: e.dma_start(
                out=r1[:].rearrange("p (k t) -> p k t", k=KD),
                in_=x1Ts[:, t0:t0 + TSUB].rearrange("(k p) t -> p k t", p=128)),
                inc=(ld, 16))
            nl = sch.emit("sync", lambda e, t0=t0: e.dma_start(
                out=fsb[:].rearrange("p (k t) -> p k t", k=KD),
                in_=fspTs[:, t0:t0 + TSUB].rearrange("(k p) t -> p k t", p=128)),
                inc=(ld, 16))
            nl = sch.emit("sync", lambda e, t0=t0: e.dma_start(
                out=r2[:].rearrange("p (k t) -> p k t", k=KD),
                in_=x2Ts[:, t0:t0 + TSUB].rearrange("(k p) t -> p k t", p=128)),
                inc=(ld, 16))
            sch.wait("vector", ld, nl)
            n_a = sch.emit("vector", lambda e: e.tensor_add(
                out=r1[:], in0=r1[:], in1=fsb[:]), inc=(vs, 1))
            sch.wait("tensor", vs, n_a)
            emit_ln(r1, r1, None, gcol, bcol)
            n_a2 = sch.emit("vector", lambda e: e.tensor_add(
                out=r2[:], in0=r2[:], in1=r1[:]), inc=(vs, 1))
            sch.wait("tensor", vs, n_a2)
            emit_ln(r2, r2, x3b)
            n_f = sch.n(vs)
            sch.wait("sync", vs, n_f)
            n_sto = sch.emit("sync", lambda e, t0=t0: e.dma_start(
                out=x3T[:, t0:t0 + TSUB].rearrange("(k p) t -> p k t", p=128),
                in_=r2[:].rearrange("p (k t) -> p k t", k=KD)), inc=(sto, 16))
            n_sto = sch.emit("sync", lambda e, t0=t0: e.dma_start(
                out=x3Tb[:, t0:t0 + TSUB].rearrange("(k p) t -> p k t", p=128),
                in_=x3b[:].rearrange("p (k t) -> p k t", k=KD)), inc=(sto, 16))
        sch.wait("sync", sto, sch.n(sto))
        sch.check_deadlock()
        sch.replay(nc)
    return nc


def run_h1(x1T, fspT, x2T, g2, b2a):
    nc = cached_build("h1", build_h1)
    in_maps = []
    for c in range(NCORE):
        sl = slice(c * 1024, (c + 1) * 1024)
        in_maps.append({
            "x1Ts": np.ascontiguousarray(np.concatenate(
                [x1T[0][:, sl], x1T[1][:, sl]], axis=1)),
            "fspTs": np.ascontiguousarray(np.concatenate(
                [fspT[0][:, sl], fspT[1][:, sl]], axis=1)).astype(bf),
            "x2Ts": np.ascontiguousarray(np.concatenate(
                [x2T[0][:, sl], x2T[1][:, sl]], axis=1)),
            "g2": g2, "b2a": b2a,
        })
    res = run_bass_kernel_spmd(nc, in_maps, core_ids=list(range(NCORE)))
    x3T = np.zeros((BT, D, S), np.float32)
    x3Tb = np.zeros((BT, D, S), bf)
    for c in range(NCORE):
        sl = slice(c * 1024, (c + 1) * 1024)
        r = res.results[c]
        x3T[0][:, sl] = r["x3T"][:, :1024]; x3T[1][:, sl] = r["x3T"][:, 1024:]
        x3Tb[0][:, sl] = r["x3Tb"][:, :1024]; x3Tb[1][:, sl] = r["x3Tb"][:, 1024:]
    return x3T, x3Tb


def build_h2(variant='full'):
    nc = bass.Bass("TRN2", target_bir_lowering=False)
    x3Ts = nc.dram_tensor("x3Ts", [D, TB], F32, kind="ExternalInput").ap()
    x3Tbs = nc.dram_tensor("x3Tbs", [D, TB], BF16, kind="ExternalInput").ap()
    ipwT = nc.dram_tensor("ipwT", [D, 3 * D], F32, kind="ExternalInput").ap()
    ipb = nc.dram_tensor("ipb", [3 * D], F32, kind="ExternalInput").ap()
    opwT = nc.dram_tensor("opwT", [D, D], F32, kind="ExternalInput").ap()
    opb = nc.dram_tensor("opb", [D], F32, kind="ExternalInput").ap()
    f1g = nc.dram_tensor("f1g", [D], F32, kind="ExternalInput").ap()
    f1b = nc.dram_tensor("f1b", [D], F32, kind="ExternalInput").ap()
    wg = nc.dram_tensor("wg", [D, 8], F32, kind="ExternalInput").ap()
    y1T = nc.dram_tensor("y1T", [D, TB], F32, kind="ExternalOutput").ap()
    y1rows = nc.dram_tensor("y1rows", [TB, D], BF16, kind="ExternalOutput").ap()
    idxr = nc.dram_tensor("idxr", [1, TB], F32, kind="ExternalOutput").ap()
    gvr = nc.dram_tensor("gvr", [1, TB], F32, kind="ExternalOutput").ap()
    KD = 8
    HN = 4
    sch = Sched()
    with ExitStack() as ctx:
        e = ctx.enter_context
        ld = e(nc.semaphore("ld"))
        vs = e(nc.semaphore("vs"))
        ts = e(nc.semaphore("ts"))
        as_ = e(nc.semaphore("as"))
        gs = e(nc.semaphore("gs"))
        sto = e(nc.semaphore("sto"))
        t = ln_tiles(nc, e, sch, TSUB)
        t["vs"], t["ts"], t["as_"] = vs, ts, as_
        obf = e(nc.sbuf_tensor("obf", [128, KD * TB], BF16))
        ipbc = e(nc.sbuf_tensor("ipbc", [128, 24], F32))
        iota8 = e(nc.sbuf_tensor("iota8", [8, 1], F32))
        ones8 = e(nc.sbuf_tensor("ones8", [8, 1], F32))
        ones8r = e(nc.sbuf_tensor("ones8r", [1, 8], F32))
        identb = e(nc.sbuf_tensor("identb", [128, 128], BF16))
        emit_ln = make_ln(sch, t)
        ln_init(sch, t, ld, vs)
        sch.emit("vector", lambda e: e.memset(ones8[:], 1.0))
        sch.emit("vector", lambda e: e.memset(ones8r[:], 1.0))
        n_i8 = sch.emit("gpsimd", lambda e: e.iota(
            out=iota8[:], pattern=[[0, 1]], base=0, channel_multiplier=1,
            allow_small_or_imprecise_dtypes=True), inc=(gs, 1))
        n_id = sch.emit("gpsimd", lambda e: e.iota(
            out=identb[:], pattern=[[-1, 128]], base=0, channel_multiplier=1,
            allow_small_or_imprecise_dtypes=True), inc=(gs, 1))
        sch.wait("vector", gs, n_id)
        n_id2 = sch.emit("vector", lambda e: e.tensor_single_scalar(
            out=identb[:], in_=identb[:], scalar=0.0, op=AX.is_equal), inc=(vs, 1))
        sch.wait("tensor", vs, n_id2)
        nl = sch.emit("sync", lambda e: e.dma_start(
            out=ipbc[:], in_=ipb.rearrange("(m p) -> p m", p=128),
            allow_slow_non_contiguous=True), inc=(ld, 16))
        sch.wait("scalar", ld, nl)

        with ExitStack() as pa:
            ea = pa.enter_context
            qkvb = ea(nc.sbuf_tensor("qkvb", [128, 24 * TB], BF16))
            with ExitStack() as p1:
                e1 = p1.enter_context
                x3bb = e1(nc.sbuf_tensor("x3bb", [128, KD * TB], BF16))
                wstg = e1(nc.sbuf_tensor("wstg", [128, KD * 128], F32))
                wmb = e1(nc.sbuf_tensor("wmb", [128, KD * 128], BF16))
                psQ = e1(nc.psum_tensor("psQ", [128, TSUB], F32))
                nlx = sch.emit("sync", lambda e: e.dma_start(
                    out=x3bb[:].rearrange("p (k t) -> p k t", k=KD),
                    in_=x3Tbs.rearrange("(k p) t -> p k t", p=128)), inc=(ld, 16))
                for m in range(24):
                    nl2 = sch.emit("sync", lambda e, m=m: e.dma_start(
                        out=wstg[:].rearrange("p (k t) -> p k t", k=KD),
                        in_=ipwT[:, m * 128:(m + 1) * 128].rearrange(
                            "(k p) t -> p k t", p=128)), inc=(ld, 16))
                    sch.wait("vector", ld, nl2)
                    n_w = sch.emit("vector", lambda e: e.tensor_copy(
                        out=wmb[:], in_=wstg[:]), inc=(vs, 1))
                    sch.wait("tensor", vs, n_w)
                    sch.wait("tensor", ld, nl2)
                    for tsu in range(TB // TSUB):
                        for k in range(KD):
                            tQ = sch.emit("tensor",
                                          lambda e, k=k, tsu=tsu: e.matmul(
                                psQ[:], wmb[:, k * 128:(k + 1) * 128],
                                x3bb[:, k * TB + tsu * TSUB:
                                     k * TB + (tsu + 1) * TSUB],
                                start=(k == 0), stop=(k == KD - 1)),
                                inc=(ts, 1) if k == KD - 1 else None)
                        sch.wait("scalar", ts, tQ)
                        aQ = sch.emit("scalar", lambda e, m=m, tsu=tsu: e.activation(
                            out=qkvb[:, m * TB + tsu * TSUB:
                                     m * TB + (tsu + 1) * TSUB],
                            in_=psQ[:], func=AF.Identity,
                            bias=ipbc[:, m:m + 1]), inc=(as_, 1))
                        sch.wait("tensor", as_, aQ)
                    sch.wait("sync", as_, sch.n(as_))
            if variant == "qkv":
                sch.wait("sync", as_, sch.n(as_))
                nst = sch.emit("sync", lambda e: e.dma_start(
                    out=y1T[:, 0:TB], in_=qkvb[:, 0:TB].rearrange(
                        "p t -> p t") if False else qkvb[:, 0:TB],
                    allow_slow_non_contiguous=True), inc=(sto, 16)) if False else None
                sch.check_deadlock()
                sch.replay(nc)
                return nc
            with ExitStack() as p2:
                e2 = p2.enter_context
                scS = [e2(nc.sbuf_tensor(f"scS{ij}", [4, TB], F32))
                       for ij in range(4)]
                A1 = e2(nc.sbuf_tensor("A1", [4, TB], BF16))
                A0 = e2(nc.sbuf_tensor("A0", [4, TB], BF16))
                ebuf2 = [e2(nc.sbuf_tensor(f"ebuf{i}", [128, TSUB], BF16))
                         for i in range(2)]
                ebf2 = e2(nc.sbuf_tensor("ebf2", [128, TSUB], F32))
                ind = e2(nc.sbuf_tensor("ind", [128, 4 * HN], BF16))
                sel = e2(nc.sbuf_tensor("sel", [4, 128 * HN], BF16))
                ps4 = e2(nc.psum_tensor("ps4", [4, TSUB], F32))
                psB0 = e2(nc.psum_tensor("psB0", [128, TSUB], F32))
                psB1 = e2(nc.psum_tensor("psB1", [128, TSUB], F32))
                sch.wait("vector", ts, sch.n(ts))
                sch.wait("vector", as_, sch.n(as_))
                sch.wait("gpsimd", ts, sch.n(ts))
                sch.wait("gpsimd", as_, sch.n(as_))
                n_in = sch.emit("vector", lambda e: e.memset(ind[:], 0.0),
                                inc=(vs, 1))
                for q in range(HN):
                    n_in = sch.emit("vector", lambda e, q=q: e.memset(
                        ind[:, q * 4 + q:q * 4 + q + 1], 1.0), inc=(vs, 1))
                n_io = sch.emit("gpsimd", lambda e: e.iota(
                    out=sel[:].rearrange("p (h m) -> p h m", h=HN),
                    pattern=[[1, HN], [0, 128]], base=0, channel_multiplier=-1,
                    allow_small_or_imprecise_dtypes=True), inc=(gs, 1))
                sch.wait("vector", gs, n_io)
                n_in = sch.emit("vector", lambda e: e.tensor_single_scalar(
                    out=sel[:], in_=sel[:], scalar=0.0, op=AX.is_equal),
                    inc=(vs, 1))
                sch.wait("tensor", vs, n_in)
                for ij in range(4):
                    i, j = ij // 2, ij % 2
                    for tsu in range(2):
                        cq = i * 1024 + tsu * TSUB
                        ck = j * 1024 + tsu * TSUB
                        mmS_done = [0, 0]
                        for ch in range(KD):
                            q = ch // 2
                            eb = ch % 2
                            sch.wait("vector", ts, mmS_done[eb])
                            n_e = sch.emit("vector",
                                           lambda e, ch=ch, cq=cq, ck=ck, eb=eb:
                                           e.tensor_mul(
                                out=ebuf2[eb][:],
                                in0=qkvb[:, ch * TB + cq:ch * TB + cq + TSUB],
                                in1=qkvb[:, (8 + ch) * TB + ck:
                                         (8 + ch) * TB + ck + TSUB]),
                                inc=(vs, 1))
                            sch.wait("tensor", vs, n_e)
                            mmS_done[eb] = sch.emit("tensor",
                                          lambda e, ch=ch, q=q, eb=eb: e.matmul(
                                ps4[:], ind[:, q * 4:(q + 1) * 4], ebuf2[eb][:],
                                start=(ch == 0), stop=(ch == KD - 1)),
                                inc=(ts, 1))
                        sch.wait("vector", ts, max(mmS_done))
                        n_sc = sch.emit("vector",
                                        lambda e, ij=ij, cq=cq: e.tensor_copy(
                            out=scS[ij][:, cq:cq + TSUB], in_=ps4[:]),
                            inc=(vs, 1))
                        sch.wait("tensor", vs, n_sc)
                for i in range(2):
                    n_d = sch.emit("vector", lambda e, i=i: e.tensor_sub(
                        out=A1[:, i * 1024:(i + 1) * 1024],
                        in0=scS[i * 2 + 1][:, i * 1024:(i + 1) * 1024],
                        in1=scS[i * 2 + 0][:, i * 1024:(i + 1) * 1024]),
                        inc=(vs, 1))
                sch.wait("scalar", vs, sch.n(vs))
                a_sg = sch.emit("scalar", lambda e: e.activation(
                    out=A1[:], in_=A1[:], func=AF.Sigmoid, scale=1.0 / 16),
                    inc=(as_, 1))
                sch.wait("vector", as_, a_sg)
                n_a0 = sch.emit("vector", lambda e: e.tensor_scalar(
                    out=A0[:], in0=A1[:], scalar1=-1.0, scalar2=1.0,
                    op0=AX.mult, op1=AX.add), inc=(vs, 1))
                sch.wait("tensor", vs, n_a0)
                for ch in range(KD):
                    h = ch // 2
                    for i in range(2):
                        for tsu in range(2):
                            c0 = i * 1024 + tsu * TSUB
                            cn = tsu * TSUB
                            sch.emit("tensor", lambda e, h=h, c0=c0: e.matmul(
                                psB0[:], sel[:, h * 128:(h + 1) * 128],
                                A0[:, c0:c0 + TSUB], start=True, stop=True))
                            tB1 = sch.emit("tensor",
                                           lambda e, h=h, c0=c0: e.matmul(
                                psB1[:], sel[:, h * 128:(h + 1) * 128],
                                A1[:, c0:c0 + TSUB], start=True, stop=True),
                                inc=(ts, 1))
                            sch.wait("vector", ts, tB1)
                            sch.emit("vector", lambda e, ch=ch, cn=cn: e.tensor_mul(
                                out=ebf2[:],
                                in0=qkvb[:, (16 + ch) * TB + cn:
                                         (16 + ch) * TB + cn + TSUB],
                                in1=psB0[:]))
                            sch.emit("vector", lambda e, ch=ch, cn=cn, c0=c0:
                                     e.tensor_mul(
                                out=obf[:, ch * TB + c0:ch * TB + c0 + TSUB],
                                in0=qkvb[:, (16 + ch) * TB + 1024 + cn:
                                         (16 + ch) * TB + 1024 + cn + TSUB],
                                in1=psB1[:]))
                            n_o = sch.emit("vector", lambda e, ch=ch, c0=c0:
                                           e.tensor_add(
                                out=obf[:, ch * TB + c0:ch * TB + c0 + TSUB],
                                in0=obf[:, ch * TB + c0:ch * TB + c0 + TSUB],
                                in1=ebf2[:]), inc=(vs, 1))
                            sch.wait("tensor", vs, n_o)
        if variant == "mha":
            sch.wait("sync", vs, sch.n(vs))
            sch.wait("sync", ts, sch.n(ts))
            sch.check_deadlock()
            sch.replay(nc)
            return nc
        with ExitStack() as p3:
            e3 = p3.enter_context
            opwb = e3(nc.sbuf_tensor("opwb", [128, KD * D], BF16))
            opbc = e3(nc.sbuf_tensor("opbc", [128, KD], F32))
            fgc = e3(nc.sbuf_tensor("fgc", [128, KD], F32))
            fbc = e3(nc.sbuf_tensor("fbc", [128, KD], F32))
            wgb = e3(nc.sbuf_tensor("wgb", [128, KD * 8], BF16))
            x3s = e3(nc.sbuf_tensor("x3s", [128, KD * TSUB], F32))
            y1s = e3(nc.sbuf_tensor("y1s", [128, KD * TSUB], F32))
            y1bs = e3(nc.sbuf_tensor("y1bs", [128, KD * TSUB], BF16))
            rowst = e3(nc.sbuf_tensor("rowst", [128, KD * 128], BF16))
            lgs = e3(nc.sbuf_tensor("lgs", [8, TB], F32))
            mrow = e3(nc.sbuf_tensor("mrow", [4, TB], F32))
            m2r = e3(nc.sbuf_tensor("m2r", [2, TB], F32))
            mx = e3(nc.sbuf_tensor("mx", [1, TB], F32))
            idxrow = e3(nc.sbuf_tensor("idxrow", [1, TB], F32))
            gvrow = e3(nc.sbuf_tensor("gvrow", [1, TB], F32))
            e8 = e3(nc.sbuf_tensor("e8", [8, TSUB], F32))
            psA = e3(nc.psum_tensor("psA", [128, TSUB], F32))
            psG = e3(nc.psum_tensor("psG", [8, TSUB], F32))
            psTr2 = [e3(nc.psum_tensor(f"psTr{i}", [128, 128], BF16))
                     for i in range(2)]
            sch.wait("sync", vs, sch.n(vs))
            sch.wait("sync", ts, sch.n(ts))
            sch.wait("sync", as_, sch.n(as_))
            cp_done = [0, 0]
            with nc.sbuf_tensor("wstg2", [128, KD * D], F32) as wstg2:
                nl = sch.emit("sync", lambda e: e.dma_start(
                    out=wstg2[:].rearrange("p (k m) -> p k m", k=KD),
                    in_=opwT.rearrange("(k p) m -> p k m", p=128)), inc=(ld, 16))
                sch.wait("vector", ld, nl)
                n_w = sch.emit("vector", lambda e: e.tensor_copy(
                    out=opwb[:], in_=wstg2[:]), inc=(vs, 1))
                sch.wait("sync", vs, n_w)
                nl = sch.emit("sync", lambda e: e.dma_start(
                    out=wstg2[:, 0:KD * 8].rearrange("p (k m) -> p k m", k=KD),
                    in_=wg.rearrange("(k p) m -> p k m", p=128)), inc=(ld, 16))
                sch.wait("vector", ld, nl)
                n_w = sch.emit("vector", lambda e: e.tensor_copy(
                    out=wgb[:], in_=wstg2[:, 0:KD * 8]), inc=(vs, 1))
                sch.wait("sync", vs, n_w)
            nl = sch.emit("sync", lambda e: e.dma_start(
                out=opbc[:], in_=opb.rearrange("(m p) -> p m", p=128),
                allow_slow_non_contiguous=True), inc=(ld, 16))
            nl = sch.emit("sync", lambda e: e.dma_start(
                out=fgc[:], in_=f1g.rearrange("(m p) -> p m", p=128),
                allow_slow_non_contiguous=True), inc=(ld, 16))
            nl = sch.emit("sync", lambda e: e.dma_start(
                out=fbc[:], in_=f1b.rearrange("(m p) -> p m", p=128),
                allow_slow_non_contiguous=True), inc=(ld, 16))
            sch.wait("vector", ld, nl)
            sch.wait("tensor", vs, sch.n(vs))
            n_sto = 0
            for su in range(TB // TSUB):
                t0 = su * TSUB
                sch.wait("sync", sto, n_sto)
                nl = sch.emit("sync", lambda e, t0=t0: e.dma_start(
                    out=x3s[:].rearrange("p (k t) -> p k t", k=KD),
                    in_=x3Ts[:, t0:t0 + TSUB].rearrange(
                        "(k p) t -> p k t", p=128)), inc=(ld, 16))
                sch.wait("vector", ld, nl)
                for m in range(KD):
                    for k in range(KD):
                        tA = sch.emit("tensor", lambda e, m=m, k=k, t0=t0: e.matmul(
                            psA[:], opwb[:, k * D + m * 128:k * D + m * 128 + 128],
                            obf[:, k * TB + t0:k * TB + t0 + TSUB],
                            start=(k == 0), stop=(k == KD - 1)),
                            inc=(ts, 1) if k == KD - 1 else None)
                    sch.wait("vector", ts, tA)
                    sch.emit("vector", lambda e, m=m: e.tensor_scalar(
                        out=t["d1"][:], in0=psA[:], scalar1=opbc[:, m:m + 1],
                        scalar2=None, op0=AX.add))
                    n_res = sch.emit("vector", lambda e, m=m: e.tensor_add(
                        out=x3s[:, m * TSUB:(m + 1) * TSUB],
                        in0=x3s[:, m * TSUB:(m + 1) * TSUB],
                        in1=t["d1"][:]), inc=(vs, 1))
                    sch.wait("tensor", vs, n_res)
                emit_ln(x3s, y1s, y1bs, fgc, fbc)
                n_f = sch.n(vs)
                sch.wait("sync", vs, n_f)
                n_sto = sch.emit("sync", lambda e, t0=t0: e.dma_start(
                    out=y1T[:, t0:t0 + TSUB].rearrange("(k p) t -> p k t", p=128),
                    in_=y1s[:].rearrange("p (k t) -> p k t", k=KD)), inc=(sto, 16))
                sch.wait("tensor", vs, n_f)
                for tt in range(TSUB // 128):
                    sch.wait("vector", sto, sch.n(sto))
                    for k in range(KD):
                        pt = k % 2
                        sch.wait("tensor", vs, cp_done[pt])
                        tT = sch.emit("tensor", lambda e, tt=tt, k=k, pt=pt:
                                      e.transpose(
                            psTr2[pt][:], y1bs[:, k * TSUB + tt * 128:
                                          k * TSUB + tt * 128 + 128],
                            identb[:]), inc=(ts, 1))
                        sch.wait("vector", ts, tT)
                        cp_done[pt] = sch.emit("vector",
                                               lambda e, k=k, pt=pt: e.tensor_copy(
                            out=rowst[:, k * 128:(k + 1) * 128], in_=psTr2[pt][:]),
                            inc=(vs, 1))
                    sch.wait("sync", vs, sch.n(vs))
                    n_sto = sch.emit("sync", lambda e, t0=t0, tt=tt: e.dma_start(
                        out=y1rows[t0 + tt * 128:t0 + tt * 128 + 128, :],
                        in_=rowst[:]), inc=(sto, 16))
                for k in range(KD):
                    tG = sch.emit("tensor", lambda e, k=k: e.matmul(
                        psG[:], wgb[:, k * 8:(k + 1) * 8],
                        y1bs[:, k * TSUB:(k + 1) * TSUB],
                        start=(k == 0), stop=(k == KD - 1)),
                        inc=(ts, 1) if k == KD - 1 else None)
                sch.wait("vector", ts, tG)
                n_g = sch.emit("vector", lambda e, t0=t0: e.tensor_copy(
                    out=lgs[:, t0:t0 + TSUB], in_=psG[:]), inc=(vs, 1))
                sch.wait("tensor", vs, n_g)
            if variant == "proj":
                sch.wait("sync", sto, sch.n(sto))
                sch.check_deadlock()
                sch.replay(nc)
                return nc
            sch.wait("sync", vs, sch.n(vs))
            nd = sch.emit("sync", lambda e: e.dma_start(
                out=mrow[:], in_=lgs[4:8, :]), inc=(ld, 16))
            sch.wait("vector", ld, nd)
            n_m = sch.emit("vector", lambda e: e.tensor_tensor(
                out=mrow[:], in0=mrow[:], in1=lgs[0:4, :], op=AX.max), inc=(vs, 1))
            sch.wait("sync", vs, n_m)
            nd = sch.emit("sync", lambda e: e.dma_start(
                out=m2r[:], in_=mrow[2:4, :]), inc=(ld, 16))
            sch.wait("vector", ld, nd)
            n_m = sch.emit("vector", lambda e: e.tensor_tensor(
                out=m2r[:], in0=m2r[:], in1=mrow[0:2, :], op=AX.max), inc=(vs, 1))
            sch.wait("sync", vs, n_m)
            nd = sch.emit("sync", lambda e: e.dma_start(
                out=mx[:], in_=m2r[1:2, :]), inc=(ld, 16))
            sch.wait("vector", ld, nd)
            n_m = sch.emit("vector", lambda e: e.tensor_tensor(
                out=mx[:], in0=mx[:], in1=m2r[0:1, :], op=AX.max), inc=(vs, 1))
            sch.wait("tensor", vs, n_m)
            for su in range(TB // TSUB):
                t0 = su * TSUB
                tBm = sch.emit("tensor", lambda e, t0=t0: e.matmul(
                    psA[0:8, :], ones8r[:], mx[:, t0:t0 + TSUB],
                    start=True, stop=True), inc=(ts, 1))
                sch.wait("vector", ts, tBm)
                n_d8 = sch.emit("vector", lambda e, t0=t0: e.tensor_sub(
                    out=e8[:], in0=lgs[:, t0:t0 + TSUB], in1=psA[0:8, :]),
                    inc=(vs, 1))
                sch.wait("vector", vs, n_d8)
                n_eq = sch.emit("vector", lambda e: e.tensor_single_scalar(
                    out=e8[:], in_=e8[:], scalar=0.0, op=AX.is_equal), inc=(vs, 1))
                sch.wait("tensor", vs, n_eq)
                tI = sch.emit("tensor", lambda e: e.matmul(
                    psA[0:1, :], iota8[:], e8[:], start=True, stop=True),
                    inc=(ts, 1))
                sch.wait("vector", ts, tI)
                n_ix = sch.emit("vector", lambda e, t0=t0: e.tensor_copy(
                    out=idxrow[:, t0:t0 + TSUB], in_=psA[0:1, :]), inc=(vs, 1))
                sch.wait("tensor", vs, n_ix)
                tBm = sch.emit("tensor", lambda e, t0=t0: e.matmul(
                    psA[0:8, :], ones8r[:], mx[:, t0:t0 + TSUB],
                    start=True, stop=True), inc=(ts, 1))
                sch.wait("vector", ts, tBm)
                n_d8 = sch.emit("vector", lambda e, t0=t0: e.tensor_sub(
                    out=e8[:], in0=lgs[:, t0:t0 + TSUB], in1=psA[0:8, :]),
                    inc=(vs, 1))
                sch.wait("scalar", vs, n_d8)
                a_e = sch.emit("scalar", lambda e: e.activation(
                    out=e8[:], in_=e8[:], func=AF.Exp), inc=(as_, 1))
                sch.wait("tensor", as_, a_e)
                tE = sch.emit("tensor", lambda e: e.matmul(
                    psA[0:1, :], ones8[:], e8[:], start=True, stop=True),
                    inc=(ts, 1))
                sch.wait("vector", ts, tE)
                n_gv = sch.emit("vector", lambda e, t0=t0: e.reciprocal(
                    out=gvrow[:, t0:t0 + TSUB], in_=psA[0:1, :]), inc=(vs, 1))
                sch.wait("tensor", vs, n_gv)
            n_f = sch.n(vs)
            sch.wait("sync", vs, n_f)
            n_sto = sch.emit("sync", lambda e: e.dma_start(
                out=idxr, in_=idxrow[:]), inc=(sto, 16))
            n_sto = sch.emit("sync", lambda e: e.dma_start(
                out=gvr, in_=gvrow[:]), inc=(sto, 16))
            sch.wait("sync", sto, sch.n(sto))
        sch.check_deadlock()
        sch.replay(nc)
    return nc


def run_h2(x3T, x3Tb, ipw, ipb, opw, opb, f1g, f1b, wg):
    nc = cached_build("h2", build_h2)
    ipwT = np.ascontiguousarray(ipw.T)
    opwT = np.ascontiguousarray(opw.T)
    in_maps = []
    for c in range(NCORE):
        sl = slice(c * 1024, (c + 1) * 1024)
        in_maps.append({
            "x3Ts": np.ascontiguousarray(np.concatenate(
                [x3T[0][:, sl], x3T[1][:, sl]], axis=1)),
            "x3Tbs": np.ascontiguousarray(np.concatenate(
                [x3Tb[0][:, sl], x3Tb[1][:, sl]], axis=1)),
            "ipwT": ipwT, "ipb": ipb, "opwT": opwT, "opb": opb,
            "f1g": f1g, "f1b": f1b, "wg": np.ascontiguousarray(wg),
        })
    res = run_bass_kernel_spmd(nc, in_maps, core_ids=list(range(NCORE)))
    y1T = np.zeros((BT, D, S), np.float32)
    y1rows = np.zeros((NT, D), bf)
    idxf = np.zeros(NT, np.float32)
    gvf = np.zeros(NT, np.float32)
    for c in range(NCORE):
        sl = slice(c * 1024, (c + 1) * 1024)
        r = res.results[c]
        y1T[0][:, sl] = r["y1T"][:, :1024]; y1T[1][:, sl] = r["y1T"][:, 1024:]
        y1rows[c * 1024:(c + 1) * 1024] = r["y1rows"][:1024]
        y1rows[8192 + c * 1024:8192 + (c + 1) * 1024] = r["y1rows"][1024:]
        idxf[c * 1024:(c + 1) * 1024] = r["idxr"][0, :1024]
        idxf[8192 + c * 1024:8192 + (c + 1) * 1024] = r["idxr"][0, 1024:]
        gvf[c * 1024:(c + 1) * 1024] = r["gvr"][0, :1024]
        gvf[8192 + c * 1024:8192 + (c + 1) * 1024] = r["gvr"][0, 1024:]
    return y1T, y1rows, idxf, gvf


def build_route():
    nc = bass.Bass("TRN2", target_bir_lowering=False)
    idxf = nc.dram_tensor("idxf", [1, NT], F32, kind="ExternalInput").ap()
    gvf = nc.dram_tensor("gvf", [1, NT], F32, kind="ExternalInput").ap()
    keepgv = nc.dram_tensor("keepgv", [1, NT], F32, kind="ExternalOutput").ap()
    slotT = nc.dram_tensor("slotT", [128, 128], U32, kind="ExternalOutput").ap()
    src2 = nc.dram_tensor("src2", [16384 + 8 * 14336 + 2, 2], U32,
                          kind="ExternalOutput").ap()
    slsc = nc.dram_tensor("slsc", [NT], U32, kind="Internal").ap()
    CH = 4096
    NCH = NT // CH
    sch = Sched()
    with ExitStack() as ctx:
        e = ctx.enter_context
        ld = e(nc.semaphore("ld"))
        vs = e(nc.semaphore("vs"))
        ts = e(nc.semaphore("ts"))
        gs = e(nc.semaphore("gs"))
        sto = e(nc.semaphore("sto"))
        idx8 = e(nc.sbuf_tensor("idx8", [8, CH], F32))
        oneh = e(nc.sbuf_tensor("oneh", [8, CH], F32))
        ones8w = e(nc.sbuf_tensor("ones8w", [8, CH], F32))
        csum = e(nc.sbuf_tensor("csum", [8, CH], F32))
        car8 = e(nc.sbuf_tensor("car8", [8, 1], F32))
        iota8 = e(nc.sbuf_tensor("iota8", [8, 1], F32))
        ones8 = e(nc.sbuf_tensor("ones8", [8, 1], F32))
        locc = e(nc.sbuf_tensor("locc", [1, CH], F32))
        keepc = e(nc.sbuf_tensor("keepc", [1, CH], F32))
        gvc = e(nc.sbuf_tensor("gvc", [1, CH], F32))
        slotr = e(nc.sbuf_tensor("slotr", [1, CH], F32))
        t1r = e(nc.sbuf_tensor("t1r", [1, CH], F32))
        slotu = e(nc.sbuf_tensor("slotu", [1, CH], U32))
        idxTc = e(nc.sbuf_tensor("idxTc", [128, CH // 128], U32))
        iodup = e(nc.sbuf_tensor("iodup", [128, 2 * (CH // 128)], U32))
        sltt = e(nc.sbuf_tensor("sltt", [128, 128], U32))
        psL = e(nc.psum_tensor("psL", [1, 512], F32))
        n_i = sch.emit("gpsimd", lambda e: e.iota(
            out=iota8[:], pattern=[[0, 1]], base=0, channel_multiplier=1,
            allow_small_or_imprecise_dtypes=True), inc=(gs, 1))
        sch.emit("vector", lambda e: e.memset(ones8w[:], 1.0))
        n_z = sch.emit("vector", lambda e: e.memset(car8[:], 0.0), inc=(vs, 1))
        n_z = sch.emit("vector", lambda e: e.memset(ones8[:], 1.0), inc=(vs, 1))
        sch.wait("vector", gs, n_i)
        sch.wait("tensor", vs, n_z)
        n_stp = 0
        for ch in range(NCH):
            c0 = ch * CH
            sch.wait("sync", sto, n_stp)
            for r in range(8):
                nl = sch.emit("sync", lambda e, r=r, c0=c0: e.dma_start(
                    out=idx8[r:r + 1, :], in_=idxf[:, c0:c0 + CH]), inc=(ld, 16))
            nl = sch.emit("sync", lambda e, c0=c0: e.dma_start(
                out=gvc[:], in_=gvf[:, c0:c0 + CH]), inc=(ld, 16))
            sch.wait("vector", ld, nl)
            n_oh = sch.emit("vector", lambda e: e.tensor_scalar(
                out=oneh[:], in0=idx8[:], scalar1=iota8[:], scalar2=None,
                op0=AX.is_equal), inc=(vs, 1))
            sch.wait("vector", vs, n_oh)
            ini = 0.0 if ch == 0 else car8[:]
            n_cs = sch.emit("vector", lambda e, ini=ini: e.tensor_tensor_scan(
                out=csum[:], data0=ones8w[:], data1=oneh[:], initial=ini,
                op0=AX.mult, op1=AX.add), inc=(vs, 1))
            sch.wait("vector", vs, n_cs)
            n_cr = sch.emit("vector", lambda e: e.tensor_copy(
                out=car8[:], in_=csum[:, CH - 1:CH]), inc=(vs, 1))
            sch.wait("vector", vs, n_cr)
            n_tm = sch.emit("vector", lambda e: e.tensor_mul(
                out=oneh[:], in0=oneh[:], in1=csum[:]), inc=(vs, 1))
            sch.wait("tensor", vs, n_tm)
            for sub in range(CH // 512):
                tL = sch.emit("tensor", lambda e, sub=sub: e.matmul(
                    psL[:], ones8[:], oneh[:, sub * 512:(sub + 1) * 512],
                    start=True, stop=True), inc=(ts, 1))
                sch.wait("vector", ts, tL)
                n_lc = sch.emit("vector", lambda e, sub=sub: e.tensor_scalar(
                    out=locc[:, sub * 512:(sub + 1) * 512], in0=psL[:],
                    scalar1=-1.0, scalar2=None, op0=AX.add), inc=(vs, 1))
                sch.wait("tensor", vs, n_lc)
            sch.emit("vector", lambda e: e.tensor_single_scalar(
                out=keepc[:], in_=locc[:], scalar=2048.0, op=AX.is_lt))
            n_kg = sch.emit("vector", lambda e: e.tensor_mul(
                out=gvc[:], in0=gvc[:], in1=keepc[:]), inc=(vs, 1))
            sch.wait("sync", vs, n_kg)
            n_st1 = sch.emit("sync", lambda e, c0=c0: e.dma_start(
                out=keepgv[:, c0:c0 + CH], in_=gvc[:]), inc=(sto, 16))
            sch.emit("vector", lambda e: e.scalar_tensor_tensor(
                out=slotr[:], in0=idx8[0:1, :], scalar=2048.0, in1=locc[:],
                op0=AX.mult, op1=AX.add))
            sch.emit("vector", lambda e: e.scalar_tensor_tensor(
                out=t1r[:], in0=idx8[0:1, :], scalar=14336.0, in1=locc[:],
                op0=AX.mult, op1=AX.add))
            n_s1 = sch.emit("vector", lambda e: e.tensor_scalar(
                out=t1r[:], in0=t1r[:], scalar1=14336.0, scalar2=None,
                op0=AX.add), inc=(vs, 1))
            sch.wait("vector", vs, n_s1)
            sch.emit("vector", lambda e: e.tensor_sub(
                out=slotr[:], in0=slotr[:], in1=t1r[:]))
            sch.emit("vector", lambda e: e.tensor_mul(
                out=slotr[:], in0=slotr[:], in1=keepc[:]))
            sch.emit("vector", lambda e: e.tensor_add(
                out=slotr[:], in0=slotr[:], in1=t1r[:]))
            n_su = sch.emit("vector", lambda e: e.tensor_copy(
                out=slotu[:], in_=slotr[:]), inc=(vs, 1))
            sch.wait("sync", vs, n_su)
            n_sd = sch.emit("sync", lambda e, c0=c0: e.dma_start(
                out=slsc[c0:c0 + CH].rearrange("(one t) -> one t", one=1),
                in_=slotu[:]), inc=(sto, 16))
            sch.wait("sync", sto, n_sd)
            sch.wait("gpsimd", sto, n_stp)
            n_io = sch.emit("gpsimd", lambda e, c0=c0: e.iota(
                out=iodup[:].rearrange("p (i c) -> p i c", c=2),
                pattern=[[128, CH // 128], [0, 2]], base=c0,
                channel_multiplier=1,
                allow_small_or_imprecise_dtypes=True), inc=(gs, 1))
            for i in range(CH // 128):
                n_ci = sch.emit("sync", lambda e, c0=c0, i=i: e.dma_start(
                    out=idxTc[:, i:i + 1],
                    in_=slsc[c0 + i * 128:c0 + i * 128 + 128].rearrange(
                        "(p one) -> p one", one=1)), inc=(ld, 16))
            sch.wait("gpsimd", ld, n_ci)
            for i in range(CH // 128):
                n_sc = sch.emit("gpsimd", lambda e, i=i: e.indirect_dma_start(
                    out=src2,
                    out_offset=bass.IndirectOffsetOnAxis(
                        ap=idxTc[:, i:i + 1], axis=0),
                    in_=iodup[:, 2 * i:2 * i + 2], in_offset=None),
                    inc=(sto, 16))
            n_stp = sch.n(sto)
            sch.wait("sync", sto, n_stp)
        for col in range(128):
            n_tr = sch.emit("sync", lambda e, col=col: e.dma_start(
                out=sltt[:, col:col + 1],
                in_=slsc[col * 128:col * 128 + 128].rearrange(
                    "(p one) -> p one", one=1)), inc=(ld, 16))
        sch.wait("vector", ld, n_tr)
        n_cl = sch.emit("vector", lambda e: e.tensor_single_scalar(
            out=sltt[:], in_=sltt[:], scalar=16383.0, op=AX.min), inc=(vs, 1))
        sch.wait("sync", vs, n_cl)
        n_f = sch.emit("sync", lambda e: e.dma_start(
            out=slotT, in_=sltt[:]), inc=(sto, 16))
        sch.wait("sync", sto, sch.n(sto))
        sch.check_deadlock()
        sch.replay(nc)
    return nc


def build_expert():
    nc = bass.Bass("TRN2", target_bir_lowering=False)
    y1r = nc.dram_tensor("y1r", [NT, D], BF16, kind="ExternalInput").ap()
    src2 = nc.dram_tensor("src2", [2048, 2], U32, kind="ExternalInput").ap()
    w1e = nc.dram_tensor("w1e", [D, 4096], F32, kind="ExternalInput").ap()
    b1e = nc.dram_tensor("b1e", [4096], F32, kind="ExternalInput").ap()
    w2e = nc.dram_tensor("w2e", [4096, D], F32, kind="ExternalInput").ap()
    b2e = nc.dram_tensor("b2e", [D], F32, kind="ExternalInput").ap()
    erow = nc.dram_tensor("erow", [2048, D], BF16, kind="ExternalOutput").ap()
    KD = 8
    KF = 32
    CAP = 2048
    sch = Sched()
    with ExitStack() as ctx:
        e = ctx.enter_context
        ld = e(nc.semaphore("ld"))
        vs = e(nc.semaphore("vs"))
        ts = e(nc.semaphore("ts"))
        as_ = e(nc.semaphore("as"))
        gs = e(nc.semaphore("gs"))
        sto = e(nc.semaphore("sto"))
        w1b = e(nc.sbuf_tensor("w1b", [128, KD * 4096], BF16))
        w2b = e(nc.sbuf_tensor("w2b", [128, KF * D], BF16))
        b1c = e(nc.sbuf_tensor("b1c", [128, KF], F32))
        b2c = e(nc.sbuf_tensor("b2c", [128, KD], F32))
        idxt = e(nc.sbuf_tensor("idxt", [128, 16], U32))
        identb = e(nc.sbuf_tensor("identb", [128, 128], BF16))
        with nc.sbuf_tensor("wstg", [128, KD * 2048], F32) as wstg:
            for q in range(2):
                nl = sch.emit("sync", lambda e, q=q: e.dma_start(
                    out=wstg[:].rearrange("p (k m) -> p k m", k=KD),
                    in_=w1e[:, q * 2048:(q + 1) * 2048].rearrange(
                        "(k p) m -> p k m", p=128)), inc=(ld, 16))
                sch.wait("vector", ld, nl)
                def cst(e, q=q):
                    dst = w1b[:].rearrange("p (k m) -> p k m", k=KD)[
                        :, :, q * 2048:(q + 1) * 2048]
                    return e.tensor_copy(
                        out=dst, in_=wstg[:].rearrange("p (k m) -> p k m", k=KD))
                n_c = sch.emit("vector", cst, inc=(vs, 1))
                sch.wait("sync", vs, n_c)
            for q in range(2):
                nl = sch.emit("sync", lambda e, q=q: e.dma_start(
                    out=wstg[:, 0:KF * 512].rearrange("p (k m) -> p k m", k=KF),
                    in_=w2e[:, q * 512:(q + 1) * 512].rearrange(
                        "(k p) m -> p k m", p=128)), inc=(ld, 16))
                sch.wait("vector", ld, nl)
                def cst2(e, q=q):
                    dst = w2b[:].rearrange("p (k m) -> p k m", k=KF)[
                        :, :, q * 512:(q + 1) * 512]
                    return e.tensor_copy(
                        out=dst,
                        in_=wstg[:, 0:KF * 512].rearrange("p (k m) -> p k m", k=KF))
                n_c = sch.emit("vector", cst2, inc=(vs, 1))
                sch.wait("sync", vs, n_c)
        dspT = e(nc.sbuf_tensor("dspT", [128, KD * CAP], BF16))
        hsb = e(nc.sbuf_tensor("hsb", [128, KF * 512], BF16))
        yty = e(nc.sbuf_tensor("yty", [128, KD * 512], BF16))
        rowst = e(nc.sbuf_tensor("rowst", [128, KD * 128], BF16))
        dsp2 = [e(nc.sbuf_tensor(f"dsp{i}", [128, D], BF16)) for i in range(2)]
        psH2 = [e(nc.psum_tensor(f"psH{i}", [128, 512], F32)) for i in range(2)]
        psTr2 = [e(nc.psum_tensor(f"psTr{i}", [128, 128], BF16)) for i in range(2)]
        nl = sch.emit("sync", lambda e: e.dma_start(
            out=b1c[:], in_=b1e.rearrange("(m p) -> p m", p=128),
            allow_slow_non_contiguous=True), inc=(ld, 16))
        nl = sch.emit("sync", lambda e: e.dma_start(
            out=b2c[:], in_=b2e.rearrange("(m p) -> p m", p=128),
            allow_slow_non_contiguous=True), inc=(ld, 16))
        nid = sch.emit("gpsimd", lambda e: e.iota(
            out=identb[:], pattern=[[-1, 128]], base=0, channel_multiplier=1,
            allow_small_or_imprecise_dtypes=True), inc=(gs, 1))
        sch.wait("vector", gs, nid)
        n_id2 = sch.emit("vector", lambda e: e.tensor_single_scalar(
            out=identb[:], in_=identb[:], scalar=0.0, op=AX.is_equal), inc=(vs, 1))
        sch.wait("tensor", vs, n_id2)
        cpo = [0, 0]
        nl = sch.emit("sync", lambda e: e.dma_start(
            out=idxt[:], in_=src2[0:2048, 0:1].rearrange("(i p) c -> p (i c)", p=128),
            allow_slow_non_contiguous=True), inc=(ld, 16))
        sch.wait("gpsimd", ld, nl)
        sch.wait("gpsimd", vs, sch.n(vs))
        cp_done = [0, 0]
        tr_done = [0, 0]
        for i in range(16):
            db = i % 2
            sch.wait("gpsimd", ts, tr_done[db])
            n_g = sch.emit("gpsimd", lambda e, i=i, db=db: e.indirect_dma_start(
                out=dsp2[db][:], out_offset=None, in_=y1r,
                in_offset=bass.IndirectOffsetOnAxis(ap=idxt[:, i:i + 1], axis=0)),
                inc=(gs, 16))
            sch.wait("tensor", gs, sch.n(gs))
            for k in range(KD):
                pt = k % 2
                sch.wait("tensor", vs, cp_done[pt])
                tT = sch.emit("tensor", lambda e, k=k, pt=pt, db=db: e.transpose(
                    psTr2[pt][:], dsp2[db][:, k * 128:(k + 1) * 128], identb[:]),
                    inc=(ts, 1))
                sch.wait("vector", ts, tT)
                cp_done[pt] = sch.emit("vector", lambda e, k=k, i=i, pt=pt:
                                       e.tensor_copy(
                    out=dspT[:, k * CAP + i * 128:k * CAP + i * 128 + 128],
                    in_=psTr2[pt][:]), inc=(vs, 1))
            tr_done[db] = tT
        n_stp = 0
        for tsu in range(4):
            t0 = tsu * 512
            ev_h = [0, 0]
            for m in range(KF):
                pb = m % 2
                sch.wait("tensor", as_, ev_h[pb])
                for k in range(KD):
                    tH = sch.emit("tensor", lambda e, m=m, k=k, t0=t0, pb=pb:
                                  e.matmul(
                        psH2[pb][:],
                        w1b[:, k * 4096 + m * 128:k * 4096 + m * 128 + 128],
                        dspT[:, k * CAP + t0:k * CAP + t0 + 512],
                        start=(k == 0), stop=(k == KD - 1)),
                        inc=(ts, 1) if k == KD - 1 else None)
                sch.wait("scalar", ts, tH)
                ev_h[pb] = sch.emit("scalar", lambda e, m=m, pb=pb: e.activation(
                    out=hsb[:, m * 512:(m + 1) * 512], in_=psH2[pb][:],
                    func=AF.Identity, bias=b1c[:, m:m + 1]), inc=(as_, 1))
            sch.wait("tensor", as_, max(ev_h))
            ev_y = [0, 0]
            for m in range(KD):
                pb = m % 2
                sch.wait("tensor", as_, ev_y[pb])
                for k in range(KF):
                    tY = sch.emit("tensor", lambda e, m=m, k=k, pb=pb: e.matmul(
                        psH2[pb][:], w2b[:, k * D + m * 128:k * D + m * 128 + 128],
                        hsb[:, k * 512:(k + 1) * 512],
                        start=(k == 0), stop=(k == KF - 1)),
                        inc=(ts, 1) if k == KF - 1 else None)
                sch.wait("scalar", ts, tY)
                ev_y[pb] = sch.emit("scalar", lambda e, m=m, pb=pb: e.activation(
                    out=yty[:, m * 512:(m + 1) * 512], in_=psH2[pb][:],
                    func=AF.Identity, bias=b2c[:, m:m + 1]), inc=(as_, 1))
                sch.wait("tensor", as_, ev_y[pb]) if False else None
            sch.wait("vector", as_, max(ev_y))
            for tt in range(4):
                sch.wait("vector", sto, sch.n(sto))
                for k in range(KD):
                    pt = k % 2
                    sch.wait("tensor", vs, cpo[pt])
                    tT = sch.emit("tensor", lambda e, tt=tt, k=k, pt=pt: e.transpose(
                        psTr2[pt][:],
                        yty[:, k * 512 + tt * 128:k * 512 + tt * 128 + 128],
                        identb[:]), inc=(ts, 1))
                    sch.wait("vector", ts, tT)
                    cpo[pt] = sch.emit("vector", lambda e, k=k, pt=pt: e.tensor_copy(
                        out=rowst[:, k * 128:(k + 1) * 128], in_=psTr2[pt][:]),
                        inc=(vs, 1))
                sch.wait("sync", vs, sch.n(vs))
                n_stp = sch.emit("sync", lambda e, t0=t0, tt=tt: e.dma_start(
                    out=erow[t0 + tt * 128:t0 + tt * 128 + 128, :],
                    in_=rowst[:]), inc=(sto, 16))
            sch.wait("scalar", sto, n_stp)
        sch.wait("sync", sto, sch.n(sto))
        sch.check_deadlock()
        sch.replay(nc)
    return nc


def build_combine():
    nc = bass.Bass("TRN2", target_bir_lowering=False)
    yall = nc.dram_tensor("yall", [NT, D], BF16, kind="ExternalInput").ap()
    slotTs = nc.dram_tensor("slotTs", [128, 16], U32, kind="ExternalInput").ap()
    kgs = nc.dram_tensor("kgs", [1, TB], F32, kind="ExternalInput").ap()
    y1Ts = nc.dram_tensor("y1Ts", [D, TB], F32, kind="ExternalInput").ap()
    x3Ts = nc.dram_tensor("x3Ts", [D, TB], F32, kind="ExternalInput").ap()
    g2 = nc.dram_tensor("g2", [D], F32, kind="ExternalInput").ap()
    b2a = nc.dram_tensor("b2a", [D], F32, kind="ExternalInput").ap()
    outT = nc.dram_tensor("outT", [D, TB], F32, kind="ExternalOutput").ap()
    KD = 8
    sch = Sched()
    with ExitStack() as ctx:
        e = ctx.enter_context
        ld = e(nc.semaphore("ld"))
        vs = e(nc.semaphore("vs"))
        ts = e(nc.semaphore("ts"))
        as_ = e(nc.semaphore("as"))
        gs = e(nc.semaphore("gs"))
        sto = e(nc.semaphore("sto"))
        t = ln_tiles(nc, e, sch, TSUB)
        t["vs"], t["ts"], t["as_"] = vs, ts, as_
        emit_ln = make_ln(sch, t)
        ln_init(sch, t, ld, vs)
        ymT = e(nc.sbuf_tensor("ymT", [128, KD * TB], BF16))
        idxt = e(nc.sbuf_tensor("idxt", [128, 16], U32))
        identb = e(nc.sbuf_tensor("identb", [128, 128], BF16))
        dsp2 = [e(nc.sbuf_tensor(f"dsp{i}", [128, D], BF16)) for i in range(2)]
        gcol = e(nc.sbuf_tensor("gcol", [128, KD], F32))
        bcol = e(nc.sbuf_tensor("bcol", [128, KD], F32))
        kgr = e(nc.sbuf_tensor("kgr", [1, TB], F32))
        y1s = e(nc.sbuf_tensor("y1s", [128, KD * TSUB], F32))
        x3s = e(nc.sbuf_tensor("x3s", [128, KD * TSUB], F32))
        kb = e(nc.sbuf_tensor("kb", [128, TSUB], F32))
        onesr1 = e(nc.sbuf_tensor("onesr1", [1, 128], F32))
        psB = e(nc.psum_tensor("psB", [128, TSUB], F32))
        psTr2 = [e(nc.psum_tensor(f"psTr{i}", [128, 128], BF16)) for i in range(2)]
        nl = sch.emit("sync", lambda e: e.dma_start(
            out=gcol[:], in_=g2.rearrange("(m p) -> p m", p=128),
            allow_slow_non_contiguous=True), inc=(ld, 16))
        nl = sch.emit("sync", lambda e: e.dma_start(
            out=bcol[:], in_=b2a.rearrange("(m p) -> p m", p=128),
            allow_slow_non_contiguous=True), inc=(ld, 16))
        nl = sch.emit("sync", lambda e: e.dma_start(out=idxt[:], in_=slotTs),
                      inc=(ld, 16))
        nl = sch.emit("sync", lambda e: e.dma_start(out=kgr[:], in_=kgs),
                      inc=(ld, 16))
        sch.wait("vector", ld, nl)
        sch.emit("vector", lambda e: e.memset(onesr1[:], 1.0))
        ln_init(sch, t, ld, vs)
        nid = sch.emit("gpsimd", lambda e: e.iota(
            out=identb[:], pattern=[[-1, 128]], base=0, channel_multiplier=1,
            allow_small_or_imprecise_dtypes=True), inc=(gs, 1))
        sch.wait("vector", gs, nid)
        n_id2 = sch.emit("vector", lambda e: e.tensor_single_scalar(
            out=identb[:], in_=identb[:], scalar=0.0, op=AX.is_equal), inc=(vs, 1))
        sch.wait("tensor", vs, n_id2)
        sch.wait("gpsimd", ld, nl)
        cp_done = [0, 0]
        tr_done = [0, 0]
        for i in range(16):
            db = i % 2
            sch.wait("gpsimd", ts, tr_done[db])
            n_g = sch.emit("gpsimd", lambda e, i=i, db=db: e.indirect_dma_start(
                out=dsp2[db][:], out_offset=None, in_=yall,
                in_offset=bass.IndirectOffsetOnAxis(ap=idxt[:, i:i + 1], axis=0)),
                inc=(gs, 16))
            sch.wait("tensor", gs, sch.n(gs))
            for k in range(KD):
                pt = k % 2
                sch.wait("tensor", vs, cp_done[pt])
                tT = sch.emit("tensor", lambda e, k=k, pt=pt, db=db: e.transpose(
                    psTr2[pt][:], dsp2[db][:, k * 128:(k + 1) * 128], identb[:]),
                    inc=(ts, 1))
                sch.wait("vector", ts, tT)
                cp_done[pt] = sch.emit("vector", lambda e, k=k, i=i, pt=pt:
                                       e.tensor_copy(
                    out=ymT[:, k * TB + i * 128:k * TB + i * 128 + 128],
                    in_=psTr2[pt][:]), inc=(vs, 1))
            tr_done[db] = tT
        n_sto = 0
        for su in range(TB // TSUB):
            t0 = su * TSUB
            sch.wait("sync", sto, n_sto)
            nl = sch.emit("sync", lambda e, t0=t0: e.dma_start(
                out=y1s[:].rearrange("p (k t) -> p k t", k=KD),
                in_=y1Ts[:, t0:t0 + TSUB].rearrange("(k p) t -> p k t", p=128)),
                inc=(ld, 16))
            nl = sch.emit("sync", lambda e, t0=t0: e.dma_start(
                out=x3s[:].rearrange("p (k t) -> p k t", k=KD),
                in_=x3Ts[:, t0:t0 + TSUB].rearrange("(k p) t -> p k t", p=128)),
                inc=(ld, 16))
            sch.wait("tensor", ld, nl)
            tB = sch.emit("tensor", lambda e, t0=t0: e.matmul(
                psB[:], onesr1[:], kgr[:, t0:t0 + TSUB], start=True, stop=True),
                inc=(ts, 1))
            sch.wait("vector", ts, tB)
            sch.wait("vector", ld, nl)
            n_kb = sch.emit("vector", lambda e: e.tensor_copy(
                out=kb[:], in_=psB[:]), inc=(vs, 1))
            for m in range(KD):
                sch.emit("vector", lambda e, m=m, t0=t0: e.tensor_mul(
                    out=t["d1"][:], in0=ymT[:, m * TB + t0:m * TB + t0 + TSUB],
                    in1=kb[:]))
                n_r = sch.emit("vector", lambda e, m=m: e.tensor_add(
                    out=y1s[:, m * TSUB:(m + 1) * TSUB],
                    in0=y1s[:, m * TSUB:(m + 1) * TSUB], in1=t["d1"][:]),
                    inc=(vs, 1))
            emit_ln(y1s, y1s, None, gcol, bcol)
            n_a2 = sch.emit("vector", lambda e: e.tensor_add(
                out=x3s[:], in0=x3s[:], in1=y1s[:]), inc=(vs, 1))
            sch.wait("tensor", vs, n_a2)
            emit_ln(x3s, x3s)
            n_f = sch.n(vs)
            sch.wait("sync", vs, n_f)
            n_sto = sch.emit("sync", lambda e, t0=t0: e.dma_start(
                out=outT[:, t0:t0 + TSUB].rearrange("(k p) t -> p k t", p=128),
                in_=x3s[:].rearrange("p (k t) -> p k t", k=KD)), inc=(sto, 16))
        sch.wait("sync", sto, sch.n(sto))
        sch.check_deadlock()
        sch.replay(nc)
    return nc


def run_route(idxf, gvf):
    nc = cached_build("route", build_route)
    im = {"idxf": idxf.reshape(1, NT).astype(np.float32),
          "gvf": gvf.reshape(1, NT).astype(np.float32)}
    res = run_bass_kernel_spmd(nc, [im] * NCORE, core_ids=list(range(NCORE)))
    r = res.results[0]
    return (np.asarray(r["keepgv"])[0], np.asarray(r["slotT"]),
            np.asarray(r["src2"]))


def run_expert(y1rows, src2, w1, b1, w2, b2):
    nc = cached_build("expert", build_expert)
    in_maps = []
    for c in range(NCORE):
        in_maps.append({
            "y1r": y1rows,
            "src2": np.ascontiguousarray(src2[c * 2048:(c + 1) * 2048]),
            "w1e": np.ascontiguousarray(w1[c]),
            "b1e": np.ascontiguousarray(b1[c]),
            "w2e": np.ascontiguousarray(w2[c]),
            "b2e": np.ascontiguousarray(b2[c]),
        })
    res = run_bass_kernel_spmd(nc, in_maps, core_ids=list(range(NCORE)))
    return np.concatenate(
        [np.asarray(res.results[c]["erow"]) for c in range(NCORE)], axis=0)


def run_combine(yall, slotT, keepgv, y1T, x3T, g2, b2a):
    nc = cached_build("combine", build_combine)
    in_maps = []
    for c in range(NCORE):
        sl = slice(c * 1024, (c + 1) * 1024)
        slotTs = np.ascontiguousarray(np.concatenate(
            [slotT[:, c * 8:(c + 1) * 8], slotT[:, 64 + c * 8:64 + (c + 1) * 8]],
            axis=1))
        kgs = np.concatenate(
            [keepgv[c * 1024:(c + 1) * 1024],
             keepgv[8192 + c * 1024:8192 + (c + 1) * 1024]]).reshape(1, TB)
        in_maps.append({
            "yall": yall, "slotTs": slotTs,
            "kgs": np.ascontiguousarray(kgs.astype(np.float32)),
            "y1Ts": np.ascontiguousarray(np.concatenate(
                [y1T[0][:, sl], y1T[1][:, sl]], axis=1)),
            "x3Ts": np.ascontiguousarray(np.concatenate(
                [x3T[0][:, sl], x3T[1][:, sl]], axis=1)),
            "g2": g2, "b2a": b2a,
        })
    res = run_bass_kernel_spmd(nc, in_maps, core_ids=list(range(NCORE)))
    outT = np.zeros((BT, D, S), np.float32)
    for c in range(NCORE):
        sl = slice(c * 1024, (c + 1) * 1024)
        outT[0][:, sl] = res.results[c]["outT"][:, :1024]
        outT[1][:, sl] = res.results[c]["outT"][:, 1024:]
    return outT


def kernel(L0_attn_x, L0_attn_Wq, L0_attn_Wk, L0_attn_Wv, L0_attn_Wo,
           L0_ffn_kernel, L0_ffn_input_scale, L0_ffn_kernel_scale, L0_ffn_bias,
           L1_attn_Wr, L1_attn_Wk, L1_attn_Wv, L1_attn_Wo, L1_attn_W1,
           L1_attn_b1, L1_attn_W2, L1_attn_b2, L1_attn_ln1_g, L1_attn_ln1_b,
           L1_attn_ln2_g, L1_attn_ln2_b, L1_ffn_in_proj_w, L1_ffn_in_proj_b,
           L1_ffn_out_proj_w, L1_ffn_out_proj_b, L1_ffn_ln1_g, L1_ffn_ln1_b,
           L1_ffn_ln2_g, L1_ffn_ln2_b, L1_ffn_wg, L1_ffn_w1, L1_ffn_b1,
           L1_ffn_w2, L1_ffn_b2):
    x = np.asarray(L0_attn_x, np.float32)
    f32 = lambda a: np.asarray(a, np.float32)
    oT = run_l0_attn(x, f32(L0_attn_Wq), f32(L0_attn_Wk), f32(L0_attn_Wv))
    x2T, x2Tb = run_l0_rest(
        x, oT, f32(L0_attn_Wo), np.asarray(L0_ffn_kernel),
        float(np.asarray(L0_ffn_input_scale)[0]),
        float(np.asarray(L0_ffn_kernel_scale)[0]), np.asarray(L0_ffn_bias))
    yT = run_rkv(x2Tb, f32(L1_attn_Wr), f32(L1_attn_Wk), f32(L1_attn_Wv))
    x1T, x1Tb, h1T = run_postrkv(
        x2T, yT, f32(L1_attn_Wo), f32(L1_attn_ln1_g), f32(L1_attn_ln1_b),
        f32(L1_attn_W1), f32(L1_attn_b1))
    s1 = run_lif_phase(h1T, 32, 2, 8)
    h2T = run_w2(s1, f32(L1_attn_W2), f32(L1_attn_b2))
    fsp = run_lif_phase(h2T, 64, 1, 2)
    x3T, x3Tb = run_h1(x1T, fsp, x2T, f32(L1_attn_ln2_g), f32(L1_attn_ln2_b))
    y1T, y1rows, idxf, gvf = run_h2(
        x3T, x3Tb, f32(L1_ffn_in_proj_w), f32(L1_ffn_in_proj_b),
        f32(L1_ffn_out_proj_w), f32(L1_ffn_out_proj_b),
        f32(L1_ffn_ln1_g), f32(L1_ffn_ln1_b), f32(L1_ffn_wg))
    keepgv, slotT, src2 = run_route(idxf, gvf)
    yall = run_expert(y1rows, src2, f32(L1_ffn_w1), f32(L1_ffn_b1),
                      f32(L1_ffn_w2), f32(L1_ffn_b2))
    outT = run_combine(yall, slotT, keepgv, y1T, x3T,
                       f32(L1_ffn_ln2_g), f32(L1_ffn_ln2_b))
    return np.ascontiguousarray(outT.transpose(0, 2, 1))
